# version 04
# baseline (speedup 1.0000x reference)
import numpy as np
import ml_dtypes
import concourse.bass as bass
import concourse.mybir as mybir
from concourse.bass_utils import run_bass_kernel_spmd

F32 = mybir.dt.float32
BF16 = mybir.dt.bfloat16
AF = mybir.ActivationFunctionType
OP = mybir.AluOpType
AX = mybir.AxisListType
BF16NP = ml_dtypes.bfloat16

T = 2048
D = 1024
KT = 8
CH = 8
LR = 0.02

ENGS = ("PE", "DVE", "ACT", "POOL", "SP")


class Sem:
    def __init__(self, h, name):
        self.h = h; self.name = name; self.n = 0


class Builder:
    def __init__(self, nc):
        self.nc = nc
        self.q = {e: [] for e in ENGS}
        self.sems = {}
        self.last_wait = {e: {} for e in ENGS}
        self._ctx = []

    def sem(self, name):
        ctx = self.nc.semaphore(name)
        h = ctx.__enter__()
        self._ctx.append(ctx)
        s = Sem(h, name)
        self.sems[name] = s
        return s

    def rec(self, eng, make, inc=None, inc_n=1):
        if inc is not None:
            inc.n += inc_n
            def f(e, make=make, s=inc, k=inc_n):
                make(e).then_inc(s.h, k)
            self.q[eng].append(f)
        else:
            self.q[eng].append(lambda e, make=make: make(e))

    def wait(self, eng, sem, val=None):
        if val is None:
            val = sem.n
        lw = self.last_wait[eng]
        if lw.get(sem.name, -1) >= val:
            return
        lw[sem.name] = val
        self.q[eng].append(lambda e, s=sem, v=val: e.wait_ge(s.h, v))

    def wait_many(self, eng, toks):
        for (sem, val) in toks:
            self.wait(eng, sem, val)

    def play(self):
        with self.nc.Block() as block:
            @block.tensor
            def _(e):
                for f in self.q["PE"]: f(e)
            @block.vector
            def _(e):
                for f in self.q["DVE"]: f(e)
            @block.scalar
            def _(e):
                for f in self.q["ACT"]: f(e)
            @block.gpsimd
            def _(e):
                for f in self.q["POOL"]: f(e)
            @block.sync
            def _(e):
                for f in self.q["SP"]: f(e)


def build_program(n_stages=4, taps=(), sub4="all"):
    nc = bass.Bass("TRN2", target_bir_lowering=False)
    b = Builder(nc)

    di = {}
    def din(name, shape, dtype=F32):
        di[name] = nc.dram_tensor(name, list(shape), dtype, kind="ExternalInput").ap()
        return di[name]

    din("x_tf", [T, D]); din("x_ft", [D, T])
    din("WbT", [D, D]); din("WpT", [D, D])
    din("initBT", [D, 16]); din("initA", [D, 16]); din("initAT", [16, D]); din("initB", [16, D])
    din("g0", [128, KT]); din("be0", [128, KT])
    din("routerT", [D, 8]); din("w1", [D, 8 * D]); din("w2", [8 * D, D])
    din("n1w", [128, KT]); din("n2w", [128, KT])
    din("ipw", [D, 4 * D]); din("ipb", [128, 32])
    din("convw", [128, 48]); din("convb", [128, 16])
    din("dtw", [2 * D, 64]); din("dtb", [64, 1])
    din("bpw", [2 * D, 64]); din("bpb", [64, 1])
    din("cpw", [2 * D, 64]); din("cpb", [64, 1])
    din("s2iw", [64, 2 * D]); din("s2ib", [128, 16]); din("Dpar", [128, 16])
    din("outw", [2 * D, D]); din("outb", [128, KT])
    din("gatew", [D, 4]); din("gateb", [4, 1]); din("gateb_bc", [128, 4]); din("eb2all", [4, 1024], BF16)
    din("ew1", [4, D, 4 * D]); din("eb1", [4, 128, 32])
    din("ew2", [4, 4 * D, D]); din("eb2", [4, 128, KT])
    din("Wg", [D, 4 * D]); din("bg", [128, 32])
    din("Wu", [D, 4 * D]); din("bu", [128, 32])
    din("Wd", [4 * D, D]); din("bd", [128, KT])
    din("Wq", [D, 512]); din("keysT", [64, 512], BF16)
    din("deT", [D, 4 * D]); din("ue", [4 * D, D])
    din("eye_b", [128, 128], BF16); din("eye_f", [128, 128])
    din("sel8", [8, 1024])
    din("ones_col", [128, 1]); din("ones_row", [1, 128])
    din("constv", [128, 8])
    din("mask_prev", [8, 1]); din("mask_next_b", [8, 1], BF16)

    y_out = nc.dram_tensor("y", [D, T], F32, kind="ExternalOutput").ap()
    dbg = {}
    for name, shape, dt in taps:
        dbg[name] = nc.dram_tensor("dbg_" + name, list(shape), dt, kind="ExternalOutput").ap()

    cc_src_d = nc.dram_tensor("cc_src_d", [128, 8], BF16).ap()
    cc_dst_d = nc.dram_tensor("cc_dst_d", [8 * 128, 8], BF16).ap()
    cc_srcA = nc.dram_tensor("cc_srcA", [16, D], F32).ap()
    cc_dstA = nc.dram_tensor("cc_dstA", [8 * 16, D], F32).ap()
    cc_srcB = nc.dram_tensor("cc_srcB", [128, 128], F32).ap()
    cc_dstB = nc.dram_tensor("cc_dstB", [8 * 128, 128], F32).ap()
    cc_srcH = nc.dram_tensor("cc_srcH", [128, KT * 2], F32).ap()
    cc_dstH = nc.dram_tensor("cc_dstH", [8 * 128, KT * 2], F32).ap()
    cc_srcS = nc.dram_tensor("cc_srcS", [64, 1], F32).ap()
    cc_dstS = nc.dram_tensor("cc_dstS", [8 * 64, 1], F32).ap()
    scr_xm = nc.dram_tensor("scr_xm", [2 * D, T], BF16).ap()
    scr_sg = nc.dram_tensor("scr_sg", [2 * D, T], BF16).ap()

    sbctx = []
    def sbuf(name, shape, dtype=F32):
        ctx = nc.sbuf_tensor(name, list(shape), dtype)
        t = ctx.__enter__(); sbctx.append(ctx)
        return t

    AF32 = sbuf("AF32", [128, 24576])
    ABF = sbuf("ABF", [128, 45056], BF16)
    wbuf = sbuf("wbuf", [128, 2 * 4096], BF16)
    kc = sbuf("kc", [128, 448])
    selc = sbuf("selc", [8, 1024])
    kcb = sbuf("kcb", [128, 264], BF16)

    def f32v(off, cols, part=128, p0=0):
        return AF32[p0:p0 + part, off:off + cols]
    def bfv(off, cols, part=128, p0=0):
        return ABF[p0:p0 + part, off:off + cols]

    xs_o = 0
    def xs(k):
        return AF32[:, xs_o + k * T: xs_o + (k + 1) * T]
    B1_o = 16384
    SQ_o = 17408
    SM_o = 18432
    ACCA_o = 18944
    ACCB_o = 19968
    FREE_F32 = 20096

    eye_bc = kcb[:, 0:128]
    ones_col = kc[:, 0:1]
    ones_row = kc[0:1, 8:136]
    g0c = kc[:, 136:144]
    be0c = kc[:, 144:152]
    maskbc = kc[:, 152:160]
    n1wc = kc[:, 160:168]
    n2wc = kc[:, 168:176]
    eye_fc = kc[:, 184:312]
    CONSTS = [0.0, 1.0, 2.0, 1e-5, 1e-6, -1.0, 0.5, -0.5]
    for ci_, cv_ in enumerate(CONSTS):
        nc.const_aps.register(F32, cv_, kc[:, 176 + ci_:177 + ci_])

    psctx = []
    def psum(name, shape, dtype=F32):
        ctx = nc.psum_tensor(name, list(shape), dtype)
        t = ctx.__enter__(); psctx.append(ctx)
        return t

    psA = psum("psA", [128, 512])
    psB = psum("psB", [128, 512])
    psC = psum("psC", [128, 1024])
    psD = psum("psD", [128, 512])
    psT = psum("psT", [128, 2048], BF16)

    PSMAIN = [psA, psB]

    s_sp = b.sem("s_sp"); s_sc = b.sem("s_sc"); s_gd = b.sem("s_gd")
    s_cst = b.sem("s_cst"); s_ini = b.sem("s_ini"); s_gsl = b.sem("s_gsl")
    s_sg = [b.sem("s_sg0"), b.sem("s_sg1")]
    s_wl = [b.sem("s_wl0"), b.sem("s_wl1")]
    s_pe = b.sem("s_pe"); s_dve = b.sem("s_dve"); s_act = b.sem("s_act")
    s_pool = b.sem("s_pool"); s_cc = b.sem("s_cc")

    def tok(sem):
        return (sem, sem.n)

    GATED_DMA_SEMS = ("s_sp", "s_sc", "s_cst", "s_ini", "s_gsl")

    def dma(stream, sem, out, in_, **kw):
        b.rec(stream, lambda e, o=out, i=in_, kw=kw: e.dma_start(out=o, in_=i, **kw),
              inc=sem, inc_n=16)
        if sem.name in GATED_DMA_SEMS:
            b.wait(stream, sem)
        return tok(sem)

    def tap(name, src_ap, after=()):
        if name in dbg:
            b.wait_many("SP", after)
            dma("SP", s_sp, dbg[name], src_ap)
            for e_ in ENGS:
                b.wait(e_, s_sp)

    def fence_all(extra=()):
        toks = [tok(s_pe), tok(s_dve), tok(s_act), tok(s_pool), tok(s_sp), tok(s_sc),
                tok(s_gd), tok(s_wl[0]), tok(s_wl[1]), tok(s_cc)] + list(extra)
        for e in ENGS:
            b.wait_many(e, toks)

    class WStream:
        def __init__(self):
            self.idx = 0
            self.freed = [[], []]
        def load(self, pieces, n_k):
            par = self.idx % 2; self.idx += 1
            base = par * 4096
            for (sem, val) in self.freed[par]:
                b.wait("POOL", sem, val)
            self.freed[par] = []
            aps = pieces if isinstance(pieces, list) else [pieces]
            tot = sum(a.shape[-1] for a in aps)
            assert n_k * tot <= 4096, (n_k, tot)
            c0 = 0
            for a in aps:
                cw = a.shape[-1]
                dst = wbuf[:, base:base + n_k * tot].rearrange(
                    "p (k c) -> p k c", k=n_k)[:, :, c0:c0 + cw]
                b.rec("POOL", lambda e, o=dst, i=a.rearrange("(k p) c -> p k c", p=128):
                      e.dma_start(out=o, in_=i), inc=s_wl[par], inc_n=16)
                c0 += cw
            t = tok(s_wl[par])
            def view(k, c0=0, cw=None, base=base, tot=tot):
                cw = tot - c0 if cw is None else cw
                return wbuf[:, base + k * tot + c0: base + k * tot + c0 + cw]
            return view, t, par
        def consumed(self, par, t):
            self.freed[par].append(t)

    ws = WStream()

    def ln_full(src_fn, kt, n_feat, eps, with_mean, dst_rows, wait_toks=(), n_tok=T):
        rstd_row, negmur_row = dst_rows
        nn = (n_tok + 511) // 512
        pe_hist = []
        for n in range(nn):
            n0 = n * 512
            nw = min(512, n_tok - n0)
            b.wait("PE", s_dve)
            b.wait("PE", s_act)
            for k in range(kt):
                slot = k % 2
                sq = AF32[:, SQ_o + slot * 512: SQ_o + slot * 512 + nw]
                if len(pe_hist) >= 2:
                    b.wait("ACT", s_pe, pe_hist[-2])
                elif len(pe_hist) == 0:
                    b.wait("ACT", s_pe)
                b.wait_many("ACT", wait_toks)
                b.rec("ACT", lambda e, i=src_fn(k)[:, n0:n0 + nw], o=sq:
                      e.activation(out=o, in_=i, func=AF.Square), inc=s_act)
                b.wait("PE", s_act)
                b.wait_many("PE", wait_toks)
                if with_mean:
                    b.rec("PE", lambda e, r=src_fn(k)[:, n0:n0 + nw], p=psD[0:1, 0:nw],
                                 st=(k == 0), lt=(k == kt - 1):
                          e.matmul(p, ones_col, r, start=st, stop=lt))
                b.rec("PE", lambda e, r=sq, p=psC[0:1, 0:nw], st=(k == 0), lt=(k == kt - 1):
                      e.matmul(p, ones_col, r, start=st, stop=lt), inc=s_pe)
                pe_hist.append(s_pe.n)
            b.wait("DVE", s_pe)
            if with_mean:
                b.rec("DVE", lambda e, p=psD[0:1, 0:nw], o=rowA(nw) if False else negmur_row[:, n0:n0 + nw]:
                      e.tensor_scalar(o, p, 1.0 / n_feat, None, OP.mult), inc=s_dve)
                b.wait("DVE", s_dve)
                b.rec("DVE", lambda e, mu=negmur_row[:, n0:n0 + nw], o=rstd_row[:, n0:n0 + nw]:
                      e.tensor_tensor(o, mu, mu, OP.mult), inc=s_dve)
                b.wait("DVE", s_dve)
                b.rec("DVE", lambda e, p=psC[0:1, 0:nw], m2=rstd_row[:, n0:n0 + nw],
                             o=rstd_row[:, n0:n0 + nw]:
                      e.scalar_tensor_tensor(o, p, 1.0 / n_feat, m2, OP.mult, OP.subtract), inc=s_dve)
            else:
                b.rec("DVE", lambda e, p=psC[0:1, 0:nw], o=rstd_row[:, n0:n0 + nw]:
                      e.tensor_scalar(o, p, 1.0 / n_feat, None, OP.mult), inc=s_dve)
        dve_t = tok(s_dve)
        b.wait_many("ACT", [dve_t])
        b.rec("ACT", lambda e, v=rstd_row, o=rstd_row:
              e.activation(out=o, in_=v, func=AF.Ln, bias=float(eps)), inc=s_act)
        b.wait("ACT", s_act)
        b.rec("ACT", lambda e, v=rstd_row, o=rstd_row:
              e.activation(out=o, in_=v, func=AF.Exp, scale=-0.5), inc=s_act)
        t_act = tok(s_act)
        if with_mean:
            b.wait_many("DVE", [t_act])
            b.rec("DVE", lambda e, mu=negmur_row, r=rstd_row, o=negmur_row:
                  e.scalar_tensor_tensor(o, mu, -1.0, r, OP.mult, OP.mult), inc=s_dve)
            return tok(s_dve)
        return t_act

    def bcast_chunk(row_ap, n0, nw, dst):
        b.wait("PE", s_dve)
        b.wait("PE", s_act)
        b.rec("PE", lambda e, r=row_ap[:, n0:n0 + nw], p=psD[:, 0:nw]:
              e.matmul(p, ones_row, r, start=True, stop=True), inc=s_pe)
        b.wait("DVE", s_pe)
        b.rec("DVE", lambda e, p=psD[:, 0:nw], o=dst[:, 0:nw]:
              e.tensor_copy(o, p), inc=s_dve)
        return tok(s_dve)

    dma("SP", s_sp, kc[:, 0:1], di["ones_col"])
    dma("SP", s_sp, kc[0:1, 8:136], di["ones_row"])
    dma("SP", s_sp, kc[:, 136:144], di["g0"])
    dma("SP", s_sp, kc[:, 144:152], di["be0"])
    dma("SP", s_sp, kc[:, 152:160],
        di["mask_prev"].rearrange("r c -> (r c)").partition_broadcast(128))
    dma("SP", s_sp, kc[:, 160:168], di["n1w"])
    dma("SP", s_sp, kc[:, 168:176], di["n2w"])
    dma("SP", s_sp, kc[:, 176:184], di["constv"])
    dma("SP", s_sp, kc[:, 184:312], di["eye_f"])
    dma("SP", s_sp, selc[:], di["sel8"])
    dma("POOL", s_cst, kcb[:, 0:128], di["eye_b"])
    dma("POOL", s_cst, kcb[0:8, 128:129], di["mask_next_b"])
    const_t = tok(s_sp); const_tb = tok(s_cst)

    for k in range(KT):
        dma("SP", s_sp, xs(k), di["x_ft"][k * 128:(k + 1) * 128, :])
    x_t = tok(s_sp)

    XN_o = 0
    Tp = T + 1
    def xn(k):
        return ABF[:, XN_o + k * Tp: XN_o + (k + 1) * Tp]
    XBC_o = 16392
    def xbc(k):
        return ABF[:, XBC_o + k * 512: XBC_o + (k + 1) * 512]
    VC_o = 20488
    def vch(par, k):
        return ABF[:, VC_o + par * 2048 + k * 256: VC_o + par * 2048 + (k + 1) * 256]
    S3_o = 24584
    DAB_o = 28680
    DBT_o = 36872
    B2T_o = 37896
    MID_o = 38024
    TMP_o = 40072
    INIT_o = 41096

    def dab(c):
        return ABF[0:16, DAB_o + c * 1024: DAB_o + (c + 1) * 1024]

    r_rstd = AF32[0:1, FREE_F32:FREE_F32 + T]
    r_negmur = AF32[0:1, FREE_F32 + T:FREE_F32 + 2 * T]
    ln1_t = ln_full(xs, KT, D, 1e-5, True, (r_rstd, r_negmur), wait_toks=[x_t, const_t])

    B1c = AF32[:, B1_o:B1_o + 512]
    B2c = AF32[:, B1_o + 512:B1_o + 1024]
    FT1 = AF32[:, SQ_o:SQ_o + 512]
    for n in range(4):
        n0 = n * 512
        b.wait_many("PE", [ln1_t])
        bcast_chunk(r_rstd, n0, 512, B1c)
        bcast_chunk(r_negmur, n0, 512, B2c)
        bc_t = tok(s_dve)
        b.wait_many("DVE", [bc_t, x_t, const_t])
        for k in range(KT):
            b.rec("DVE", lambda e, k=k, n0=n0: e.tensor_tensor(
                FT1, xs(k)[:, n0:n0 + 512], B1c, OP.mult), inc=s_dve)
            b.wait("DVE", s_dve)
            b.rec("DVE", lambda e, k=k, n0=n0: e.tensor_tensor(
                FT1, FT1, B2c, OP.add), inc=s_dve)
            b.wait("DVE", s_dve)
            b.rec("DVE", lambda e, k=k, n0=n0: e.tensor_scalar(
                xn(k)[:, n0:n0 + 512], FT1, g0c[:, k:k + 1], be0c[:, k:k + 1],
                OP.mult, OP.add), inc=s_dve)
            b.wait("DVE", s_dve)
    xn_t = tok(s_dve)

    b.wait("SP", s_pe)
    dma("SP", s_sp, AF32[0:16, FREE_F32 + 2048:FREE_F32 + 3072], di["initAT"])
    dma("SP", s_sp, AF32[:, FREE_F32 + 1152:FREE_F32 + 1280].rearrange("p (k c) -> p k c", k=KT),
        di["initBT"].rearrange("(k p) c -> p k c", p=128))
    init_tf = tok(s_sp)

    b.wait_many("DVE", [xn_t])
    for k in range(KT):
        b.rec("DVE", lambda e, k=k: e.tensor_copy(
            ABF[:, TMP_o + 96 + k:TMP_o + 97 + k], ABF[:, XN_o + k * Tp:XN_o + k * Tp + 1]),
            inc=s_dve)
    b.wait_many("SP", [tok(s_dve)])
    dma("SP", s_sp, cc_src_d, ABF[:, TMP_o + 96:TMP_o + 104])
    b.wait("POOL", s_sp)
    b.rec("POOL", lambda e: e.collective_compute(
        "AllGather", OP.bypass, replica_groups=[list(range(8))],
        ins=[cc_src_d[:].opt()], outs=[cc_dst_d[:].opt()]), inc=s_cc)
    b.wait("POOL", s_cc)
    gsl = ABF[:, TMP_o:TMP_o + 64]
    dma("POOL", s_gsl, gsl.rearrange("p (r k) -> p r k", r=8),
        cc_dst_d.rearrange("(r p) k -> p r k", p=128))
    mnb = ABF[:, TMP_o + 64:TMP_o + 72]
    dma("POOL", s_gsl, mnb, di["mask_next_b"].rearrange("r c -> (r c)").partition_broadcast(128))
    g_t = tok(s_gsl)
    b.wait_many("DVE", [g_t])
    b.rec("DVE", lambda e: e.tensor_tensor(
        gsl.rearrange("p (r k) -> p r k", r=8),
        gsl.rearrange("p (r k) -> p r k", r=8),
        mnb.rearrange("p (r o) -> p r o", o=1).broadcast_to((128, 8, 8)), OP.mult), inc=s_dve)
    b.wait("DVE", s_dve)
    def _red_shift(e):
        with nc.allow_low_precision(reason="0/1-mask select: single nonzero term"):
            return e.tensor_reduce(ABF[:, TMP_o + 72:TMP_o + 80],
                                   gsl.rearrange("p (r k) -> p k r", r=8), AX.X, OP.add)
    b.rec("DVE", _red_shift, inc=s_dve)
    b.wait("DVE", s_dve)
    for k in range(KT):
        b.rec("DVE", lambda e, k=k: e.tensor_copy(
            ABF[:, XN_o + k * Tp + T:XN_o + k * Tp + T + 1], ABF[:, TMP_o + 72 + k:TMP_o + 73 + k]),
            inc=s_dve)
    shift_t = tok(s_dve)

    initBTb = ABF[:, INIT_o:INIT_o + 128]
    initAb = ABF[:, INIT_o + 128:INIT_o + 256]
    dma("POOL", s_ini, initBTb.rearrange("p (k c) -> p k c", k=KT),
        di["initBT"].rearrange("(k p) c -> p k c", p=128))
    dma("POOL", s_ini, initAb.rearrange("p (k c) -> p k c", k=KT),
        di["initA"].rearrange("(k p) c -> p k c", p=128))
    init_tb = tok(s_ini)
    GA = AF32[0:16, FREE_F32:FREE_F32 + 1024]
    GB = AF32[:, FREE_F32 + 1024:FREE_F32 + 1152]
    initBT_f = AF32[:, FREE_F32 + 1152:FREE_F32 + 1280]
    initAT_f = AF32[0:16, FREE_F32 + 2048:FREE_F32 + 3072]

    wv0, wv0_t, _ = ws.load(di["WpT"][:, 0:512], KT)
    wv1, wv1_t, _ = ws.load(di["WpT"][:, 512:1024], KT)
    def wpT(k, m):
        if m < 4:
            return wv0(k, m * 128, 128)
        return wv1(k, (m - 4) * 128, 128)

    projin = ABF[:, INIT_o + 256:INIT_o + 256 + 32]
    projerr = ABF[:, INIT_o + 288:INIT_o + 288 + 32]
    prev_evac = {}
    tp_i = 0
    tp_evac = {}
    for c in range(CH):
        par = c % 2
        t0 = c * 256
        b.wait_many("PE", [shift_t, wv0_t, wv1_t])
        for m in range(KT):
            ps = PSMAIN[m % 2]
            if (c, m) != (0, 0) and (c, m) != (0, 1):
                pm = (c, m - 2) if m >= 2 else (c - 1, m + 6)
                b.wait("PE", s_dve, prev_evac[pm][1])
            for k in range(KT):
                b.rec("PE", lambda e, k=k, m=m, t0=t0, ps=ps: e.matmul(
                    ps[:, 0:256], wpT(k, m), xn(k)[:, t0 + 1:t0 + 257],
                    start=(k == 0), stop=(k == KT - 1)), inc=(s_pe if k == KT - 1 else None))
            b.wait("DVE", s_pe)
            b.rec("DVE", lambda e, m=m, par=par, ps=ps: e.tensor_copy(
                vch(par, m), ps[:, 0:256]), inc=s_dve)
            prev_evac[(c, m)] = tok(s_dve)
        v_t = tok(s_dve)
        xT = lambda h: ABF[:, S3_o + 2048 + h * 1024: S3_o + 2048 + (h + 1) * 1024]
        b.wait("POOL", s_pe)
        dma("POOL", s_gd, ABF[:, S3_o + 2048:S3_o + 4096].rearrange("p (h f) -> p h f", h=2),
            di["x_tf"][t0:t0 + 256, :].rearrange("(h p) f -> p h f", p=128))
        xT_t = tok(s_gd)
        VT = lambda h: ABF[:, S3_o + h * 1024: S3_o + (h + 1) * 1024]
        for h in range(2):
            for k in range(KT):
                slot = tp_i % 2
                b.wait_many("PE", [v_t, const_tb])
                if tp_i >= 2:
                    b.wait("PE", s_dve, tp_evac[tp_i - 2])
                b.rec("PE", lambda e, h=h, k=k, par=par, slot=slot: e.transpose(
                    psT[:, slot * 1024:slot * 1024 + 128],
                    vch(par, k)[:, h * 128:(h + 1) * 128], eye_bc),
                    inc=s_pe)
                b.wait("DVE", s_pe)
                b.rec("DVE", lambda e, h=h, k=k, slot=slot: e.tensor_copy(
                    VT(h)[:, k * 128:(k + 1) * 128], psT[:, slot * 1024:slot * 1024 + 128]),
                    inc=s_dve)
                tp_evac[tp_i] = s_dve.n
                tp_i += 1
        vt_t = tok(s_dve)
        b.wait("DVE", s_pe)
        b.wait_many("DVE", [x_t])
        for k in range(KT):
            b.rec("DVE", lambda e, k=k, t0=t0: e.tensor_copy(
                xbc(k)[:, 0:256], xs(k)[:, t0:t0 + 256]), inc=s_dve)
        xc_t = tok(s_dve)
        b.wait_many("PE", [xc_t, init_tb, v_t])
        b.wait("PE", s_dve)
        for h in range(2):
            if h == 1:
                b.wait("PE", s_dve)
            for k in range(KT):
                b.rec("PE", lambda e, h=h, k=k: e.matmul(
                    psC[0:128, 0:16], xbc(k)[:, h * 128:(h + 1) * 128],
                    initBTb.rearrange("p (k c) -> p k c", k=KT)[:, k, :],
                    start=(k == 0), stop=(k == KT - 1)))
                b.rec("PE", lambda e, h=h, k=k, par=par: e.matmul(
                    psC[0:128, 512:528], vch(par, k)[:, h * 128:(h + 1) * 128],
                    initAb.rearrange("p (k c) -> p k c", k=KT)[:, k, :],
                    start=(k == 0), stop=(k == KT - 1)), inc=(s_pe if k == KT - 1 else None))
            b.wait("DVE", s_pe)
            b.rec("DVE", lambda e, h=h: e.tensor_copy(projin[:, h * 16:(h + 1) * 16], psC[0:128, 0:16]), inc=s_dve)
            b.rec("DVE", lambda e, h=h: e.tensor_copy(projerr[:, h * 16:(h + 1) * 16], psC[0:128, 512:528]), inc=s_dve)
        pr_t = tok(s_dve)
        b.wait_many("PE", [pr_t, vt_t, xT_t])
        for h in range(2):
            for nn2 in range(2):
                b.rec("PE", lambda e, h=h, nn2=nn2: e.matmul(
                    psC[0:16, nn2 * 512:(nn2 + 1) * 512], projin[:, h * 16:(h + 1) * 16],
                    VT(h)[:, nn2 * 512:(nn2 + 1) * 512], start=(h == 0), stop=(h == 1)))
        for h in range(2):
            for m in range(KT):
                b.rec("PE", lambda e, m=m, h=h: e.matmul(
                    psD[0:128, m * 16:(m + 1) * 16],
                    xT(h)[:, m * 128:(m + 1) * 128], projerr[:, h * 16:(h + 1) * 16],
                    start=(h == 0 and m == 0), stop=(h == 1 and m == KT - 1)),
                    inc=(s_pe if (m == KT - 1 and h == 1) else None))
        b.wait("DVE", s_pe)
        b.rec("DVE", lambda e, c=c: e.tensor_copy(dab(c), psC[0:16, 0:1024]), inc=s_dve)
        b.rec("DVE", lambda e, c=c: e.tensor_copy(
            ABF[:, DBT_o + c * 128:DBT_o + (c + 1) * 128], psD[0:128, 0:128]), inc=s_dve)
    d_t = tok(s_dve)
    ws.consumed(0, tok(s_pe))
    ws.consumed(1, tok(s_pe))

    accA = AF32[0:16, ACCA_o:ACCA_o + 1024]
    accBT = AF32[:, ACCB_o:ACCB_o + 128]
    b.wait_many("DVE", [d_t])
    b.rec("DVE", lambda e: e.memset(accA, 0.0), inc=s_dve)
    b.rec("DVE", lambda e: e.memset(accBT, 0.0), inc=s_dve)
    for c in range(CH):
        b.wait("DVE", s_dve)
        b.rec("DVE", lambda e, c=c: e.tensor_tensor(accA, accA, dab(c), OP.add), inc=s_dve)
        b.rec("DVE", lambda e, c=c: e.tensor_tensor(
            accBT, accBT, ABF[:, DBT_o + c * 128:DBT_o + (c + 1) * 128], OP.add), inc=s_dve)
    sum_t = tok(s_dve)
    b.wait_many("SP", [sum_t])
    dma("SP", s_sp, cc_srcA, accA)
    dma("SP", s_sp, cc_srcB, accBT)
    b.wait("POOL", s_sp)
    b.rec("POOL", lambda e: e.collective_compute(
        "AllGather", OP.bypass, replica_groups=[list(range(8))],
        ins=[cc_srcA[:].opt()], outs=[cc_dstA[:].opt()]), inc=s_cc)
    b.rec("POOL", lambda e: e.collective_compute(
        "AllGather", OP.bypass, replica_groups=[list(range(8))],
        ins=[cc_srcB[:].opt()], outs=[cc_dstB[:].opt()]), inc=s_cc)
    cc_ab = tok(s_cc)

    b.wait_many("DVE", [cc_ab])
    b.rec("DVE", lambda e: e.memset(accA, 0.0), inc=s_dve)
    b.rec("DVE", lambda e: e.memset(accBT, 0.0), inc=s_dve)
    for r in range(8):
        b.wait_many("SP", [cc_ab, tok(s_dve)])
        dma("SP", s_sp, GA, cc_dstA[r * 16:(r + 1) * 16, :])
        dma("SP", s_sp, GB, cc_dstB[r * 128:(r + 1) * 128, :])
        b.wait("DVE", s_sp)
        b.rec("DVE", lambda e, r=r: e.scalar_tensor_tensor(
            accA, GA, maskbc[0:16, r:r + 1], accA, OP.mult, OP.add), inc=s_dve)
        b.rec("DVE", lambda e, r=r: e.scalar_tensor_tensor(
            accBT, GB, maskbc[0:128, r:r + 1], accBT, OP.mult, OP.add), inc=s_dve)
    acc_t = tok(s_dve)

    midv = lambda c: ABF[0:16, MID_o + c * 256:MID_o + (c + 1) * 256]
    B2Tb = ABF[:, B2T_o:B2T_o + 128]
    coefcol = AF32[:, SM_o:SM_o + 1]
    sq16 = AF32[0:16, SQ_o:SQ_o + 1024]
    sq128 = AF32[:, B1_o:B1_o + 128]
    b.wait_many("DVE", [init_tf])
    for c in range(CH):
        dep = acc_t if c == 0 else tok(s_dve)
        b.wait_many("ACT", [dep])
        b.rec("ACT", lambda e: e.activation(out=sq16, in_=accA, func=AF.Square), inc=s_act)
        b.rec("ACT", lambda e: e.activation(out=sq128, in_=accBT, func=AF.Square), inc=s_act)
        b.wait("DVE", s_act)
        b.rec("DVE", lambda e: e.tensor_reduce(AF32[0:16, SM_o + 8:SM_o + 9], sq16, AX.X, OP.add), inc=s_dve)
        b.rec("DVE", lambda e: e.tensor_reduce(AF32[0:128, SM_o + 9:SM_o + 10], sq128, AX.X, OP.add), inc=s_dve)
        b.wait("PE", s_dve)
        b.rec("PE", lambda e: e.matmul(psD[0:1, 0:1], ones_col[0:16, :], AF32[0:16, SM_o + 8:SM_o + 9],
                                       start=True, stop=False))
        b.rec("PE", lambda e: e.matmul(psD[0:1, 0:1], ones_col, AF32[0:128, SM_o + 9:SM_o + 10],
                                       start=False, stop=True), inc=s_pe)
        b.wait("ACT", s_pe)
        b.rec("ACT", lambda e: e.activation(out=AF32[0:1, SM_o + 10:SM_o + 11], in_=psD[0:1, 0:1],
                                            func=AF.Sqrt, scale=LR * LR), inc=s_act)
        b.wait("DVE", s_act)
        b.rec("DVE", lambda e: e.tensor_scalar(AF32[0:1, SM_o + 11:SM_o + 12],
                                               AF32[0:1, SM_o + 10:SM_o + 11], 1e-6, None, OP.add), inc=s_dve)
        b.wait("DVE", s_dve)
        b.rec("DVE", lambda e: e.reciprocal(AF32[0:1, SM_o + 12:SM_o + 13],
                                            AF32[0:1, SM_o + 11:SM_o + 12]), inc=s_dve)
        b.wait("DVE", s_dve)
        b.rec("DVE", lambda e: e.tensor_scalar(AF32[0:1, SM_o + 13:SM_o + 14],
                                               AF32[0:1, SM_o + 12:SM_o + 13], 1.0, None, OP.min), inc=s_dve)
        b.wait("PE", s_dve)
        b.rec("PE", lambda e: e.matmul(psD[0:128, 4:5], ones_row, AF32[0:1, SM_o + 13:SM_o + 14],
                                       start=True, stop=True), inc=s_pe)
        b.wait("DVE", s_pe)
        b.rec("DVE", lambda e: e.tensor_scalar(coefcol, psD[0:128, 4:5], -LR, None, OP.mult), inc=s_dve)
        b.wait("DVE", s_dve)
        b.rec("DVE", lambda e, c=c: e.tensor_copy(ABF[0:16, TMP_o:TMP_o + 1024], dab(c)), inc=s_dve)
        b.wait("DVE", s_dve)
        b.rec("DVE", lambda e, c=c: e.scalar_tensor_tensor(
            dab(c), accA, coefcol[0:16, :], initAT_f, OP.mult, OP.add), inc=s_dve)
        b.rec("DVE", lambda e, c=c: e.scalar_tensor_tensor(
            B2Tb, accBT, coefcol[0:128, :], initBT_f, OP.mult, OP.add), inc=s_dve)
        ab_t = tok(s_dve)
        b.wait("DVE", s_dve)
        b.rec("DVE", lambda e, c=c: e.tensor_tensor(accA, accA, ABF[0:16, TMP_o:TMP_o + 1024], OP.add), inc=s_dve)
        b.rec("DVE", lambda e, c=c: e.tensor_tensor(
            accBT, accBT, ABF[:, DBT_o + c * 128:DBT_o + (c + 1) * 128], OP.add), inc=s_dve)
        b.wait("DVE", s_pe)
        b.wait_many("DVE", [x_t])
        for k in range(KT):
            b.rec("DVE", lambda e, k=k, c=c: e.tensor_copy(
                xbc(k)[:, 0:256], xs(k)[:, c * 256:c * 256 + 256]), inc=s_dve)
        xc2_t = tok(s_dve)
        b.wait_many("PE", [ab_t, xc2_t])
        for k in range(KT):
            b.rec("PE", lambda e, k=k, lt=(k == KT - 1): e.matmul(
                psC[0:16, 0:256], B2Tb[:, k * 16:(k + 1) * 16], xbc(k)[:, 0:256],
                start=(k == 0), stop=lt), inc=(s_pe if k == KT - 1 else None))
        b.wait("ACT", s_pe)
        b.rec("ACT", lambda e, c=c: e.activation(out=midv(c), in_=psC[0:16, 0:256],
                                                 func=AF.Copy, scale=2.0), inc=s_act)
        b.wait("DVE", s_pe)
    mid_t = tok(s_act)
    a2_t = tok(s_dve)

    wb0, wb0_t, wb0p = ws.load(di["WbT"][:, 0:512], KT)
    wb1, wb1_t, wb1p = ws.load(di["WbT"][:, 512:1024], KT)
    def wbT(k, m):
        return wb0(k, m * 128, 128) if m < 4 else wb1(k, (m - 4) * 128, 128)
    ev_prev = {}
    g = 0
    for n in range(4):
        n0 = n * 512
        b.wait_many("DVE", [x_t, mid_t])
        if n >= 1:
            b.wait("DVE", s_pe)
        for k in range(KT):
            b.rec("DVE", lambda e, k=k, n0=n0: e.tensor_copy(
                xbc(k), xs(k)[:, n0:n0 + 512]), inc=s_dve)
        cast_t = tok(s_dve)
        b.wait_many("PE", [cast_t, wb0_t, wb1_t, mid_t, a2_t])
        for m in range(KT):
            ps = PSMAIN[g % 2]
            if g >= 2:
                b.wait("PE", *ev_prev[g - 2])
            for k in range(KT):
                b.rec("PE", lambda e, k=k, m=m, ps=ps: e.matmul(
                    ps[:, 0:512], wbT(k, m), xbc(k), start=(k == 0), stop=False))
            for ci in range(2):
                c = n * 2 + ci
                b.rec("PE", lambda e, c=c, m=m, ci=ci, ps=ps: e.matmul(
                    ps[:, ci * 256:(ci + 1) * 256],
                    dab(c)[:, m * 128:(m + 1) * 128], midv(c),
                    start=False, stop=(ci == 1)), inc=(s_pe if ci == 1 else None))
            b.wait("DVE", s_pe)
            b.rec("DVE", lambda e, m=m, n0=n0, ps=ps: e.tensor_tensor(
                xs(m)[:, n0:n0 + 512], xs(m)[:, n0:n0 + 512], ps[:, 0:512], OP.add), inc=s_dve)
            ev_prev[g] = tok(s_dve)
            g += 1
    ws.consumed(wb0p, tok(s_dve))
    ws.consumed(wb1p, tok(s_dve))
    attn_t = tok(s_dve)
    tap("res1", AF32[:, 0:16384], [attn_t])

    def post_ln(src_tok):
        lt = ln_full(xs, KT, D, 1e-5, True, (r_rstd, r_negmur), wait_toks=[src_tok])
        for n in range(4):
            n0 = n * 512
            b.wait_many("PE", [lt])
            bcast_chunk(r_rstd, n0, 512, B1c)
            bcast_chunk(r_negmur, n0, 512, B2c)
            bc_t = tok(s_dve)
            b.wait_many("DVE", [bc_t])
            for k in range(KT):
                b.rec("DVE", lambda e, k=k, n0=n0: e.tensor_tensor(
                    FT1, xs(k)[:, n0:n0 + 512], B1c, OP.mult), inc=s_dve)
                b.wait("DVE", s_dve)
                b.rec("DVE", lambda e, k=k, n0=n0: e.tensor_tensor(
                    xs(k)[:, n0:n0 + 512], FT1, B2c, OP.add), inc=s_dve)
                b.wait("DVE", s_dve)
        return tok(s_dve)

    st1_t = post_ln(attn_t)
    tap("x1", AF32[:, 0:16384], [st1_t])
    last_t = st1_t

    if n_stages >= 2:
        def xb2(k):
            return ABF[:, k * T:(k + 1) * T]
        def rbuf(k):
            return ABF[:, 16384 + k * T:16384 + (k + 1) * T]
        routb = ABF[:, 32768:32832]
        B_cw = AF32[:, FREE_F32:FREE_F32 + T]
        cwF = AF32[0:8, FREE_F32 + 2048:FREE_F32 + 2048 + T]
        probs_all = AF32[:, SM_o:SM_o + 128]
        cw8_all = AF32[:, SM_o + 128:SM_o + 256]

        b.wait_many("DVE", [st1_t])
        b.wait("DVE", s_pe)
        b.wait("DVE", s_gsl)
        for k in range(KT):
            b.rec("DVE", lambda e, k=k: e.tensor_copy(xb2(k), xs(k)), inc=s_dve)
        sh_t = tok(s_dve)

        b.wait("POOL", s_pe)
        dma("POOL", s_ini, routb.rearrange("p (k c) -> p k c", k=KT),
            di["routerT"].rearrange("(k p) c -> p k c", p=128))
        rt_t = tok(s_ini)

        b.wait_many("PE", [sh_t, rt_t])
        for tt in range(16):
            b.wait("PE", s_dve)
            b.wait("PE", s_act)
            for k in range(KT):
                b.rec("PE", lambda e, tt=tt, k=k: e.matmul(
                    psC[0:128, 0:8], xb2(k)[:, tt * 128:(tt + 1) * 128],
                    routb.rearrange("p (k c) -> p k c", k=KT)[:, k, :],
                    start=(k == 0), stop=(k == KT - 1)), inc=(s_pe if k == KT - 1 else None))
            b.wait("ACT", s_pe)
            b.rec("ACT", lambda e, tt=tt: e.activation(
                out=probs_all[:, tt * 8:(tt + 1) * 8], in_=psC[0:128, 0:8],
                func=AF.Sigmoid), inc=s_act)
            b.wait("DVE", s_act)
            m8 = AF32[:, SM_o + 256:SM_o + 264]
            sA = AF32[:, SM_o + 264:SM_o + 265]
            sB = AF32[:, SM_o + 265:SM_o + 266]
            b.rec("DVE", lambda e, tt=tt: e.max(out=m8, in_=probs_all[:, tt * 8:(tt + 1) * 8]), inc=s_dve)
            b.wait("DVE", s_dve)
            b.rec("DVE", lambda e: e.tensor_tensor(sA, m8[:, 0:1], m8[:, 1:2], OP.add), inc=s_dve)
            b.wait("DVE", s_dve)
            b.rec("DVE", lambda e: e.tensor_scalar(sA, sA, 1e-20, None, OP.add), inc=s_dve)
            b.wait("DVE", s_dve)
            b.rec("DVE", lambda e: e.reciprocal(sB, sA), inc=s_dve)
            b.wait("DVE", s_dve)
            c1 = AF32[:, SM_o + 266:SM_o + 267]
            c2 = AF32[:, SM_o + 267:SM_o + 268]
            b.rec("DVE", lambda e: e.tensor_tensor(c1, m8[:, 0:1], sB, OP.mult), inc=s_dve)
            b.rec("DVE", lambda e: e.tensor_tensor(c2, m8[:, 1:2], sB, OP.mult), inc=s_dve)
            b.wait("DVE", s_dve)
            t8 = AF32[:, SM_o + 268:SM_o + 276]
            b.rec("DVE", lambda e, tt=tt: e.tensor_scalar(
                t8, probs_all[:, tt * 8:(tt + 1) * 8], m8[:, 0:1], c1, OP.is_equal, OP.mult), inc=s_dve)
            b.rec("DVE", lambda e, tt=tt: e.tensor_scalar(
                cw8_all[:, tt * 8:(tt + 1) * 8], probs_all[:, tt * 8:(tt + 1) * 8],
                m8[:, 1:2], c2, OP.is_equal, OP.mult), inc=s_dve)
            b.wait("DVE", s_dve)
            b.rec("DVE", lambda e, tt=tt: e.tensor_tensor(
                cw8_all[:, tt * 8:(tt + 1) * 8], cw8_all[:, tt * 8:(tt + 1) * 8], t8, OP.add), inc=s_dve)
            cw_t = tok(s_dve)
            b.wait_many("PE", [cw_t, const_t])
            b.rec("PE", lambda e, tt=tt: e.transpose(
                psC[0:8, 512:640], cw8_all[:, tt * 8:(tt + 1) * 8], eye_fc), inc=s_pe)
            b.wait("DVE", s_pe)
            b.rec("DVE", lambda e, tt=tt: e.tensor_copy(
                cwF[:, tt * 128:(tt + 1) * 128], psC[0:8, 512:640]), inc=s_dve)
        cwf_t = tok(s_dve)

        ev2 = {}
        g2 = 0
        for ex in range(8):
            b.wait_many("PE", [cwf_t, const_t])
            b.wait("PE", s_dve)
            for n in range(4):
                b.wait("PE", s_dve)
                b.rec("PE", lambda e, ex=ex, n=n: e.matmul(
                    psD[:, 0:512], selc[0:8, ex * 128:(ex + 1) * 128],
                    cwF[:, n * 512:(n + 1) * 512], start=True, stop=True), inc=s_pe)
                b.wait("DVE", s_pe)
                b.wait("DVE", s_pool)
                b.rec("DVE", lambda e, n=n: e.tensor_copy(
                    B_cw[:, n * 512:(n + 1) * 512], psD[:, 0:512]), inc=s_dve)
            bcw_t = tok(s_dve)
            for p in range(4):
                wv, wv_t, wv_par = ws.load(di["w1"][:, ex * D + p * 256: ex * D + (p + 1) * 256], KT)
                b.wait_many("PE", [wv_t, sh_t])
                for m2 in range(2):
                    m = p * 2 + m2
                    for n in range(4):
                        ps = PSMAIN[g2 % 2]
                        prev = ev2.get(g2 - 2)
                        if prev is not None:
                            b.wait("PE", *prev)
                        for k in range(KT):
                            b.rec("PE", lambda e, k=k, m2=m2, n=n, ps=ps, wv=wv: e.matmul(
                                ps[:, 0:512], wv(k, m2 * 128, 128), xb2(k)[:, n * 512:(n + 1) * 512],
                                start=(k == 0), stop=(k == KT - 1)), inc=(s_pe if k == KT - 1 else None))
                        b.wait("ACT", s_pe)
                        b.rec("ACT", lambda e, m=m, n=n, ps=ps: e.activation(
                            out=rbuf(m)[:, n * 512:(n + 1) * 512], in_=ps[:, 0:512],
                            func=AF.Relu), inc=s_act)
                        ev2[g2] = tok(s_act)
                        g2 += 1
                ws.consumed(wv_par, tok(s_act))
            relu_t = tok(s_act)
            b.wait_many("DVE", [relu_t])
            for k in range(KT):
                b.rec("DVE", lambda e, k=k: e.tensor_tensor(rbuf(k), rbuf(k), rbuf(k), OP.mult), inc=s_dve)
            sq_t = tok(s_dve)
            b.wait_many("POOL", [sq_t, bcw_t])
            for k in range(KT):
                b.rec("POOL", lambda e, k=k: e.tensor_tensor(rbuf(k), rbuf(k), B_cw, OP.mult), inc=s_pool)
            a_t = tok(s_pool)
            for p in range(4):
                wv, wv_t, wv_par = ws.load(di["w2"][ex * D:(ex + 1) * D, p * 256:(p + 1) * 256], KT)
                b.wait_many("PE", [wv_t, a_t])
                for m2 in range(2):
                    m = p * 2 + m2
                    for n in range(4):
                        ps = PSMAIN[g2 % 2]
                        prev = ev2.get(g2 - 2)
                        if prev is not None:
                            b.wait("PE", *prev)
                        for k in range(KT):
                            b.rec("PE", lambda e, k=k, m2=m2, n=n, ps=ps, wv=wv: e.matmul(
                                ps[:, 0:512], wv(k, m2 * 128, 128), rbuf(k)[:, n * 512:(n + 1) * 512],
                                start=(k == 0), stop=(k == KT - 1)), inc=(s_pe if k == KT - 1 else None))
                        b.wait("DVE", s_pe)
                        b.rec("DVE", lambda e, m=m, n=n, ps=ps: e.tensor_tensor(
                            xs(m)[:, n * 512:(n + 1) * 512], xs(m)[:, n * 512:(n + 1) * 512],
                            ps[:, 0:512], OP.add), inc=s_dve)
                        ev2[g2] = tok(s_dve)
                        g2 += 1
                ws.consumed(wv_par, tok(s_dve))
        moe_t = tok(s_dve)
        tap("res2", AF32[:, 0:16384], [moe_t])
        st2_t = post_ln(moe_t)
        tap("x2", AF32[:, 0:16384], [st2_t])
        last_t = st2_t

    if n_stages >= 3:
        DTv = AF32[0:64, 16384:16384 + T]
        BMv = AF32[0:64, 18432:18432 + T]
        CMv = AF32[0:64, 20480:20480 + T]
        SSv = AF32[0:64, 22528:22528 + T]
        XM_o = 0
        def xmt(k):
            return ABF[:, XM_o + k * T:XM_o + (k + 1) * T]
        HCH_o = 32768
        def hch(k):
            return ABF[:, HCH_o + k * 512:HCH_o + (k + 1) * 512]
        SGD_o = 36864
        DTW_o = 32768
        S2IW_o = 43008
        YN_o = 40960
        SGI_o = 32768
        halo_pre = kcb[:, 130:162]
        halo_save = kcb[:, 162:194]
        masksum = kc[:, 320:321]
        halo_x = kc[:, 328:344]
        halo_sq = kc[:, 344:360]
        halo_bc = kc[:, 360:362]
        halo_h = kcb[:, 194:210]

        b.wait_many("SP", [st2_t])
        dma("SP", s_sp, cc_srcH,
            AF32[:, 0:16384].rearrange("p (k t) -> p k t", k=KT)[:, :, T - 2:T])
        b.wait("POOL", s_sp)
        b.rec("POOL", lambda e: e.collective_compute(
            "AllGather", OP.bypass, replica_groups=[list(range(8))],
            ins=[cc_srcH[:].opt()], outs=[cc_dstH[:].opt()]), inc=s_cc)
        cc_h = tok(s_cc)
        b.wait_many("DVE", [const_t])
        b.rec("DVE", lambda e: e.tensor_reduce(masksum, maskbc, AX.X, OP.add), inc=s_dve)
        b.wait("POOL", *cc_h)
        b.wait("POOL", s_dve)
        gslH = ABF[:, SGD_o:SGD_o + 128]
        gh_f = AF32[:, 24448:24448 + 128]
        dma("POOL", s_gsl, gh_f.rearrange("p (r c) -> p r c", r=8),
            cc_dstH.rearrange("(r p) c -> p r c", p=128))
        ghf_t = tok(s_gsl)
        b.wait_many("DVE", [ghf_t])
        b.rec("DVE", lambda e: e.tensor_tensor(
            gh_f.rearrange("p (r c) -> p r c", r=8),
            gh_f.rearrange("p (r c) -> p r c", r=8),
            maskbc.rearrange("p (r o) -> p r o", o=1).broadcast_to((128, 8, 16)), OP.mult), inc=s_dve)
        b.wait("DVE", s_dve)
        b.rec("DVE", lambda e: e.tensor_reduce(
            halo_x, gh_f.rearrange("p (r c) -> p c r", r=8), AX.X, OP.add), inc=s_dve)
        halo_t = tok(s_dve)
        b.wait_many("ACT", [halo_t])
        b.rec("ACT", lambda e: e.activation(out=halo_sq, in_=halo_x, func=AF.Square), inc=s_act)
        b.wait("PE", s_act)
        b.wait("PE", s_dve)
        b.rec("PE", lambda e: e.matmul(psD[0:1, 0:16], ones_col, halo_sq, start=True, stop=True), inc=s_pe)
        b.wait("DVE", s_pe)
        b.rec("DVE", lambda e: e.tensor_reduce(
            AF32[0:1, 24400:24402], psD[0:1, 0:16].rearrange("p (k c) -> p c k", k=KT),
            AX.X, OP.add), inc=s_dve)
        b.wait("ACT", s_dve)
        b.rec("ACT", lambda e: e.activation(out=AF32[0:1, 24402:24404], in_=AF32[0:1, 24400:24402],
                                            func=AF.Ln, bias=1e-6, scale=1.0 / D), inc=s_act)
        b.wait("ACT", s_act)
        b.rec("ACT", lambda e: e.activation(out=AF32[0:1, 24402:24404], in_=AF32[0:1, 24402:24404],
                                            func=AF.Exp, scale=-0.5), inc=s_act)
        b.wait("PE", s_act)
        b.rec("PE", lambda e: e.matmul(psD[0:128, 16:18], ones_row, AF32[0:1, 24402:24404],
                                       start=True, stop=True), inc=s_pe)
        b.wait("DVE", s_pe)
        b.rec("DVE", lambda e: e.tensor_copy(AF32[:, 24404:24406], psD[0:128, 16:18]), inc=s_dve)
        b.wait("DVE", s_dve)
        b.rec("DVE", lambda e: e.tensor_tensor(
            halo_x.rearrange("p (k c) -> p k c", k=KT),
            halo_x.rearrange("p (k c) -> p k c", k=KT),
            AF32[:, 24404:24406].rearrange("p (o c) -> p o c", o=1).broadcast_to((128, 8, 2)), OP.mult), inc=s_dve)
        b.wait("DVE", s_dve)
        b.rec("DVE", lambda e: e.tensor_tensor(
            halo_h.rearrange("p (k c) -> p k c", k=KT),
            halo_x.rearrange("p (k c) -> p k c", k=KT),
            n1wc.rearrange("p (k o) -> p k o", o=1).broadcast_to((128, 8, 2)), OP.mult), inc=s_dve)
        haloh_t = tok(s_dve)

        b.wait("SP", s_act)
        b.wait("SP", s_dve)
        dma("SP", s_sp, AF32[:, 24512:24544], di["ipb"])
        ipb_t = tok(s_sp)
        ipbc = AF32[:, 24512:24544]
        r_rstd3 = AF32[0:1, 19456:19456 + T]
        rms1_t = ln_full(xs, KT, D, 1e-6, False, (r_rstd3, None), wait_toks=[st2_t])

        B1c3 = AF32[:, 18432:18432 + 512]
        FT3 = AF32[:, 18432 + 512:18432 + 1024]
        ga = 0
        ev3 = {}
        sg_par = 0
        b.wait_many("ACT", [ipb_t])
        for n in range(4):
            n0 = n * 512
            b.wait_many("PE", [rms1_t])
            bcast_chunk(r_rstd3, n0, 512, B1c3)
            bc3_t = tok(s_dve)
            b.wait_many("DVE", [bc3_t])
            b.wait("DVE", s_pe)
            for k in range(KT):
                b.rec("DVE", lambda e, k=k, n0=n0: e.tensor_tensor(
                    FT3, xs(k)[:, n0:n0 + 512], B1c3, OP.mult), inc=s_dve)
                b.wait("DVE", s_dve)
                b.rec("DVE", lambda e, k=k: e.tensor_scalar(
                    hch(k), FT3, n1wc[:, k:k + 1], None, OP.mult), inc=s_dve)
                b.wait("DVE", s_dve)
            hc_t = tok(s_dve)
            for p in range(16):
                wv, wv_t, wv_par = ws.load(di["ipw"][:, p * 256:(p + 1) * 256], KT)
                b.wait_many("PE", [wv_t, hc_t])
                if n == 0 and p < 8:
                    b.wait_many("PE", [haloh_t, const_t])
                for m2 in range(2):
                    m = p * 2 + m2
                    ps = PSMAIN[ga % 2]
                    prev = ev3.get(ga - 2)
                    if prev is not None:
                        b.wait("PE", *prev)
                    for k in range(KT):
                        b.rec("PE", lambda e, k=k, m2=m2, ps=ps, wv=wv: e.matmul(
                            ps[:, 0:512], wv(k, m2 * 128, 128), hch(k),
                            start=(k == 0), stop=(k == KT - 1)), inc=(s_pe if k == KT - 1 else None))
                    if m < 16:
                        b.wait("ACT", s_pe)
                        b.rec("ACT", lambda e, m=m, n0=n0, ps=ps: e.activation(
                            out=xmt(m)[:, n0:n0 + 512], in_=ps[:, 0:512],
                            func=AF.Identity, bias=ipbc[:, m:m + 1]), inc=s_act)
                        ev3[ga] = tok(s_act)
                    else:
                        spar = sg_par % 2
                        sgd = ABF[:, SGD_o + spar * 512:SGD_o + spar * 512 + 512]
                        b.wait("ACT", s_pe)
                        b.wait("ACT", s_sg[spar], 16 * (sg_par // 2))
                        b.rec("ACT", lambda e, ps=ps, sgd=sgd, m=m: e.activation(
                            out=sgd, in_=ps[:, 0:512], func=AF.Sigmoid,
                            bias=ipbc[:, m:m + 1]), inc=s_act)
                        ev3[ga] = tok(s_act)
                        b.wait_many("ACT", [ev3[ga]])
                        dma("ACT", s_sg[spar], scr_sg[(m - 16) * 128:(m - 16) * 128 + 128, n0:n0 + 512], sgd)
                        sg_par += 1
                    ga += 1
                if n == 0 and p < 8:
                    for m2 in range(2):
                        m = p * 2 + m2
                        b.wait("PE", s_dve)
                        for k in range(KT):
                            b.rec("PE", lambda e, k=k, m2=m2, wv=wv: e.matmul(
                                psD[0:128, 20:22], wv(k, m2 * 128, 128),
                                halo_h.rearrange("p (k c) -> p k c", k=KT)[:, k, :],
                                start=(k == 0), stop=(k == KT - 1)), inc=(s_pe if k == KT - 1 else None))
                        b.wait("DVE", s_pe)
                        b.wait_many("DVE", [ipb_t])
                        b.rec("DVE", lambda e, m=m: e.tensor_scalar(
                            halo_pre[:, m * 2:(m + 1) * 2], psD[0:128, 20:22],
                            ipbc[:, m:m + 1], masksum, OP.add, OP.mult), inc=s_dve)
                if n == 0 and p < 8:
                    ws.consumed(wv_par, tok(s_dve))
                else:
                    ws.consumed(wv_par, tok(s_act))
        inp_t = tok(s_act)
        inp_td = tok(s_dve)

        b.wait("POOL", s_pe)
        dma("POOL", s_ini, ABF[:, DTW_o:DTW_o + 1024].rearrange("p (k c) -> p k c", k=16),
            di["dtw"].rearrange("(k p) c -> p k c", p=128))
        dma("POOL", s_ini, ABF[:, DTW_o + 1024:DTW_o + 2048].rearrange("p (k c) -> p k c", k=16),
            di["bpw"].rearrange("(k p) c -> p k c", p=128))
        dma("POOL", s_ini, ABF[:, DTW_o + 2048:DTW_o + 3072].rearrange("p (k c) -> p k c", k=16),
            di["cpw"].rearrange("(k p) c -> p k c", p=128))
        b.wait("SP", s_act)
        b.wait("SP", s_pe)
        dma("SP", s_sp, AF32[:, 24528:24528 + 48], di["convw"])
        dma("SP", s_sp, AF32[:, 24512:24512 + 16], di["convb"])
        dma("SP", s_sp, AF32[0:64, 24448:24449], di["dtb"])
        dma("SP", s_sp, AF32[0:64, 24449:24450], di["bpb"])
        dma("SP", s_sp, AF32[0:64, 24450:24451], di["cpb"])
        w3_t = tok(s_ini); w3f_t = tok(s_sp)
        convw_c = AF32[:, 24528:24528 + 48]
        convb_c = AF32[:, 24512:24512 + 16]
        CS1 = ABF[:, SGD_o + 1024:SGD_o + 1536]
        CS2 = ABF[:, SGD_o + 1536:SGD_o + 2048]
        for c in range(8):
            t0 = c * 256
            for kk in range(16):
                xk = xmt(kk)
                w0 = convw_c[:, kk * 3:kk * 3 + 1]
                w1_ = convw_c[:, kk * 3 + 1:kk * 3 + 2]
                w2_ = convw_c[:, kk * 3 + 2:kk * 3 + 3]
                csA = (CS1 if kk % 2 == 0 else CS2)[:, 0:256]
                csB = (CS1 if kk % 2 == 0 else CS2)[:, 256:512]
                b.wait_many("POOL", [inp_t, inp_td, w3f_t])
                b.wait("POOL", s_act)
                b.wait("POOL", s_dve)
                hsrc = halo_pre if c == 0 else halo_save
                b.rec("POOL", lambda e, hsrc=hsrc, w0=w0, csA=csA, kk=kk: e.tensor_scalar(
                    csA[:, 0:2], hsrc[:, kk * 2:(kk + 1) * 2], w0, None, OP.mult), inc=s_pool)
                b.rec("POOL", lambda e, xk=xk, w0=w0, csA=csA, t0=t0: e.tensor_scalar(
                    csA[:, 2:256], xk[:, t0:t0 + 254], w0, None, OP.mult), inc=s_pool)
                b.rec("POOL", lambda e, hsrc=hsrc, w1_=w1_, csB=csB, kk=kk: e.tensor_scalar(
                    csB[:, 0:1], hsrc[:, kk * 2 + 1:kk * 2 + 2], w1_, None, OP.mult), inc=s_pool)
                b.rec("POOL", lambda e, xk=xk, w1_=w1_, csB=csB, t0=t0: e.tensor_scalar(
                    csB[:, 1:256], xk[:, t0:t0 + 255], w1_, None, OP.mult), inc=s_pool)
                b.wait("POOL", s_pool)
                b.rec("POOL", lambda e, csA=csA, csB=csB: e.tensor_tensor(
                    csA[:, 0:256], csA[:, 0:256], csB[:, 0:256], OP.add), inc=s_pool)
                b.wait("POOL", s_pool)
                b.rec("POOL", lambda e, xk=xk, w2_=w2_, csB=csB, t0=t0: e.tensor_scalar(
                    csB[:, 0:256], xk[:, t0:t0 + 256], w2_, None, OP.mult), inc=s_pool)
                b.wait("POOL", s_pool)
                b.rec("POOL", lambda e, csA=csA, csB=csB: e.tensor_tensor(
                    csA[:, 0:256], csA[:, 0:256], csB[:, 0:256], OP.add), inc=s_pool)
                pool_t = tok(s_pool)
                b.wait_many("DVE", [pool_t])
                b.rec("DVE", lambda e, xk=xk, t0=t0, kk=kk: e.tensor_copy(
                    halo_save[:, kk * 2:(kk + 1) * 2], xk[:, t0 + 254:t0 + 256]), inc=s_dve)
                hs_t = tok(s_dve)
                b.wait_many("ACT", [pool_t, hs_t])
                b.rec("ACT", lambda e, xk=xk, csA=csA, t0=t0, kk=kk: e.activation(
                    out=xk[:, t0:t0 + 256], in_=csA[:, 0:256], func=AF.Silu,
                    bias=convb_c[:, kk:kk + 1]), inc=s_act)
            conv_t = tok(s_act)
            b.wait_many("PE", [conv_t, w3_t])
            b.wait("PE", s_dve)
            for kk in range(16):
                b.rec("PE", lambda e, kk=kk, t0=t0: e.matmul(
                    psC[0:64, 0:256], ABF[:, DTW_o + kk * 64:DTW_o + (kk + 1) * 64],
                    xmt(kk)[:, t0:t0 + 256], start=(kk == 0), stop=(kk == 15)))
                b.rec("PE", lambda e, kk=kk, t0=t0: e.matmul(
                    psC[0:64, 512:768], ABF[:, DTW_o + 1024 + kk * 64:DTW_o + 1024 + (kk + 1) * 64],
                    xmt(kk)[:, t0:t0 + 256], start=(kk == 0), stop=(kk == 15)))
                b.rec("PE", lambda e, kk=kk, t0=t0: e.matmul(
                    psD[0:64, 0:256], ABF[:, DTW_o + 2048 + kk * 64:DTW_o + 2048 + (kk + 1) * 64],
                    xmt(kk)[:, t0:t0 + 256], start=(kk == 0), stop=(kk == 15)),
                    inc=(s_pe if kk == 15 else None))
            b.wait("ACT", s_pe)
            b.rec("ACT", lambda e, t0=t0: e.activation(
                out=DTv[:, t0:t0 + 256], in_=psC[0:64, 0:256], func=AF.Sigmoid,
                bias=AF32[0:64, 24448:24449]), inc=s_act)
            b.rec("ACT", lambda e, t0=t0: e.activation(
                out=BMv[:, t0:t0 + 256], in_=psC[0:64, 512:768], func=AF.Identity,
                bias=AF32[0:64, 24449:24450]), inc=s_act)
            b.rec("ACT", lambda e, t0=t0: e.activation(
                out=CMv[:, t0:t0 + 256], in_=psD[0:64, 0:256], func=AF.Identity,
                bias=AF32[0:64, 24450:24451]), inc=s_act)
            b.wait("DVE", s_act)
            b.wait("PE", s_act)
        proj_t = tok(s_act)

        b.wait_many("DVE", [proj_t])
        b.rec("DVE", lambda e: e.tensor_tensor(BMv, DTv, BMv, OP.mult), inc=s_dve)
        b.wait("DVE", s_dve)
        b.rec("DVE", lambda e: e.tensor_scalar(DTv, DTv, -1.0, 1.0, OP.mult, OP.add), inc=s_dve)
        b.wait("DVE", s_dve)
        b.rec("DVE", lambda e: e.tensor_tensor_scan(SSv, DTv, BMv, 0.0, OP.mult, OP.add), inc=s_dve)
        b.wait("DVE", s_dve)
        b.rec("DVE", lambda e: e.tensor_tensor_scan(
            BMv, DTv, kc[0:64, 176:177].broadcast_to((64, T)), 1.0, OP.mult, OP.mult), inc=s_dve)
        scan_t = tok(s_dve)
        b.wait_many("SP", [scan_t])
        dma("SP", s_sp, cc_srcS, SSv[:, T - 1:T])
        b.wait("POOL", s_sp)
        b.rec("POOL", lambda e: e.collective_compute(
            "AllGather", OP.bypass, replica_groups=[list(range(8))],
            ins=[cc_srcS[:].opt()], outs=[cc_dstS[:].opt()]), inc=s_cc)
        b.wait("POOL", s_cc)
        b.wait("POOL", s_dve)
        dma("POOL", s_gsl, AF32[0:64, 16384:16392].rearrange("p (r o) -> p r o", r=8),
            cc_dstS.rearrange("(r p) o -> p r o", p=64), allow_slow_non_contiguous=True)
        gs_t = tok(s_gsl)
        b.wait_many("DVE", [gs_t])
        b.rec("DVE", lambda e: e.tensor_tensor(
            AF32[0:64, 16384:16392], AF32[0:64, 16384:16392],
            maskbc[0:64, :], OP.mult), inc=s_dve)
        b.wait("DVE", s_dve)
        b.rec("DVE", lambda e: e.tensor_reduce(
            AF32[0:64, 16392:16393], AF32[0:64, 16384:16392], AX.X, OP.add), inc=s_dve)
        b.wait("DVE", s_dve)
        b.rec("DVE", lambda e: e.scalar_tensor_tensor(
            SSv, BMv, AF32[0:64, 16392:16393], SSv, OP.mult, OP.add), inc=s_dve)
        b.wait("DVE", s_dve)
        b.rec("DVE", lambda e: e.tensor_tensor(CMv, CMv, SSv, OP.mult), inc=s_dve)
        y_t = tok(s_dve)

        r64a = AF32[0:1, 16384:16384 + T]
        r64b = AF32[0:1, 18432:18432 + T]
        sq64 = AF32[0:64, 22528:22528 + 512]
        nn64 = 4
        for n in range(nn64):
            n0 = n * 512
            b.wait_many("ACT", [y_t])
            b.wait("ACT", s_pe)
            b.rec("ACT", lambda e, n0=n0: e.activation(out=sq64, in_=CMv[:, n0:n0 + 512],
                                                       func=AF.Square), inc=s_act)
            b.wait("PE", s_act)
            b.wait_many("PE", [y_t])
            b.wait("PE", s_dve)
            b.rec("PE", lambda e, n0=n0: e.matmul(psD[0:1, 0:512], ones_col[0:64, :],
                                                  CMv[:, n0:n0 + 512], start=True, stop=True))
            b.rec("PE", lambda e, n0=n0: e.matmul(psC[0:1, 0:512], ones_col[0:64, :],
                                                  sq64, start=True, stop=True), inc=s_pe)
            b.wait("DVE", s_pe)
            b.rec("DVE", lambda e, n0=n0: e.tensor_scalar(
                r64b[:, n0:n0 + 512], psD[0:1, 0:512], 1.0 / 64, None, OP.mult), inc=s_dve)
            b.wait("DVE", s_dve)
            b.rec("DVE", lambda e, n0=n0: e.tensor_tensor(
                r64a[:, n0:n0 + 512], r64b[:, n0:n0 + 512], r64b[:, n0:n0 + 512], OP.mult), inc=s_dve)
            b.wait("DVE", s_dve)
            b.rec("DVE", lambda e, n0=n0: e.scalar_tensor_tensor(
                r64a[:, n0:n0 + 512], psC[0:1, 0:512], 1.0 / 64, r64a[:, n0:n0 + 512],
                OP.mult, OP.subtract), inc=s_dve)
        b.wait("ACT", s_dve)
        b.rec("ACT", lambda e: e.activation(out=r64a, in_=r64a, func=AF.Ln, bias=1e-5), inc=s_act)
        b.wait("ACT", s_act)
        b.rec("ACT", lambda e: e.activation(out=r64a, in_=r64a, func=AF.Exp, scale=-0.5), inc=s_act)
        b.wait("DVE", s_act)
        b.rec("DVE", lambda e: e.scalar_tensor_tensor(r64b, r64b, -1.0, r64a, OP.mult, OP.mult), inc=s_dve)
        ln64_t = tok(s_dve)
        ynv = ABF[0:64, YN_o:YN_o + T]
        for n in range(4):
            n0 = n * 512
            b.wait_many("PE", [ln64_t])
            b.wait("PE", s_dve)
            b.wait("PE", s_act)
            b.rec("PE", lambda e, n0=n0: e.matmul(psD[0:64, 0:512], ones_row[:, 0:64], r64a[:, n0:n0 + 512],
                                                  start=True, stop=True))
            b.rec("PE", lambda e, n0=n0: e.matmul(psC[0:64, 0:512], ones_row[:, 0:64], r64b[:, n0:n0 + 512],
                                                  start=True, stop=True), inc=s_pe)
            b.wait("DVE", s_pe)
            b.rec("DVE", lambda e, n0=n0: e.tensor_tensor(
                SSv[:, 0:512], CMv[:, n0:n0 + 512], psD[0:64, 0:512], OP.mult), inc=s_dve)
            b.wait("DVE", s_dve)
            b.rec("DVE", lambda e, n0=n0: e.tensor_tensor(
                ynv[:, n0:n0 + 512], SSv[:, 0:512], psC[0:64, 0:512], OP.add), inc=s_dve)
            b.wait("DVE", s_dve)
        yn_t = tok(s_dve)

        b.wait("POOL", s_pe)
        dma("POOL", s_ini, ABF[0:64, S2IW_o:S2IW_o + 2048], di["s2iw"])
        s2i_t = tok(s_ini)
        b.wait("SP", s_pool)
        b.wait("SP", s_act)
        b.wait("SP", s_pe)
        b.wait("SP", s_dve)
        dma("SP", s_sp, AF32[:, 19968:19968 + 16], di["s2ib"])
        dma("SP", s_sp, AF32[:, 19984:19984 + 16], di["Dpar"])
        dma("SP", s_sp, AF32[:, 20000:20000 + 8], di["outb"])
        s2ib_c = AF32[:, 19968:19968 + 16]
        dpar_c = AF32[:, 19984:19984 + 16]
        outb_c = AF32[:, 20000:20000 + 8]
        w4f_t = tok(s_sp)
        scr_h2 = nc.dram_tensor("scr_h2", [D, T], BF16).ap()
        SGI_o3 = 32768
        def y3v(m):
            return ABF[:, SGI_o3 + m * 512:SGI_o3 + (m + 1) * 512]
        Achunk = AF32[:, 20480:20480 + KT * 512]
        def ach(k):
            return AF32[:, 20480 + k * 512:20480 + (k + 1) * 512]
        sq3 = AF32[:, 18432:18432 + 1024]
        r_rstd2 = AF32[0:1, 16384:16384 + T]
        B1rm2 = AF32[:, 19456:19456 + 512]
        ga4 = 0
        ev4 = {}
        for n in range(4):
            n0 = n * 512
            b.wait("POOL", s_pe)
            b.wait("POOL", s_sc)
            b.wait("POOL", s_sg[0])
            b.wait("POOL", s_sg[1])
            dma("POOL", s_gd, ABF[:, SGI_o3:SGI_o3 + 8192].rearrange("p (k t) -> p k t", k=16),
                scr_sg.rearrange("(k p) t -> p k t", p=128)[:, :, n0:n0 + 512])
            sg_t = tok(s_gd)
            for m in range(16):
                ps = PSMAIN[ga4 % 2]
                pstb = AF32[:, 20480 + (m % 2) * 512:20480 + (m % 2) * 512 + 512]
                prev = ev4.get(ga4 - 2)
                b.wait_many("PE", [yn_t, s2i_t])
                if prev is not None:
                    b.wait("PE", *prev)
                b.rec("PE", lambda e, m=m, n0=n0, ps=ps: e.matmul(
                    ps[:, 0:512], ABF[0:64, S2IW_o + m * 128:S2IW_o + (m + 1) * 128],
                    ynv[:, n0:n0 + 512], start=True, stop=True), inc=s_pe)
                b.wait("ACT", s_pe)
                b.wait_many("ACT", [w4f_t])
                if ga4 >= 2:
                    b.wait("ACT", s_dve, ev4[ga4 - 2][1])
                b.rec("ACT", lambda e, m=m, ps=ps, pstb=pstb: e.activation(
                    out=pstb, in_=ps[:, 0:512], func=AF.Identity,
                    bias=s2ib_c[:, m:m + 1]), inc=s_act)
                act_t = tok(s_act)
                b.wait_many("DVE", [act_t, inp_t, sg_t])
                b.rec("DVE", lambda e, m=m, n0=n0, pstb=pstb: e.scalar_tensor_tensor(
                    pstb, xmt(m)[:, n0:n0 + 512], dpar_c[:, m:m + 1], pstb,
                    OP.mult, OP.add), inc=s_dve)
                b.wait("DVE", s_dve)
                b.rec("DVE", lambda e, m=m, pstb=pstb: e.tensor_tensor(
                    y3v(m), y3v(m), pstb, OP.mult), inc=s_dve)
                ev4[ga4] = tok(s_dve)
                ga4 += 1
            y3_t = tok(s_dve)
            for p in range(4):
                wv, wv_t, wv_par = ws.load(di["outw"][:, p * 256:(p + 1) * 256], 16)
                b.wait_many("PE", [wv_t, y3_t])
                for m2 in range(2):
                    m = p * 2 + m2
                    ps = PSMAIN[ga4 % 2]
                    prev = ev4.get(ga4 - 2)
                    if prev is not None:
                        b.wait("PE", *prev)
                    for k in range(16):
                        b.rec("PE", lambda e, k=k, m2=m2, ps=ps, wv=wv: e.matmul(
                            ps[:, 0:512], wv(k, m2 * 128, 128), y3v(k),
                            start=(k == 0), stop=(k == 15)), inc=(s_pe if k == 15 else None))
                    b.wait("DVE", s_pe)
                    b.rec("DVE", lambda e, m=m, n0=n0, ps=ps: e.scalar_tensor_tensor(
                        ach(m), ps[:, 0:512], outb_c[:, m:m + 1], xs(m)[:, n0:n0 + 512],
                        OP.add, OP.add), inc=s_dve)
                    ev4[ga4] = tok(s_dve)
                    ga4 += 1
                ws.consumed(wv_par, tok(s_dve))
            a_t = tok(s_dve)
            b.wait_many("DVE", [a_t])
            for m in range(KT):
                b.rec("DVE", lambda e, m=m, n0=n0: e.tensor_tensor(
                    xs(m)[:, n0:n0 + 512], xs(m)[:, n0:n0 + 512], ach(m), OP.add), inc=s_dve)
            b.wait("PE", s_dve)
            b.wait("PE", s_act)
            pe_h = []
            for k in range(KT):
                sl = sq3[:, (k % 2) * 512:(k % 2) * 512 + 512]
                if len(pe_h) >= 2:
                    b.wait("ACT", s_pe, pe_h[-2])
                b.wait_many("ACT", [a_t])
                b.rec("ACT", lambda e, k=k, sl=sl: e.activation(
                    out=sl, in_=ach(k), func=AF.Square), inc=s_act)
                b.wait("PE", s_act)
                b.rec("PE", lambda e, k=k, sl=sl: e.matmul(
                    psC[0:1, 0:512], ones_col, sl, start=(k == 0), stop=(k == KT - 1)), inc=s_pe)
                pe_h.append(s_pe.n)
            b.wait("DVE", s_pe)
            b.rec("DVE", lambda e, n0=n0: e.tensor_scalar(
                r_rstd2[:, n0:n0 + 512], psC[0:1, 0:512], 1.0 / D, None, OP.mult), inc=s_dve)
            b.wait("ACT", s_dve)
            b.rec("ACT", lambda e, n0=n0: e.activation(
                out=r_rstd2[:, n0:n0 + 512], in_=r_rstd2[:, n0:n0 + 512], func=AF.Ln, bias=1e-6), inc=s_act)
            b.wait("ACT", s_act)
            b.rec("ACT", lambda e, n0=n0: e.activation(
                out=r_rstd2[:, n0:n0 + 512], in_=r_rstd2[:, n0:n0 + 512], func=AF.Exp, scale=-0.5), inc=s_act)
            rr2_t = tok(s_act)
            b.wait_many("PE", [rr2_t])
            b.wait("PE", s_dve)
            b.rec("PE", lambda e, n0=n0: e.matmul(psD[:, 0:512], ones_row,
                                                  r_rstd2[:, n0:n0 + 512], start=True, stop=True), inc=s_pe)
            b.wait("DVE", s_pe)
            b.rec("DVE", lambda e: e.tensor_copy(B1rm2, psD[:, 0:512]), inc=s_dve)
            b.wait("DVE", s_dve)
            for k in range(KT):
                b.rec("DVE", lambda e, k=k: e.tensor_tensor(sq3[:, 0:512], ach(k), B1rm2, OP.mult), inc=s_dve)
                b.wait("DVE", s_dve)
                b.rec("DVE", lambda e, k=k: e.tensor_scalar(
                    y3v(k), sq3[:, 0:512], n2wc[:, k:k + 1], None, OP.mult), inc=s_dve)
                b.wait("DVE", s_dve)
            h2_t = tok(s_dve)
            b.wait_many("ACT", [h2_t])
            dma("ACT", s_sc, scr_h2.rearrange("(k p) t -> p k t", p=128)[:, :, n0:n0 + 512],
                ABF[:, SGI_o3:SGI_o3 + 4096].rearrange("p (k t) -> p k t", k=KT))
        mix_t = tok(s_dve)
        tap("xmix", AF32[:, 0:16384], [mix_t])

        def h2v(k):
            return ABF[:, k * T:(k + 1) * T]
        def ghv(k):
            return ABF[:, 16384 + k * T:16384 + (k + 1) * T]
        cw4Fb = ABF[0:4, 32768:32768 + T]
        eb2ab = ABF[0:4, 34816:34816 + 1024]
        gatewb = ABF[:, 35840:35840 + 32]
        B_cw3 = AF32[:, 18432:18432 + T]
        glog = AF32[:, 20480:20480 + 128]
        cw4_all = AF32[:, 20608:20608 + 64]
        gm8 = AF32[:, 20672:20672 + 8]
        gsm = AF32[:, 20680:20680 + 8]
        cwF4 = AF32[0:4, 22528:22528 + T]
        eb1c = AF32[:, 20704:20704 + 32]

        b.wait_many("SP", [mix_t])
        b.wait("POOL", s_dve)
        b.wait("POOL", s_sc)
        dma("POOL", s_gd, ABF[:, 0:16384].rearrange("p (k t) -> p k t", k=KT),
            scr_h2.rearrange("(k p) t -> p k t", p=128))
        dma("POOL", s_ini, gatewb.rearrange("p (k c) -> p k c", k=KT),
            di["gatew"].rearrange("(k p) c -> p k c", p=128))
        dma("POOL", s_ini, eb2ab, di["eb2all"])
        dma("SP", s_sp, AF32[:, 20696:20700], di["gateb_bc"])
        h2_tt = tok(s_gd); gw_t = tok(s_ini); gb_t = tok(s_sp)

        b.wait_many("DVE", [mix_t])
        b.rec("DVE", lambda e: e.memset(glog, -1e30), inc=s_dve)
        gpad_t = tok(s_dve)
        grow_t = gb_t
        b.wait_many("PE", [h2_tt, gw_t, gpad_t])
        for tt in range(16):
            b.wait("PE", s_dve)
            for k in range(KT):
                b.rec("PE", lambda e, tt=tt, k=k: e.matmul(
                    psC[0:128, 0:4], h2v(k)[:, tt * 128:(tt + 1) * 128],
                    gatewb.rearrange("p (k c) -> p k c", k=KT)[:, k, :],
                    start=(k == 0), stop=(k == KT - 1)), inc=(s_pe if k == KT - 1 else None))
            b.wait("DVE", s_pe)
            b.wait_many("DVE", [grow_t])
            b.rec("DVE", lambda e, tt=tt: e.tensor_tensor(
                glog[:, tt * 8:tt * 8 + 4], psC[0:128, 0:4],
                AF32[:, 20696:20700], OP.add), inc=s_dve)
            b.wait("DVE", s_dve)
            b.rec("DVE", lambda e, tt=tt: e.max(out=gm8, in_=glog[:, tt * 8:(tt + 1) * 8]), inc=s_dve)
            b.wait("DVE", s_dve)
            b.rec("DVE", lambda e: e.tensor_scalar(gsm[:, 0:1], gm8[:, 0:1], -1.0, None, OP.mult), inc=s_dve)
            b.wait("ACT", s_dve)
            b.rec("ACT", lambda e: e.activation(out=gsm[:, 1:2], in_=gm8[:, 1:2], func=AF.Exp,
                                                bias=gsm[:, 0:1]), inc=s_act)
            b.wait("DVE", s_act)
            b.rec("DVE", lambda e: e.tensor_scalar(gsm[:, 2:3], gsm[:, 1:2], 1.0, None, OP.add), inc=s_dve)
            b.wait("DVE", s_dve)
            b.rec("DVE", lambda e: e.reciprocal(gsm[:, 3:4], gsm[:, 2:3]), inc=s_dve)
            b.wait("DVE", s_dve)
            b.rec("DVE", lambda e: e.tensor_tensor(gsm[:, 4:5], gsm[:, 1:2], gsm[:, 3:4], OP.mult), inc=s_dve)
            b.wait("DVE", s_dve)
            t4 = AF32[:, 20692:20696]
            b.rec("DVE", lambda e, tt=tt: e.tensor_scalar(
                t4, glog[:, tt * 8:tt * 8 + 4], gm8[:, 0:1], gsm[:, 3:4], OP.is_equal, OP.mult), inc=s_dve)
            b.rec("DVE", lambda e, tt=tt: e.tensor_scalar(
                cw4_all[:, tt * 4:(tt + 1) * 4], glog[:, tt * 8:tt * 8 + 4],
                gm8[:, 1:2], gsm[:, 4:5], OP.is_equal, OP.mult), inc=s_dve)
            b.wait("DVE", s_dve)
            b.rec("DVE", lambda e, tt=tt: e.tensor_tensor(
                cw4_all[:, tt * 4:(tt + 1) * 4], cw4_all[:, tt * 4:(tt + 1) * 4], t4, OP.add), inc=s_dve)
            cwt_t = tok(s_dve)
            b.wait_many("PE", [cwt_t])
            b.rec("PE", lambda e, tt=tt: e.transpose(
                psC[0:4, 512:640], cw4_all[:, tt * 4:(tt + 1) * 4], eye_fc), inc=s_pe)
            b.wait("DVE", s_pe)
            b.rec("DVE", lambda e, tt=tt: e.tensor_copy(
                cwF4[:, tt * 128:(tt + 1) * 128], psC[0:4, 512:640]), inc=s_dve)
        cwf4_t = tok(s_dve)
        b.wait("DVE", s_dve)
        b.rec("DVE", lambda e: e.tensor_copy(cw4Fb, cwF4), inc=s_dve)
        cwf4b_t = tok(s_dve)

        b.wait_many("PE", [cwf4b_t, gw_t])
        g5 = 0
        ev5 = {}
        for m in range(KT):
            for n in range(4):
                ps = PSMAIN[g5 % 2]
                prev = ev5.get(g5 - 2)
                if prev is not None:
                    b.wait("PE", *prev)
                b.rec("PE", lambda e, m=m, n=n, ps=ps: e.matmul(
                    ps[:, 0:512], eb2ab[:, m * 128:(m + 1) * 128],
                    cw4Fb[:, n * 512:(n + 1) * 512], start=True, stop=True), inc=s_pe)
                b.wait("DVE", s_pe)
                b.rec("DVE", lambda e, m=m, n=n, ps=ps: e.tensor_tensor(
                    xs(m)[:, n * 512:(n + 1) * 512], xs(m)[:, n * 512:(n + 1) * 512],
                    ps[:, 0:512], OP.add), inc=s_dve)
                ev5[g5] = tok(s_dve)
                g5 += 1

        for ex in range(4):
            b.wait_many("PE", [cwf4_t, const_t])
            for n in range(4):
                b.wait("PE", s_dve)
                b.wait("PE", s_pool)
                b.rec("PE", lambda e, ex=ex, n=n: e.matmul(
                    psD[:, 0:512], selc[0:4, ex * 128:(ex + 1) * 128],
                    cwF4[:, n * 512:(n + 1) * 512], start=True, stop=True), inc=s_pe)
                b.wait("DVE", s_pe)
                b.rec("DVE", lambda e, n=n: e.tensor_copy(
                    B_cw3[:, n * 512:(n + 1) * 512], psD[:, 0:512]), inc=s_dve)
            bcw3_t = tok(s_dve)
            b.wait("SP", s_act)
            dma("SP", s_sp, eb1c, di["eb1"][ex])
            eb1_t = tok(s_sp)
            for hidc in range(4):
                for p in range(4):
                    wv, wv_t, wv_par = ws.load(
                        di["ew1"][ex][:, hidc * 1024 + p * 256:hidc * 1024 + (p + 1) * 256], KT)
                    b.wait_many("PE", [wv_t, h2_tt])
                    for m2 in range(2):
                        m = p * 2 + m2
                        for n in range(4):
                            ps = PSMAIN[g5 % 2]
                            prev = ev5.get(g5 - 2)
                            if prev is not None:
                                b.wait("PE", *prev)
                            for k in range(KT):
                                b.rec("PE", lambda e, k=k, m2=m2, n=n, ps=ps, wv=wv: e.matmul(
                                    ps[:, 0:512], wv(k, m2 * 128, 128), h2v(k)[:, n * 512:(n + 1) * 512],
                                    start=(k == 0), stop=(k == KT - 1)), inc=(s_pe if k == KT - 1 else None))
                            b.wait("ACT", s_pe)
                            b.wait_many("ACT", [eb1_t])
                            hm = hidc * 8 + m
                            b.rec("ACT", lambda e, m=m, n=n, ps=ps, hm=hm: e.activation(
                                out=ghv(m)[:, n * 512:(n + 1) * 512], in_=ps[:, 0:512],
                                func=AF.Gelu, bias=eb1c[:, hm:hm + 1]), inc=s_act)
                            ev5[g5] = tok(s_act)
                            g5 += 1
                    ws.consumed(wv_par, tok(s_act))
                gelu_t = tok(s_act)
                b.wait_many("POOL", [gelu_t, bcw3_t])
                for k in range(KT):
                    b.rec("POOL", lambda e, k=k: e.tensor_tensor(ghv(k), ghv(k), B_cw3, OP.mult), inc=s_pool)
                ghs_t = tok(s_pool)
                for p in range(4):
                    wv, wv_t, wv_par = ws.load(
                        di["ew2"][ex][hidc * 1024:(hidc + 1) * 1024, p * 256:(p + 1) * 256], KT)
                    b.wait_many("PE", [wv_t, ghs_t])
                    for m2 in range(2):
                        m = p * 2 + m2
                        for n in range(4):
                            ps = PSMAIN[g5 % 2]
                            prev = ev5.get(g5 - 2)
                            if prev is not None:
                                b.wait("PE", *prev)
                            for k in range(KT):
                                b.rec("PE", lambda e, k=k, m2=m2, n=n, ps=ps, wv=wv: e.matmul(
                                    ps[:, 0:512], wv(k, m2 * 128, 128), ghv(k)[:, n * 512:(n + 1) * 512],
                                    start=(k == 0), stop=(k == KT - 1)), inc=(s_pe if k == KT - 1 else None))
                            b.wait("DVE", s_pe)
                            b.rec("DVE", lambda e, m=m, n=n, ps=ps: e.tensor_tensor(
                                xs(m)[:, n * 512:(n + 1) * 512], xs(m)[:, n * 512:(n + 1) * 512],
                                ps[:, 0:512], OP.add), inc=s_dve)
                            ev5[g5] = tok(s_dve)
                            g5 += 1
                    ws.consumed(wv_par, tok(s_dve))
        moe3_t = tok(s_dve)
        tap("res3", AF32[:, 0:16384], [moe3_t])
        st3_t = post_ln(moe3_t)
        tap("x3", AF32[:, 0:16384], [st3_t])
        last_t = st3_t

    if n_stages >= 4:
        def xb4(k):
            return ABF[:, k * T:(k + 1) * T]
        def guv(k):
            return ABF[:, 16384 + k * T:16384 + (k + 1) * T]
        sims = AF32[:, 16384:16384 + 512]
        sx8 = AF32[:, 16896:16896 + 8]
        sy8 = AF32[:, 16904:16904 + 8]
        p8f = AF32[:, 16912:16912 + 8]
        pden = AF32[:, 16920:16920 + 4]
        bgc = AF32[:, 19968:19968 + 32]
        buc = AF32[:, 20000:20000 + 32]
        bdc = AF32[:, 20032:20032 + 8]

        b.wait_many("DVE", [st3_t])
        b.wait("DVE", s_pe)
        b.wait("DVE", s_pool)
        for k in range(KT):
            b.rec("DVE", lambda e, k=k: e.tensor_copy(xb4(k), xs(k)), inc=s_dve)
        sh4_t = tok(s_dve)
        b.wait("SP", s_dve)
        b.wait("SP", s_act)
        b.wait("SP", s_pe)
        dma("SP", s_sp, bgc, di["bg"])
        dma("SP", s_sp, buc, di["bu"])
        dma("SP", s_sp, bdc, di["bd"])
        b4_t = tok(s_sp)

        DO_MLP = sub4 in ("all", "mlp")
        DO_PEER = sub4.startswith("peer") or sub4 == "all"
        g6 = 0
        ev6 = {}
        silu_sc = ABF[:, 43520:43520 + 1024]
        for hg in range(4 if DO_MLP else 0):
            for p in range(4):
                wg, wg_t, wg_par = ws.load(di["Wg"][:, hg * 1024 + p * 256:hg * 1024 + (p + 1) * 256], KT)
                wu, wu_t, wu_par = ws.load(di["Wu"][:, hg * 1024 + p * 256:hg * 1024 + (p + 1) * 256], KT)
                b.wait_many("PE", [wg_t, wu_t, sh4_t])
                for m2 in range(2):
                    m = p * 2 + m2
                    hm = hg * 8 + m
                    for n in range(4):
                        ps = PSMAIN[g6 % 2]
                        ssl = silu_sc[:, (g6 % 2) * 512:(g6 % 2) * 512 + 512]
                        prev = ev6.get(g6 - 2)
                        if prev is not None:
                            b.wait("PE", *prev)
                        for k in range(KT):
                            b.rec("PE", lambda e, k=k, m2=m2, n=n, ps=ps, wg=wg: e.matmul(
                                ps[:, 0:512], wg(k, m2 * 128, 128), xb4(k)[:, n * 512:(n + 1) * 512],
                                start=(k == 0), stop=(k == KT - 1)), inc=(s_pe if k == KT - 1 else None))
                        b.wait("ACT", s_pe)
                        b.wait_many("ACT", [b4_t])
                        if g6 >= 2:
                            b.wait("ACT", s_dve, ev6[g6 - 2][1])
                        b.rec("ACT", lambda e, ps=ps, ssl=ssl, hm=hm: e.activation(
                            out=ssl, in_=ps[:, 0:512], func=AF.Silu,
                            bias=bgc[:, hm // 8 * 0 + hm:hm + 1][:, 0:1] if False else bgc[:, hm:hm + 1]), inc=s_act)
                        sil_t = tok(s_act)
                        b.wait("PE", *sil_t)
                        for k in range(KT):
                            b.rec("PE", lambda e, k=k, m2=m2, n=n, ps=ps, wu=wu: e.matmul(
                                ps[:, 0:512], wu(k, m2 * 128, 128), xb4(k)[:, n * 512:(n + 1) * 512],
                                start=(k == 0), stop=(k == KT - 1)), inc=(s_pe if k == KT - 1 else None))
                        b.wait("DVE", s_pe)
                        b.wait_many("DVE", [sil_t, b4_t])
                        b.rec("DVE", lambda e, m=m, n=n, ps=ps, ssl=ssl, hm=hm: e.scalar_tensor_tensor(
                            guv(m)[:, n * 512:(n + 1) * 512], ps[:, 0:512], buc[:, hm:hm + 1],
                            ssl, OP.add, OP.mult), inc=s_dve)
                        ev6[g6] = tok(s_dve)
                        g6 += 1
                ws.consumed(wg_par, tok(s_dve))
                ws.consumed(wu_par, tok(s_dve))
            gu_t = tok(s_dve)
            for p in range(4):
                wd, wd_t, wd_par = ws.load(
                    di["Wd"][hg * 1024:(hg + 1) * 1024, p * 256:(p + 1) * 256], KT)
                b.wait_many("PE", [wd_t, gu_t])
                for m2 in range(2):
                    m = p * 2 + m2
                    for n in range(4):
                        ps = PSMAIN[g6 % 2]
                        prev = ev6.get(g6 - 2)
                        if prev is not None:
                            b.wait("PE", *prev)
                        for k in range(KT):
                            b.rec("PE", lambda e, k=k, m2=m2, n=n, ps=ps, wd=wd: e.matmul(
                                ps[:, 0:512], wd(k, m2 * 128, 128), guv(k)[:, n * 512:(n + 1) * 512],
                                start=(k == 0), stop=(k == KT - 1)), inc=(s_pe if k == KT - 1 else None))
                        b.wait("DVE", s_pe)
                        if hg == 0:
                            b.rec("DVE", lambda e, m=m, n=n, ps=ps: e.scalar_tensor_tensor(
                                xs(m)[:, n * 512:(n + 1) * 512], ps[:, 0:512], bdc[:, m:m + 1],
                                xs(m)[:, n * 512:(n + 1) * 512], OP.add, OP.add), inc=s_dve)
                        else:
                            b.rec("DVE", lambda e, m=m, n=n, ps=ps: e.tensor_tensor(
                                xs(m)[:, n * 512:(n + 1) * 512], xs(m)[:, n * 512:(n + 1) * 512],
                                ps[:, 0:512], OP.add), inc=s_dve)
                        ev6[g6] = tok(s_dve)
                        g6 += 1
                ws.consumed(wd_par, tok(s_dve))
        mlp_t = tok(s_dve)

        Mo = 16384
        def mv(tc):
            return ABF[:, Mo + tc * 4096:Mo + (tc + 1) * 4096]
        QF_o = 32768
        SW_o = 32768
        PB_o = 36864
        WT_o = 36864
        SCR_o = 40960
        KB_o = 43008
        kbuf = ABF[0:64, KB_o:KB_o + 512]
        b.wait("POOL", s_pe)
        b.wait("POOL", s_dve)
        dma("POOL", s_ini, kbuf, di["keysT"])
        kb_t = tok(s_ini)
        g7 = 0
        ev7 = {}
        for nch in range(4 if DO_PEER else 0):
            n0 = nch * 512
            wq, wq_t, wq_par = ws.load(di["Wq"], KT)
            b.wait_many("PE", [wq_t, sh4_t, mlp_t])
            for m in range(8):
                ps = PSMAIN[g7 % 2]
                prev = ev7.get(g7 - 2)
                if prev is not None:
                    b.wait("PE", *prev)
                for k in range(KT):
                    b.rec("PE", lambda e, k=k, m=m, n0=n0, ps=ps, wq=wq: e.matmul(
                        ps[0:64, 0:512], wq(k, m * 64, 64), xb4(k)[:, n0:n0 + 512],
                        start=(k == 0), stop=(k == KT - 1)), inc=(s_pe if k == KT - 1 else None))
                b.wait("DVE", s_pe)
                b.wait("DVE", s_pool)
                b.wait("DVE", s_act)
                b.rec("DVE", lambda e, m=m, ps=ps: e.tensor_copy(
                    ABF[0:64, QF_o + m * 512:QF_o + (m + 1) * 512], ps[0:64, 0:512]), inc=s_dve)
                ev7[g7] = tok(s_dve)
                g7 += 1
            ws.consumed(wq_par, tok(s_dve))
            q_t = tok(s_dve)
            for tc in range(4 if sub4 not in ("peer_q",) else 0):
                b.wait_many("PE", [q_t, kb_t])
                b.wait("PE", s_dve)
                for pp in range(2):
                    for h in range(4):
                        hp = pp * 4 + h
                        b.rec("PE", lambda e, hp=hp, tc=tc: e.matmul(
                            psC[0:128, hp * 64:(hp + 1) * 64],
                            ABF[0:64, QF_o + hp * 512 + tc * 128:QF_o + hp * 512 + (tc + 1) * 128],
                            kbuf[0:64, hp * 64:(hp + 1) * 64],
                            start=True, stop=True), inc=(s_pe if hp == 7 else None))
                b.wait("DVE", s_pe)
                b.rec("DVE", lambda e: e.tensor_copy(sims, psC[0:128, 0:512]), inc=s_dve)
                sim_t = tok(s_dve)
                for h in range(4 if sub4 != "peer_sim" else 0):
                    sxv = sims[:, h * 64:(h + 1) * 64]
                    syv = sims[:, 256 + h * 64:256 + (h + 1) * 64]
                    exv = ABF[:, SCR_o:SCR_o + 64]
                    eyv = ABF[:, SCR_o + 64:SCR_o + 128]
                    ex8 = ABF[:, SCR_o + 128:SCR_o + 136]
                    ey8 = ABF[:, SCR_o + 136:SCR_o + 144]
                    pcv = ABF[:, SCR_o + 144:SCR_o + 208]
                    p8v = ABF[:, SCR_o + 208:SCR_o + 216]
                    pbuf = ABF[:, PB_o:PB_o + 4096]
                    b.wait_many("DVE", [sim_t])
                    b.rec("DVE", lambda e, sxv=sxv: e.max(out=sx8, in_=sxv), inc=s_dve)
                    b.rec("DVE", lambda e, syv=syv: e.max(out=sy8, in_=syv), inc=s_dve)
                    b.wait("DVE", s_dve)
                    b.rec("DVE", lambda e: e.tensor_scalar(pden[:, 2:3], sx8[:, 0:1], -1.0, None, OP.mult), inc=s_dve)
                    b.rec("DVE", lambda e: e.tensor_scalar(pden[:, 3:4], sy8[:, 0:1], -1.0, None, OP.mult), inc=s_dve)
                    b.wait("ACT", s_dve)
                    b.rec("ACT", lambda e, sxv=sxv, exv=exv: e.activation(
                        out=exv, in_=sxv, func=AF.Exp, bias=pden[:, 2:3]), inc=s_act)
                    b.rec("ACT", lambda e, syv=syv, eyv=eyv: e.activation(
                        out=eyv, in_=syv, func=AF.Exp, bias=pden[:, 3:4]), inc=s_act)
                    b.rec("ACT", lambda e, ex8=ex8: e.activation(
                        out=ex8, in_=sx8, func=AF.Exp, bias=pden[:, 2:3]), inc=s_act)
                    b.rec("ACT", lambda e, ey8=ey8: e.activation(
                        out=ey8, in_=sy8, func=AF.Exp, bias=pden[:, 3:4]), inc=s_act)
                    e8_t = tok(s_act)
                    if sub4 == "peer_exp":
                        continue
                    b.wait_many("DVE", [e8_t])
                    b.rec("DVE", lambda e, pcv=pcv, ex8=ex8, ey8=ey8: e.tensor_tensor(
                        pcv.rearrange("p (i j) -> p i j", i=8),
                        ex8[:, :, None].broadcast_to((128, 8, 8)),
                        ey8[:, None, :].broadcast_to((128, 8, 8)), OP.mult), inc=s_dve)
                    b.wait("DVE", s_dve)
                    b.rec("DVE", lambda e, p8v=p8v, pcv=pcv: e.max(out=p8v, in_=pcv), inc=s_dve)
                    b.wait("DVE", s_dve)
                    b.rec("DVE", lambda e, p8v=p8v: e.tensor_copy(p8f, p8v), inc=s_dve)
                    b.wait("DVE", s_dve)
                    b.rec("DVE", lambda e: e.tensor_reduce(pden[:, 0:1], p8f, AX.X, OP.add), inc=s_dve)
                    b.wait("DVE", s_dve)
                    b.rec("DVE", lambda e: e.reciprocal(pden[:, 1:2], pden[:, 0:1]), inc=s_dve)
                    pr_t4 = tok(s_dve)
                    if sub4 == "peer_pc":
                        continue
                    b.wait("DVE", s_pe)
                    b.rec("DVE", lambda e, pbuf=pbuf, exv=exv, eyv=eyv: e.tensor_tensor(
                        pbuf.rearrange("p (i j) -> p i j", i=64),
                        exv[:, :, None].broadcast_to((128, 64, 64)),
                        eyv[:, None, :].broadcast_to((128, 64, 64)), OP.mult), inc=s_dve)
                    b.wait("DVE", s_dve)
                    b.rec("DVE", lambda e, pbuf=pbuf: e.scalar_tensor_tensor(
                        pbuf, pbuf, p8f[:, 7:8], pbuf, OP.is_ge, OP.mult), inc=s_dve)
                    msk_t = tok(s_dve)
                    if sub4 == "peer_praw":
                        continue
                    b.wait_many("POOL", [msk_t, pr_t4])
                    if h == 0:
                        b.rec("POOL", lambda e, tc=tc, pbuf=pbuf: e.tensor_scalar(
                            mv(tc), pbuf, pden[:, 1:2], None, OP.mult), inc=s_pool)
                    else:
                        b.rec("POOL", lambda e, pbuf=pbuf: e.tensor_scalar(
                            pbuf, pbuf, pden[:, 1:2], None, OP.mult), inc=s_pool)
                        b.wait("POOL", s_pool)
                        b.rec("POOL", lambda e, tc=tc, pbuf=pbuf: e.tensor_tensor(
                            mv(tc), mv(tc), pbuf, OP.add), inc=s_pool)
                    b.wait("DVE", s_pool)
            m_t = tok(s_pool)
            for eg in range(4 if sub4 not in ("peer_mask",) else 0):
                swg = ABF[:, SW_o:SW_o + 4096]
                for p in range(2):
                    dv, dv_t, dv_par = ws.load(
                        di["deT"][:, eg * 1024 + p * 512:eg * 1024 + (p + 1) * 512], KT)
                    b.wait_many("PE", [dv_t, sh4_t, m_t])
                    for m2 in range(4):
                        met = p * 4 + m2
                        ps = PSMAIN[g7 % 2]
                        prev = ev7.get(g7 - 2)
                        if prev is not None:
                            b.wait("PE", *prev)
                        for k in range(KT):
                            b.rec("PE", lambda e, k=k, m2=m2, n0=n0, ps=ps, dv=dv: e.matmul(
                                ps[:, 0:512], dv(k, m2 * 128, 128), xb4(k)[:, n0:n0 + 512],
                                start=(k == 0), stop=(k == KT - 1)), inc=(s_pe if k == KT - 1 else None))
                        b.wait("ACT", s_pe)
                        b.wait("ACT", s_pool)
                        b.rec("ACT", lambda e, met=met, ps=ps: e.activation(
                            out=swg[:, met * 512:(met + 1) * 512], in_=ps[:, 0:512],
                            func=AF.Silu), inc=s_act)
                        ev7[g7] = tok(s_act)
                        g7 += 1
                    ws.consumed(dv_par, tok(s_act))
                sw_t = tok(s_act)
                tpi = 0
                tp_ev = {}
                for et in range(8):
                    for tc in range(4):
                        slot = tpi % 2
                        b.wait_many("PE", [m_t, const_tb])
                        if tpi >= 2:
                            b.wait("PE", s_dve, tp_ev[tpi - 2])
                        b.rec("PE", lambda e, et=et, tc=tc, slot=slot: e.transpose(
                            psT[:, slot * 1024:slot * 1024 + 128],
                            mv(tc)[:, eg * 1024 + et * 128:eg * 1024 + (et + 1) * 128], eye_bc),
                            inc=s_pe)
                        b.wait("DVE", s_pe)
                        b.wait_many("DVE", [sw_t])
                        b.rec("DVE", lambda e, et=et, tc=tc, slot=slot: e.tensor_tensor(
                            ABF[:, WT_o + et * 512 + tc * 128:WT_o + et * 512 + (tc + 1) * 128],
                            psT[:, slot * 1024:slot * 1024 + 128],
                            swg[:, et * 512 + tc * 128:et * 512 + (tc + 1) * 128], OP.mult), inc=s_dve)
                        tp_ev[tpi] = s_dve.n
                        tpi += 1
                wt_t = tok(s_dve)
                for p in range(4 if sub4 not in ("peer_sw",) else 0):
                    uv, uv_t, uv_par = ws.load(
                        di["ue"][eg * 1024:(eg + 1) * 1024, p * 256:(p + 1) * 256], KT)
                    b.wait_many("PE", [uv_t, wt_t])
                    for m2 in range(2):
                        m = p * 2 + m2
                        ps = PSMAIN[g7 % 2]
                        prev = ev7.get(g7 - 2)
                        if prev is not None:
                            b.wait("PE", *prev)
                        for k in range(KT):
                            b.rec("PE", lambda e, k=k, m2=m2, ps=ps, uv=uv: e.matmul(
                                ps[:, 0:512], uv(k, m2 * 128, 128),
                                ABF[:, WT_o + k * 512:WT_o + (k + 1) * 512],
                                start=(k == 0), stop=(k == KT - 1)), inc=(s_pe if k == KT - 1 else None))
                        b.wait("DVE", s_pe)
                        b.rec("DVE", lambda e, m=m, n0=n0, ps=ps: e.tensor_tensor(
                            xs(m)[:, n0:n0 + 512], xs(m)[:, n0:n0 + 512], ps[:, 0:512], OP.add), inc=s_dve)
                        ev7[g7] = tok(s_dve)
                        g7 += 1
                    ws.consumed(uv_par, tok(s_dve))
        peer_t = tok(s_dve)
        tap("res4", AF32[:, 0:16384], [peer_t])
        st4_t = post_ln(peer_t)
        last_t = st4_t


    b.wait_many("ACT", [last_t])
    for k in range(KT):
        dma("ACT", s_sc, y_out[k * 128:(k + 1) * 128, :], xs(k))
    b.wait("ACT", s_sc)
    b.wait("SP", s_sc)

    b.play()
    return nc


def prep_inputs(inputs, core):
    ci = {}
    bidx, half = core // 2, core % 2
    x = np.asarray(inputs["L0_attn_x"])
    sl = x[bidx, half * T:(half + 1) * T, :]
    ci["x_tf"] = np.ascontiguousarray(sl)
    ci["x_ft"] = np.ascontiguousarray(sl.T)

    def coltile(v, n):
        return np.ascontiguousarray(np.asarray(v).reshape(n, 128).T)

    ci["WbT"] = np.ascontiguousarray(np.asarray(inputs["L0_attn_Wb"]).T)
    ci["WpT"] = np.ascontiguousarray(np.asarray(inputs["L0_attn_Wp"]).T)
    ci["initBT"] = np.ascontiguousarray(np.asarray(inputs["L0_attn_init_B"]).T)
    ci["initA"] = np.ascontiguousarray(np.asarray(inputs["L0_attn_init_A"]))
    ci["initAT"] = np.ascontiguousarray(np.asarray(inputs["L0_attn_init_A"]).T)
    ci["initB"] = np.ascontiguousarray(np.asarray(inputs["L0_attn_init_B"]))
    ci["g0"] = coltile(inputs["L0_attn_ln_gamma"], KT)
    ci["be0"] = coltile(inputs["L0_attn_ln_beta"], KT)
    ci["routerT"] = np.ascontiguousarray(np.asarray(inputs["L0_ffn_router_w"]).T)
    ci["w1"] = np.asarray(inputs["L0_ffn_w1"])
    ci["w2"] = np.asarray(inputs["L0_ffn_w2"])
    ci["n1w"] = coltile(inputs["L1_attn_norm1_w"], KT)
    ci["n2w"] = coltile(inputs["L1_attn_norm2_w"], KT)
    ci["ipw"] = np.asarray(inputs["L1_attn_in_proj_w"])
    ci["ipb"] = coltile(inputs["L1_attn_in_proj_b"], 32)
    ci["convw"] = np.ascontiguousarray(
        np.asarray(inputs["L1_attn_conv_w"]).reshape(2048, 3).reshape(16, 128, 3).transpose(1, 0, 2).reshape(128, 48))
    ci["convb"] = coltile(inputs["L1_attn_conv_b"], 16)
    ci["dtw"] = np.asarray(inputs["L1_attn_dt_w"])
    ci["dtb"] = np.asarray(inputs["L1_attn_dt_b"]).reshape(64, 1)
    ci["bpw"] = np.asarray(inputs["L1_attn_bp_w"])
    ci["bpb"] = np.asarray(inputs["L1_attn_bp_b"]).reshape(64, 1)
    ci["cpw"] = np.asarray(inputs["L1_attn_cp_w"])
    ci["cpb"] = np.asarray(inputs["L1_attn_cp_b"]).reshape(64, 1)
    ci["s2iw"] = np.asarray(inputs["L1_attn_s2i_w"])
    ci["s2ib"] = coltile(inputs["L1_attn_s2i_b"], 16)
    ci["Dpar"] = coltile(inputs["L1_attn_D_param"], 16)
    ci["outw"] = np.asarray(inputs["L1_attn_out_w"])
    ci["outb"] = coltile(inputs["L1_attn_out_b"], KT)
    ci["gatew"] = np.asarray(inputs["L1_attn_gate_w"])
    ci["gateb"] = np.asarray(inputs["L1_attn_gate_b"]).reshape(4, 1)
    ci["gateb_bc"] = np.tile(np.asarray(inputs["L1_attn_gate_b"]).reshape(1, 4), (128, 1))
    ci["eb2all"] = np.asarray(inputs["L1_attn_e_b2"]).astype(BF16NP)
    ci["ew1"] = np.asarray(inputs["L1_attn_e_w1"])
    ci["eb1"] = np.stack([coltile(np.asarray(inputs["L1_attn_e_b1"])[e], 32) for e in range(4)])
    ci["ew2"] = np.asarray(inputs["L1_attn_e_w2"])
    ci["eb2"] = np.stack([coltile(np.asarray(inputs["L1_attn_e_b2"])[e], KT) for e in range(4)])
    ci["Wg"] = np.asarray(inputs["L1_ffn_Wg"])
    ci["bg"] = coltile(inputs["L1_ffn_bg"], 32)
    ci["Wu"] = np.asarray(inputs["L1_ffn_Wu"])
    ci["bu"] = coltile(inputs["L1_ffn_bu"], 32)
    ci["Wd"] = np.asarray(inputs["L1_ffn_Wd"])
    ci["bd"] = coltile(inputs["L1_ffn_bd"], KT)
    ci["Wq"] = np.asarray(inputs["L1_ffn_Wq"])
    kT = np.ascontiguousarray(
        np.asarray(inputs["L1_ffn_keys"]).transpose(3, 2, 0, 1).reshape(64, 2 * 4 * 64))
    ci["keysT"] = kT.astype(BF16NP)
    ci["deT"] = np.ascontiguousarray(np.asarray(inputs["L1_ffn_down_embed"]).T)
    ci["ue"] = np.asarray(inputs["L1_ffn_up_embed"])
    ci["eye_b"] = np.eye(128, dtype=np.float32).astype(BF16NP)
    ci["eye_f"] = np.eye(128, dtype=np.float32)
    sel = np.zeros((8, 1024), np.float32)
    for e_ in range(8):
        sel[e_, e_ * 128:(e_ + 1) * 128] = 1.0
    ci["sel8"] = sel
    ci["ones_col"] = np.ones((128, 1), np.float32)
    ci["ones_row"] = np.ones((1, 128), np.float32)
    ci["constv"] = np.tile(np.array([0.0, 1.0, 2.0, 1e-5, 1e-6, -1.0, 0.5, -0.5], np.float32), (128, 1))
    mp = np.zeros((8, 1), np.float32)
    mn = np.zeros((8, 1), np.float32)
    if half == 1:
        mp[2 * bidx] = 1.0
    if half == 0:
        mn[2 * bidx + 1] = 1.0
    ci["mask_prev"] = mp
    ci["mask_next_b"] = mn.astype(BF16NP)
    for k in list(ci):
        v = ci[k]
        if v.dtype not in (np.float32, BF16NP):
            ci[k] = np.ascontiguousarray(v, dtype=np.float32)
    return ci


def kernel(**inputs):
    nc = build_program(n_stages=4)
    in_maps = [prep_inputs(inputs, c) for c in range(8)]
    res = run_bass_kernel_spmd(nc, in_maps, core_ids=list(range(8)))
    out = np.empty((4, 4096, D), np.float32)
    for c in range(8):
        bidx, half = c // 2, c % 2
        out[bidx, half * T:(half + 1) * T, :] = res.results[c]["y"].T
    return out


# version 05
# speedup vs baseline: 1.5159x; 1.5159x over previous
import numpy as np
import ml_dtypes
import concourse.bass as bass
import concourse.mybir as mybir
from concourse.bass_utils import run_bass_kernel_spmd

F32 = mybir.dt.float32
BF16 = mybir.dt.bfloat16
AF = mybir.ActivationFunctionType
OP = mybir.AluOpType
AX = mybir.AxisListType
BF16NP = ml_dtypes.bfloat16

T = 2048
D = 1024
KT = 8
CH = 8
LR = 0.02

ENGS = ("PE", "DVE", "ACT", "POOL", "SP")


class Sem:
    def __init__(self, h, name):
        self.h = h; self.name = name; self.n = 0


class Builder:
    def __init__(self, nc):
        self.nc = nc
        self.q = {e: [] for e in ENGS}
        self.sems = {}
        self.last_wait = {e: {} for e in ENGS}
        self._ctx = []

    def sem(self, name):
        ctx = self.nc.semaphore(name)
        h = ctx.__enter__()
        self._ctx.append(ctx)
        s = Sem(h, name)
        self.sems[name] = s
        return s

    def rec(self, eng, make, inc=None, inc_n=1):
        if inc is not None:
            inc.n += inc_n
            def f(e, make=make, s=inc, k=inc_n):
                make(e).then_inc(s.h, k)
            self.q[eng].append(f)
        else:
            self.q[eng].append(lambda e, make=make: make(e))

    def wait(self, eng, sem, val=None):
        if val is None:
            val = sem.n
        lw = self.last_wait[eng]
        if lw.get(sem.name, -1) >= val:
            return
        lw[sem.name] = val
        self.q[eng].append(lambda e, s=sem, v=val: e.wait_ge(s.h, v))

    def wait_many(self, eng, toks):
        for (sem, val) in toks:
            self.wait(eng, sem, val)

    def play(self):
        with self.nc.Block() as block:
            @block.tensor
            def _(e):
                for f in self.q["PE"]: f(e)
            @block.vector
            def _(e):
                for f in self.q["DVE"]: f(e)
            @block.scalar
            def _(e):
                for f in self.q["ACT"]: f(e)
            @block.gpsimd
            def _(e):
                for f in self.q["POOL"]: f(e)
            @block.sync
            def _(e):
                for f in self.q["SP"]: f(e)


def build_program(n_stages=4, taps=(), sub4="all"):
    nc = bass.Bass("TRN2", target_bir_lowering=False)
    b = Builder(nc)

    di = {}
    def din(name, shape, dtype=F32):
        di[name] = nc.dram_tensor(name, list(shape), dtype, kind="ExternalInput").ap()
        return di[name]

    din("x_tf", [T, D]); din("x_ft", [D, T])
    din("WbT", [D, D]); din("WpT", [D, D])
    din("initBT", [D, 16]); din("initA", [D, 16]); din("initAT", [16, D]); din("initB", [16, D])
    din("g0", [128, KT]); din("be0", [128, KT])
    din("routerT", [D, 8]); din("w1", [D, 8 * D]); din("w2", [8 * D, D])
    din("n1w", [128, KT]); din("n2w", [128, KT])
    din("ipw", [D, 4 * D]); din("ipb", [128, 32])
    din("convw", [128, 48]); din("convb", [128, 16])
    din("dtw", [2 * D, 64]); din("dtb", [64, 1])
    din("bpw", [2 * D, 64]); din("bpb", [64, 1])
    din("cpw", [2 * D, 64]); din("cpb", [64, 1])
    din("s2iw", [64, 2 * D]); din("s2ib", [128, 16]); din("Dpar", [128, 16])
    din("outw", [2 * D, D]); din("outb", [128, KT])
    din("gatew", [D, 4]); din("gateb", [4, 1]); din("gateb_bc", [128, 4]); din("eb2all", [4, 1024], BF16)
    din("ew1", [4, D, 4 * D]); din("eb1", [4, 128, 32])
    din("ew2", [4, 4 * D, D]); din("eb2", [4, 128, KT])
    din("Wg", [D, 4 * D]); din("bg", [128, 32])
    din("Wu", [D, 4 * D]); din("bu", [128, 32])
    din("Wd", [4 * D, D]); din("bd", [128, KT])
    din("Wq", [D, 512]); din("keysT", [64, 512], BF16)
    din("deT", [D, 4 * D]); din("ue", [4 * D, D])
    din("eye_b", [128, 128], BF16); din("eye_f", [128, 128])
    din("sel8", [8, 1024])
    din("ones_col", [128, 1]); din("ones_row", [1, 128])
    din("constv", [128, 8])
    din("mask_prev", [8, 1]); din("mask_next_b", [8, 1], BF16)

    y_out = nc.dram_tensor("y", [D, T], F32, kind="ExternalOutput").ap()
    dbg = {}
    for name, shape, dt in taps:
        dbg[name] = nc.dram_tensor("dbg_" + name, list(shape), dt, kind="ExternalOutput").ap()

    cc_src_d = nc.dram_tensor("cc_src_d", [128, 8], BF16).ap()
    cc_dst_d = nc.dram_tensor("cc_dst_d", [8 * 128, 8], BF16).ap()
    cc_srcA = nc.dram_tensor("cc_srcA", [16, D], F32).ap()
    cc_dstA = nc.dram_tensor("cc_dstA", [8 * 16, D], F32).ap()
    cc_srcB = nc.dram_tensor("cc_srcB", [128, 128], F32).ap()
    cc_dstB = nc.dram_tensor("cc_dstB", [8 * 128, 128], F32).ap()
    cc_srcH = nc.dram_tensor("cc_srcH", [128, KT * 2], F32).ap()
    cc_dstH = nc.dram_tensor("cc_dstH", [8 * 128, KT * 2], F32).ap()
    cc_srcS = nc.dram_tensor("cc_srcS", [64, 1], F32).ap()
    cc_dstS = nc.dram_tensor("cc_dstS", [8 * 64, 1], F32).ap()
    scr_xm = nc.dram_tensor("scr_xm", [2 * D, T], BF16).ap()
    scr_sg = nc.dram_tensor("scr_sg", [2 * D, T], BF16).ap()

    sbctx = []
    def sbuf(name, shape, dtype=F32):
        ctx = nc.sbuf_tensor(name, list(shape), dtype)
        t = ctx.__enter__(); sbctx.append(ctx)
        return t

    AF32 = sbuf("AF32", [128, 24576])
    ABF = sbuf("ABF", [128, 45056], BF16)
    wbuf = sbuf("wbuf", [128, 2 * 4096], BF16)
    kc = sbuf("kc", [128, 448])
    selc = sbuf("selc", [8, 1024])
    kcb = sbuf("kcb", [128, 264], BF16)

    def f32v(off, cols, part=128, p0=0):
        return AF32[p0:p0 + part, off:off + cols]
    def bfv(off, cols, part=128, p0=0):
        return ABF[p0:p0 + part, off:off + cols]

    xs_o = 0
    def xs(k):
        return AF32[:, xs_o + k * T: xs_o + (k + 1) * T]
    B1_o = 16384
    SQ_o = 17408
    SM_o = 18432
    ACCA_o = 18944
    ACCB_o = 19968
    FREE_F32 = 20096

    eye_bc = kcb[:, 0:128]
    ones_col = kc[:, 0:1]
    ones_row = kc[0:1, 8:136]
    g0c = kc[:, 136:144]
    be0c = kc[:, 144:152]
    maskbc = kc[:, 152:160]
    n1wc = kc[:, 160:168]
    n2wc = kc[:, 168:176]
    eye_fc = kc[:, 184:312]
    CONSTS = [0.0, 1.0, 2.0, 1e-5, 1e-6, -1.0, 0.5, -0.5]
    for ci_, cv_ in enumerate(CONSTS):
        nc.const_aps.register(F32, cv_, kc[:, 176 + ci_:177 + ci_])

    psctx = []
    def psum(name, shape, dtype=F32):
        ctx = nc.psum_tensor(name, list(shape), dtype)
        t = ctx.__enter__(); psctx.append(ctx)
        return t

    psA = psum("psA", [128, 512])
    psB = psum("psB", [128, 512])
    psC = psum("psC", [128, 1024])
    psD = psum("psD", [128, 512])
    psT = psum("psT", [128, 2048], BF16)

    PSMAIN = [psA, psB]

    s_sp = b.sem("s_sp"); s_sc = b.sem("s_sc"); s_gd = b.sem("s_gd")
    s_cst = b.sem("s_cst"); s_ini = b.sem("s_ini"); s_gsl = b.sem("s_gsl")
    s_sg = [b.sem("s_sg0"), b.sem("s_sg1")]
    s_wl = [b.sem("s_wl0"), b.sem("s_wl1")]
    s_pe = b.sem("s_pe"); s_dve = b.sem("s_dve"); s_act = b.sem("s_act")
    s_pool = b.sem("s_pool"); s_cc = b.sem("s_cc")

    def tok(sem):
        return (sem, sem.n)

    GATED_DMA_SEMS = ("s_sp", "s_sc", "s_cst", "s_ini", "s_gsl")

    def dma(stream, sem, out, in_, **kw):
        b.rec(stream, lambda e, o=out, i=in_, kw=kw: e.dma_start(out=o, in_=i, **kw),
              inc=sem, inc_n=16)
        if sem.name in GATED_DMA_SEMS:
            b.wait(stream, sem)
        return tok(sem)

    def tap(name, src_ap, after=()):
        if name in dbg:
            b.wait_many("SP", after)
            dma("SP", s_sp, dbg[name], src_ap)
            for e_ in ENGS:
                b.wait(e_, s_sp)

    def fence_all(extra=()):
        toks = [tok(s_pe), tok(s_dve), tok(s_act), tok(s_pool), tok(s_sp), tok(s_sc),
                tok(s_gd), tok(s_wl[0]), tok(s_wl[1]), tok(s_cc)] + list(extra)
        for e in ENGS:
            b.wait_many(e, toks)

    class WStream:
        def __init__(self):
            self.idx = 0
            self.freed = [[], []]
        def load(self, pieces, n_k):
            par = self.idx % 2; self.idx += 1
            base = par * 4096
            for (sem, val) in self.freed[par]:
                b.wait("POOL", sem, val)
            self.freed[par] = []
            aps = pieces if isinstance(pieces, list) else [pieces]
            tot = sum(a.shape[-1] for a in aps)
            assert n_k * tot <= 4096, (n_k, tot)
            c0 = 0
            for a in aps:
                cw = a.shape[-1]
                dst = wbuf[:, base:base + n_k * tot].rearrange(
                    "p (k c) -> p k c", k=n_k)[:, :, c0:c0 + cw]
                b.rec("POOL", lambda e, o=dst, i=a.rearrange("(k p) c -> p k c", p=128):
                      e.dma_start(out=o, in_=i), inc=s_wl[par], inc_n=16)
                c0 += cw
            t = tok(s_wl[par])
            def view(k, c0=0, cw=None, base=base, tot=tot):
                cw = tot - c0 if cw is None else cw
                return wbuf[:, base + k * tot + c0: base + k * tot + c0 + cw]
            return view, t, par
        def consumed(self, par, t):
            self.freed[par].append(t)

    ws = WStream()

    def ln_full(src_fn, kt, n_feat, eps, with_mean, dst_rows, wait_toks=(), n_tok=T):
        rstd_row, negmur_row = dst_rows
        nn = (n_tok + 511) // 512
        pe_hist = []
        for n in range(nn):
            n0 = n * 512
            nw = min(512, n_tok - n0)
            b.wait("PE", s_dve)
            b.wait("PE", s_act)
            for k in range(kt):
                slot = k % 2
                sq = AF32[:, SQ_o + slot * 512: SQ_o + slot * 512 + nw]
                if len(pe_hist) >= 2:
                    b.wait("ACT", s_pe, pe_hist[-2])
                elif len(pe_hist) == 0:
                    b.wait("ACT", s_pe)
                b.wait_many("ACT", wait_toks)
                b.rec("ACT", lambda e, i=src_fn(k)[:, n0:n0 + nw], o=sq:
                      e.activation(out=o, in_=i, func=AF.Square), inc=s_act)
                b.wait("PE", s_act)
                b.wait_many("PE", wait_toks)
                if with_mean:
                    b.rec("PE", lambda e, r=src_fn(k)[:, n0:n0 + nw], p=psD[0:1, 0:nw],
                                 st=(k == 0), lt=(k == kt - 1):
                          e.matmul(p, ones_col, r, start=st, stop=lt))
                b.rec("PE", lambda e, r=sq, p=psC[0:1, 0:nw], st=(k == 0), lt=(k == kt - 1):
                      e.matmul(p, ones_col, r, start=st, stop=lt), inc=s_pe)
                pe_hist.append(s_pe.n)
            b.wait("DVE", s_pe)
            if with_mean:
                b.rec("DVE", lambda e, p=psD[0:1, 0:nw], o=rowA(nw) if False else negmur_row[:, n0:n0 + nw]:
                      e.tensor_scalar(o, p, 1.0 / n_feat, None, OP.mult), inc=s_dve)
                b.wait("DVE", s_dve)
                b.rec("DVE", lambda e, mu=negmur_row[:, n0:n0 + nw], o=rstd_row[:, n0:n0 + nw]:
                      e.tensor_tensor(o, mu, mu, OP.mult), inc=s_dve)
                b.wait("DVE", s_dve)
                b.rec("DVE", lambda e, p=psC[0:1, 0:nw], m2=rstd_row[:, n0:n0 + nw],
                             o=rstd_row[:, n0:n0 + nw]:
                      e.scalar_tensor_tensor(o, p, 1.0 / n_feat, m2, OP.mult, OP.subtract), inc=s_dve)
            else:
                b.rec("DVE", lambda e, p=psC[0:1, 0:nw], o=rstd_row[:, n0:n0 + nw]:
                      e.tensor_scalar(o, p, 1.0 / n_feat, None, OP.mult), inc=s_dve)
        dve_t = tok(s_dve)
        b.wait_many("ACT", [dve_t])
        b.rec("ACT", lambda e, v=rstd_row, o=rstd_row:
              e.activation(out=o, in_=v, func=AF.Ln, bias=float(eps)), inc=s_act)
        b.wait("ACT", s_act)
        b.rec("ACT", lambda e, v=rstd_row, o=rstd_row:
              e.activation(out=o, in_=v, func=AF.Exp, scale=-0.5), inc=s_act)
        t_act = tok(s_act)
        if with_mean:
            b.wait_many("DVE", [t_act])
            b.rec("DVE", lambda e, mu=negmur_row, r=rstd_row, o=negmur_row:
                  e.scalar_tensor_tensor(o, mu, -1.0, r, OP.mult, OP.mult), inc=s_dve)
            return tok(s_dve)
        return t_act

    def bcast_chunk(row_ap, n0, nw, dst):
        b.wait("PE", s_dve)
        b.wait("PE", s_act)
        b.rec("PE", lambda e, r=row_ap[:, n0:n0 + nw], p=psD[:, 0:nw]:
              e.matmul(p, ones_row, r, start=True, stop=True), inc=s_pe)
        b.wait("DVE", s_pe)
        b.rec("DVE", lambda e, p=psD[:, 0:nw], o=dst[:, 0:nw]:
              e.tensor_copy(o, p), inc=s_dve)
        return tok(s_dve)

    dma("SP", s_sp, kc[:, 0:1], di["ones_col"])
    dma("SP", s_sp, kc[0:1, 8:136], di["ones_row"])
    dma("SP", s_sp, kc[:, 136:144], di["g0"])
    dma("SP", s_sp, kc[:, 144:152], di["be0"])
    dma("SP", s_sp, kc[:, 152:160],
        di["mask_prev"].rearrange("r c -> (r c)").partition_broadcast(128))
    dma("SP", s_sp, kc[:, 160:168], di["n1w"])
    dma("SP", s_sp, kc[:, 168:176], di["n2w"])
    dma("SP", s_sp, kc[:, 176:184], di["constv"])
    dma("SP", s_sp, kc[:, 184:312], di["eye_f"])
    dma("SP", s_sp, selc[:], di["sel8"])
    dma("POOL", s_cst, kcb[:, 0:128], di["eye_b"])
    dma("POOL", s_cst, kcb[0:8, 128:129], di["mask_next_b"])
    const_t = tok(s_sp); const_tb = tok(s_cst)

    for k in range(KT):
        dma("SP", s_sp, xs(k), di["x_ft"][k * 128:(k + 1) * 128, :])
    x_t = tok(s_sp)

    XN_o = 0
    Tp = T + 1
    def xn(k):
        return ABF[:, XN_o + k * Tp: XN_o + (k + 1) * Tp]
    XBC_o = 16392
    def xbc(k):
        return ABF[:, XBC_o + k * 512: XBC_o + (k + 1) * 512]
    VC_o = 20488
    def vch(par, k):
        return ABF[:, VC_o + par * 2048 + k * 256: VC_o + par * 2048 + (k + 1) * 256]
    S3_o = 24584
    DAB_o = 28680
    DBT_o = 36872
    B2T_o = 37896
    MID_o = 38024
    TMP_o = 40072
    INIT_o = 41096

    def dab(c):
        return ABF[0:16, DAB_o + c * 1024: DAB_o + (c + 1) * 1024]

    r_rstd = AF32[0:1, FREE_F32:FREE_F32 + T]
    r_negmur = AF32[0:1, FREE_F32 + T:FREE_F32 + 2 * T]
    ln1_t = ln_full(xs, KT, D, 1e-5, True, (r_rstd, r_negmur), wait_toks=[x_t, const_t])

    B1c = AF32[:, B1_o:B1_o + 512]
    B2c = AF32[:, B1_o + 512:B1_o + 1024]
    FT1 = AF32[:, SQ_o:SQ_o + 512]
    for n in range(4):
        n0 = n * 512
        b.wait_many("PE", [ln1_t])
        bcast_chunk(r_rstd, n0, 512, B1c)
        bcast_chunk(r_negmur, n0, 512, B2c)
        bc_t = tok(s_dve)
        b.wait_many("DVE", [bc_t, x_t, const_t])
        for k in range(KT):
            b.rec("DVE", lambda e, k=k, n0=n0: e.tensor_tensor(
                FT1, xs(k)[:, n0:n0 + 512], B1c, OP.mult), inc=s_dve)
            b.wait("DVE", s_dve)
            b.rec("DVE", lambda e, k=k, n0=n0: e.tensor_tensor(
                FT1, FT1, B2c, OP.add), inc=s_dve)
            b.wait("DVE", s_dve)
            b.rec("DVE", lambda e, k=k, n0=n0: e.tensor_scalar(
                xn(k)[:, n0:n0 + 512], FT1, g0c[:, k:k + 1], be0c[:, k:k + 1],
                OP.mult, OP.add), inc=s_dve)
            b.wait("DVE", s_dve)
    xn_t = tok(s_dve)

    b.wait("SP", s_pe)
    dma("SP", s_sp, AF32[0:16, FREE_F32 + 2048:FREE_F32 + 3072], di["initAT"])
    dma("SP", s_sp, AF32[:, FREE_F32 + 1152:FREE_F32 + 1280].rearrange("p (k c) -> p k c", k=KT),
        di["initBT"].rearrange("(k p) c -> p k c", p=128))
    init_tf = tok(s_sp)

    b.wait_many("DVE", [xn_t])
    for k in range(KT):
        b.rec("DVE", lambda e, k=k: e.tensor_copy(
            ABF[:, TMP_o + 96 + k:TMP_o + 97 + k], ABF[:, XN_o + k * Tp:XN_o + k * Tp + 1]),
            inc=s_dve)
    b.wait_many("SP", [tok(s_dve)])
    dma("SP", s_sp, cc_src_d, ABF[:, TMP_o + 96:TMP_o + 104])
    b.wait("POOL", s_sp)
    b.rec("POOL", lambda e: e.collective_compute(
        "AllGather", OP.bypass, replica_groups=[list(range(8))],
        ins=[cc_src_d[:].opt()], outs=[cc_dst_d[:].opt()]), inc=s_cc)
    b.wait("POOL", s_cc)
    gsl = ABF[:, TMP_o:TMP_o + 64]
    dma("POOL", s_gsl, gsl.rearrange("p (r k) -> p r k", r=8),
        cc_dst_d.rearrange("(r p) k -> p r k", p=128))
    mnb = ABF[:, TMP_o + 64:TMP_o + 72]
    dma("POOL", s_gsl, mnb, di["mask_next_b"].rearrange("r c -> (r c)").partition_broadcast(128))
    g_t = tok(s_gsl)
    b.wait_many("DVE", [g_t])
    b.rec("DVE", lambda e: e.tensor_tensor(
        gsl.rearrange("p (r k) -> p r k", r=8),
        gsl.rearrange("p (r k) -> p r k", r=8),
        mnb.rearrange("p (r o) -> p r o", o=1).broadcast_to((128, 8, 8)), OP.mult), inc=s_dve)
    b.wait("DVE", s_dve)
    def _red_shift(e):
        with nc.allow_low_precision(reason="0/1-mask select: single nonzero term"):
            return e.tensor_reduce(ABF[:, TMP_o + 72:TMP_o + 80],
                                   gsl.rearrange("p (r k) -> p k r", r=8), AX.X, OP.add)
    b.rec("DVE", _red_shift, inc=s_dve)
    b.wait("DVE", s_dve)
    for k in range(KT):
        b.rec("DVE", lambda e, k=k: e.tensor_copy(
            ABF[:, XN_o + k * Tp + T:XN_o + k * Tp + T + 1], ABF[:, TMP_o + 72 + k:TMP_o + 73 + k]),
            inc=s_dve)
    shift_t = tok(s_dve)

    initBTb = ABF[:, INIT_o:INIT_o + 128]
    initAb = ABF[:, INIT_o + 128:INIT_o + 256]
    dma("POOL", s_ini, initBTb.rearrange("p (k c) -> p k c", k=KT),
        di["initBT"].rearrange("(k p) c -> p k c", p=128))
    dma("POOL", s_ini, initAb.rearrange("p (k c) -> p k c", k=KT),
        di["initA"].rearrange("(k p) c -> p k c", p=128))
    init_tb = tok(s_ini)
    GA = AF32[0:16, FREE_F32:FREE_F32 + 1024]
    GB = AF32[:, FREE_F32 + 1024:FREE_F32 + 1152]
    initBT_f = AF32[:, FREE_F32 + 1152:FREE_F32 + 1280]
    initAT_f = AF32[0:16, FREE_F32 + 2048:FREE_F32 + 3072]

    wv0, wv0_t, _ = ws.load(di["WpT"][:, 0:512], KT)
    wv1, wv1_t, _ = ws.load(di["WpT"][:, 512:1024], KT)
    def wpT(k, m):
        if m < 4:
            return wv0(k, m * 128, 128)
        return wv1(k, (m - 4) * 128, 128)

    projin = ABF[:, INIT_o + 256:INIT_o + 256 + 32]
    projerr = ABF[:, INIT_o + 288:INIT_o + 288 + 32]
    prev_evac = {}
    tp_i = 0
    tp_evac = {}
    for c in range(CH):
        par = c % 2
        t0 = c * 256
        b.wait_many("PE", [shift_t, wv0_t, wv1_t])
        for m in range(KT):
            ps = PSMAIN[m % 2]
            if (c, m) != (0, 0) and (c, m) != (0, 1):
                pm = (c, m - 2) if m >= 2 else (c - 1, m + 6)
                b.wait("PE", s_dve, prev_evac[pm][1])
            for k in range(KT):
                b.rec("PE", lambda e, k=k, m=m, t0=t0, ps=ps: e.matmul(
                    ps[:, 0:256], wpT(k, m), xn(k)[:, t0 + 1:t0 + 257],
                    start=(k == 0), stop=(k == KT - 1)), inc=(s_pe if k == KT - 1 else None))
            b.wait("DVE", s_pe)
            b.rec("DVE", lambda e, m=m, par=par, ps=ps: e.tensor_copy(
                vch(par, m), ps[:, 0:256]), inc=s_dve)
            prev_evac[(c, m)] = tok(s_dve)
        v_t = tok(s_dve)
        xT = lambda h: ABF[:, S3_o + 2048 + h * 1024: S3_o + 2048 + (h + 1) * 1024]
        b.wait("POOL", s_pe)
        dma("POOL", s_gd, ABF[:, S3_o + 2048:S3_o + 4096].rearrange("p (h f) -> p h f", h=2),
            di["x_tf"][t0:t0 + 256, :].rearrange("(h p) f -> p h f", p=128))
        xT_t = tok(s_gd)
        VT = lambda h: ABF[:, S3_o + h * 1024: S3_o + (h + 1) * 1024]
        for h in range(2):
            for k in range(KT):
                slot = tp_i % 2
                b.wait_many("PE", [v_t, const_tb])
                if tp_i >= 2:
                    b.wait("PE", s_dve, tp_evac[tp_i - 2])
                b.rec("PE", lambda e, h=h, k=k, par=par, slot=slot: e.transpose(
                    psT[:, slot * 1024:slot * 1024 + 128],
                    vch(par, k)[:, h * 128:(h + 1) * 128], eye_bc),
                    inc=s_pe)
                b.wait("DVE", s_pe)
                b.rec("DVE", lambda e, h=h, k=k, slot=slot: e.tensor_copy(
                    VT(h)[:, k * 128:(k + 1) * 128], psT[:, slot * 1024:slot * 1024 + 128]),
                    inc=s_dve)
                tp_evac[tp_i] = s_dve.n
                tp_i += 1
        vt_t = tok(s_dve)
        b.wait("DVE", s_pe)
        b.wait_many("DVE", [x_t])
        for k in range(KT):
            b.rec("DVE", lambda e, k=k, t0=t0: e.tensor_copy(
                xbc(k)[:, 0:256], xs(k)[:, t0:t0 + 256]), inc=s_dve)
        xc_t = tok(s_dve)
        b.wait_many("PE", [xc_t, init_tb, v_t])
        b.wait("PE", s_dve)
        for h in range(2):
            if h == 1:
                b.wait("PE", s_dve)
            for k in range(KT):
                b.rec("PE", lambda e, h=h, k=k: e.matmul(
                    psC[0:128, 0:16], xbc(k)[:, h * 128:(h + 1) * 128],
                    initBTb.rearrange("p (k c) -> p k c", k=KT)[:, k, :],
                    start=(k == 0), stop=(k == KT - 1)))
                b.rec("PE", lambda e, h=h, k=k, par=par: e.matmul(
                    psC[0:128, 512:528], vch(par, k)[:, h * 128:(h + 1) * 128],
                    initAb.rearrange("p (k c) -> p k c", k=KT)[:, k, :],
                    start=(k == 0), stop=(k == KT - 1)), inc=(s_pe if k == KT - 1 else None))
            b.wait("DVE", s_pe)
            b.rec("DVE", lambda e, h=h: e.tensor_copy(projin[:, h * 16:(h + 1) * 16], psC[0:128, 0:16]), inc=s_dve)
            b.rec("DVE", lambda e, h=h: e.tensor_copy(projerr[:, h * 16:(h + 1) * 16], psC[0:128, 512:528]), inc=s_dve)
        pr_t = tok(s_dve)
        b.wait_many("PE", [pr_t, vt_t, xT_t])
        for h in range(2):
            for nn2 in range(2):
                b.rec("PE", lambda e, h=h, nn2=nn2: e.matmul(
                    psC[0:16, nn2 * 512:(nn2 + 1) * 512], projin[:, h * 16:(h + 1) * 16],
                    VT(h)[:, nn2 * 512:(nn2 + 1) * 512], start=(h == 0), stop=(h == 1)))
        for h in range(2):
            for m in range(KT):
                b.rec("PE", lambda e, m=m, h=h: e.matmul(
                    psD[0:128, m * 16:(m + 1) * 16],
                    xT(h)[:, m * 128:(m + 1) * 128], projerr[:, h * 16:(h + 1) * 16],
                    start=(h == 0 and m == 0), stop=(h == 1 and m == KT - 1)),
                    inc=(s_pe if (m == KT - 1 and h == 1) else None))
        b.wait("DVE", s_pe)
        b.rec("DVE", lambda e, c=c: e.tensor_copy(dab(c), psC[0:16, 0:1024]), inc=s_dve)
        b.rec("DVE", lambda e, c=c: e.tensor_copy(
            ABF[:, DBT_o + c * 128:DBT_o + (c + 1) * 128], psD[0:128, 0:128]), inc=s_dve)
    d_t = tok(s_dve)
    ws.consumed(0, tok(s_pe))
    ws.consumed(1, tok(s_pe))

    accA = AF32[0:16, ACCA_o:ACCA_o + 1024]
    accBT = AF32[:, ACCB_o:ACCB_o + 128]
    b.wait_many("DVE", [d_t])
    b.rec("DVE", lambda e: e.memset(accA, 0.0), inc=s_dve)
    b.rec("DVE", lambda e: e.memset(accBT, 0.0), inc=s_dve)
    for c in range(CH):
        b.wait("DVE", s_dve)
        b.rec("DVE", lambda e, c=c: e.tensor_tensor(accA, accA, dab(c), OP.add), inc=s_dve)
        b.rec("DVE", lambda e, c=c: e.tensor_tensor(
            accBT, accBT, ABF[:, DBT_o + c * 128:DBT_o + (c + 1) * 128], OP.add), inc=s_dve)
    sum_t = tok(s_dve)
    b.wait_many("SP", [sum_t])
    dma("SP", s_sp, cc_srcA, accA)
    dma("SP", s_sp, cc_srcB, accBT)
    b.wait("POOL", s_sp)
    b.rec("POOL", lambda e: e.collective_compute(
        "AllGather", OP.bypass, replica_groups=[list(range(8))],
        ins=[cc_srcA[:].opt()], outs=[cc_dstA[:].opt()]), inc=s_cc)
    b.rec("POOL", lambda e: e.collective_compute(
        "AllGather", OP.bypass, replica_groups=[list(range(8))],
        ins=[cc_srcB[:].opt()], outs=[cc_dstB[:].opt()]), inc=s_cc)
    cc_ab = tok(s_cc)

    b.wait_many("DVE", [cc_ab])
    b.rec("DVE", lambda e: e.memset(accA, 0.0), inc=s_dve)
    b.rec("DVE", lambda e: e.memset(accBT, 0.0), inc=s_dve)
    for r in range(8):
        b.wait_many("SP", [cc_ab, tok(s_dve)])
        dma("SP", s_sp, GA, cc_dstA[r * 16:(r + 1) * 16, :])
        dma("SP", s_sp, GB, cc_dstB[r * 128:(r + 1) * 128, :])
        b.wait("DVE", s_sp)
        b.rec("DVE", lambda e, r=r: e.scalar_tensor_tensor(
            accA, GA, maskbc[0:16, r:r + 1], accA, OP.mult, OP.add), inc=s_dve)
        b.rec("DVE", lambda e, r=r: e.scalar_tensor_tensor(
            accBT, GB, maskbc[0:128, r:r + 1], accBT, OP.mult, OP.add), inc=s_dve)
    acc_t = tok(s_dve)

    midv = lambda c: ABF[0:16, MID_o + c * 256:MID_o + (c + 1) * 256]
    B2Tb = ABF[:, B2T_o:B2T_o + 128]
    coefcol = AF32[:, SM_o:SM_o + 1]
    sq16 = AF32[0:16, SQ_o:SQ_o + 1024]
    sq128 = AF32[:, B1_o:B1_o + 128]
    b.wait_many("DVE", [init_tf])
    for c in range(CH):
        dep = acc_t if c == 0 else tok(s_dve)
        b.wait_many("ACT", [dep])
        b.rec("ACT", lambda e: e.activation(out=sq16, in_=accA, func=AF.Square), inc=s_act)
        b.rec("ACT", lambda e: e.activation(out=sq128, in_=accBT, func=AF.Square), inc=s_act)
        b.wait("DVE", s_act)
        b.rec("DVE", lambda e: e.tensor_reduce(AF32[0:16, SM_o + 8:SM_o + 9], sq16, AX.X, OP.add), inc=s_dve)
        b.rec("DVE", lambda e: e.tensor_reduce(AF32[0:128, SM_o + 9:SM_o + 10], sq128, AX.X, OP.add), inc=s_dve)
        b.wait("PE", s_dve)
        b.rec("PE", lambda e: e.matmul(psD[0:1, 0:1], ones_col[0:16, :], AF32[0:16, SM_o + 8:SM_o + 9],
                                       start=True, stop=False))
        b.rec("PE", lambda e: e.matmul(psD[0:1, 0:1], ones_col, AF32[0:128, SM_o + 9:SM_o + 10],
                                       start=False, stop=True), inc=s_pe)
        b.wait("ACT", s_pe)
        b.rec("ACT", lambda e: e.activation(out=AF32[0:1, SM_o + 10:SM_o + 11], in_=psD[0:1, 0:1],
                                            func=AF.Sqrt, scale=LR * LR), inc=s_act)
        b.wait("DVE", s_act)
        b.rec("DVE", lambda e: e.tensor_scalar(AF32[0:1, SM_o + 11:SM_o + 12],
                                               AF32[0:1, SM_o + 10:SM_o + 11], 1e-6, None, OP.add), inc=s_dve)
        b.wait("DVE", s_dve)
        b.rec("DVE", lambda e: e.reciprocal(AF32[0:1, SM_o + 12:SM_o + 13],
                                            AF32[0:1, SM_o + 11:SM_o + 12]), inc=s_dve)
        b.wait("DVE", s_dve)
        b.rec("DVE", lambda e: e.tensor_scalar(AF32[0:1, SM_o + 13:SM_o + 14],
                                               AF32[0:1, SM_o + 12:SM_o + 13], 1.0, None, OP.min), inc=s_dve)
        b.wait("PE", s_dve)
        b.rec("PE", lambda e: e.matmul(psD[0:128, 4:5], ones_row, AF32[0:1, SM_o + 13:SM_o + 14],
                                       start=True, stop=True), inc=s_pe)
        b.wait("DVE", s_pe)
        b.rec("DVE", lambda e: e.tensor_scalar(coefcol, psD[0:128, 4:5], -LR, None, OP.mult), inc=s_dve)
        b.wait("DVE", s_dve)
        b.rec("DVE", lambda e, c=c: e.tensor_copy(ABF[0:16, TMP_o:TMP_o + 1024], dab(c)), inc=s_dve)
        b.wait("DVE", s_dve)
        b.rec("DVE", lambda e, c=c: e.scalar_tensor_tensor(
            dab(c), accA, coefcol[0:16, :], initAT_f, OP.mult, OP.add), inc=s_dve)
        b.rec("DVE", lambda e, c=c: e.scalar_tensor_tensor(
            B2Tb, accBT, coefcol[0:128, :], initBT_f, OP.mult, OP.add), inc=s_dve)
        ab_t = tok(s_dve)
        b.wait("DVE", s_dve)
        b.rec("DVE", lambda e, c=c: e.tensor_tensor(accA, accA, ABF[0:16, TMP_o:TMP_o + 1024], OP.add), inc=s_dve)
        b.rec("DVE", lambda e, c=c: e.tensor_tensor(
            accBT, accBT, ABF[:, DBT_o + c * 128:DBT_o + (c + 1) * 128], OP.add), inc=s_dve)
        b.wait("DVE", s_pe)
        b.wait_many("DVE", [x_t])
        for k in range(KT):
            b.rec("DVE", lambda e, k=k, c=c: e.tensor_copy(
                xbc(k)[:, 0:256], xs(k)[:, c * 256:c * 256 + 256]), inc=s_dve)
        xc2_t = tok(s_dve)
        b.wait_many("PE", [ab_t, xc2_t])
        for k in range(KT):
            b.rec("PE", lambda e, k=k, lt=(k == KT - 1): e.matmul(
                psC[0:16, 0:256], B2Tb[:, k * 16:(k + 1) * 16], xbc(k)[:, 0:256],
                start=(k == 0), stop=lt), inc=(s_pe if k == KT - 1 else None))
        b.wait("ACT", s_pe)
        b.rec("ACT", lambda e, c=c: e.activation(out=midv(c), in_=psC[0:16, 0:256],
                                                 func=AF.Copy, scale=2.0), inc=s_act)
        b.wait("DVE", s_pe)
    mid_t = tok(s_act)
    a2_t = tok(s_dve)

    wb0, wb0_t, wb0p = ws.load(di["WbT"][:, 0:512], KT)
    wb1, wb1_t, wb1p = ws.load(di["WbT"][:, 512:1024], KT)
    def wbT(k, m):
        return wb0(k, m * 128, 128) if m < 4 else wb1(k, (m - 4) * 128, 128)
    ev_prev = {}
    g = 0
    for n in range(4):
        n0 = n * 512
        b.wait_many("DVE", [x_t, mid_t])
        if n >= 1:
            b.wait("DVE", s_pe)
        for k in range(KT):
            b.rec("DVE", lambda e, k=k, n0=n0: e.tensor_copy(
                xbc(k), xs(k)[:, n0:n0 + 512]), inc=s_dve)
        cast_t = tok(s_dve)
        b.wait_many("PE", [cast_t, wb0_t, wb1_t, mid_t, a2_t])
        for m in range(KT):
            ps = PSMAIN[g % 2]
            if g >= 2:
                b.wait("PE", *ev_prev[g - 2])
            for k in range(KT):
                b.rec("PE", lambda e, k=k, m=m, ps=ps: e.matmul(
                    ps[:, 0:512], wbT(k, m), xbc(k), start=(k == 0), stop=False))
            for ci in range(2):
                c = n * 2 + ci
                b.rec("PE", lambda e, c=c, m=m, ci=ci, ps=ps: e.matmul(
                    ps[:, ci * 256:(ci + 1) * 256],
                    dab(c)[:, m * 128:(m + 1) * 128], midv(c),
                    start=False, stop=(ci == 1)), inc=(s_pe if ci == 1 else None))
            b.wait("DVE", s_pe)
            b.rec("DVE", lambda e, m=m, n0=n0, ps=ps: e.tensor_tensor(
                xs(m)[:, n0:n0 + 512], xs(m)[:, n0:n0 + 512], ps[:, 0:512], OP.add), inc=s_dve)
            ev_prev[g] = tok(s_dve)
            g += 1
    ws.consumed(wb0p, tok(s_dve))
    ws.consumed(wb1p, tok(s_dve))
    attn_t = tok(s_dve)
    tap("res1", AF32[:, 0:16384], [attn_t])

    def post_ln(src_tok):
        lt = ln_full(xs, KT, D, 1e-5, True, (r_rstd, r_negmur), wait_toks=[src_tok])
        for n in range(4):
            n0 = n * 512
            b.wait_many("PE", [lt])
            bcast_chunk(r_rstd, n0, 512, B1c)
            bcast_chunk(r_negmur, n0, 512, B2c)
            bc_t = tok(s_dve)
            b.wait_many("DVE", [bc_t])
            for k in range(KT):
                b.rec("DVE", lambda e, k=k, n0=n0: e.tensor_tensor(
                    FT1, xs(k)[:, n0:n0 + 512], B1c, OP.mult), inc=s_dve)
                b.wait("DVE", s_dve)
                b.rec("DVE", lambda e, k=k, n0=n0: e.tensor_tensor(
                    xs(k)[:, n0:n0 + 512], FT1, B2c, OP.add), inc=s_dve)
                b.wait("DVE", s_dve)
        return tok(s_dve)

    st1_t = post_ln(attn_t)
    tap("x1", AF32[:, 0:16384], [st1_t])
    last_t = st1_t

    if n_stages >= 2:
        def xb2(k):
            return ABF[:, k * T:(k + 1) * T]
        def rbuf(k):
            return ABF[:, 16384 + k * T:16384 + (k + 1) * T]
        routb = ABF[:, 32768:32832]
        B_cw = AF32[:, FREE_F32:FREE_F32 + T]
        cwF = AF32[0:8, FREE_F32 + 2048:FREE_F32 + 2048 + T]
        probs_all = AF32[:, SM_o:SM_o + 128]
        cw8_all = AF32[:, SM_o + 128:SM_o + 256]

        b.wait_many("DVE", [st1_t])
        b.wait("DVE", s_pe)
        b.wait("DVE", s_gsl)
        for k in range(KT):
            b.rec("DVE", lambda e, k=k: e.tensor_copy(xb2(k), xs(k)), inc=s_dve)
        sh_t = tok(s_dve)

        b.wait("POOL", s_pe)
        dma("POOL", s_ini, routb.rearrange("p (k c) -> p k c", k=KT),
            di["routerT"].rearrange("(k p) c -> p k c", p=128))
        rt_t = tok(s_ini)

        b.wait_many("PE", [sh_t, rt_t])
        for tt in range(16):
            b.wait("PE", s_dve)
            b.wait("PE", s_act)
            for k in range(KT):
                b.rec("PE", lambda e, tt=tt, k=k: e.matmul(
                    psC[0:128, 0:8], xb2(k)[:, tt * 128:(tt + 1) * 128],
                    routb.rearrange("p (k c) -> p k c", k=KT)[:, k, :],
                    start=(k == 0), stop=(k == KT - 1)), inc=(s_pe if k == KT - 1 else None))
            b.wait("ACT", s_pe)
            b.rec("ACT", lambda e, tt=tt: e.activation(
                out=probs_all[:, tt * 8:(tt + 1) * 8], in_=psC[0:128, 0:8],
                func=AF.Sigmoid), inc=s_act)
            b.wait("DVE", s_act)
            m8 = AF32[:, SM_o + 256:SM_o + 264]
            sA = AF32[:, SM_o + 264:SM_o + 265]
            sB = AF32[:, SM_o + 265:SM_o + 266]
            b.rec("DVE", lambda e, tt=tt: e.max(out=m8, in_=probs_all[:, tt * 8:(tt + 1) * 8]), inc=s_dve)
            b.wait("DVE", s_dve)
            b.rec("DVE", lambda e: e.tensor_tensor(sA, m8[:, 0:1], m8[:, 1:2], OP.add), inc=s_dve)
            b.wait("DVE", s_dve)
            b.rec("DVE", lambda e: e.tensor_scalar(sA, sA, 1e-20, None, OP.add), inc=s_dve)
            b.wait("DVE", s_dve)
            b.rec("DVE", lambda e: e.reciprocal(sB, sA), inc=s_dve)
            b.wait("DVE", s_dve)
            c1 = AF32[:, SM_o + 266:SM_o + 267]
            c2 = AF32[:, SM_o + 267:SM_o + 268]
            b.rec("DVE", lambda e: e.tensor_tensor(c1, m8[:, 0:1], sB, OP.mult), inc=s_dve)
            b.rec("DVE", lambda e: e.tensor_tensor(c2, m8[:, 1:2], sB, OP.mult), inc=s_dve)
            b.wait("DVE", s_dve)
            t8 = AF32[:, SM_o + 268:SM_o + 276]
            b.rec("DVE", lambda e, tt=tt: e.tensor_scalar(
                t8, probs_all[:, tt * 8:(tt + 1) * 8], m8[:, 0:1], c1, OP.is_equal, OP.mult), inc=s_dve)
            b.rec("DVE", lambda e, tt=tt: e.tensor_scalar(
                cw8_all[:, tt * 8:(tt + 1) * 8], probs_all[:, tt * 8:(tt + 1) * 8],
                m8[:, 1:2], c2, OP.is_equal, OP.mult), inc=s_dve)
            b.wait("DVE", s_dve)
            b.rec("DVE", lambda e, tt=tt: e.tensor_tensor(
                cw8_all[:, tt * 8:(tt + 1) * 8], cw8_all[:, tt * 8:(tt + 1) * 8], t8, OP.add), inc=s_dve)
            cw_t = tok(s_dve)
            b.wait_many("PE", [cw_t, const_t])
            b.rec("PE", lambda e, tt=tt: e.transpose(
                psC[0:8, 512:640], cw8_all[:, tt * 8:(tt + 1) * 8], eye_fc), inc=s_pe)
            b.wait("DVE", s_pe)
            b.rec("DVE", lambda e, tt=tt: e.tensor_copy(
                cwF[:, tt * 128:(tt + 1) * 128], psC[0:8, 512:640]), inc=s_dve)
        cwf_t = tok(s_dve)

        ev2 = {}
        g2 = 0
        for ex in range(8):
            b.wait_many("PE", [cwf_t, const_t])
            b.wait("PE", s_dve)
            for n in range(4):
                b.wait("PE", s_dve)
                b.rec("PE", lambda e, ex=ex, n=n: e.matmul(
                    psD[:, 0:512], selc[0:8, ex * 128:(ex + 1) * 128],
                    cwF[:, n * 512:(n + 1) * 512], start=True, stop=True), inc=s_pe)
                b.wait("DVE", s_pe)
                b.wait("DVE", s_pool)
                b.rec("DVE", lambda e, n=n: e.tensor_copy(
                    B_cw[:, n * 512:(n + 1) * 512], psD[:, 0:512]), inc=s_dve)
            bcw_t = tok(s_dve)
            for p in range(4):
                wv, wv_t, wv_par = ws.load(di["w1"][:, ex * D + p * 256: ex * D + (p + 1) * 256], KT)
                b.wait_many("PE", [wv_t, sh_t])
                for m2 in range(2):
                    m = p * 2 + m2
                    for n in range(4):
                        ps = PSMAIN[g2 % 2]
                        prev = ev2.get(g2 - 2)
                        if prev is not None:
                            b.wait("PE", *prev)
                        for k in range(KT):
                            b.rec("PE", lambda e, k=k, m2=m2, n=n, ps=ps, wv=wv: e.matmul(
                                ps[:, 0:512], wv(k, m2 * 128, 128), xb2(k)[:, n * 512:(n + 1) * 512],
                                start=(k == 0), stop=(k == KT - 1)), inc=(s_pe if k == KT - 1 else None))
                        b.wait("ACT", s_pe)
                        b.rec("ACT", lambda e, m=m, n=n, ps=ps: e.activation(
                            out=rbuf(m)[:, n * 512:(n + 1) * 512], in_=ps[:, 0:512],
                            func=AF.Relu), inc=s_act)
                        ev2[g2] = tok(s_act)
                        g2 += 1
                ws.consumed(wv_par, tok(s_act))
            relu_t = tok(s_act)
            b.wait_many("DVE", [relu_t])
            for k in range(KT):
                b.rec("DVE", lambda e, k=k: e.tensor_tensor(rbuf(k), rbuf(k), rbuf(k), OP.mult), inc=s_dve)
            sq_t = tok(s_dve)
            b.wait_many("POOL", [sq_t, bcw_t])
            for k in range(KT):
                b.rec("POOL", lambda e, k=k: e.tensor_tensor(rbuf(k), rbuf(k), B_cw, OP.mult), inc=s_pool)
            a_t = tok(s_pool)
            for p in range(4):
                wv, wv_t, wv_par = ws.load(di["w2"][ex * D:(ex + 1) * D, p * 256:(p + 1) * 256], KT)
                b.wait_many("PE", [wv_t, a_t])
                for m2 in range(2):
                    m = p * 2 + m2
                    for n in range(4):
                        ps = PSMAIN[g2 % 2]
                        prev = ev2.get(g2 - 2)
                        if prev is not None:
                            b.wait("PE", *prev)
                        for k in range(KT):
                            b.rec("PE", lambda e, k=k, m2=m2, n=n, ps=ps, wv=wv: e.matmul(
                                ps[:, 0:512], wv(k, m2 * 128, 128), rbuf(k)[:, n * 512:(n + 1) * 512],
                                start=(k == 0), stop=(k == KT - 1)), inc=(s_pe if k == KT - 1 else None))
                        b.wait("DVE", s_pe)
                        b.rec("DVE", lambda e, m=m, n=n, ps=ps: e.tensor_tensor(
                            xs(m)[:, n * 512:(n + 1) * 512], xs(m)[:, n * 512:(n + 1) * 512],
                            ps[:, 0:512], OP.add), inc=s_dve)
                        ev2[g2] = tok(s_dve)
                        g2 += 1
                ws.consumed(wv_par, tok(s_dve))
        moe_t = tok(s_dve)
        tap("res2", AF32[:, 0:16384], [moe_t])
        st2_t = post_ln(moe_t)
        tap("x2", AF32[:, 0:16384], [st2_t])
        last_t = st2_t

    if n_stages >= 3:
        DTv = AF32[0:64, 16384:16384 + T]
        BMv = AF32[0:64, 18432:18432 + T]
        CMv = AF32[0:64, 20480:20480 + T]
        SSv = AF32[0:64, 22528:22528 + T]
        XM_o = 0
        def xmt(k):
            return ABF[:, XM_o + k * T:XM_o + (k + 1) * T]
        HCH_o = 32768
        def hch(k):
            return ABF[:, HCH_o + k * 512:HCH_o + (k + 1) * 512]
        SGD_o = 36864
        DTW_o = 32768
        S2IW_o = 43008
        YN_o = 40960
        SGI_o = 32768
        halo_pre = kcb[:, 130:162]
        halo_save = kcb[:, 162:194]
        masksum = kc[:, 320:321]
        halo_x = kc[:, 328:344]
        halo_sq = kc[:, 344:360]
        halo_bc = kc[:, 360:362]
        halo_h = kcb[:, 194:210]

        b.wait_many("SP", [st2_t])
        dma("SP", s_sp, cc_srcH,
            AF32[:, 0:16384].rearrange("p (k t) -> p k t", k=KT)[:, :, T - 2:T])
        b.wait("POOL", s_sp)
        b.rec("POOL", lambda e: e.collective_compute(
            "AllGather", OP.bypass, replica_groups=[list(range(8))],
            ins=[cc_srcH[:].opt()], outs=[cc_dstH[:].opt()]), inc=s_cc)
        cc_h = tok(s_cc)
        b.wait_many("DVE", [const_t])
        b.rec("DVE", lambda e: e.tensor_reduce(masksum, maskbc, AX.X, OP.add), inc=s_dve)
        b.wait("POOL", *cc_h)
        b.wait("POOL", s_dve)
        gslH = ABF[:, SGD_o:SGD_o + 128]
        gh_f = AF32[:, 24448:24448 + 128]
        dma("POOL", s_gsl, gh_f.rearrange("p (r c) -> p r c", r=8),
            cc_dstH.rearrange("(r p) c -> p r c", p=128))
        ghf_t = tok(s_gsl)
        b.wait_many("DVE", [ghf_t])
        b.rec("DVE", lambda e: e.tensor_tensor(
            gh_f.rearrange("p (r c) -> p r c", r=8),
            gh_f.rearrange("p (r c) -> p r c", r=8),
            maskbc.rearrange("p (r o) -> p r o", o=1).broadcast_to((128, 8, 16)), OP.mult), inc=s_dve)
        b.wait("DVE", s_dve)
        b.rec("DVE", lambda e: e.tensor_reduce(
            halo_x, gh_f.rearrange("p (r c) -> p c r", r=8), AX.X, OP.add), inc=s_dve)
        halo_t = tok(s_dve)
        b.wait_many("ACT", [halo_t])
        b.rec("ACT", lambda e: e.activation(out=halo_sq, in_=halo_x, func=AF.Square), inc=s_act)
        b.wait("PE", s_act)
        b.wait("PE", s_dve)
        b.rec("PE", lambda e: e.matmul(psD[0:1, 0:16], ones_col, halo_sq, start=True, stop=True), inc=s_pe)
        b.wait("DVE", s_pe)
        b.rec("DVE", lambda e: e.tensor_reduce(
            AF32[0:1, 24400:24402], psD[0:1, 0:16].rearrange("p (k c) -> p c k", k=KT),
            AX.X, OP.add), inc=s_dve)
        b.wait("ACT", s_dve)
        b.rec("ACT", lambda e: e.activation(out=AF32[0:1, 24402:24404], in_=AF32[0:1, 24400:24402],
                                            func=AF.Ln, bias=1e-6, scale=1.0 / D), inc=s_act)
        b.wait("ACT", s_act)
        b.rec("ACT", lambda e: e.activation(out=AF32[0:1, 24402:24404], in_=AF32[0:1, 24402:24404],
                                            func=AF.Exp, scale=-0.5), inc=s_act)
        b.wait("PE", s_act)
        b.rec("PE", lambda e: e.matmul(psD[0:128, 16:18], ones_row, AF32[0:1, 24402:24404],
                                       start=True, stop=True), inc=s_pe)
        b.wait("DVE", s_pe)
        b.rec("DVE", lambda e: e.tensor_copy(AF32[:, 24404:24406], psD[0:128, 16:18]), inc=s_dve)
        b.wait("DVE", s_dve)
        b.rec("DVE", lambda e: e.tensor_tensor(
            halo_x.rearrange("p (k c) -> p k c", k=KT),
            halo_x.rearrange("p (k c) -> p k c", k=KT),
            AF32[:, 24404:24406].rearrange("p (o c) -> p o c", o=1).broadcast_to((128, 8, 2)), OP.mult), inc=s_dve)
        b.wait("DVE", s_dve)
        b.rec("DVE", lambda e: e.tensor_tensor(
            halo_h.rearrange("p (k c) -> p k c", k=KT),
            halo_x.rearrange("p (k c) -> p k c", k=KT),
            n1wc.rearrange("p (k o) -> p k o", o=1).broadcast_to((128, 8, 2)), OP.mult), inc=s_dve)
        haloh_t = tok(s_dve)

        b.wait("SP", s_act)
        b.wait("SP", s_dve)
        dma("SP", s_sp, AF32[:, 24512:24544], di["ipb"])
        ipb_t = tok(s_sp)
        ipbc = AF32[:, 24512:24544]
        r_rstd3 = AF32[0:1, 19456:19456 + T]
        rms1_t = ln_full(xs, KT, D, 1e-6, False, (r_rstd3, None), wait_toks=[st2_t])

        B1c3 = AF32[:, 18432:18432 + 512]
        FT3 = AF32[:, 18432 + 512:18432 + 1024]
        ga = 0
        ev3 = {}
        sg_par = 0
        b.wait_many("ACT", [ipb_t])
        for n in range(4):
            n0 = n * 512
            b.wait_many("PE", [rms1_t])
            bcast_chunk(r_rstd3, n0, 512, B1c3)
            bc3_t = tok(s_dve)
            b.wait_many("DVE", [bc3_t])
            b.wait("DVE", s_pe)
            for k in range(KT):
                b.rec("DVE", lambda e, k=k, n0=n0: e.tensor_tensor(
                    FT3, xs(k)[:, n0:n0 + 512], B1c3, OP.mult), inc=s_dve)
                b.wait("DVE", s_dve)
                b.rec("DVE", lambda e, k=k: e.tensor_scalar(
                    hch(k), FT3, n1wc[:, k:k + 1], None, OP.mult), inc=s_dve)
                b.wait("DVE", s_dve)
            hc_t = tok(s_dve)
            for p in range(16):
                wv, wv_t, wv_par = ws.load(di["ipw"][:, p * 256:(p + 1) * 256], KT)
                b.wait_many("PE", [wv_t, hc_t])
                if n == 0 and p < 8:
                    b.wait_many("PE", [haloh_t, const_t])
                for m2 in range(2):
                    m = p * 2 + m2
                    ps = PSMAIN[ga % 2]
                    prev = ev3.get(ga - 2)
                    if prev is not None:
                        b.wait("PE", *prev)
                    for k in range(KT):
                        b.rec("PE", lambda e, k=k, m2=m2, ps=ps, wv=wv: e.matmul(
                            ps[:, 0:512], wv(k, m2 * 128, 128), hch(k),
                            start=(k == 0), stop=(k == KT - 1)), inc=(s_pe if k == KT - 1 else None))
                    if m < 16:
                        b.wait("ACT", s_pe)
                        b.rec("ACT", lambda e, m=m, n0=n0, ps=ps: e.activation(
                            out=xmt(m)[:, n0:n0 + 512], in_=ps[:, 0:512],
                            func=AF.Identity, bias=ipbc[:, m:m + 1]), inc=s_act)
                        ev3[ga] = tok(s_act)
                    else:
                        spar = sg_par % 2
                        sgd = ABF[:, SGD_o + spar * 512:SGD_o + spar * 512 + 512]
                        b.wait("ACT", s_pe)
                        b.wait("ACT", s_sg[spar], 16 * (sg_par // 2))
                        b.rec("ACT", lambda e, ps=ps, sgd=sgd, m=m: e.activation(
                            out=sgd, in_=ps[:, 0:512], func=AF.Sigmoid,
                            bias=ipbc[:, m:m + 1]), inc=s_act)
                        ev3[ga] = tok(s_act)
                        b.wait_many("ACT", [ev3[ga]])
                        dma("ACT", s_sg[spar], scr_sg[(m - 16) * 128:(m - 16) * 128 + 128, n0:n0 + 512], sgd)
                        sg_par += 1
                    ga += 1
                if n == 0 and p < 8:
                    for m2 in range(2):
                        m = p * 2 + m2
                        b.wait("PE", s_dve)
                        for k in range(KT):
                            b.rec("PE", lambda e, k=k, m2=m2, wv=wv: e.matmul(
                                psD[0:128, 20:22], wv(k, m2 * 128, 128),
                                halo_h.rearrange("p (k c) -> p k c", k=KT)[:, k, :],
                                start=(k == 0), stop=(k == KT - 1)), inc=(s_pe if k == KT - 1 else None))
                        b.wait("DVE", s_pe)
                        b.wait_many("DVE", [ipb_t])
                        b.rec("DVE", lambda e, m=m: e.tensor_scalar(
                            halo_pre[:, m * 2:(m + 1) * 2], psD[0:128, 20:22],
                            ipbc[:, m:m + 1], masksum, OP.add, OP.mult), inc=s_dve)
                if n == 0 and p < 8:
                    ws.consumed(wv_par, tok(s_dve))
                else:
                    ws.consumed(wv_par, tok(s_act))
        inp_t = tok(s_act)
        inp_td = tok(s_dve)

        b.wait("POOL", s_pe)
        dma("POOL", s_ini, ABF[:, DTW_o:DTW_o + 1024].rearrange("p (k c) -> p k c", k=16),
            di["dtw"].rearrange("(k p) c -> p k c", p=128))
        dma("POOL", s_ini, ABF[:, DTW_o + 1024:DTW_o + 2048].rearrange("p (k c) -> p k c", k=16),
            di["bpw"].rearrange("(k p) c -> p k c", p=128))
        dma("POOL", s_ini, ABF[:, DTW_o + 2048:DTW_o + 3072].rearrange("p (k c) -> p k c", k=16),
            di["cpw"].rearrange("(k p) c -> p k c", p=128))
        b.wait("SP", s_act)
        b.wait("SP", s_pe)
        dma("SP", s_sp, AF32[:, 24528:24528 + 48], di["convw"])
        dma("SP", s_sp, AF32[:, 24512:24512 + 16], di["convb"])
        dma("SP", s_sp, AF32[0:64, 24448:24449], di["dtb"])
        dma("SP", s_sp, AF32[0:64, 24449:24450], di["bpb"])
        dma("SP", s_sp, AF32[0:64, 24450:24451], di["cpb"])
        w3_t = tok(s_ini); w3f_t = tok(s_sp)
        convw_c = AF32[:, 24528:24528 + 48]
        convb_c = AF32[:, 24512:24512 + 16]
        CS1 = ABF[:, SGD_o + 1024:SGD_o + 1536]
        CS2 = ABF[:, SGD_o + 1536:SGD_o + 2048]
        silu_hist = []
        for c in range(8):
            t0 = c * 256
            for kk in range(16):
                xk = xmt(kk)
                w0 = convw_c[:, kk * 3:kk * 3 + 1]
                w1_ = convw_c[:, kk * 3 + 1:kk * 3 + 2]
                w2_ = convw_c[:, kk * 3 + 2:kk * 3 + 3]
                csA = (CS1 if kk % 2 == 0 else CS2)[:, 0:256]
                csB = (CS1 if kk % 2 == 0 else CS2)[:, 256:512]
                b.wait_many("DVE", [inp_t, inp_td, w3f_t])
                if len(silu_hist) >= 2:
                    b.wait("DVE", s_act, silu_hist[-2])
                hsrc = halo_pre if c == 0 else halo_save
                b.rec("DVE", lambda e, hsrc=hsrc, w0=w0, csA=csA, kk=kk: e.tensor_scalar(
                    csA[:, 0:2], hsrc[:, kk * 2:(kk + 1) * 2], w0, None, OP.mult), inc=s_dve)
                b.rec("DVE", lambda e, xk=xk, w0=w0, csA=csA, t0=t0: e.tensor_scalar(
                    csA[:, 2:256], xk[:, t0:t0 + 254], w0, None, OP.mult), inc=s_dve)
                b.rec("DVE", lambda e, hsrc=hsrc, w1_=w1_, csB=csB, kk=kk: e.tensor_scalar(
                    csB[:, 0:1], hsrc[:, kk * 2 + 1:kk * 2 + 2], w1_, None, OP.mult), inc=s_dve)
                b.rec("DVE", lambda e, xk=xk, w1_=w1_, csB=csB, t0=t0: e.tensor_scalar(
                    csB[:, 1:256], xk[:, t0:t0 + 255], w1_, None, OP.mult), inc=s_dve)
                b.wait("DVE", s_dve)
                b.rec("DVE", lambda e, csA=csA, csB=csB: e.tensor_tensor(
                    csA[:, 0:256], csA[:, 0:256], csB[:, 0:256], OP.add), inc=s_dve)
                b.wait("DVE", s_dve)
                b.rec("DVE", lambda e, xk=xk, w2_=w2_, csB=csB, t0=t0: e.tensor_scalar(
                    csB[:, 0:256], xk[:, t0:t0 + 256], w2_, None, OP.mult), inc=s_dve)
                b.wait("DVE", s_dve)
                b.rec("DVE", lambda e, csA=csA, csB=csB: e.tensor_tensor(
                    csA[:, 0:256], csA[:, 0:256], csB[:, 0:256], OP.add), inc=s_dve)
                pool_t = tok(s_dve)
                b.rec("DVE", lambda e, xk=xk, t0=t0, kk=kk: e.tensor_copy(
                    halo_save[:, kk * 2:(kk + 1) * 2], xk[:, t0 + 254:t0 + 256]), inc=s_dve)
                hs_t = tok(s_dve)
                b.wait_many("ACT", [pool_t, hs_t])
                b.rec("ACT", lambda e, xk=xk, csA=csA, t0=t0, kk=kk: e.activation(
                    out=xk[:, t0:t0 + 256], in_=csA[:, 0:256], func=AF.Silu,
                    bias=convb_c[:, kk:kk + 1]), inc=s_act)
                silu_hist.append(s_act.n)
            conv_t = tok(s_act)
            b.wait_many("PE", [conv_t, w3_t])
            b.wait("PE", s_dve)
            for kk in range(16):
                b.rec("PE", lambda e, kk=kk, t0=t0: e.matmul(
                    psC[0:64, 0:256], ABF[:, DTW_o + kk * 64:DTW_o + (kk + 1) * 64],
                    xmt(kk)[:, t0:t0 + 256], start=(kk == 0), stop=(kk == 15)))
                b.rec("PE", lambda e, kk=kk, t0=t0: e.matmul(
                    psC[0:64, 512:768], ABF[:, DTW_o + 1024 + kk * 64:DTW_o + 1024 + (kk + 1) * 64],
                    xmt(kk)[:, t0:t0 + 256], start=(kk == 0), stop=(kk == 15)))
                b.rec("PE", lambda e, kk=kk, t0=t0: e.matmul(
                    psD[0:64, 0:256], ABF[:, DTW_o + 2048 + kk * 64:DTW_o + 2048 + (kk + 1) * 64],
                    xmt(kk)[:, t0:t0 + 256], start=(kk == 0), stop=(kk == 15)),
                    inc=(s_pe if kk == 15 else None))
            b.wait("ACT", s_pe)
            b.rec("ACT", lambda e, t0=t0: e.activation(
                out=DTv[:, t0:t0 + 256], in_=psC[0:64, 0:256], func=AF.Sigmoid,
                bias=AF32[0:64, 24448:24449]), inc=s_act)
            b.rec("ACT", lambda e, t0=t0: e.activation(
                out=BMv[:, t0:t0 + 256], in_=psC[0:64, 512:768], func=AF.Identity,
                bias=AF32[0:64, 24449:24450]), inc=s_act)
            b.rec("ACT", lambda e, t0=t0: e.activation(
                out=CMv[:, t0:t0 + 256], in_=psD[0:64, 0:256], func=AF.Identity,
                bias=AF32[0:64, 24450:24451]), inc=s_act)
            b.wait("DVE", s_act)
            b.wait("PE", s_act)
        proj_t = tok(s_act)

        b.wait_many("DVE", [proj_t])
        b.rec("DVE", lambda e: e.tensor_tensor(BMv, DTv, BMv, OP.mult), inc=s_dve)
        b.wait("DVE", s_dve)
        b.rec("DVE", lambda e: e.tensor_scalar(DTv, DTv, -1.0, 1.0, OP.mult, OP.add), inc=s_dve)
        b.wait("DVE", s_dve)
        b.rec("DVE", lambda e: e.tensor_tensor_scan(SSv, DTv, BMv, 0.0, OP.mult, OP.add), inc=s_dve)
        b.wait("DVE", s_dve)
        b.rec("DVE", lambda e: e.tensor_tensor_scan(
            BMv, DTv, kc[0:64, 176:177].broadcast_to((64, T)), 1.0, OP.mult, OP.mult), inc=s_dve)
        scan_t = tok(s_dve)
        b.wait_many("SP", [scan_t])
        dma("SP", s_sp, cc_srcS, SSv[:, T - 1:T])
        b.wait("POOL", s_sp)
        b.rec("POOL", lambda e: e.collective_compute(
            "AllGather", OP.bypass, replica_groups=[list(range(8))],
            ins=[cc_srcS[:].opt()], outs=[cc_dstS[:].opt()]), inc=s_cc)
        b.wait("POOL", s_cc)
        b.wait("POOL", s_dve)
        dma("POOL", s_gsl, AF32[0:64, 16384:16392].rearrange("p (r o) -> p r o", r=8),
            cc_dstS.rearrange("(r p) o -> p r o", p=64), allow_slow_non_contiguous=True)
        gs_t = tok(s_gsl)
        b.wait_many("DVE", [gs_t])
        b.rec("DVE", lambda e: e.tensor_tensor(
            AF32[0:64, 16384:16392], AF32[0:64, 16384:16392],
            maskbc[0:64, :], OP.mult), inc=s_dve)
        b.wait("DVE", s_dve)
        b.rec("DVE", lambda e: e.tensor_reduce(
            AF32[0:64, 16392:16393], AF32[0:64, 16384:16392], AX.X, OP.add), inc=s_dve)
        b.wait("DVE", s_dve)
        b.rec("DVE", lambda e: e.scalar_tensor_tensor(
            SSv, BMv, AF32[0:64, 16392:16393], SSv, OP.mult, OP.add), inc=s_dve)
        b.wait("DVE", s_dve)
        b.rec("DVE", lambda e: e.tensor_tensor(CMv, CMv, SSv, OP.mult), inc=s_dve)
        y_t = tok(s_dve)

        r64a = AF32[0:1, 16384:16384 + T]
        r64b = AF32[0:1, 18432:18432 + T]
        sq64 = AF32[0:64, 22528:22528 + 512]
        nn64 = 4
        for n in range(nn64):
            n0 = n * 512
            b.wait_many("ACT", [y_t])
            b.wait("ACT", s_pe)
            b.rec("ACT", lambda e, n0=n0: e.activation(out=sq64, in_=CMv[:, n0:n0 + 512],
                                                       func=AF.Square), inc=s_act)
            b.wait("PE", s_act)
            b.wait_many("PE", [y_t])
            b.wait("PE", s_dve)
            b.rec("PE", lambda e, n0=n0: e.matmul(psD[0:1, 0:512], ones_col[0:64, :],
                                                  CMv[:, n0:n0 + 512], start=True, stop=True))
            b.rec("PE", lambda e, n0=n0: e.matmul(psC[0:1, 0:512], ones_col[0:64, :],
                                                  sq64, start=True, stop=True), inc=s_pe)
            b.wait("DVE", s_pe)
            b.rec("DVE", lambda e, n0=n0: e.tensor_scalar(
                r64b[:, n0:n0 + 512], psD[0:1, 0:512], 1.0 / 64, None, OP.mult), inc=s_dve)
            b.wait("DVE", s_dve)
            b.rec("DVE", lambda e, n0=n0: e.tensor_tensor(
                r64a[:, n0:n0 + 512], r64b[:, n0:n0 + 512], r64b[:, n0:n0 + 512], OP.mult), inc=s_dve)
            b.wait("DVE", s_dve)
            b.rec("DVE", lambda e, n0=n0: e.scalar_tensor_tensor(
                r64a[:, n0:n0 + 512], psC[0:1, 0:512], 1.0 / 64, r64a[:, n0:n0 + 512],
                OP.mult, OP.subtract), inc=s_dve)
        b.wait("ACT", s_dve)
        b.rec("ACT", lambda e: e.activation(out=r64a, in_=r64a, func=AF.Ln, bias=1e-5), inc=s_act)
        b.wait("ACT", s_act)
        b.rec("ACT", lambda e: e.activation(out=r64a, in_=r64a, func=AF.Exp, scale=-0.5), inc=s_act)
        b.wait("DVE", s_act)
        b.rec("DVE", lambda e: e.scalar_tensor_tensor(r64b, r64b, -1.0, r64a, OP.mult, OP.mult), inc=s_dve)
        ln64_t = tok(s_dve)
        ynv = ABF[0:64, YN_o:YN_o + T]
        for n in range(4):
            n0 = n * 512
            b.wait_many("PE", [ln64_t])
            b.wait("PE", s_dve)
            b.wait("PE", s_act)
            b.rec("PE", lambda e, n0=n0: e.matmul(psD[0:64, 0:512], ones_row[:, 0:64], r64a[:, n0:n0 + 512],
                                                  start=True, stop=True))
            b.rec("PE", lambda e, n0=n0: e.matmul(psC[0:64, 0:512], ones_row[:, 0:64], r64b[:, n0:n0 + 512],
                                                  start=True, stop=True), inc=s_pe)
            b.wait("DVE", s_pe)
            b.rec("DVE", lambda e, n0=n0: e.tensor_tensor(
                SSv[:, 0:512], CMv[:, n0:n0 + 512], psD[0:64, 0:512], OP.mult), inc=s_dve)
            b.wait("DVE", s_dve)
            b.rec("DVE", lambda e, n0=n0: e.tensor_tensor(
                ynv[:, n0:n0 + 512], SSv[:, 0:512], psC[0:64, 0:512], OP.add), inc=s_dve)
            b.wait("DVE", s_dve)
        yn_t = tok(s_dve)

        b.wait("POOL", s_pe)
        dma("POOL", s_ini, ABF[0:64, S2IW_o:S2IW_o + 2048], di["s2iw"])
        s2i_t = tok(s_ini)
        b.wait("SP", s_pool)
        b.wait("SP", s_act)
        b.wait("SP", s_pe)
        b.wait("SP", s_dve)
        dma("SP", s_sp, AF32[:, 19968:19968 + 16], di["s2ib"])
        dma("SP", s_sp, AF32[:, 19984:19984 + 16], di["Dpar"])
        dma("SP", s_sp, AF32[:, 20000:20000 + 8], di["outb"])
        s2ib_c = AF32[:, 19968:19968 + 16]
        dpar_c = AF32[:, 19984:19984 + 16]
        outb_c = AF32[:, 20000:20000 + 8]
        w4f_t = tok(s_sp)
        scr_h2 = nc.dram_tensor("scr_h2", [D, T], BF16).ap()
        SGI_o3 = 32768
        def y3v(m):
            return ABF[:, SGI_o3 + m * 512:SGI_o3 + (m + 1) * 512]
        Achunk = AF32[:, 20480:20480 + KT * 512]
        def ach(k):
            return AF32[:, 20480 + k * 512:20480 + (k + 1) * 512]
        sq3 = AF32[:, 18432:18432 + 1024]
        r_rstd2 = AF32[0:1, 16384:16384 + T]
        B1rm2 = AF32[:, 19456:19456 + 512]
        ga4 = 0
        ev4 = {}
        for n in range(4):
            n0 = n * 512
            b.wait("POOL", s_pe)
            b.wait("POOL", s_sc)
            b.wait("POOL", s_sg[0])
            b.wait("POOL", s_sg[1])
            dma("POOL", s_gd, ABF[:, SGI_o3:SGI_o3 + 8192].rearrange("p (k t) -> p k t", k=16),
                scr_sg.rearrange("(k p) t -> p k t", p=128)[:, :, n0:n0 + 512])
            sg_t = tok(s_gd)
            for m in range(16):
                ps = PSMAIN[ga4 % 2]
                pstb = AF32[:, 20480 + (m % 2) * 512:20480 + (m % 2) * 512 + 512]
                prev = ev4.get(ga4 - 2)
                b.wait_many("PE", [yn_t, s2i_t])
                if prev is not None:
                    b.wait("PE", *prev)
                b.rec("PE", lambda e, m=m, n0=n0, ps=ps: e.matmul(
                    ps[:, 0:512], ABF[0:64, S2IW_o + m * 128:S2IW_o + (m + 1) * 128],
                    ynv[:, n0:n0 + 512], start=True, stop=True), inc=s_pe)
                b.wait("ACT", s_pe)
                b.wait_many("ACT", [w4f_t])
                if ga4 >= 2:
                    b.wait("ACT", s_dve, ev4[ga4 - 2][1])
                b.rec("ACT", lambda e, m=m, ps=ps, pstb=pstb: e.activation(
                    out=pstb, in_=ps[:, 0:512], func=AF.Identity,
                    bias=s2ib_c[:, m:m + 1]), inc=s_act)
                act_t = tok(s_act)
                b.wait_many("DVE", [act_t, inp_t, sg_t])
                b.rec("DVE", lambda e, m=m, n0=n0, pstb=pstb: e.scalar_tensor_tensor(
                    pstb, xmt(m)[:, n0:n0 + 512], dpar_c[:, m:m + 1], pstb,
                    OP.mult, OP.add), inc=s_dve)
                b.wait("DVE", s_dve)
                b.rec("DVE", lambda e, m=m, pstb=pstb: e.tensor_tensor(
                    y3v(m), y3v(m), pstb, OP.mult), inc=s_dve)
                ev4[ga4] = tok(s_dve)
                ga4 += 1
            y3_t = tok(s_dve)
            for p in range(4):
                wv, wv_t, wv_par = ws.load(di["outw"][:, p * 256:(p + 1) * 256], 16)
                b.wait_many("PE", [wv_t, y3_t])
                for m2 in range(2):
                    m = p * 2 + m2
                    ps = PSMAIN[ga4 % 2]
                    prev = ev4.get(ga4 - 2)
                    if prev is not None:
                        b.wait("PE", *prev)
                    for k in range(16):
                        b.rec("PE", lambda e, k=k, m2=m2, ps=ps, wv=wv: e.matmul(
                            ps[:, 0:512], wv(k, m2 * 128, 128), y3v(k),
                            start=(k == 0), stop=(k == 15)), inc=(s_pe if k == 15 else None))
                    b.wait("DVE", s_pe)
                    b.rec("DVE", lambda e, m=m, n0=n0, ps=ps: e.scalar_tensor_tensor(
                        ach(m), ps[:, 0:512], outb_c[:, m:m + 1], xs(m)[:, n0:n0 + 512],
                        OP.add, OP.add), inc=s_dve)
                    ev4[ga4] = tok(s_dve)
                    ga4 += 1
                ws.consumed(wv_par, tok(s_dve))
            a_t = tok(s_dve)
            b.wait_many("DVE", [a_t])
            for m in range(KT):
                b.rec("DVE", lambda e, m=m, n0=n0: e.tensor_tensor(
                    xs(m)[:, n0:n0 + 512], xs(m)[:, n0:n0 + 512], ach(m), OP.add), inc=s_dve)
            b.wait("PE", s_dve)
            b.wait("PE", s_act)
            pe_h = []
            for k in range(KT):
                sl = sq3[:, (k % 2) * 512:(k % 2) * 512 + 512]
                if len(pe_h) >= 2:
                    b.wait("ACT", s_pe, pe_h[-2])
                b.wait_many("ACT", [a_t])
                b.rec("ACT", lambda e, k=k, sl=sl: e.activation(
                    out=sl, in_=ach(k), func=AF.Square), inc=s_act)
                b.wait("PE", s_act)
                b.rec("PE", lambda e, k=k, sl=sl: e.matmul(
                    psC[0:1, 0:512], ones_col, sl, start=(k == 0), stop=(k == KT - 1)), inc=s_pe)
                pe_h.append(s_pe.n)
            b.wait("DVE", s_pe)
            b.rec("DVE", lambda e, n0=n0: e.tensor_scalar(
                r_rstd2[:, n0:n0 + 512], psC[0:1, 0:512], 1.0 / D, None, OP.mult), inc=s_dve)
            b.wait("ACT", s_dve)
            b.rec("ACT", lambda e, n0=n0: e.activation(
                out=r_rstd2[:, n0:n0 + 512], in_=r_rstd2[:, n0:n0 + 512], func=AF.Ln, bias=1e-6), inc=s_act)
            b.wait("ACT", s_act)
            b.rec("ACT", lambda e, n0=n0: e.activation(
                out=r_rstd2[:, n0:n0 + 512], in_=r_rstd2[:, n0:n0 + 512], func=AF.Exp, scale=-0.5), inc=s_act)
            rr2_t = tok(s_act)
            b.wait_many("PE", [rr2_t])
            b.wait("PE", s_dve)
            b.rec("PE", lambda e, n0=n0: e.matmul(psD[:, 0:512], ones_row,
                                                  r_rstd2[:, n0:n0 + 512], start=True, stop=True), inc=s_pe)
            b.wait("DVE", s_pe)
            b.rec("DVE", lambda e: e.tensor_copy(B1rm2, psD[:, 0:512]), inc=s_dve)
            b.wait("DVE", s_dve)
            for k in range(KT):
                b.rec("DVE", lambda e, k=k: e.tensor_tensor(sq3[:, 0:512], ach(k), B1rm2, OP.mult), inc=s_dve)
                b.wait("DVE", s_dve)
                b.rec("DVE", lambda e, k=k: e.tensor_scalar(
                    y3v(k), sq3[:, 0:512], n2wc[:, k:k + 1], None, OP.mult), inc=s_dve)
                b.wait("DVE", s_dve)
            h2_t = tok(s_dve)
            b.wait_many("ACT", [h2_t])
            dma("ACT", s_sc, scr_h2.rearrange("(k p) t -> p k t", p=128)[:, :, n0:n0 + 512],
                ABF[:, SGI_o3:SGI_o3 + 4096].rearrange("p (k t) -> p k t", k=KT))
        mix_t = tok(s_dve)
        tap("xmix", AF32[:, 0:16384], [mix_t])

        def h2v(k):
            return ABF[:, k * T:(k + 1) * T]
        def ghv(k):
            return ABF[:, 16384 + k * T:16384 + (k + 1) * T]
        cw4Fb = ABF[0:4, 32768:32768 + T]
        eb2ab = ABF[0:4, 34816:34816 + 1024]
        gatewb = ABF[:, 35840:35840 + 32]
        B_cw3 = AF32[:, 18432:18432 + T]
        glog = AF32[:, 20480:20480 + 128]
        cw4_all = AF32[:, 20608:20608 + 64]
        gm8 = AF32[:, 20672:20672 + 8]
        gsm = AF32[:, 20680:20680 + 8]
        cwF4 = AF32[0:4, 22528:22528 + T]
        eb1c = AF32[:, 20704:20704 + 32]

        b.wait_many("SP", [mix_t])
        b.wait("POOL", s_dve)
        b.wait("POOL", s_sc)
        dma("POOL", s_gd, ABF[:, 0:16384].rearrange("p (k t) -> p k t", k=KT),
            scr_h2.rearrange("(k p) t -> p k t", p=128))
        dma("POOL", s_ini, gatewb.rearrange("p (k c) -> p k c", k=KT),
            di["gatew"].rearrange("(k p) c -> p k c", p=128))
        dma("POOL", s_ini, eb2ab, di["eb2all"])
        dma("SP", s_sp, AF32[:, 20696:20700], di["gateb_bc"])
        h2_tt = tok(s_gd); gw_t = tok(s_ini); gb_t = tok(s_sp)

        b.wait_many("DVE", [mix_t])
        b.rec("DVE", lambda e: e.memset(glog, -1e30), inc=s_dve)
        gpad_t = tok(s_dve)
        grow_t = gb_t
        b.wait_many("PE", [h2_tt, gw_t, gpad_t])
        for tt in range(16):
            b.wait("PE", s_dve)
            for k in range(KT):
                b.rec("PE", lambda e, tt=tt, k=k: e.matmul(
                    psC[0:128, 0:4], h2v(k)[:, tt * 128:(tt + 1) * 128],
                    gatewb.rearrange("p (k c) -> p k c", k=KT)[:, k, :],
                    start=(k == 0), stop=(k == KT - 1)), inc=(s_pe if k == KT - 1 else None))
            b.wait("DVE", s_pe)
            b.wait_many("DVE", [grow_t])
            b.rec("DVE", lambda e, tt=tt: e.tensor_tensor(
                glog[:, tt * 8:tt * 8 + 4], psC[0:128, 0:4],
                AF32[:, 20696:20700], OP.add), inc=s_dve)
            b.wait("DVE", s_dve)
            b.rec("DVE", lambda e, tt=tt: e.max(out=gm8, in_=glog[:, tt * 8:(tt + 1) * 8]), inc=s_dve)
            b.wait("DVE", s_dve)
            b.rec("DVE", lambda e: e.tensor_scalar(gsm[:, 0:1], gm8[:, 0:1], -1.0, None, OP.mult), inc=s_dve)
            b.wait("ACT", s_dve)
            b.rec("ACT", lambda e: e.activation(out=gsm[:, 1:2], in_=gm8[:, 1:2], func=AF.Exp,
                                                bias=gsm[:, 0:1]), inc=s_act)
            b.wait("DVE", s_act)
            b.rec("DVE", lambda e: e.tensor_scalar(gsm[:, 2:3], gsm[:, 1:2], 1.0, None, OP.add), inc=s_dve)
            b.wait("DVE", s_dve)
            b.rec("DVE", lambda e: e.reciprocal(gsm[:, 3:4], gsm[:, 2:3]), inc=s_dve)
            b.wait("DVE", s_dve)
            b.rec("DVE", lambda e: e.tensor_tensor(gsm[:, 4:5], gsm[:, 1:2], gsm[:, 3:4], OP.mult), inc=s_dve)
            b.wait("DVE", s_dve)
            t4 = AF32[:, 20692:20696]
            b.rec("DVE", lambda e, tt=tt: e.tensor_scalar(
                t4, glog[:, tt * 8:tt * 8 + 4], gm8[:, 0:1], gsm[:, 3:4], OP.is_equal, OP.mult), inc=s_dve)
            b.rec("DVE", lambda e, tt=tt: e.tensor_scalar(
                cw4_all[:, tt * 4:(tt + 1) * 4], glog[:, tt * 8:tt * 8 + 4],
                gm8[:, 1:2], gsm[:, 4:5], OP.is_equal, OP.mult), inc=s_dve)
            b.wait("DVE", s_dve)
            b.rec("DVE", lambda e, tt=tt: e.tensor_tensor(
                cw4_all[:, tt * 4:(tt + 1) * 4], cw4_all[:, tt * 4:(tt + 1) * 4], t4, OP.add), inc=s_dve)
            cwt_t = tok(s_dve)
            b.wait_many("PE", [cwt_t])
            b.rec("PE", lambda e, tt=tt: e.transpose(
                psC[0:4, 512:640], cw4_all[:, tt * 4:(tt + 1) * 4], eye_fc), inc=s_pe)
            b.wait("DVE", s_pe)
            b.rec("DVE", lambda e, tt=tt: e.tensor_copy(
                cwF4[:, tt * 128:(tt + 1) * 128], psC[0:4, 512:640]), inc=s_dve)
        cwf4_t = tok(s_dve)
        b.wait("DVE", s_dve)
        b.rec("DVE", lambda e: e.tensor_copy(cw4Fb, cwF4), inc=s_dve)
        cwf4b_t = tok(s_dve)

        b.wait_many("PE", [cwf4b_t, gw_t])
        g5 = 0
        ev5 = {}
        for m in range(KT):
            for n in range(4):
                ps = PSMAIN[g5 % 2]
                prev = ev5.get(g5 - 2)
                if prev is not None:
                    b.wait("PE", *prev)
                b.rec("PE", lambda e, m=m, n=n, ps=ps: e.matmul(
                    ps[:, 0:512], eb2ab[:, m * 128:(m + 1) * 128],
                    cw4Fb[:, n * 512:(n + 1) * 512], start=True, stop=True), inc=s_pe)
                b.wait("DVE", s_pe)
                b.rec("DVE", lambda e, m=m, n=n, ps=ps: e.tensor_tensor(
                    xs(m)[:, n * 512:(n + 1) * 512], xs(m)[:, n * 512:(n + 1) * 512],
                    ps[:, 0:512], OP.add), inc=s_dve)
                ev5[g5] = tok(s_dve)
                g5 += 1

        for ex in range(4):
            b.wait_many("PE", [cwf4_t, const_t])
            for n in range(4):
                b.wait("PE", s_dve)
                b.wait("PE", s_pool)
                b.rec("PE", lambda e, ex=ex, n=n: e.matmul(
                    psD[:, 0:512], selc[0:4, ex * 128:(ex + 1) * 128],
                    cwF4[:, n * 512:(n + 1) * 512], start=True, stop=True), inc=s_pe)
                b.wait("DVE", s_pe)
                b.rec("DVE", lambda e, n=n: e.tensor_copy(
                    B_cw3[:, n * 512:(n + 1) * 512], psD[:, 0:512]), inc=s_dve)
            bcw3_t = tok(s_dve)
            b.wait("SP", s_act)
            dma("SP", s_sp, eb1c, di["eb1"][ex])
            eb1_t = tok(s_sp)
            for hidc in range(4):
                for p in range(4):
                    wv, wv_t, wv_par = ws.load(
                        di["ew1"][ex][:, hidc * 1024 + p * 256:hidc * 1024 + (p + 1) * 256], KT)
                    b.wait_many("PE", [wv_t, h2_tt])
                    for m2 in range(2):
                        m = p * 2 + m2
                        for n in range(4):
                            ps = PSMAIN[g5 % 2]
                            prev = ev5.get(g5 - 2)
                            if prev is not None:
                                b.wait("PE", *prev)
                            for k in range(KT):
                                b.rec("PE", lambda e, k=k, m2=m2, n=n, ps=ps, wv=wv: e.matmul(
                                    ps[:, 0:512], wv(k, m2 * 128, 128), h2v(k)[:, n * 512:(n + 1) * 512],
                                    start=(k == 0), stop=(k == KT - 1)), inc=(s_pe if k == KT - 1 else None))
                            b.wait("ACT", s_pe)
                            b.wait_many("ACT", [eb1_t])
                            hm = hidc * 8 + m
                            b.rec("ACT", lambda e, m=m, n=n, ps=ps, hm=hm: e.activation(
                                out=ghv(m)[:, n * 512:(n + 1) * 512], in_=ps[:, 0:512],
                                func=AF.Gelu, bias=eb1c[:, hm:hm + 1]), inc=s_act)
                            ev5[g5] = tok(s_act)
                            g5 += 1
                    ws.consumed(wv_par, tok(s_act))
                gelu_t = tok(s_act)
                b.wait_many("POOL", [gelu_t, bcw3_t])
                for k in range(KT):
                    b.rec("POOL", lambda e, k=k: e.tensor_tensor(ghv(k), ghv(k), B_cw3, OP.mult), inc=s_pool)
                ghs_t = tok(s_pool)
                for p in range(4):
                    wv, wv_t, wv_par = ws.load(
                        di["ew2"][ex][hidc * 1024:(hidc + 1) * 1024, p * 256:(p + 1) * 256], KT)
                    b.wait_many("PE", [wv_t, ghs_t])
                    for m2 in range(2):
                        m = p * 2 + m2
                        for n in range(4):
                            ps = PSMAIN[g5 % 2]
                            prev = ev5.get(g5 - 2)
                            if prev is not None:
                                b.wait("PE", *prev)
                            for k in range(KT):
                                b.rec("PE", lambda e, k=k, m2=m2, n=n, ps=ps, wv=wv: e.matmul(
                                    ps[:, 0:512], wv(k, m2 * 128, 128), ghv(k)[:, n * 512:(n + 1) * 512],
                                    start=(k == 0), stop=(k == KT - 1)), inc=(s_pe if k == KT - 1 else None))
                            b.wait("DVE", s_pe)
                            b.rec("DVE", lambda e, m=m, n=n, ps=ps: e.tensor_tensor(
                                xs(m)[:, n * 512:(n + 1) * 512], xs(m)[:, n * 512:(n + 1) * 512],
                                ps[:, 0:512], OP.add), inc=s_dve)
                            ev5[g5] = tok(s_dve)
                            g5 += 1
                    ws.consumed(wv_par, tok(s_dve))
        moe3_t = tok(s_dve)
        tap("res3", AF32[:, 0:16384], [moe3_t])
        st3_t = post_ln(moe3_t)
        tap("x3", AF32[:, 0:16384], [st3_t])
        last_t = st3_t

    if n_stages >= 4:
        def xb4(k):
            return ABF[:, k * T:(k + 1) * T]
        def guv(k):
            return ABF[:, 16384 + k * T:16384 + (k + 1) * T]
        sims = AF32[:, 16384:16384 + 512]
        sx8 = AF32[:, 16896:16896 + 8]
        sy8 = AF32[:, 16904:16904 + 8]
        p8f = AF32[:, 16912:16912 + 8]
        pden = AF32[:, 16920:16920 + 4]
        bgc = AF32[:, 19968:19968 + 32]
        buc = AF32[:, 20000:20000 + 32]
        bdc = AF32[:, 20032:20032 + 8]

        b.wait_many("DVE", [st3_t])
        b.wait("DVE", s_pe)
        b.wait("DVE", s_pool)
        for k in range(KT):
            b.rec("DVE", lambda e, k=k: e.tensor_copy(xb4(k), xs(k)), inc=s_dve)
        sh4_t = tok(s_dve)
        b.wait("SP", s_dve)
        b.wait("SP", s_act)
        b.wait("SP", s_pe)
        dma("SP", s_sp, bgc, di["bg"])
        dma("SP", s_sp, buc, di["bu"])
        dma("SP", s_sp, bdc, di["bd"])
        b4_t = tok(s_sp)

        DO_MLP = sub4 in ("all", "mlp")
        DO_PEER = sub4.startswith("peer") or sub4 == "all"
        g6 = 0
        ev6 = {}
        silu_sc = ABF[:, 43520:43520 + 1024]
        for hg in range(4 if DO_MLP else 0):
            for p in range(4):
                wg, wg_t, wg_par = ws.load(di["Wg"][:, hg * 1024 + p * 256:hg * 1024 + (p + 1) * 256], KT)
                wu, wu_t, wu_par = ws.load(di["Wu"][:, hg * 1024 + p * 256:hg * 1024 + (p + 1) * 256], KT)
                b.wait_many("PE", [wg_t, wu_t, sh4_t])
                for m2 in range(2):
                    m = p * 2 + m2
                    hm = hg * 8 + m
                    for n in range(4):
                        ps = PSMAIN[g6 % 2]
                        ssl = silu_sc[:, (g6 % 2) * 512:(g6 % 2) * 512 + 512]
                        prev = ev6.get(g6 - 2)
                        if prev is not None:
                            b.wait("PE", *prev)
                        for k in range(KT):
                            b.rec("PE", lambda e, k=k, m2=m2, n=n, ps=ps, wg=wg: e.matmul(
                                ps[:, 0:512], wg(k, m2 * 128, 128), xb4(k)[:, n * 512:(n + 1) * 512],
                                start=(k == 0), stop=(k == KT - 1)), inc=(s_pe if k == KT - 1 else None))
                        b.wait("ACT", s_pe)
                        b.wait_many("ACT", [b4_t])
                        if g6 >= 2:
                            b.wait("ACT", s_dve, ev6[g6 - 2][1])
                        b.rec("ACT", lambda e, ps=ps, ssl=ssl, hm=hm: e.activation(
                            out=ssl, in_=ps[:, 0:512], func=AF.Silu,
                            bias=bgc[:, hm // 8 * 0 + hm:hm + 1][:, 0:1] if False else bgc[:, hm:hm + 1]), inc=s_act)
                        sil_t = tok(s_act)
                        b.wait("PE", *sil_t)
                        for k in range(KT):
                            b.rec("PE", lambda e, k=k, m2=m2, n=n, ps=ps, wu=wu: e.matmul(
                                ps[:, 0:512], wu(k, m2 * 128, 128), xb4(k)[:, n * 512:(n + 1) * 512],
                                start=(k == 0), stop=(k == KT - 1)), inc=(s_pe if k == KT - 1 else None))
                        b.wait("DVE", s_pe)
                        b.wait_many("DVE", [sil_t, b4_t])
                        b.rec("DVE", lambda e, m=m, n=n, ps=ps, ssl=ssl, hm=hm: e.scalar_tensor_tensor(
                            guv(m)[:, n * 512:(n + 1) * 512], ps[:, 0:512], buc[:, hm:hm + 1],
                            ssl, OP.add, OP.mult), inc=s_dve)
                        ev6[g6] = tok(s_dve)
                        g6 += 1
                ws.consumed(wg_par, tok(s_dve))
                ws.consumed(wu_par, tok(s_dve))
            gu_t = tok(s_dve)
            for p in range(4):
                wd, wd_t, wd_par = ws.load(
                    di["Wd"][hg * 1024:(hg + 1) * 1024, p * 256:(p + 1) * 256], KT)
                b.wait_many("PE", [wd_t, gu_t])
                for m2 in range(2):
                    m = p * 2 + m2
                    for n in range(4):
                        ps = PSMAIN[g6 % 2]
                        prev = ev6.get(g6 - 2)
                        if prev is not None:
                            b.wait("PE", *prev)
                        for k in range(KT):
                            b.rec("PE", lambda e, k=k, m2=m2, n=n, ps=ps, wd=wd: e.matmul(
                                ps[:, 0:512], wd(k, m2 * 128, 128), guv(k)[:, n * 512:(n + 1) * 512],
                                start=(k == 0), stop=(k == KT - 1)), inc=(s_pe if k == KT - 1 else None))
                        b.wait("DVE", s_pe)
                        if hg == 0:
                            b.rec("DVE", lambda e, m=m, n=n, ps=ps: e.scalar_tensor_tensor(
                                xs(m)[:, n * 512:(n + 1) * 512], ps[:, 0:512], bdc[:, m:m + 1],
                                xs(m)[:, n * 512:(n + 1) * 512], OP.add, OP.add), inc=s_dve)
                        else:
                            b.rec("DVE", lambda e, m=m, n=n, ps=ps: e.tensor_tensor(
                                xs(m)[:, n * 512:(n + 1) * 512], xs(m)[:, n * 512:(n + 1) * 512],
                                ps[:, 0:512], OP.add), inc=s_dve)
                        ev6[g6] = tok(s_dve)
                        g6 += 1
                ws.consumed(wd_par, tok(s_dve))
        mlp_t = tok(s_dve)

        Mo = 16384
        def mv(tc):
            return ABF[:, Mo + tc * 4096:Mo + (tc + 1) * 4096]
        QF_o = 32768
        SW_o = 32768
        PB_o = 36864
        WT_o = 36864
        SCR_o = 40960
        KB_o = 43008
        kbuf = ABF[0:64, KB_o:KB_o + 512]
        b.wait("POOL", s_pe)
        b.wait("POOL", s_dve)
        dma("POOL", s_ini, kbuf, di["keysT"])
        kb_t = tok(s_ini)
        g7 = 0
        ev7 = {}
        for nch in range(4 if DO_PEER else 0):
            n0 = nch * 512
            wq, wq_t, wq_par = ws.load(di["Wq"], KT)
            b.wait_many("PE", [wq_t, sh4_t, mlp_t])
            for m in range(8):
                ps = PSMAIN[g7 % 2]
                prev = ev7.get(g7 - 2)
                if prev is not None:
                    b.wait("PE", *prev)
                for k in range(KT):
                    b.rec("PE", lambda e, k=k, m=m, n0=n0, ps=ps, wq=wq: e.matmul(
                        ps[0:64, 0:512], wq(k, m * 64, 64), xb4(k)[:, n0:n0 + 512],
                        start=(k == 0), stop=(k == KT - 1)), inc=(s_pe if k == KT - 1 else None))
                b.wait("DVE", s_pe)
                b.wait("DVE", s_pool)
                b.wait("DVE", s_act)
                b.rec("DVE", lambda e, m=m, ps=ps: e.tensor_copy(
                    ABF[0:64, QF_o + m * 512:QF_o + (m + 1) * 512], ps[0:64, 0:512]), inc=s_dve)
                ev7[g7] = tok(s_dve)
                g7 += 1
            ws.consumed(wq_par, tok(s_dve))
            q_t = tok(s_dve)
            for tc in range(4 if sub4 not in ("peer_q",) else 0):
                b.wait_many("PE", [q_t, kb_t])
                b.wait("PE", s_dve)
                for pp in range(2):
                    for h in range(4):
                        hp = pp * 4 + h
                        b.rec("PE", lambda e, hp=hp, tc=tc: e.matmul(
                            psC[0:128, hp * 64:(hp + 1) * 64],
                            ABF[0:64, QF_o + hp * 512 + tc * 128:QF_o + hp * 512 + (tc + 1) * 128],
                            kbuf[0:64, hp * 64:(hp + 1) * 64],
                            start=True, stop=True), inc=(s_pe if hp == 7 else None))
                b.wait("DVE", s_pe)
                b.rec("DVE", lambda e: e.tensor_copy(sims, psC[0:128, 0:512]), inc=s_dve)
                sim_t = tok(s_dve)
                sx8a = AF32[:, 16896:16928]
                sy8a = AF32[:, 16928:16960]
                p8fa = AF32[:, 16960:16992]
                rdna = AF32[:, 16992:16996]
                negc = AF32[:, 17000:17016]
                exall = ABF[:, SCR_o:SCR_o + 256]
                eyall = ABF[:, SCR_o + 256:SCR_o + 512]
                ex8a = ABF[:, SCR_o + 512:SCR_o + 544]
                ey8a = ABF[:, SCR_o + 544:SCR_o + 576]
                pca = ABF[:, SCR_o + 576:SCR_o + 832]
                p8a = ABF[:, SCR_o + 832:SCR_o + 864]
                if sub4 == "peer_sim":
                    continue
                b.wait_many("DVE", [sim_t])
                for h in range(4):
                    b.rec("DVE", lambda e, h=h: e.max(out=sx8a[:, h * 8:(h + 1) * 8],
                                                      in_=sims[:, h * 64:(h + 1) * 64]), inc=s_dve)
                    b.rec("DVE", lambda e, h=h: e.max(out=sy8a[:, h * 8:(h + 1) * 8],
                                                      in_=sims[:, 256 + h * 64:256 + (h + 1) * 64]), inc=s_dve)
                b.wait("DVE", s_dve)
                b.rec("DVE", lambda e: e.tensor_scalar(
                    negc[:, 0:8].rearrange("p (a o) -> p a o", o=1),
                    AF32[:, 16896:16960].rearrange("p (a c) -> p a c", c=8)[:, :, 0:1],
                    -1.0, None, OP.mult), inc=s_dve)
                st1_m = tok(s_dve)
                b.wait_many("ACT", [st1_m])
                for h in range(4):
                    b.rec("ACT", lambda e, h=h: e.activation(
                        out=exall[:, h * 64:(h + 1) * 64], in_=sims[:, h * 64:(h + 1) * 64],
                        func=AF.Exp, bias=negc[:, h:h + 1]), inc=s_act)
                    b.rec("ACT", lambda e, h=h: e.activation(
                        out=eyall[:, h * 64:(h + 1) * 64], in_=sims[:, 256 + h * 64:256 + (h + 1) * 64],
                        func=AF.Exp, bias=negc[:, 4 + h:5 + h]), inc=s_act)
                    b.rec("ACT", lambda e, h=h: e.activation(
                        out=ex8a[:, h * 8:(h + 1) * 8], in_=sx8a[:, h * 8:(h + 1) * 8],
                        func=AF.Exp, bias=negc[:, h:h + 1]), inc=s_act)
                    b.rec("ACT", lambda e, h=h: e.activation(
                        out=ey8a[:, h * 8:(h + 1) * 8], in_=sy8a[:, h * 8:(h + 1) * 8],
                        func=AF.Exp, bias=negc[:, 4 + h:5 + h]), inc=s_act)
                exp_t = tok(s_act)
                b.wait_many("DVE", [exp_t])
                for h in range(4):
                    b.rec("DVE", lambda e, h=h: e.tensor_tensor(
                        pca[:, h * 64:(h + 1) * 64].rearrange("p (i j) -> p i j", i=8),
                        ex8a[:, h * 8:(h + 1) * 8][:, :, None].broadcast_to((128, 8, 8)),
                        ey8a[:, h * 8:(h + 1) * 8][:, None, :].broadcast_to((128, 8, 8)), OP.mult), inc=s_dve)
                b.wait("DVE", s_dve)
                for h in range(4):
                    b.rec("DVE", lambda e, h=h: e.max(out=p8a[:, h * 8:(h + 1) * 8],
                                                      in_=pca[:, h * 64:(h + 1) * 64]), inc=s_dve)
                b.wait("DVE", s_dve)
                b.rec("DVE", lambda e: e.tensor_copy(p8fa, p8a), inc=s_dve)
                b.wait("DVE", s_dve)
                b.rec("DVE", lambda e: e.tensor_reduce(
                    rdna, p8fa.rearrange("p (h c) -> p h c", h=4), AX.X, OP.add), inc=s_dve)
                b.wait("DVE", s_dve)
                b.rec("DVE", lambda e: e.reciprocal(rdna, rdna), inc=s_dve)
                b.wait("DVE", s_dve)
                for h in range(4):
                    b.rec("DVE", lambda e, h=h: e.tensor_tensor(
                        p8fa[:, h * 8 + 7:h * 8 + 8], p8fa[:, h * 8 + 7:h * 8 + 8],
                        rdna[:, h:h + 1], OP.mult), inc=s_dve)
                    b.rec("DVE", lambda e, h=h: e.tensor_scalar(
                        eyall[:, h * 64:(h + 1) * 64], eyall[:, h * 64:(h + 1) * 64],
                        rdna[:, h:h + 1], None, OP.mult), inc=s_dve)
                sc_t = tok(s_dve)
                pbuf = ABF[:, PB_o:PB_o + 4096]
                b.wait("DVE", s_pe)
                for h in range(4):
                    b.wait_many("DVE", [sc_t])
                    b.wait("DVE", s_pool)
                    b.rec("DVE", lambda e, h=h: e.tensor_tensor(
                        pbuf.rearrange("p (i j) -> p i j", i=64),
                        exall[:, h * 64:(h + 1) * 64][:, :, None].broadcast_to((128, 64, 64)),
                        eyall[:, h * 64:(h + 1) * 64][:, None, :].broadcast_to((128, 64, 64)),
                        OP.mult), inc=s_dve)
                    b.wait("DVE", s_dve)
                    b.rec("DVE", lambda e, h=h: e.scalar_tensor_tensor(
                        pbuf, pbuf, p8fa[:, h * 8 + 7:h * 8 + 8], pbuf, OP.is_ge, OP.mult), inc=s_dve)
                    msk_t = tok(s_dve)
                    if h == 0:
                        b.wait_many("POOL", [msk_t])
                        b.rec("POOL", lambda e, tc=tc: e.tensor_copy(mv(tc), pbuf), inc=s_pool)
                    else:
                        b.wait_many("POOL", [msk_t])
                        b.rec("POOL", lambda e, tc=tc: e.tensor_tensor(
                            mv(tc), mv(tc), pbuf, OP.add), inc=s_pool)
            m_t = tok(s_pool)
            for eg in range(4 if sub4 not in ("peer_mask",) else 0):
                swg = ABF[:, SW_o:SW_o + 4096]
                for p in range(2):
                    dv, dv_t, dv_par = ws.load(
                        di["deT"][:, eg * 1024 + p * 512:eg * 1024 + (p + 1) * 512], KT)
                    b.wait_many("PE", [dv_t, sh4_t, m_t])
                    for m2 in range(4):
                        met = p * 4 + m2
                        ps = PSMAIN[g7 % 2]
                        prev = ev7.get(g7 - 2)
                        if prev is not None:
                            b.wait("PE", *prev)
                        for k in range(KT):
                            b.rec("PE", lambda e, k=k, m2=m2, n0=n0, ps=ps, dv=dv: e.matmul(
                                ps[:, 0:512], dv(k, m2 * 128, 128), xb4(k)[:, n0:n0 + 512],
                                start=(k == 0), stop=(k == KT - 1)), inc=(s_pe if k == KT - 1 else None))
                        b.wait("ACT", s_pe)
                        b.wait("ACT", s_pool)
                        b.rec("ACT", lambda e, met=met, ps=ps: e.activation(
                            out=swg[:, met * 512:(met + 1) * 512], in_=ps[:, 0:512],
                            func=AF.Silu), inc=s_act)
                        ev7[g7] = tok(s_act)
                        g7 += 1
                    ws.consumed(dv_par, tok(s_act))
                sw_t = tok(s_act)
                tpi = 0
                tp_ev = {}
                for et in range(8):
                    for tc in range(4):
                        slot = tpi % 2
                        b.wait_many("PE", [m_t, const_tb])
                        if tpi >= 2:
                            b.wait("PE", s_dve, tp_ev[tpi - 2])
                        b.rec("PE", lambda e, et=et, tc=tc, slot=slot: e.transpose(
                            psT[:, slot * 1024:slot * 1024 + 128],
                            mv(tc)[:, eg * 1024 + et * 128:eg * 1024 + (et + 1) * 128], eye_bc),
                            inc=s_pe)
                        b.wait("DVE", s_pe)
                        b.wait_many("DVE", [sw_t])
                        b.rec("DVE", lambda e, et=et, tc=tc, slot=slot: e.tensor_tensor(
                            ABF[:, WT_o + et * 512 + tc * 128:WT_o + et * 512 + (tc + 1) * 128],
                            psT[:, slot * 1024:slot * 1024 + 128],
                            swg[:, et * 512 + tc * 128:et * 512 + (tc + 1) * 128], OP.mult), inc=s_dve)
                        tp_ev[tpi] = s_dve.n
                        tpi += 1
                wt_t = tok(s_dve)
                for p in range(4 if sub4 not in ("peer_sw",) else 0):
                    uv, uv_t, uv_par = ws.load(
                        di["ue"][eg * 1024:(eg + 1) * 1024, p * 256:(p + 1) * 256], KT)
                    b.wait_many("PE", [uv_t, wt_t])
                    for m2 in range(2):
                        m = p * 2 + m2
                        ps = PSMAIN[g7 % 2]
                        prev = ev7.get(g7 - 2)
                        if prev is not None:
                            b.wait("PE", *prev)
                        for k in range(KT):
                            b.rec("PE", lambda e, k=k, m2=m2, ps=ps, uv=uv: e.matmul(
                                ps[:, 0:512], uv(k, m2 * 128, 128),
                                ABF[:, WT_o + k * 512:WT_o + (k + 1) * 512],
                                start=(k == 0), stop=(k == KT - 1)), inc=(s_pe if k == KT - 1 else None))
                        b.wait("DVE", s_pe)
                        b.rec("DVE", lambda e, m=m, n0=n0, ps=ps: e.tensor_tensor(
                            xs(m)[:, n0:n0 + 512], xs(m)[:, n0:n0 + 512], ps[:, 0:512], OP.add), inc=s_dve)
                        ev7[g7] = tok(s_dve)
                        g7 += 1
                    ws.consumed(uv_par, tok(s_dve))
        peer_t = tok(s_dve)
        tap("res4", AF32[:, 0:16384], [peer_t])
        st4_t = post_ln(peer_t)
        last_t = st4_t


    b.wait_many("ACT", [last_t])
    for k in range(KT):
        dma("ACT", s_sc, y_out[k * 128:(k + 1) * 128, :], xs(k))
    b.wait("ACT", s_sc)
    b.wait("SP", s_sc)

    b.play()
    return nc


def prep_inputs(inputs, core):
    ci = {}
    bidx, half = core // 2, core % 2
    x = np.asarray(inputs["L0_attn_x"])
    sl = x[bidx, half * T:(half + 1) * T, :]
    ci["x_tf"] = np.ascontiguousarray(sl)
    ci["x_ft"] = np.ascontiguousarray(sl.T)

    def coltile(v, n):
        return np.ascontiguousarray(np.asarray(v).reshape(n, 128).T)

    ci["WbT"] = np.ascontiguousarray(np.asarray(inputs["L0_attn_Wb"]).T)
    ci["WpT"] = np.ascontiguousarray(np.asarray(inputs["L0_attn_Wp"]).T)
    ci["initBT"] = np.ascontiguousarray(np.asarray(inputs["L0_attn_init_B"]).T)
    ci["initA"] = np.ascontiguousarray(np.asarray(inputs["L0_attn_init_A"]))
    ci["initAT"] = np.ascontiguousarray(np.asarray(inputs["L0_attn_init_A"]).T)
    ci["initB"] = np.ascontiguousarray(np.asarray(inputs["L0_attn_init_B"]))
    ci["g0"] = coltile(inputs["L0_attn_ln_gamma"], KT)
    ci["be0"] = coltile(inputs["L0_attn_ln_beta"], KT)
    ci["routerT"] = np.ascontiguousarray(np.asarray(inputs["L0_ffn_router_w"]).T)
    ci["w1"] = np.asarray(inputs["L0_ffn_w1"])
    ci["w2"] = np.asarray(inputs["L0_ffn_w2"])
    ci["n1w"] = coltile(inputs["L1_attn_norm1_w"], KT)
    ci["n2w"] = coltile(inputs["L1_attn_norm2_w"], KT)
    ci["ipw"] = np.asarray(inputs["L1_attn_in_proj_w"])
    ci["ipb"] = coltile(inputs["L1_attn_in_proj_b"], 32)
    ci["convw"] = np.ascontiguousarray(
        np.asarray(inputs["L1_attn_conv_w"]).reshape(2048, 3).reshape(16, 128, 3).transpose(1, 0, 2).reshape(128, 48))
    ci["convb"] = coltile(inputs["L1_attn_conv_b"], 16)
    ci["dtw"] = np.asarray(inputs["L1_attn_dt_w"])
    ci["dtb"] = np.asarray(inputs["L1_attn_dt_b"]).reshape(64, 1)
    ci["bpw"] = np.asarray(inputs["L1_attn_bp_w"])
    ci["bpb"] = np.asarray(inputs["L1_attn_bp_b"]).reshape(64, 1)
    ci["cpw"] = np.asarray(inputs["L1_attn_cp_w"])
    ci["cpb"] = np.asarray(inputs["L1_attn_cp_b"]).reshape(64, 1)
    ci["s2iw"] = np.asarray(inputs["L1_attn_s2i_w"])
    ci["s2ib"] = coltile(inputs["L1_attn_s2i_b"], 16)
    ci["Dpar"] = coltile(inputs["L1_attn_D_param"], 16)
    ci["outw"] = np.asarray(inputs["L1_attn_out_w"])
    ci["outb"] = coltile(inputs["L1_attn_out_b"], KT)
    ci["gatew"] = np.asarray(inputs["L1_attn_gate_w"])
    ci["gateb"] = np.asarray(inputs["L1_attn_gate_b"]).reshape(4, 1)
    ci["gateb_bc"] = np.tile(np.asarray(inputs["L1_attn_gate_b"]).reshape(1, 4), (128, 1))
    ci["eb2all"] = np.asarray(inputs["L1_attn_e_b2"]).astype(BF16NP)
    ci["ew1"] = np.asarray(inputs["L1_attn_e_w1"])
    ci["eb1"] = np.stack([coltile(np.asarray(inputs["L1_attn_e_b1"])[e], 32) for e in range(4)])
    ci["ew2"] = np.asarray(inputs["L1_attn_e_w2"])
    ci["eb2"] = np.stack([coltile(np.asarray(inputs["L1_attn_e_b2"])[e], KT) for e in range(4)])
    ci["Wg"] = np.asarray(inputs["L1_ffn_Wg"])
    ci["bg"] = coltile(inputs["L1_ffn_bg"], 32)
    ci["Wu"] = np.asarray(inputs["L1_ffn_Wu"])
    ci["bu"] = coltile(inputs["L1_ffn_bu"], 32)
    ci["Wd"] = np.asarray(inputs["L1_ffn_Wd"])
    ci["bd"] = coltile(inputs["L1_ffn_bd"], KT)
    ci["Wq"] = np.asarray(inputs["L1_ffn_Wq"])
    kT = np.ascontiguousarray(
        np.asarray(inputs["L1_ffn_keys"]).transpose(3, 2, 0, 1).reshape(64, 2 * 4 * 64))
    ci["keysT"] = kT.astype(BF16NP)
    ci["deT"] = np.ascontiguousarray(np.asarray(inputs["L1_ffn_down_embed"]).T)
    ci["ue"] = np.asarray(inputs["L1_ffn_up_embed"])
    ci["eye_b"] = np.eye(128, dtype=np.float32).astype(BF16NP)
    ci["eye_f"] = np.eye(128, dtype=np.float32)
    sel = np.zeros((8, 1024), np.float32)
    for e_ in range(8):
        sel[e_, e_ * 128:(e_ + 1) * 128] = 1.0
    ci["sel8"] = sel
    ci["ones_col"] = np.ones((128, 1), np.float32)
    ci["ones_row"] = np.ones((1, 128), np.float32)
    ci["constv"] = np.tile(np.array([0.0, 1.0, 2.0, 1e-5, 1e-6, -1.0, 0.5, -0.5], np.float32), (128, 1))
    mp = np.zeros((8, 1), np.float32)
    mn = np.zeros((8, 1), np.float32)
    if half == 1:
        mp[2 * bidx] = 1.0
    if half == 0:
        mn[2 * bidx + 1] = 1.0
    ci["mask_prev"] = mp
    ci["mask_next_b"] = mn.astype(BF16NP)
    for k in list(ci):
        v = ci[k]
        if v.dtype not in (np.float32, BF16NP):
            ci[k] = np.ascontiguousarray(v, dtype=np.float32)
    return ci


def kernel(**inputs):
    nc = build_program(n_stages=4)
    in_maps = [prep_inputs(inputs, c) for c in range(8)]
    res = run_bass_kernel_spmd(nc, in_maps, core_ids=list(range(8)))
    out = np.empty((4, 4096, D), np.float32)
    for c in range(8):
        bidx, half = c // 2, c % 2
        out[bidx, half * T:(half + 1) * T, :] = res.results[c]["y"].T
    return out


# version 06
# speedup vs baseline: 1.7366x; 1.1456x over previous
import numpy as np
import ml_dtypes
import concourse.bass as bass
import concourse.mybir as mybir
from concourse.bass_utils import run_bass_kernel_spmd

F32 = mybir.dt.float32
BF16 = mybir.dt.bfloat16
AF = mybir.ActivationFunctionType
OP = mybir.AluOpType
AX = mybir.AxisListType
BF16NP = ml_dtypes.bfloat16

T = 2048
D = 1024
KT = 8
CH = 8
LR = 0.02

ENGS = ("PE", "DVE", "ACT", "POOL", "SP")


class Sem:
    def __init__(self, h, name):
        self.h = h; self.name = name; self.n = 0


class Builder:
    def __init__(self, nc):
        self.nc = nc
        self.q = {e: [] for e in ENGS}
        self.sems = {}
        self.last_wait = {e: {} for e in ENGS}
        self._ctx = []

    def sem(self, name):
        ctx = self.nc.semaphore(name)
        h = ctx.__enter__()
        self._ctx.append(ctx)
        s = Sem(h, name)
        self.sems[name] = s
        return s

    def rec(self, eng, make, inc=None, inc_n=1):
        if inc is not None:
            inc.n += inc_n
            def f(e, make=make, s=inc, k=inc_n):
                make(e).then_inc(s.h, k)
            self.q[eng].append(f)
        else:
            self.q[eng].append(lambda e, make=make: make(e))

    def wait(self, eng, sem, val=None):
        if val is None:
            val = sem.n
        lw = self.last_wait[eng]
        if lw.get(sem.name, -1) >= val:
            return
        lw[sem.name] = val
        self.q[eng].append(lambda e, s=sem, v=val: e.wait_ge(s.h, v))

    def wait_many(self, eng, toks):
        for (sem, val) in toks:
            self.wait(eng, sem, val)

    def play(self):
        with self.nc.Block() as block:
            @block.tensor
            def _(e):
                for f in self.q["PE"]: f(e)
            @block.vector
            def _(e):
                for f in self.q["DVE"]: f(e)
            @block.scalar
            def _(e):
                for f in self.q["ACT"]: f(e)
            @block.gpsimd
            def _(e):
                for f in self.q["POOL"]: f(e)
            @block.sync
            def _(e):
                for f in self.q["SP"]: f(e)


def build_program(n_stages=4, taps=(), sub4="all"):
    nc = bass.Bass("TRN2", target_bir_lowering=False)
    b = Builder(nc)

    di = {}
    def din(name, shape, dtype=F32):
        di[name] = nc.dram_tensor(name, list(shape), dtype, kind="ExternalInput").ap()
        return di[name]

    din("x_tf", [T, D]); din("x_ft", [D, T])
    din("WbT", [D, D]); din("WpT", [D, D])
    din("initBT", [D, 16]); din("initA", [D, 16]); din("initAT", [16, D]); din("initB", [16, D])
    din("g0", [128, KT]); din("be0", [128, KT])
    din("routerT", [D, 8]); din("w1", [D, 8 * D]); din("w2", [8 * D, D])
    din("n1w", [128, KT]); din("n2w", [128, KT])
    din("ipw", [D, 4 * D]); din("ipb", [128, 32])
    din("convw", [128, 48]); din("convb", [128, 16])
    din("dtw", [2 * D, 64]); din("dtb", [64, 1])
    din("bpw", [2 * D, 64]); din("bpb", [64, 1])
    din("cpw", [2 * D, 64]); din("cpb", [64, 1])
    din("s2iw", [64, 2 * D]); din("s2ib", [128, 16]); din("Dpar", [128, 16])
    din("outw", [2 * D, D]); din("outb", [128, KT])
    din("gatew", [D, 4]); din("gateb", [4, 1]); din("gateb_bc", [128, 4]); din("eb2all", [4, 1024], BF16)
    din("ew1", [4, D, 4 * D]); din("eb1", [4, 128, 32])
    din("ew2", [4, 4 * D, D]); din("eb2", [4, 128, KT])
    din("Wg", [D, 4 * D]); din("bg", [128, 32])
    din("Wu", [D, 4 * D]); din("bu", [128, 32])
    din("Wd", [4 * D, D]); din("bd", [128, KT])
    din("Wq", [D, 512]); din("keysT", [64, 512], BF16)
    din("deT", [D, 4 * D]); din("ue", [4 * D, D])
    din("eye_b", [128, 128], BF16); din("eye_f", [128, 128])
    din("sel8", [8, 1024])
    din("ones_col", [128, 1]); din("ones_row", [1, 128])
    din("constv", [128, 8])
    din("mask_prev", [8, 1]); din("mask_next_b", [8, 1], BF16)

    y_out = nc.dram_tensor("y", [D, T], F32, kind="ExternalOutput").ap()
    dbg = {}
    for name, shape, dt in taps:
        dbg[name] = nc.dram_tensor("dbg_" + name, list(shape), dt, kind="ExternalOutput").ap()

    cc_src_d = nc.dram_tensor("cc_src_d", [128, 8], BF16).ap()
    cc_dst_d = nc.dram_tensor("cc_dst_d", [8 * 128, 8], BF16).ap()
    cc_srcA = nc.dram_tensor("cc_srcA", [16, D], F32).ap()
    cc_dstA = nc.dram_tensor("cc_dstA", [8 * 16, D], F32).ap()
    cc_srcB = nc.dram_tensor("cc_srcB", [128, 128], F32).ap()
    cc_dstB = nc.dram_tensor("cc_dstB", [8 * 128, 128], F32).ap()
    cc_srcH = nc.dram_tensor("cc_srcH", [128, KT * 2], F32).ap()
    cc_dstH = nc.dram_tensor("cc_dstH", [8 * 128, KT * 2], F32).ap()
    cc_srcS = nc.dram_tensor("cc_srcS", [64, 1], F32).ap()
    cc_dstS = nc.dram_tensor("cc_dstS", [8 * 64, 1], F32).ap()
    scr_xm = nc.dram_tensor("scr_xm", [2 * D, T], BF16).ap()
    scr_sg = nc.dram_tensor("scr_sg", [2 * D, T], BF16).ap()

    sbctx = []
    def sbuf(name, shape, dtype=F32):
        ctx = nc.sbuf_tensor(name, list(shape), dtype)
        t = ctx.__enter__(); sbctx.append(ctx)
        return t

    AF32 = sbuf("AF32", [128, 24576])
    ABF = sbuf("ABF", [128, 45056], BF16)
    wbuf = sbuf("wbuf", [128, 2 * 4096], BF16)
    kc = sbuf("kc", [128, 448])
    selc = sbuf("selc", [8, 1024])
    kcb = sbuf("kcb", [128, 264], BF16)

    def f32v(off, cols, part=128, p0=0):
        return AF32[p0:p0 + part, off:off + cols]
    def bfv(off, cols, part=128, p0=0):
        return ABF[p0:p0 + part, off:off + cols]

    xs_o = 0
    def xs(k):
        return AF32[:, xs_o + k * T: xs_o + (k + 1) * T]
    B1_o = 16384
    SQ_o = 17408
    SM_o = 18432
    ACCA_o = 18944
    ACCB_o = 19968
    FREE_F32 = 20096

    eye_bc = kcb[:, 0:128]
    ones_col = kc[:, 0:1]
    ones_row = kc[0:1, 8:136]
    g0c = kc[:, 136:144]
    be0c = kc[:, 144:152]
    maskbc = kc[:, 152:160]
    n1wc = kc[:, 160:168]
    n2wc = kc[:, 168:176]
    eye_fc = kc[:, 184:312]
    CONSTS = [0.0, 1.0, 2.0, 1e-5, 1e-6, -1.0, 0.5, -0.5]
    for ci_, cv_ in enumerate(CONSTS):
        nc.const_aps.register(F32, cv_, kc[:, 176 + ci_:177 + ci_])

    psctx = []
    def psum(name, shape, dtype=F32):
        ctx = nc.psum_tensor(name, list(shape), dtype)
        t = ctx.__enter__(); psctx.append(ctx)
        return t

    psA = psum("psA", [128, 512])
    psB = psum("psB", [128, 512])
    psE = psum("psE", [128, 512])
    psC = psum("psC", [128, 1024])
    psD = psum("psD", [128, 512])
    psT = psum("psT", [128, 2048], BF16)

    PSMAIN = [psA, psB, psE]

    s_sp = b.sem("s_sp"); s_sc = b.sem("s_sc"); s_gd = b.sem("s_gd")
    s_cst = b.sem("s_cst"); s_ini = b.sem("s_ini"); s_gsl = b.sem("s_gsl")
    s_sg = [b.sem("s_sg0"), b.sem("s_sg1")]
    s_wl = [b.sem("s_wl0"), b.sem("s_wl1")]
    s_pe = b.sem("s_pe"); s_dve = b.sem("s_dve"); s_act = b.sem("s_act")
    s_pool = b.sem("s_pool"); s_cc = b.sem("s_cc")

    def tok(sem):
        return (sem, sem.n)

    GATED_DMA_SEMS = ("s_sp", "s_sc", "s_cst", "s_ini", "s_gsl")

    def dma(stream, sem, out, in_, **kw):
        b.rec(stream, lambda e, o=out, i=in_, kw=kw: e.dma_start(out=o, in_=i, **kw),
              inc=sem, inc_n=16)
        if sem.name in GATED_DMA_SEMS:
            b.wait(stream, sem)
        return tok(sem)

    def tap(name, src_ap, after=()):
        if name in dbg:
            b.wait_many("SP", after)
            dma("SP", s_sp, dbg[name], src_ap)
            for e_ in ENGS:
                b.wait(e_, s_sp)

    def fence_all(extra=()):
        toks = [tok(s_pe), tok(s_dve), tok(s_act), tok(s_pool), tok(s_sp), tok(s_sc),
                tok(s_gd), tok(s_wl[0]), tok(s_wl[1]), tok(s_cc)] + list(extra)
        for e in ENGS:
            b.wait_many(e, toks)

    class WStream:
        def __init__(self):
            self.idx = 0
            self.freed = [[], []]
        def load(self, pieces, n_k):
            par = self.idx % 2; self.idx += 1
            base = par * 4096
            for (sem, val) in self.freed[par]:
                b.wait("POOL", sem, val)
            self.freed[par] = []
            aps = pieces if isinstance(pieces, list) else [pieces]
            tot = sum(a.shape[-1] for a in aps)
            assert n_k * tot <= 4096, (n_k, tot)
            c0 = 0
            for a in aps:
                cw = a.shape[-1]
                dst = wbuf[:, base:base + n_k * tot].rearrange(
                    "p (k c) -> p k c", k=n_k)[:, :, c0:c0 + cw]
                b.rec("POOL", lambda e, o=dst, i=a.rearrange("(k p) c -> p k c", p=128):
                      e.dma_start(out=o, in_=i), inc=s_wl[par], inc_n=16)
                c0 += cw
            t = tok(s_wl[par])
            def view(k, c0=0, cw=None, base=base, tot=tot):
                cw = tot - c0 if cw is None else cw
                return wbuf[:, base + k * tot + c0: base + k * tot + c0 + cw]
            return view, t, par
        def consumed(self, par, t):
            self.freed[par].append(t)

    ws = WStream()

    def ln_full(src_fn, kt, n_feat, eps, with_mean, dst_rows, wait_toks=(), n_tok=T):
        rstd_row, negmur_row = dst_rows
        nn = (n_tok + 511) // 512
        pe_hist = []
        for n in range(nn):
            n0 = n * 512
            nw = min(512, n_tok - n0)
            b.wait("PE", s_dve)
            b.wait("PE", s_act)
            for k in range(kt):
                slot = k % 2
                sq = AF32[:, SQ_o + slot * 512: SQ_o + slot * 512 + nw]
                if len(pe_hist) >= 2:
                    b.wait("ACT", s_pe, pe_hist[-2])
                elif len(pe_hist) == 0:
                    b.wait("ACT", s_pe)
                b.wait_many("ACT", wait_toks)
                b.rec("ACT", lambda e, i=src_fn(k)[:, n0:n0 + nw], o=sq:
                      e.activation(out=o, in_=i, func=AF.Square), inc=s_act)
                b.wait("PE", s_act)
                b.wait_many("PE", wait_toks)
                if with_mean:
                    b.rec("PE", lambda e, r=src_fn(k)[:, n0:n0 + nw], p=psD[0:1, 0:nw],
                                 st=(k == 0), lt=(k == kt - 1):
                          e.matmul(p, ones_col, r, start=st, stop=lt))
                b.rec("PE", lambda e, r=sq, p=psC[0:1, 0:nw], st=(k == 0), lt=(k == kt - 1):
                      e.matmul(p, ones_col, r, start=st, stop=lt), inc=s_pe)
                pe_hist.append(s_pe.n)
            b.wait("DVE", s_pe)
            if with_mean:
                b.rec("DVE", lambda e, p=psD[0:1, 0:nw], o=rowA(nw) if False else negmur_row[:, n0:n0 + nw]:
                      e.tensor_scalar(o, p, 1.0 / n_feat, None, OP.mult), inc=s_dve)
                b.wait("DVE", s_dve)
                b.rec("DVE", lambda e, mu=negmur_row[:, n0:n0 + nw], o=rstd_row[:, n0:n0 + nw]:
                      e.tensor_tensor(o, mu, mu, OP.mult), inc=s_dve)
                b.wait("DVE", s_dve)
                b.rec("DVE", lambda e, p=psC[0:1, 0:nw], m2=rstd_row[:, n0:n0 + nw],
                             o=rstd_row[:, n0:n0 + nw]:
                      e.scalar_tensor_tensor(o, p, 1.0 / n_feat, m2, OP.mult, OP.subtract), inc=s_dve)
            else:
                b.rec("DVE", lambda e, p=psC[0:1, 0:nw], o=rstd_row[:, n0:n0 + nw]:
                      e.tensor_scalar(o, p, 1.0 / n_feat, None, OP.mult), inc=s_dve)
        dve_t = tok(s_dve)
        b.wait_many("ACT", [dve_t])
        b.rec("ACT", lambda e, v=rstd_row, o=rstd_row:
              e.activation(out=o, in_=v, func=AF.Ln, bias=float(eps)), inc=s_act)
        b.wait("ACT", s_act)
        b.rec("ACT", lambda e, v=rstd_row, o=rstd_row:
              e.activation(out=o, in_=v, func=AF.Exp, scale=-0.5), inc=s_act)
        t_act = tok(s_act)
        if with_mean:
            b.wait_many("DVE", [t_act])
            b.rec("DVE", lambda e, mu=negmur_row, r=rstd_row, o=negmur_row:
                  e.scalar_tensor_tensor(o, mu, -1.0, r, OP.mult, OP.mult), inc=s_dve)
            return tok(s_dve)
        return t_act

    def bcast_chunk(row_ap, n0, nw, dst):
        b.wait("PE", s_dve)
        b.wait("PE", s_act)
        b.rec("PE", lambda e, r=row_ap[:, n0:n0 + nw], p=psD[:, 0:nw]:
              e.matmul(p, ones_row, r, start=True, stop=True), inc=s_pe)
        b.wait("DVE", s_pe)
        b.rec("DVE", lambda e, p=psD[:, 0:nw], o=dst[:, 0:nw]:
              e.tensor_copy(o, p), inc=s_dve)
        return tok(s_dve)

    dma("SP", s_sp, kc[:, 0:1], di["ones_col"])
    dma("SP", s_sp, kc[0:1, 8:136], di["ones_row"])
    dma("SP", s_sp, kc[:, 136:144], di["g0"])
    dma("SP", s_sp, kc[:, 144:152], di["be0"])
    dma("SP", s_sp, kc[:, 152:160],
        di["mask_prev"].rearrange("r c -> (r c)").partition_broadcast(128))
    dma("SP", s_sp, kc[:, 160:168], di["n1w"])
    dma("SP", s_sp, kc[:, 168:176], di["n2w"])
    dma("SP", s_sp, kc[:, 176:184], di["constv"])
    dma("SP", s_sp, kc[:, 184:312], di["eye_f"])
    dma("SP", s_sp, selc[:], di["sel8"])
    dma("POOL", s_cst, kcb[:, 0:128], di["eye_b"])
    dma("POOL", s_cst, kcb[0:8, 128:129], di["mask_next_b"])
    const_t = tok(s_sp); const_tb = tok(s_cst)

    for k in range(KT):
        dma("SP", s_sp, xs(k), di["x_ft"][k * 128:(k + 1) * 128, :])
    x_t = tok(s_sp)

    XN_o = 0
    Tp = T + 1
    def xn(k):
        return ABF[:, XN_o + k * Tp: XN_o + (k + 1) * Tp]
    XBC_o = 16392
    def xbc(k):
        return ABF[:, XBC_o + k * 512: XBC_o + (k + 1) * 512]
    VC_o = 20488
    def vch(par, k):
        return ABF[:, VC_o + par * 2048 + k * 256: VC_o + par * 2048 + (k + 1) * 256]
    S3_o = 24584
    DAB_o = 28680
    DBT_o = 36872
    B2T_o = 37896
    MID_o = 38024
    TMP_o = 40072
    INIT_o = 41096

    def dab(c):
        return ABF[0:16, DAB_o + c * 1024: DAB_o + (c + 1) * 1024]

    r_rstd = AF32[0:1, FREE_F32:FREE_F32 + T]
    r_negmur = AF32[0:1, FREE_F32 + T:FREE_F32 + 2 * T]
    ln1_t = ln_full(xs, KT, D, 1e-5, True, (r_rstd, r_negmur), wait_toks=[x_t, const_t])

    B1c = AF32[:, B1_o:B1_o + 512]
    B2c = AF32[:, B1_o + 512:B1_o + 1024]
    FT1 = AF32[:, SQ_o:SQ_o + 512]
    for n in range(4):
        n0 = n * 512
        b.wait_many("PE", [ln1_t])
        bcast_chunk(r_rstd, n0, 512, B1c)
        bcast_chunk(r_negmur, n0, 512, B2c)
        bc_t = tok(s_dve)
        b.wait_many("DVE", [bc_t, x_t, const_t])
        for k in range(KT):
            b.rec("DVE", lambda e, k=k, n0=n0: e.tensor_tensor(
                FT1, xs(k)[:, n0:n0 + 512], B1c, OP.mult), inc=s_dve)
            b.wait("DVE", s_dve)
            b.rec("DVE", lambda e, k=k, n0=n0: e.tensor_tensor(
                FT1, FT1, B2c, OP.add), inc=s_dve)
            b.wait("DVE", s_dve)
            b.rec("DVE", lambda e, k=k, n0=n0: e.tensor_scalar(
                xn(k)[:, n0:n0 + 512], FT1, g0c[:, k:k + 1], be0c[:, k:k + 1],
                OP.mult, OP.add), inc=s_dve)
            b.wait("DVE", s_dve)
    xn_t = tok(s_dve)

    b.wait("SP", s_pe)
    dma("SP", s_sp, AF32[0:16, FREE_F32 + 2048:FREE_F32 + 3072], di["initAT"])
    dma("SP", s_sp, AF32[:, FREE_F32 + 1152:FREE_F32 + 1280].rearrange("p (k c) -> p k c", k=KT),
        di["initBT"].rearrange("(k p) c -> p k c", p=128))
    init_tf = tok(s_sp)

    b.wait_many("DVE", [xn_t])
    for k in range(KT):
        b.rec("DVE", lambda e, k=k: e.tensor_copy(
            ABF[:, TMP_o + 96 + k:TMP_o + 97 + k], ABF[:, XN_o + k * Tp:XN_o + k * Tp + 1]),
            inc=s_dve)
    b.wait_many("SP", [tok(s_dve)])
    dma("SP", s_sp, cc_src_d, ABF[:, TMP_o + 96:TMP_o + 104])
    b.wait("POOL", s_sp)
    b.rec("POOL", lambda e: e.collective_compute(
        "AllGather", OP.bypass, replica_groups=[list(range(8))],
        ins=[cc_src_d[:].opt()], outs=[cc_dst_d[:].opt()]), inc=s_cc)
    b.wait("POOL", s_cc)
    gsl = ABF[:, TMP_o:TMP_o + 64]
    dma("POOL", s_gsl, gsl.rearrange("p (r k) -> p r k", r=8),
        cc_dst_d.rearrange("(r p) k -> p r k", p=128))
    mnb = ABF[:, TMP_o + 64:TMP_o + 72]
    dma("POOL", s_gsl, mnb, di["mask_next_b"].rearrange("r c -> (r c)").partition_broadcast(128))
    g_t = tok(s_gsl)
    b.wait_many("DVE", [g_t])
    b.rec("DVE", lambda e: e.tensor_tensor(
        gsl.rearrange("p (r k) -> p r k", r=8),
        gsl.rearrange("p (r k) -> p r k", r=8),
        mnb.rearrange("p (r o) -> p r o", o=1).broadcast_to((128, 8, 8)), OP.mult), inc=s_dve)
    b.wait("DVE", s_dve)
    def _red_shift(e):
        with nc.allow_low_precision(reason="0/1-mask select: single nonzero term"):
            return e.tensor_reduce(ABF[:, TMP_o + 72:TMP_o + 80],
                                   gsl.rearrange("p (r k) -> p k r", r=8), AX.X, OP.add)
    b.rec("DVE", _red_shift, inc=s_dve)
    b.wait("DVE", s_dve)
    for k in range(KT):
        b.rec("DVE", lambda e, k=k: e.tensor_copy(
            ABF[:, XN_o + k * Tp + T:XN_o + k * Tp + T + 1], ABF[:, TMP_o + 72 + k:TMP_o + 73 + k]),
            inc=s_dve)
    shift_t = tok(s_dve)

    initBTb = ABF[:, INIT_o:INIT_o + 128]
    initAb = ABF[:, INIT_o + 128:INIT_o + 256]
    dma("POOL", s_ini, initBTb.rearrange("p (k c) -> p k c", k=KT),
        di["initBT"].rearrange("(k p) c -> p k c", p=128))
    dma("POOL", s_ini, initAb.rearrange("p (k c) -> p k c", k=KT),
        di["initA"].rearrange("(k p) c -> p k c", p=128))
    init_tb = tok(s_ini)
    GA = AF32[0:16, FREE_F32:FREE_F32 + 1024]
    GB = AF32[:, FREE_F32 + 1024:FREE_F32 + 1152]
    initBT_f = AF32[:, FREE_F32 + 1152:FREE_F32 + 1280]
    initAT_f = AF32[0:16, FREE_F32 + 2048:FREE_F32 + 3072]

    wv0, wv0_t, _ = ws.load(di["WpT"][:, 0:512], KT)
    wv1, wv1_t, _ = ws.load(di["WpT"][:, 512:1024], KT)
    def wpT(k, m):
        if m < 4:
            return wv0(k, m * 128, 128)
        return wv1(k, (m - 4) * 128, 128)

    projin = ABF[:, INIT_o + 256:INIT_o + 256 + 32]
    projerr = ABF[:, INIT_o + 288:INIT_o + 288 + 32]
    prev_evac = {}
    tp_i = 0
    tp_evac = {}
    for c in range(CH):
        par = c % 2
        t0 = c * 256
        b.wait_many("PE", [shift_t, wv0_t, wv1_t])
        for m in range(KT):
            ps = PSMAIN[m % 3]
            if c > 0 or m >= 3:
                pm = (c, m - 3) if m >= 3 else (c - 1, m + 5)
                b.wait("PE", s_dve, prev_evac[pm][1])
            for k in range(KT):
                b.rec("PE", lambda e, k=k, m=m, t0=t0, ps=ps: e.matmul(
                    ps[:, 0:256], wpT(k, m), xn(k)[:, t0 + 1:t0 + 257],
                    start=(k == 0), stop=(k == KT - 1)), inc=(s_pe if k == KT - 1 else None))
            b.wait("DVE", s_pe)
            b.rec("DVE", lambda e, m=m, par=par, ps=ps: e.tensor_copy(
                vch(par, m), ps[:, 0:256]), inc=s_dve)
            prev_evac[(c, m)] = tok(s_dve)
        v_t = tok(s_dve)
        xT = lambda h: ABF[:, S3_o + 2048 + h * 1024: S3_o + 2048 + (h + 1) * 1024]
        b.wait("POOL", s_pe)
        dma("POOL", s_gd, ABF[:, S3_o + 2048:S3_o + 4096].rearrange("p (h f) -> p h f", h=2),
            di["x_tf"][t0:t0 + 256, :].rearrange("(h p) f -> p h f", p=128))
        xT_t = tok(s_gd)
        VT = lambda h: ABF[:, S3_o + h * 1024: S3_o + (h + 1) * 1024]
        for h in range(2):
            for k in range(KT):
                slot = tp_i % 2
                b.wait_many("PE", [v_t, const_tb])
                if tp_i >= 2:
                    b.wait("PE", s_dve, tp_evac[tp_i - 2])
                b.rec("PE", lambda e, h=h, k=k, par=par, slot=slot: e.transpose(
                    psT[:, slot * 1024:slot * 1024 + 128],
                    vch(par, k)[:, h * 128:(h + 1) * 128], eye_bc),
                    inc=s_pe)
                b.wait("DVE", s_pe)
                b.rec("DVE", lambda e, h=h, k=k, slot=slot: e.tensor_copy(
                    VT(h)[:, k * 128:(k + 1) * 128], psT[:, slot * 1024:slot * 1024 + 128]),
                    inc=s_dve)
                tp_evac[tp_i] = s_dve.n
                tp_i += 1
        vt_t = tok(s_dve)
        b.wait("DVE", s_pe)
        b.wait_many("DVE", [x_t])
        for k in range(KT):
            b.rec("DVE", lambda e, k=k, t0=t0: e.tensor_copy(
                xbc(k)[:, 0:256], xs(k)[:, t0:t0 + 256]), inc=s_dve)
        xc_t = tok(s_dve)
        b.wait_many("PE", [xc_t, init_tb, v_t])
        b.wait("PE", s_dve)
        for h in range(2):
            if h == 1:
                b.wait("PE", s_dve)
            for k in range(KT):
                b.rec("PE", lambda e, h=h, k=k: e.matmul(
                    psC[0:128, 0:16], xbc(k)[:, h * 128:(h + 1) * 128],
                    initBTb.rearrange("p (k c) -> p k c", k=KT)[:, k, :],
                    start=(k == 0), stop=(k == KT - 1)))
                b.rec("PE", lambda e, h=h, k=k, par=par: e.matmul(
                    psC[0:128, 512:528], vch(par, k)[:, h * 128:(h + 1) * 128],
                    initAb.rearrange("p (k c) -> p k c", k=KT)[:, k, :],
                    start=(k == 0), stop=(k == KT - 1)), inc=(s_pe if k == KT - 1 else None))
            b.wait("DVE", s_pe)
            b.rec("DVE", lambda e, h=h: e.tensor_copy(projin[:, h * 16:(h + 1) * 16], psC[0:128, 0:16]), inc=s_dve)
            b.rec("DVE", lambda e, h=h: e.tensor_copy(projerr[:, h * 16:(h + 1) * 16], psC[0:128, 512:528]), inc=s_dve)
        pr_t = tok(s_dve)
        b.wait_many("PE", [pr_t, vt_t, xT_t])
        for h in range(2):
            for nn2 in range(2):
                b.rec("PE", lambda e, h=h, nn2=nn2: e.matmul(
                    psC[0:16, nn2 * 512:(nn2 + 1) * 512], projin[:, h * 16:(h + 1) * 16],
                    VT(h)[:, nn2 * 512:(nn2 + 1) * 512], start=(h == 0), stop=(h == 1)))
        for h in range(2):
            for m in range(KT):
                b.rec("PE", lambda e, m=m, h=h: e.matmul(
                    psD[0:128, m * 16:(m + 1) * 16],
                    xT(h)[:, m * 128:(m + 1) * 128], projerr[:, h * 16:(h + 1) * 16],
                    start=(h == 0 and m == 0), stop=(h == 1 and m == KT - 1)),
                    inc=(s_pe if (m == KT - 1 and h == 1) else None))
        b.wait("DVE", s_pe)
        b.rec("DVE", lambda e, c=c: e.tensor_copy(dab(c), psC[0:16, 0:1024]), inc=s_dve)
        b.rec("DVE", lambda e, c=c: e.tensor_copy(
            ABF[:, DBT_o + c * 128:DBT_o + (c + 1) * 128], psD[0:128, 0:128]), inc=s_dve)
    d_t = tok(s_dve)
    ws.consumed(0, tok(s_pe))
    ws.consumed(1, tok(s_pe))

    accA = AF32[0:16, ACCA_o:ACCA_o + 1024]
    accBT = AF32[:, ACCB_o:ACCB_o + 128]
    b.wait_many("DVE", [d_t])
    b.rec("DVE", lambda e: e.memset(accA, 0.0), inc=s_dve)
    b.rec("DVE", lambda e: e.memset(accBT, 0.0), inc=s_dve)
    for c in range(CH):
        b.wait("DVE", s_dve)
        b.rec("DVE", lambda e, c=c: e.tensor_tensor(accA, accA, dab(c), OP.add), inc=s_dve)
        b.rec("DVE", lambda e, c=c: e.tensor_tensor(
            accBT, accBT, ABF[:, DBT_o + c * 128:DBT_o + (c + 1) * 128], OP.add), inc=s_dve)
    sum_t = tok(s_dve)
    b.wait_many("SP", [sum_t])
    dma("SP", s_sp, cc_srcA, accA)
    dma("SP", s_sp, cc_srcB, accBT)
    b.wait("POOL", s_sp)
    b.rec("POOL", lambda e: e.collective_compute(
        "AllGather", OP.bypass, replica_groups=[list(range(8))],
        ins=[cc_srcA[:].opt()], outs=[cc_dstA[:].opt()]), inc=s_cc)
    b.rec("POOL", lambda e: e.collective_compute(
        "AllGather", OP.bypass, replica_groups=[list(range(8))],
        ins=[cc_srcB[:].opt()], outs=[cc_dstB[:].opt()]), inc=s_cc)
    cc_ab = tok(s_cc)

    b.wait_many("DVE", [cc_ab])
    b.rec("DVE", lambda e: e.memset(accA, 0.0), inc=s_dve)
    b.rec("DVE", lambda e: e.memset(accBT, 0.0), inc=s_dve)
    for r in range(8):
        b.wait_many("SP", [cc_ab, tok(s_dve)])
        dma("SP", s_sp, GA, cc_dstA[r * 16:(r + 1) * 16, :])
        dma("SP", s_sp, GB, cc_dstB[r * 128:(r + 1) * 128, :])
        b.wait("DVE", s_sp)
        b.rec("DVE", lambda e, r=r: e.scalar_tensor_tensor(
            accA, GA, maskbc[0:16, r:r + 1], accA, OP.mult, OP.add), inc=s_dve)
        b.rec("DVE", lambda e, r=r: e.scalar_tensor_tensor(
            accBT, GB, maskbc[0:128, r:r + 1], accBT, OP.mult, OP.add), inc=s_dve)
    acc_t = tok(s_dve)

    midv = lambda c: ABF[0:16, MID_o + c * 256:MID_o + (c + 1) * 256]
    B2Tb = ABF[:, B2T_o:B2T_o + 128]
    coefcol = AF32[:, SM_o:SM_o + 1]
    sq16 = AF32[0:16, SQ_o:SQ_o + 1024]
    sq128 = AF32[:, B1_o:B1_o + 128]
    b.wait_many("DVE", [init_tf])
    for c in range(CH):
        dep = acc_t if c == 0 else tok(s_dve)
        b.wait_many("ACT", [dep])
        b.rec("ACT", lambda e: e.activation(out=sq16, in_=accA, func=AF.Square), inc=s_act)
        b.rec("ACT", lambda e: e.activation(out=sq128, in_=accBT, func=AF.Square), inc=s_act)
        b.wait("DVE", s_act)
        b.rec("DVE", lambda e: e.tensor_reduce(AF32[0:16, SM_o + 8:SM_o + 9], sq16, AX.X, OP.add), inc=s_dve)
        b.rec("DVE", lambda e: e.tensor_reduce(AF32[0:128, SM_o + 9:SM_o + 10], sq128, AX.X, OP.add), inc=s_dve)
        b.wait("PE", s_dve)
        b.rec("PE", lambda e: e.matmul(psD[0:1, 0:1], ones_col[0:16, :], AF32[0:16, SM_o + 8:SM_o + 9],
                                       start=True, stop=False))
        b.rec("PE", lambda e: e.matmul(psD[0:1, 0:1], ones_col, AF32[0:128, SM_o + 9:SM_o + 10],
                                       start=False, stop=True), inc=s_pe)
        b.wait("ACT", s_pe)
        b.rec("ACT", lambda e: e.activation(out=AF32[0:1, SM_o + 10:SM_o + 11], in_=psD[0:1, 0:1],
                                            func=AF.Sqrt, scale=LR * LR), inc=s_act)
        b.wait("DVE", s_act)
        b.rec("DVE", lambda e: e.tensor_scalar(AF32[0:1, SM_o + 11:SM_o + 12],
                                               AF32[0:1, SM_o + 10:SM_o + 11], 1e-6, None, OP.add), inc=s_dve)
        b.wait("DVE", s_dve)
        b.rec("DVE", lambda e: e.reciprocal(AF32[0:1, SM_o + 12:SM_o + 13],
                                            AF32[0:1, SM_o + 11:SM_o + 12]), inc=s_dve)
        b.wait("DVE", s_dve)
        b.rec("DVE", lambda e: e.tensor_scalar(AF32[0:1, SM_o + 13:SM_o + 14],
                                               AF32[0:1, SM_o + 12:SM_o + 13], 1.0, None, OP.min), inc=s_dve)
        b.wait("PE", s_dve)
        b.rec("PE", lambda e: e.matmul(psD[0:128, 4:5], ones_row, AF32[0:1, SM_o + 13:SM_o + 14],
                                       start=True, stop=True), inc=s_pe)
        b.wait("DVE", s_pe)
        b.rec("DVE", lambda e: e.tensor_scalar(coefcol, psD[0:128, 4:5], -LR, None, OP.mult), inc=s_dve)
        b.wait("DVE", s_dve)
        b.rec("DVE", lambda e, c=c: e.tensor_copy(ABF[0:16, TMP_o:TMP_o + 1024], dab(c)), inc=s_dve)
        b.wait("DVE", s_dve)
        b.rec("DVE", lambda e, c=c: e.scalar_tensor_tensor(
            dab(c), accA, coefcol[0:16, :], initAT_f, OP.mult, OP.add), inc=s_dve)
        b.rec("DVE", lambda e, c=c: e.scalar_tensor_tensor(
            B2Tb, accBT, coefcol[0:128, :], initBT_f, OP.mult, OP.add), inc=s_dve)
        ab_t = tok(s_dve)
        b.wait("DVE", s_dve)
        b.rec("DVE", lambda e, c=c: e.tensor_tensor(accA, accA, ABF[0:16, TMP_o:TMP_o + 1024], OP.add), inc=s_dve)
        b.rec("DVE", lambda e, c=c: e.tensor_tensor(
            accBT, accBT, ABF[:, DBT_o + c * 128:DBT_o + (c + 1) * 128], OP.add), inc=s_dve)
        b.wait("DVE", s_pe)
        b.wait_many("DVE", [x_t])
        for k in range(KT):
            b.rec("DVE", lambda e, k=k, c=c: e.tensor_copy(
                xbc(k)[:, 0:256], xs(k)[:, c * 256:c * 256 + 256]), inc=s_dve)
        xc2_t = tok(s_dve)
        b.wait_many("PE", [ab_t, xc2_t])
        for k in range(KT):
            b.rec("PE", lambda e, k=k, lt=(k == KT - 1): e.matmul(
                psC[0:16, 0:256], B2Tb[:, k * 16:(k + 1) * 16], xbc(k)[:, 0:256],
                start=(k == 0), stop=lt), inc=(s_pe if k == KT - 1 else None))
        b.wait("ACT", s_pe)
        b.rec("ACT", lambda e, c=c: e.activation(out=midv(c), in_=psC[0:16, 0:256],
                                                 func=AF.Copy, scale=2.0), inc=s_act)
        b.wait("DVE", s_pe)
    mid_t = tok(s_act)
    a2_t = tok(s_dve)

    wb0, wb0_t, wb0p = ws.load(di["WbT"][:, 0:512], KT)
    wb1, wb1_t, wb1p = ws.load(di["WbT"][:, 512:1024], KT)
    def wbT(k, m):
        return wb0(k, m * 128, 128) if m < 4 else wb1(k, (m - 4) * 128, 128)
    ev_prev = {}
    g = 0
    for n in range(4):
        n0 = n * 512
        b.wait_many("DVE", [x_t, mid_t])
        if n >= 1:
            b.wait("DVE", s_pe)
        for k in range(KT):
            b.rec("DVE", lambda e, k=k, n0=n0: e.tensor_copy(
                xbc(k), xs(k)[:, n0:n0 + 512]), inc=s_dve)
        cast_t = tok(s_dve)
        b.wait_many("PE", [cast_t, wb0_t, wb1_t, mid_t, a2_t])
        for m in range(KT):
            ps = PSMAIN[g % 3]
            if g >= 3:
                b.wait("PE", *ev_prev[g - 3])
            for k in range(KT):
                b.rec("PE", lambda e, k=k, m=m, ps=ps: e.matmul(
                    ps[:, 0:512], wbT(k, m), xbc(k), start=(k == 0), stop=False))
            for ci in range(2):
                c = n * 2 + ci
                b.rec("PE", lambda e, c=c, m=m, ci=ci, ps=ps: e.matmul(
                    ps[:, ci * 256:(ci + 1) * 256],
                    dab(c)[:, m * 128:(m + 1) * 128], midv(c),
                    start=False, stop=(ci == 1)), inc=(s_pe if ci == 1 else None))
            b.wait("DVE", s_pe)
            b.rec("DVE", lambda e, m=m, n0=n0, ps=ps: e.tensor_tensor(
                xs(m)[:, n0:n0 + 512], xs(m)[:, n0:n0 + 512], ps[:, 0:512], OP.add), inc=s_dve)
            ev_prev[g] = tok(s_dve)
            g += 1
    ws.consumed(wb0p, tok(s_dve))
    ws.consumed(wb1p, tok(s_dve))
    attn_t = tok(s_dve)
    tap("res1", AF32[:, 0:16384], [attn_t])

    def post_ln(src_tok):
        lt = ln_full(xs, KT, D, 1e-5, True, (r_rstd, r_negmur), wait_toks=[src_tok])
        for n in range(4):
            n0 = n * 512
            b.wait_many("PE", [lt])
            bcast_chunk(r_rstd, n0, 512, B1c)
            bcast_chunk(r_negmur, n0, 512, B2c)
            bc_t = tok(s_dve)
            b.wait_many("DVE", [bc_t])
            for k in range(KT):
                b.rec("DVE", lambda e, k=k, n0=n0: e.tensor_tensor(
                    FT1, xs(k)[:, n0:n0 + 512], B1c, OP.mult), inc=s_dve)
                b.wait("DVE", s_dve)
                b.rec("DVE", lambda e, k=k, n0=n0: e.tensor_tensor(
                    xs(k)[:, n0:n0 + 512], FT1, B2c, OP.add), inc=s_dve)
                b.wait("DVE", s_dve)
        return tok(s_dve)

    st1_t = post_ln(attn_t)
    tap("x1", AF32[:, 0:16384], [st1_t])
    last_t = st1_t

    if n_stages >= 2:
        def xb2(k):
            return ABF[:, k * T:(k + 1) * T]
        def rbuf(k):
            return ABF[:, 16384 + k * T:16384 + (k + 1) * T]
        routb = ABF[:, 32768:32832]
        B_cw = AF32[:, FREE_F32:FREE_F32 + T]
        cwF = AF32[0:8, FREE_F32 + 2048:FREE_F32 + 2048 + T]
        probs_all = AF32[:, SM_o:SM_o + 128]
        cw8_all = AF32[:, SM_o + 128:SM_o + 256]

        b.wait_many("DVE", [st1_t])
        b.wait("DVE", s_pe)
        b.wait("DVE", s_gsl)
        for k in range(KT):
            b.rec("DVE", lambda e, k=k: e.tensor_copy(xb2(k), xs(k)), inc=s_dve)
        sh_t = tok(s_dve)

        b.wait("POOL", s_pe)
        dma("POOL", s_ini, routb.rearrange("p (k c) -> p k c", k=KT),
            di["routerT"].rearrange("(k p) c -> p k c", p=128))
        rt_t = tok(s_ini)

        b.wait_many("PE", [sh_t, rt_t])
        for tt in range(16):
            b.wait("PE", s_dve)
            b.wait("PE", s_act)
            for k in range(KT):
                b.rec("PE", lambda e, tt=tt, k=k: e.matmul(
                    psC[0:128, 0:8], xb2(k)[:, tt * 128:(tt + 1) * 128],
                    routb.rearrange("p (k c) -> p k c", k=KT)[:, k, :],
                    start=(k == 0), stop=(k == KT - 1)), inc=(s_pe if k == KT - 1 else None))
            b.wait("ACT", s_pe)
            b.rec("ACT", lambda e, tt=tt: e.activation(
                out=probs_all[:, tt * 8:(tt + 1) * 8], in_=psC[0:128, 0:8],
                func=AF.Sigmoid), inc=s_act)
            b.wait("DVE", s_act)
            m8 = AF32[:, SM_o + 256:SM_o + 264]
            sA = AF32[:, SM_o + 264:SM_o + 265]
            sB = AF32[:, SM_o + 265:SM_o + 266]
            b.rec("DVE", lambda e, tt=tt: e.max(out=m8, in_=probs_all[:, tt * 8:(tt + 1) * 8]), inc=s_dve)
            b.wait("DVE", s_dve)
            b.rec("DVE", lambda e: e.tensor_tensor(sA, m8[:, 0:1], m8[:, 1:2], OP.add), inc=s_dve)
            b.wait("DVE", s_dve)
            b.rec("DVE", lambda e: e.tensor_scalar(sA, sA, 1e-20, None, OP.add), inc=s_dve)
            b.wait("DVE", s_dve)
            b.rec("DVE", lambda e: e.reciprocal(sB, sA), inc=s_dve)
            b.wait("DVE", s_dve)
            c1 = AF32[:, SM_o + 266:SM_o + 267]
            c2 = AF32[:, SM_o + 267:SM_o + 268]
            b.rec("DVE", lambda e: e.tensor_tensor(c1, m8[:, 0:1], sB, OP.mult), inc=s_dve)
            b.rec("DVE", lambda e: e.tensor_tensor(c2, m8[:, 1:2], sB, OP.mult), inc=s_dve)
            b.wait("DVE", s_dve)
            t8 = AF32[:, SM_o + 268:SM_o + 276]
            b.rec("DVE", lambda e, tt=tt: e.tensor_scalar(
                t8, probs_all[:, tt * 8:(tt + 1) * 8], m8[:, 0:1], c1, OP.is_equal, OP.mult), inc=s_dve)
            b.rec("DVE", lambda e, tt=tt: e.tensor_scalar(
                cw8_all[:, tt * 8:(tt + 1) * 8], probs_all[:, tt * 8:(tt + 1) * 8],
                m8[:, 1:2], c2, OP.is_equal, OP.mult), inc=s_dve)
            b.wait("DVE", s_dve)
            b.rec("DVE", lambda e, tt=tt: e.tensor_tensor(
                cw8_all[:, tt * 8:(tt + 1) * 8], cw8_all[:, tt * 8:(tt + 1) * 8], t8, OP.add), inc=s_dve)
            cw_t = tok(s_dve)
            b.wait_many("PE", [cw_t, const_t])
            b.rec("PE", lambda e, tt=tt: e.transpose(
                psC[0:8, 512:640], cw8_all[:, tt * 8:(tt + 1) * 8], eye_fc), inc=s_pe)
            b.wait("DVE", s_pe)
            b.rec("DVE", lambda e, tt=tt: e.tensor_copy(
                cwF[:, tt * 128:(tt + 1) * 128], psC[0:8, 512:640]), inc=s_dve)
        cwf_t = tok(s_dve)

        ev2 = {}
        g2 = 0
        for ex in range(8):
            b.wait_many("PE", [cwf_t, const_t])
            b.wait("PE", s_dve)
            for n in range(4):
                b.wait("PE", s_dve)
                b.rec("PE", lambda e, ex=ex, n=n: e.matmul(
                    psD[:, 0:512], selc[0:8, ex * 128:(ex + 1) * 128],
                    cwF[:, n * 512:(n + 1) * 512], start=True, stop=True), inc=s_pe)
                b.wait("DVE", s_pe)
                b.wait("DVE", s_pool)
                b.rec("DVE", lambda e, n=n: e.tensor_copy(
                    B_cw[:, n * 512:(n + 1) * 512], psD[:, 0:512]), inc=s_dve)
            bcw_t = tok(s_dve)
            for p in range(4):
                wv, wv_t, wv_par = ws.load(di["w1"][:, ex * D + p * 256: ex * D + (p + 1) * 256], KT)
                b.wait_many("PE", [wv_t, sh_t])
                for m2 in range(2):
                    m = p * 2 + m2
                    for n in range(4):
                        ps = PSMAIN[g2 % 3]
                        prev = ev2.get(g2 - 3)
                        if prev is not None:
                            b.wait("PE", *prev)
                        for k in range(KT):
                            b.rec("PE", lambda e, k=k, m2=m2, n=n, ps=ps, wv=wv: e.matmul(
                                ps[:, 0:512], wv(k, m2 * 128, 128), xb2(k)[:, n * 512:(n + 1) * 512],
                                start=(k == 0), stop=(k == KT - 1)), inc=(s_pe if k == KT - 1 else None))
                        b.wait("ACT", s_pe)
                        b.rec("ACT", lambda e, m=m, n=n, ps=ps: e.activation(
                            out=rbuf(m)[:, n * 512:(n + 1) * 512], in_=ps[:, 0:512],
                            func=AF.Relu), inc=s_act)
                        ev2[g2] = tok(s_act)
                        g2 += 1
                ws.consumed(wv_par, tok(s_act))
            relu_t = tok(s_act)
            b.wait_many("DVE", [relu_t])
            for k in range(KT):
                b.rec("DVE", lambda e, k=k: e.tensor_tensor(rbuf(k), rbuf(k), rbuf(k), OP.mult), inc=s_dve)
            sq_t = tok(s_dve)
            b.wait_many("DVE", [sq_t, bcw_t])
            for k in range(KT):
                b.rec("DVE", lambda e, k=k: e.tensor_tensor(rbuf(k), rbuf(k), B_cw, OP.mult), inc=s_dve)
            a_t = tok(s_dve)
            for p in range(4):
                wv, wv_t, wv_par = ws.load(di["w2"][ex * D:(ex + 1) * D, p * 256:(p + 1) * 256], KT)
                b.wait_many("PE", [wv_t, a_t])
                for m2 in range(2):
                    m = p * 2 + m2
                    for n in range(4):
                        ps = PSMAIN[g2 % 3]
                        prev = ev2.get(g2 - 3)
                        if prev is not None:
                            b.wait("PE", *prev)
                        for k in range(KT):
                            b.rec("PE", lambda e, k=k, m2=m2, n=n, ps=ps, wv=wv: e.matmul(
                                ps[:, 0:512], wv(k, m2 * 128, 128), rbuf(k)[:, n * 512:(n + 1) * 512],
                                start=(k == 0), stop=(k == KT - 1)), inc=(s_pe if k == KT - 1 else None))
                        b.wait("DVE", s_pe)
                        b.rec("DVE", lambda e, m=m, n=n, ps=ps: e.tensor_tensor(
                            xs(m)[:, n * 512:(n + 1) * 512], xs(m)[:, n * 512:(n + 1) * 512],
                            ps[:, 0:512], OP.add), inc=s_dve)
                        ev2[g2] = tok(s_dve)
                        g2 += 1
                ws.consumed(wv_par, tok(s_dve))
        moe_t = tok(s_dve)
        tap("res2", AF32[:, 0:16384], [moe_t])
        st2_t = post_ln(moe_t)
        tap("x2", AF32[:, 0:16384], [st2_t])
        last_t = st2_t

    if n_stages >= 3:
        DTv = AF32[0:64, 16384:16384 + T]
        BMv = AF32[0:64, 18432:18432 + T]
        CMv = AF32[0:64, 20480:20480 + T]
        SSv = AF32[0:64, 22528:22528 + T]
        XM_o = 0
        def xmt(k):
            return ABF[:, XM_o + k * T:XM_o + (k + 1) * T]
        HCH_o = 32768
        def hch(k):
            return ABF[:, HCH_o + k * 512:HCH_o + (k + 1) * 512]
        SGD_o = 36864
        DTW_o = 32768
        S2IW_o = 43008
        YN_o = 40960
        SGI_o = 32768
        halo_pre = kcb[:, 130:162]
        halo_save = kcb[:, 162:194]
        masksum = kc[:, 320:321]
        halo_x = kc[:, 328:344]
        halo_sq = kc[:, 344:360]
        halo_bc = kc[:, 360:362]
        halo_h = kcb[:, 194:210]

        b.wait_many("SP", [st2_t])
        dma("SP", s_sp, cc_srcH,
            AF32[:, 0:16384].rearrange("p (k t) -> p k t", k=KT)[:, :, T - 2:T])
        b.wait("POOL", s_sp)
        b.rec("POOL", lambda e: e.collective_compute(
            "AllGather", OP.bypass, replica_groups=[list(range(8))],
            ins=[cc_srcH[:].opt()], outs=[cc_dstH[:].opt()]), inc=s_cc)
        cc_h = tok(s_cc)
        b.wait_many("DVE", [const_t])
        b.rec("DVE", lambda e: e.tensor_reduce(masksum, maskbc, AX.X, OP.add), inc=s_dve)
        b.wait("POOL", *cc_h)
        b.wait("POOL", s_dve)
        gslH = ABF[:, SGD_o:SGD_o + 128]
        gh_f = AF32[:, 24448:24448 + 128]
        dma("POOL", s_gsl, gh_f.rearrange("p (r c) -> p r c", r=8),
            cc_dstH.rearrange("(r p) c -> p r c", p=128))
        ghf_t = tok(s_gsl)
        b.wait_many("DVE", [ghf_t])
        b.rec("DVE", lambda e: e.tensor_tensor(
            gh_f.rearrange("p (r c) -> p r c", r=8),
            gh_f.rearrange("p (r c) -> p r c", r=8),
            maskbc.rearrange("p (r o) -> p r o", o=1).broadcast_to((128, 8, 16)), OP.mult), inc=s_dve)
        b.wait("DVE", s_dve)
        b.rec("DVE", lambda e: e.tensor_reduce(
            halo_x, gh_f.rearrange("p (r c) -> p c r", r=8), AX.X, OP.add), inc=s_dve)
        halo_t = tok(s_dve)
        b.wait_many("ACT", [halo_t])
        b.rec("ACT", lambda e: e.activation(out=halo_sq, in_=halo_x, func=AF.Square), inc=s_act)
        b.wait("PE", s_act)
        b.wait("PE", s_dve)
        b.rec("PE", lambda e: e.matmul(psD[0:1, 0:16], ones_col, halo_sq, start=True, stop=True), inc=s_pe)
        b.wait("DVE", s_pe)
        b.rec("DVE", lambda e: e.tensor_reduce(
            AF32[0:1, 24400:24402], psD[0:1, 0:16].rearrange("p (k c) -> p c k", k=KT),
            AX.X, OP.add), inc=s_dve)
        b.wait("ACT", s_dve)
        b.rec("ACT", lambda e: e.activation(out=AF32[0:1, 24402:24404], in_=AF32[0:1, 24400:24402],
                                            func=AF.Ln, bias=1e-6, scale=1.0 / D), inc=s_act)
        b.wait("ACT", s_act)
        b.rec("ACT", lambda e: e.activation(out=AF32[0:1, 24402:24404], in_=AF32[0:1, 24402:24404],
                                            func=AF.Exp, scale=-0.5), inc=s_act)
        b.wait("PE", s_act)
        b.rec("PE", lambda e: e.matmul(psD[0:128, 16:18], ones_row, AF32[0:1, 24402:24404],
                                       start=True, stop=True), inc=s_pe)
        b.wait("DVE", s_pe)
        b.rec("DVE", lambda e: e.tensor_copy(AF32[:, 24404:24406], psD[0:128, 16:18]), inc=s_dve)
        b.wait("DVE", s_dve)
        b.rec("DVE", lambda e: e.tensor_tensor(
            halo_x.rearrange("p (k c) -> p k c", k=KT),
            halo_x.rearrange("p (k c) -> p k c", k=KT),
            AF32[:, 24404:24406].rearrange("p (o c) -> p o c", o=1).broadcast_to((128, 8, 2)), OP.mult), inc=s_dve)
        b.wait("DVE", s_dve)
        b.rec("DVE", lambda e: e.tensor_tensor(
            halo_h.rearrange("p (k c) -> p k c", k=KT),
            halo_x.rearrange("p (k c) -> p k c", k=KT),
            n1wc.rearrange("p (k o) -> p k o", o=1).broadcast_to((128, 8, 2)), OP.mult), inc=s_dve)
        haloh_t = tok(s_dve)

        b.wait("SP", s_act)
        b.wait("SP", s_dve)
        dma("SP", s_sp, AF32[:, 24512:24544], di["ipb"])
        ipb_t = tok(s_sp)
        ipbc = AF32[:, 24512:24544]
        r_rstd3 = AF32[0:1, 19456:19456 + T]
        rms1_t = ln_full(xs, KT, D, 1e-6, False, (r_rstd3, None), wait_toks=[st2_t])

        B1c3 = AF32[:, 18432:18432 + 512]
        FT3 = AF32[:, 18432 + 512:18432 + 1024]
        ga = 0
        ev3 = {}
        sg_par = 0
        b.wait_many("ACT", [ipb_t])
        for n in range(4):
            n0 = n * 512
            b.wait_many("PE", [rms1_t])
            bcast_chunk(r_rstd3, n0, 512, B1c3)
            bc3_t = tok(s_dve)
            b.wait_many("DVE", [bc3_t])
            b.wait("DVE", s_pe)
            for k in range(KT):
                b.rec("DVE", lambda e, k=k, n0=n0: e.tensor_tensor(
                    FT3, xs(k)[:, n0:n0 + 512], B1c3, OP.mult), inc=s_dve)
                b.wait("DVE", s_dve)
                b.rec("DVE", lambda e, k=k: e.tensor_scalar(
                    hch(k), FT3, n1wc[:, k:k + 1], None, OP.mult), inc=s_dve)
                b.wait("DVE", s_dve)
            hc_t = tok(s_dve)
            for p in range(16):
                wv, wv_t, wv_par = ws.load(di["ipw"][:, p * 256:(p + 1) * 256], KT)
                b.wait_many("PE", [wv_t, hc_t])
                if n == 0 and p < 8:
                    b.wait_many("PE", [haloh_t, const_t])
                for m2 in range(2):
                    m = p * 2 + m2
                    ps = PSMAIN[ga % 3]
                    prev = ev3.get(ga - 3)
                    if prev is not None:
                        b.wait("PE", *prev)
                    for k in range(KT):
                        b.rec("PE", lambda e, k=k, m2=m2, ps=ps, wv=wv: e.matmul(
                            ps[:, 0:512], wv(k, m2 * 128, 128), hch(k),
                            start=(k == 0), stop=(k == KT - 1)), inc=(s_pe if k == KT - 1 else None))
                    if m < 16:
                        b.wait("ACT", s_pe)
                        b.rec("ACT", lambda e, m=m, n0=n0, ps=ps: e.activation(
                            out=xmt(m)[:, n0:n0 + 512], in_=ps[:, 0:512],
                            func=AF.Identity, bias=ipbc[:, m:m + 1]), inc=s_act)
                        ev3[ga] = tok(s_act)
                    else:
                        spar = sg_par % 2
                        sgd = ABF[:, SGD_o + spar * 512:SGD_o + spar * 512 + 512]
                        b.wait("ACT", s_pe)
                        b.wait("ACT", s_sg[spar], 16 * (sg_par // 2))
                        b.rec("ACT", lambda e, ps=ps, sgd=sgd, m=m: e.activation(
                            out=sgd, in_=ps[:, 0:512], func=AF.Sigmoid,
                            bias=ipbc[:, m:m + 1]), inc=s_act)
                        ev3[ga] = tok(s_act)
                        b.wait_many("ACT", [ev3[ga]])
                        dma("ACT", s_sg[spar], scr_sg[(m - 16) * 128:(m - 16) * 128 + 128, n0:n0 + 512], sgd)
                        sg_par += 1
                    ga += 1
                if n == 0 and p < 8:
                    for m2 in range(2):
                        m = p * 2 + m2
                        b.wait("PE", s_dve)
                        for k in range(KT):
                            b.rec("PE", lambda e, k=k, m2=m2, wv=wv: e.matmul(
                                psD[0:128, 20:22], wv(k, m2 * 128, 128),
                                halo_h.rearrange("p (k c) -> p k c", k=KT)[:, k, :],
                                start=(k == 0), stop=(k == KT - 1)), inc=(s_pe if k == KT - 1 else None))
                        b.wait("DVE", s_pe)
                        b.wait_many("DVE", [ipb_t])
                        b.rec("DVE", lambda e, m=m: e.tensor_scalar(
                            halo_pre[:, m * 2:(m + 1) * 2], psD[0:128, 20:22],
                            ipbc[:, m:m + 1], masksum, OP.add, OP.mult), inc=s_dve)
                if n == 0 and p < 8:
                    ws.consumed(wv_par, tok(s_dve))
                else:
                    ws.consumed(wv_par, tok(s_act))
        inp_t = tok(s_act)
        inp_td = tok(s_dve)

        b.wait("POOL", s_pe)
        dma("POOL", s_ini, ABF[:, DTW_o:DTW_o + 1024].rearrange("p (k c) -> p k c", k=16),
            di["dtw"].rearrange("(k p) c -> p k c", p=128))
        dma("POOL", s_ini, ABF[:, DTW_o + 1024:DTW_o + 2048].rearrange("p (k c) -> p k c", k=16),
            di["bpw"].rearrange("(k p) c -> p k c", p=128))
        dma("POOL", s_ini, ABF[:, DTW_o + 2048:DTW_o + 3072].rearrange("p (k c) -> p k c", k=16),
            di["cpw"].rearrange("(k p) c -> p k c", p=128))
        b.wait("SP", s_act)
        b.wait("SP", s_pe)
        dma("SP", s_sp, AF32[:, 24528:24528 + 48], di["convw"])
        dma("SP", s_sp, AF32[:, 24512:24512 + 16], di["convb"])
        dma("SP", s_sp, AF32[0:64, 24448:24449], di["dtb"])
        dma("SP", s_sp, AF32[0:64, 24449:24450], di["bpb"])
        dma("SP", s_sp, AF32[0:64, 24450:24451], di["cpb"])
        w3_t = tok(s_ini); w3f_t = tok(s_sp)
        convw_c = AF32[:, 24528:24528 + 48]
        convb_c = AF32[:, 24512:24512 + 16]
        CS1 = ABF[:, SGD_o + 1024:SGD_o + 1536]
        CS2 = ABF[:, SGD_o + 1536:SGD_o + 2048]
        silu_hist = []
        for c in range(8):
            t0 = c * 256
            for kk in range(16):
                xk = xmt(kk)
                w0 = convw_c[:, kk * 3:kk * 3 + 1]
                w1_ = convw_c[:, kk * 3 + 1:kk * 3 + 2]
                w2_ = convw_c[:, kk * 3 + 2:kk * 3 + 3]
                csA = (CS1 if kk % 2 == 0 else CS2)[:, 0:256]
                csB = (CS1 if kk % 2 == 0 else CS2)[:, 256:512]
                b.wait_many("DVE", [inp_t, inp_td, w3f_t])
                if len(silu_hist) >= 2:
                    b.wait("DVE", s_act, silu_hist[-2])
                hsrc = halo_pre if c == 0 else halo_save
                b.rec("DVE", lambda e, hsrc=hsrc, w0=w0, csA=csA, kk=kk: e.tensor_scalar(
                    csA[:, 0:2], hsrc[:, kk * 2:(kk + 1) * 2], w0, None, OP.mult), inc=s_dve)
                b.rec("DVE", lambda e, xk=xk, w0=w0, csA=csA, t0=t0: e.tensor_scalar(
                    csA[:, 2:256], xk[:, t0:t0 + 254], w0, None, OP.mult), inc=s_dve)
                b.rec("DVE", lambda e, hsrc=hsrc, w1_=w1_, csB=csB, kk=kk: e.tensor_scalar(
                    csB[:, 0:1], hsrc[:, kk * 2 + 1:kk * 2 + 2], w1_, None, OP.mult), inc=s_dve)
                b.rec("DVE", lambda e, xk=xk, w1_=w1_, csB=csB, t0=t0: e.tensor_scalar(
                    csB[:, 1:256], xk[:, t0:t0 + 255], w1_, None, OP.mult), inc=s_dve)
                b.wait("DVE", s_dve)
                b.rec("DVE", lambda e, csA=csA, csB=csB: e.tensor_tensor(
                    csA[:, 0:256], csA[:, 0:256], csB[:, 0:256], OP.add), inc=s_dve)
                b.wait("DVE", s_dve)
                b.rec("DVE", lambda e, xk=xk, w2_=w2_, csB=csB, t0=t0: e.tensor_scalar(
                    csB[:, 0:256], xk[:, t0:t0 + 256], w2_, None, OP.mult), inc=s_dve)
                b.wait("DVE", s_dve)
                b.rec("DVE", lambda e, csA=csA, csB=csB: e.tensor_tensor(
                    csA[:, 0:256], csA[:, 0:256], csB[:, 0:256], OP.add), inc=s_dve)
                pool_t = tok(s_dve)
                b.rec("DVE", lambda e, xk=xk, t0=t0, kk=kk: e.tensor_copy(
                    halo_save[:, kk * 2:(kk + 1) * 2], xk[:, t0 + 254:t0 + 256]), inc=s_dve)
                hs_t = tok(s_dve)
                b.wait_many("ACT", [pool_t, hs_t])
                b.rec("ACT", lambda e, xk=xk, csA=csA, t0=t0, kk=kk: e.activation(
                    out=xk[:, t0:t0 + 256], in_=csA[:, 0:256], func=AF.Silu,
                    bias=convb_c[:, kk:kk + 1]), inc=s_act)
                silu_hist.append(s_act.n)
            conv_t = tok(s_act)
            b.wait_many("PE", [conv_t, w3_t])
            b.wait("PE", s_dve)
            for kk in range(16):
                b.rec("PE", lambda e, kk=kk, t0=t0: e.matmul(
                    psC[0:64, 0:256], ABF[:, DTW_o + kk * 64:DTW_o + (kk + 1) * 64],
                    xmt(kk)[:, t0:t0 + 256], start=(kk == 0), stop=(kk == 15)))
                b.rec("PE", lambda e, kk=kk, t0=t0: e.matmul(
                    psC[0:64, 512:768], ABF[:, DTW_o + 1024 + kk * 64:DTW_o + 1024 + (kk + 1) * 64],
                    xmt(kk)[:, t0:t0 + 256], start=(kk == 0), stop=(kk == 15)))
                b.rec("PE", lambda e, kk=kk, t0=t0: e.matmul(
                    psD[0:64, 0:256], ABF[:, DTW_o + 2048 + kk * 64:DTW_o + 2048 + (kk + 1) * 64],
                    xmt(kk)[:, t0:t0 + 256], start=(kk == 0), stop=(kk == 15)),
                    inc=(s_pe if kk == 15 else None))
            b.wait("ACT", s_pe)
            b.rec("ACT", lambda e, t0=t0: e.activation(
                out=DTv[:, t0:t0 + 256], in_=psC[0:64, 0:256], func=AF.Sigmoid,
                bias=AF32[0:64, 24448:24449]), inc=s_act)
            b.rec("ACT", lambda e, t0=t0: e.activation(
                out=BMv[:, t0:t0 + 256], in_=psC[0:64, 512:768], func=AF.Identity,
                bias=AF32[0:64, 24449:24450]), inc=s_act)
            b.rec("ACT", lambda e, t0=t0: e.activation(
                out=CMv[:, t0:t0 + 256], in_=psD[0:64, 0:256], func=AF.Identity,
                bias=AF32[0:64, 24450:24451]), inc=s_act)
            b.wait("DVE", s_act)
            b.wait("PE", s_act)
        proj_t = tok(s_act)

        b.wait_many("DVE", [proj_t])
        b.rec("DVE", lambda e: e.tensor_tensor(BMv, DTv, BMv, OP.mult), inc=s_dve)
        b.wait("DVE", s_dve)
        b.rec("DVE", lambda e: e.tensor_scalar(DTv, DTv, -1.0, 1.0, OP.mult, OP.add), inc=s_dve)
        b.wait("DVE", s_dve)
        b.rec("DVE", lambda e: e.tensor_tensor_scan(SSv, DTv, BMv, 0.0, OP.mult, OP.add), inc=s_dve)
        b.wait("DVE", s_dve)
        b.rec("DVE", lambda e: e.tensor_tensor_scan(
            BMv, DTv, kc[0:64, 176:177].broadcast_to((64, T)), 1.0, OP.mult, OP.mult), inc=s_dve)
        scan_t = tok(s_dve)
        b.wait_many("SP", [scan_t])
        dma("SP", s_sp, cc_srcS, SSv[:, T - 1:T])
        b.wait("POOL", s_sp)
        b.rec("POOL", lambda e: e.collective_compute(
            "AllGather", OP.bypass, replica_groups=[list(range(8))],
            ins=[cc_srcS[:].opt()], outs=[cc_dstS[:].opt()]), inc=s_cc)
        b.wait("POOL", s_cc)
        b.wait("POOL", s_dve)
        dma("POOL", s_gsl, AF32[0:64, 16384:16392].rearrange("p (r o) -> p r o", r=8),
            cc_dstS.rearrange("(r p) o -> p r o", p=64), allow_slow_non_contiguous=True)
        gs_t = tok(s_gsl)
        b.wait_many("DVE", [gs_t])
        b.rec("DVE", lambda e: e.tensor_tensor(
            AF32[0:64, 16384:16392], AF32[0:64, 16384:16392],
            maskbc[0:64, :], OP.mult), inc=s_dve)
        b.wait("DVE", s_dve)
        b.rec("DVE", lambda e: e.tensor_reduce(
            AF32[0:64, 16392:16393], AF32[0:64, 16384:16392], AX.X, OP.add), inc=s_dve)
        b.wait("DVE", s_dve)
        b.rec("DVE", lambda e: e.scalar_tensor_tensor(
            SSv, BMv, AF32[0:64, 16392:16393], SSv, OP.mult, OP.add), inc=s_dve)
        b.wait("DVE", s_dve)
        b.rec("DVE", lambda e: e.tensor_tensor(CMv, CMv, SSv, OP.mult), inc=s_dve)
        y_t = tok(s_dve)

        r64a = AF32[0:1, 16384:16384 + T]
        r64b = AF32[0:1, 18432:18432 + T]
        sq64 = AF32[0:64, 22528:22528 + 512]
        nn64 = 4
        for n in range(nn64):
            n0 = n * 512
            b.wait_many("ACT", [y_t])
            b.wait("ACT", s_pe)
            b.rec("ACT", lambda e, n0=n0: e.activation(out=sq64, in_=CMv[:, n0:n0 + 512],
                                                       func=AF.Square), inc=s_act)
            b.wait("PE", s_act)
            b.wait_many("PE", [y_t])
            b.wait("PE", s_dve)
            b.rec("PE", lambda e, n0=n0: e.matmul(psD[0:1, 0:512], ones_col[0:64, :],
                                                  CMv[:, n0:n0 + 512], start=True, stop=True))
            b.rec("PE", lambda e, n0=n0: e.matmul(psC[0:1, 0:512], ones_col[0:64, :],
                                                  sq64, start=True, stop=True), inc=s_pe)
            b.wait("DVE", s_pe)
            b.rec("DVE", lambda e, n0=n0: e.tensor_scalar(
                r64b[:, n0:n0 + 512], psD[0:1, 0:512], 1.0 / 64, None, OP.mult), inc=s_dve)
            b.wait("DVE", s_dve)
            b.rec("DVE", lambda e, n0=n0: e.tensor_tensor(
                r64a[:, n0:n0 + 512], r64b[:, n0:n0 + 512], r64b[:, n0:n0 + 512], OP.mult), inc=s_dve)
            b.wait("DVE", s_dve)
            b.rec("DVE", lambda e, n0=n0: e.scalar_tensor_tensor(
                r64a[:, n0:n0 + 512], psC[0:1, 0:512], 1.0 / 64, r64a[:, n0:n0 + 512],
                OP.mult, OP.subtract), inc=s_dve)
        b.wait("ACT", s_dve)
        b.rec("ACT", lambda e: e.activation(out=r64a, in_=r64a, func=AF.Ln, bias=1e-5), inc=s_act)
        b.wait("ACT", s_act)
        b.rec("ACT", lambda e: e.activation(out=r64a, in_=r64a, func=AF.Exp, scale=-0.5), inc=s_act)
        b.wait("DVE", s_act)
        b.rec("DVE", lambda e: e.scalar_tensor_tensor(r64b, r64b, -1.0, r64a, OP.mult, OP.mult), inc=s_dve)
        ln64_t = tok(s_dve)
        ynv = ABF[0:64, YN_o:YN_o + T]
        for n in range(4):
            n0 = n * 512
            b.wait_many("PE", [ln64_t])
            b.wait("PE", s_dve)
            b.wait("PE", s_act)
            b.rec("PE", lambda e, n0=n0: e.matmul(psD[0:64, 0:512], ones_row[:, 0:64], r64a[:, n0:n0 + 512],
                                                  start=True, stop=True))
            b.rec("PE", lambda e, n0=n0: e.matmul(psC[0:64, 0:512], ones_row[:, 0:64], r64b[:, n0:n0 + 512],
                                                  start=True, stop=True), inc=s_pe)
            b.wait("DVE", s_pe)
            b.rec("DVE", lambda e, n0=n0: e.tensor_tensor(
                SSv[:, 0:512], CMv[:, n0:n0 + 512], psD[0:64, 0:512], OP.mult), inc=s_dve)
            b.wait("DVE", s_dve)
            b.rec("DVE", lambda e, n0=n0: e.tensor_tensor(
                ynv[:, n0:n0 + 512], SSv[:, 0:512], psC[0:64, 0:512], OP.add), inc=s_dve)
            b.wait("DVE", s_dve)
        yn_t = tok(s_dve)

        b.wait("POOL", s_pe)
        dma("POOL", s_ini, ABF[0:64, S2IW_o:S2IW_o + 2048], di["s2iw"])
        s2i_t = tok(s_ini)
        b.wait("SP", s_pool)
        b.wait("SP", s_act)
        b.wait("SP", s_pe)
        b.wait("SP", s_dve)
        dma("SP", s_sp, AF32[:, 19968:19968 + 16], di["s2ib"])
        dma("SP", s_sp, AF32[:, 19984:19984 + 16], di["Dpar"])
        dma("SP", s_sp, AF32[:, 20000:20000 + 8], di["outb"])
        s2ib_c = AF32[:, 19968:19968 + 16]
        dpar_c = AF32[:, 19984:19984 + 16]
        outb_c = AF32[:, 20000:20000 + 8]
        w4f_t = tok(s_sp)
        scr_h2 = nc.dram_tensor("scr_h2", [D, T], BF16).ap()
        SGI_o3 = 32768
        def y3v(m):
            return ABF[:, SGI_o3 + m * 512:SGI_o3 + (m + 1) * 512]
        Achunk = AF32[:, 20480:20480 + KT * 512]
        def ach(k):
            return AF32[:, 20480 + k * 512:20480 + (k + 1) * 512]
        sq3 = AF32[:, 18432:18432 + 1024]
        r_rstd2 = AF32[0:1, 16384:16384 + T]
        B1rm2 = AF32[:, 19456:19456 + 512]
        ga4 = 0
        ev4 = {}
        for n in range(4):
            n0 = n * 512
            b.wait("POOL", s_pe)
            b.wait("POOL", s_sc)
            b.wait("POOL", s_sg[0])
            b.wait("POOL", s_sg[1])
            dma("POOL", s_gd, ABF[:, SGI_o3:SGI_o3 + 8192].rearrange("p (k t) -> p k t", k=16),
                scr_sg.rearrange("(k p) t -> p k t", p=128)[:, :, n0:n0 + 512])
            sg_t = tok(s_gd)
            for m in range(16):
                ps = PSMAIN[ga4 % 3]
                pstb = AF32[:, 20480 + (m % 2) * 512:20480 + (m % 2) * 512 + 512]
                prev = ev4.get(ga4 - 3)
                b.wait_many("PE", [yn_t, s2i_t])
                if prev is not None:
                    b.wait("PE", *prev)
                b.rec("PE", lambda e, m=m, n0=n0, ps=ps: e.matmul(
                    ps[:, 0:512], ABF[0:64, S2IW_o + m * 128:S2IW_o + (m + 1) * 128],
                    ynv[:, n0:n0 + 512], start=True, stop=True), inc=s_pe)
                b.wait("ACT", s_pe)
                b.wait_many("ACT", [w4f_t])
                if ga4 >= 2:
                    b.wait("ACT", s_dve, ev4[ga4 - 2][1])
                b.rec("ACT", lambda e, m=m, ps=ps, pstb=pstb: e.activation(
                    out=pstb, in_=ps[:, 0:512], func=AF.Identity,
                    bias=s2ib_c[:, m:m + 1]), inc=s_act)
                act_t = tok(s_act)
                b.wait_many("DVE", [act_t, inp_t, sg_t])
                b.rec("DVE", lambda e, m=m, n0=n0, pstb=pstb: e.scalar_tensor_tensor(
                    pstb, xmt(m)[:, n0:n0 + 512], dpar_c[:, m:m + 1], pstb,
                    OP.mult, OP.add), inc=s_dve)
                b.wait("DVE", s_dve)
                b.rec("DVE", lambda e, m=m, pstb=pstb: e.tensor_tensor(
                    y3v(m), y3v(m), pstb, OP.mult), inc=s_dve)
                ev4[ga4] = tok(s_dve)
                ga4 += 1
            y3_t = tok(s_dve)
            for p in range(4):
                wv, wv_t, wv_par = ws.load(di["outw"][:, p * 256:(p + 1) * 256], 16)
                b.wait_many("PE", [wv_t, y3_t])
                for m2 in range(2):
                    m = p * 2 + m2
                    ps = PSMAIN[ga4 % 3]
                    prev = ev4.get(ga4 - 3)
                    if prev is not None:
                        b.wait("PE", *prev)
                    for k in range(16):
                        b.rec("PE", lambda e, k=k, m2=m2, ps=ps, wv=wv: e.matmul(
                            ps[:, 0:512], wv(k, m2 * 128, 128), y3v(k),
                            start=(k == 0), stop=(k == 15)), inc=(s_pe if k == 15 else None))
                    b.wait("DVE", s_pe)
                    b.rec("DVE", lambda e, m=m, n0=n0, ps=ps: e.scalar_tensor_tensor(
                        ach(m), ps[:, 0:512], outb_c[:, m:m + 1], xs(m)[:, n0:n0 + 512],
                        OP.add, OP.add), inc=s_dve)
                    ev4[ga4] = tok(s_dve)
                    ga4 += 1
                ws.consumed(wv_par, tok(s_dve))
            a_t = tok(s_dve)
            b.wait_many("DVE", [a_t])
            for m in range(KT):
                b.rec("DVE", lambda e, m=m, n0=n0: e.tensor_tensor(
                    xs(m)[:, n0:n0 + 512], xs(m)[:, n0:n0 + 512], ach(m), OP.add), inc=s_dve)
            b.wait("PE", s_dve)
            b.wait("PE", s_act)
            pe_h = []
            for k in range(KT):
                sl = sq3[:, (k % 2) * 512:(k % 2) * 512 + 512]
                if len(pe_h) >= 2:
                    b.wait("ACT", s_pe, pe_h[-2])
                b.wait_many("ACT", [a_t])
                b.rec("ACT", lambda e, k=k, sl=sl: e.activation(
                    out=sl, in_=ach(k), func=AF.Square), inc=s_act)
                b.wait("PE", s_act)
                b.rec("PE", lambda e, k=k, sl=sl: e.matmul(
                    psC[0:1, 0:512], ones_col, sl, start=(k == 0), stop=(k == KT - 1)), inc=s_pe)
                pe_h.append(s_pe.n)
            b.wait("DVE", s_pe)
            b.rec("DVE", lambda e, n0=n0: e.tensor_scalar(
                r_rstd2[:, n0:n0 + 512], psC[0:1, 0:512], 1.0 / D, None, OP.mult), inc=s_dve)
            b.wait("ACT", s_dve)
            b.rec("ACT", lambda e, n0=n0: e.activation(
                out=r_rstd2[:, n0:n0 + 512], in_=r_rstd2[:, n0:n0 + 512], func=AF.Ln, bias=1e-6), inc=s_act)
            b.wait("ACT", s_act)
            b.rec("ACT", lambda e, n0=n0: e.activation(
                out=r_rstd2[:, n0:n0 + 512], in_=r_rstd2[:, n0:n0 + 512], func=AF.Exp, scale=-0.5), inc=s_act)
            rr2_t = tok(s_act)
            b.wait_many("PE", [rr2_t])
            b.wait("PE", s_dve)
            b.rec("PE", lambda e, n0=n0: e.matmul(psD[:, 0:512], ones_row,
                                                  r_rstd2[:, n0:n0 + 512], start=True, stop=True), inc=s_pe)
            b.wait("DVE", s_pe)
            b.rec("DVE", lambda e: e.tensor_copy(B1rm2, psD[:, 0:512]), inc=s_dve)
            b.wait("DVE", s_dve)
            for k in range(KT):
                b.rec("DVE", lambda e, k=k: e.tensor_tensor(sq3[:, 0:512], ach(k), B1rm2, OP.mult), inc=s_dve)
                b.wait("DVE", s_dve)
                b.rec("DVE", lambda e, k=k: e.tensor_scalar(
                    y3v(k), sq3[:, 0:512], n2wc[:, k:k + 1], None, OP.mult), inc=s_dve)
                b.wait("DVE", s_dve)
            h2_t = tok(s_dve)
            b.wait_many("ACT", [h2_t])
            dma("ACT", s_sc, scr_h2.rearrange("(k p) t -> p k t", p=128)[:, :, n0:n0 + 512],
                ABF[:, SGI_o3:SGI_o3 + 4096].rearrange("p (k t) -> p k t", k=KT))
        mix_t = tok(s_dve)
        tap("xmix", AF32[:, 0:16384], [mix_t])

        def h2v(k):
            return ABF[:, k * T:(k + 1) * T]
        def ghv(k):
            return ABF[:, 16384 + k * T:16384 + (k + 1) * T]
        cw4Fb = ABF[0:4, 32768:32768 + T]
        eb2ab = ABF[0:4, 34816:34816 + 1024]
        gatewb = ABF[:, 35840:35840 + 32]
        B_cw3 = AF32[:, 18432:18432 + T]
        glog = AF32[:, 20480:20480 + 128]
        cw4_all = AF32[:, 20608:20608 + 64]
        gm8 = AF32[:, 20672:20672 + 8]
        gsm = AF32[:, 20680:20680 + 8]
        cwF4 = AF32[0:4, 22528:22528 + T]
        eb1c = AF32[:, 20704:20704 + 32]

        b.wait_many("SP", [mix_t])
        b.wait("POOL", s_dve)
        b.wait("POOL", s_sc)
        dma("POOL", s_gd, ABF[:, 0:16384].rearrange("p (k t) -> p k t", k=KT),
            scr_h2.rearrange("(k p) t -> p k t", p=128))
        dma("POOL", s_ini, gatewb.rearrange("p (k c) -> p k c", k=KT),
            di["gatew"].rearrange("(k p) c -> p k c", p=128))
        dma("POOL", s_ini, eb2ab, di["eb2all"])
        dma("SP", s_sp, AF32[:, 20696:20700], di["gateb_bc"])
        h2_tt = tok(s_gd); gw_t = tok(s_ini); gb_t = tok(s_sp)

        b.wait_many("DVE", [mix_t])
        b.rec("DVE", lambda e: e.memset(glog, -1e30), inc=s_dve)
        gpad_t = tok(s_dve)
        grow_t = gb_t
        b.wait_many("PE", [h2_tt, gw_t, gpad_t])
        for tt in range(16):
            b.wait("PE", s_dve)
            for k in range(KT):
                b.rec("PE", lambda e, tt=tt, k=k: e.matmul(
                    psC[0:128, 0:4], h2v(k)[:, tt * 128:(tt + 1) * 128],
                    gatewb.rearrange("p (k c) -> p k c", k=KT)[:, k, :],
                    start=(k == 0), stop=(k == KT - 1)), inc=(s_pe if k == KT - 1 else None))
            b.wait("DVE", s_pe)
            b.wait_many("DVE", [grow_t])
            b.rec("DVE", lambda e, tt=tt: e.tensor_tensor(
                glog[:, tt * 8:tt * 8 + 4], psC[0:128, 0:4],
                AF32[:, 20696:20700], OP.add), inc=s_dve)
            b.wait("DVE", s_dve)
            b.rec("DVE", lambda e, tt=tt: e.max(out=gm8, in_=glog[:, tt * 8:(tt + 1) * 8]), inc=s_dve)
            b.wait("DVE", s_dve)
            b.rec("DVE", lambda e: e.tensor_scalar(gsm[:, 0:1], gm8[:, 0:1], -1.0, None, OP.mult), inc=s_dve)
            b.wait("ACT", s_dve)
            b.rec("ACT", lambda e: e.activation(out=gsm[:, 1:2], in_=gm8[:, 1:2], func=AF.Exp,
                                                bias=gsm[:, 0:1]), inc=s_act)
            b.wait("DVE", s_act)
            b.rec("DVE", lambda e: e.tensor_scalar(gsm[:, 2:3], gsm[:, 1:2], 1.0, None, OP.add), inc=s_dve)
            b.wait("DVE", s_dve)
            b.rec("DVE", lambda e: e.reciprocal(gsm[:, 3:4], gsm[:, 2:3]), inc=s_dve)
            b.wait("DVE", s_dve)
            b.rec("DVE", lambda e: e.tensor_tensor(gsm[:, 4:5], gsm[:, 1:2], gsm[:, 3:4], OP.mult), inc=s_dve)
            b.wait("DVE", s_dve)
            t4 = AF32[:, 20692:20696]
            b.rec("DVE", lambda e, tt=tt: e.tensor_scalar(
                t4, glog[:, tt * 8:tt * 8 + 4], gm8[:, 0:1], gsm[:, 3:4], OP.is_equal, OP.mult), inc=s_dve)
            b.rec("DVE", lambda e, tt=tt: e.tensor_scalar(
                cw4_all[:, tt * 4:(tt + 1) * 4], glog[:, tt * 8:tt * 8 + 4],
                gm8[:, 1:2], gsm[:, 4:5], OP.is_equal, OP.mult), inc=s_dve)
            b.wait("DVE", s_dve)
            b.rec("DVE", lambda e, tt=tt: e.tensor_tensor(
                cw4_all[:, tt * 4:(tt + 1) * 4], cw4_all[:, tt * 4:(tt + 1) * 4], t4, OP.add), inc=s_dve)
            cwt_t = tok(s_dve)
            b.wait_many("PE", [cwt_t])
            b.rec("PE", lambda e, tt=tt: e.transpose(
                psC[0:4, 512:640], cw4_all[:, tt * 4:(tt + 1) * 4], eye_fc), inc=s_pe)
            b.wait("DVE", s_pe)
            b.rec("DVE", lambda e, tt=tt: e.tensor_copy(
                cwF4[:, tt * 128:(tt + 1) * 128], psC[0:4, 512:640]), inc=s_dve)
        cwf4_t = tok(s_dve)
        b.wait("DVE", s_dve)
        b.rec("DVE", lambda e: e.tensor_copy(cw4Fb, cwF4), inc=s_dve)
        cwf4b_t = tok(s_dve)

        b.wait_many("PE", [cwf4b_t, gw_t])
        g5 = 0
        ev5 = {}
        for m in range(KT):
            for n in range(4):
                ps = PSMAIN[g5 % 3]
                prev = ev5.get(g5 - 3)
                if prev is not None:
                    b.wait("PE", *prev)
                b.rec("PE", lambda e, m=m, n=n, ps=ps: e.matmul(
                    ps[:, 0:512], eb2ab[:, m * 128:(m + 1) * 128],
                    cw4Fb[:, n * 512:(n + 1) * 512], start=True, stop=True), inc=s_pe)
                b.wait("DVE", s_pe)
                b.rec("DVE", lambda e, m=m, n=n, ps=ps: e.tensor_tensor(
                    xs(m)[:, n * 512:(n + 1) * 512], xs(m)[:, n * 512:(n + 1) * 512],
                    ps[:, 0:512], OP.add), inc=s_dve)
                ev5[g5] = tok(s_dve)
                g5 += 1

        for ex in range(4):
            b.wait_many("PE", [cwf4_t, const_t])
            for n in range(4):
                b.wait("PE", s_dve)
                b.wait("PE", s_pool)
                b.rec("PE", lambda e, ex=ex, n=n: e.matmul(
                    psD[:, 0:512], selc[0:4, ex * 128:(ex + 1) * 128],
                    cwF4[:, n * 512:(n + 1) * 512], start=True, stop=True), inc=s_pe)
                b.wait("DVE", s_pe)
                b.rec("DVE", lambda e, n=n: e.tensor_copy(
                    B_cw3[:, n * 512:(n + 1) * 512], psD[:, 0:512]), inc=s_dve)
            bcw3_t = tok(s_dve)
            b.wait("SP", s_act)
            dma("SP", s_sp, eb1c, di["eb1"][ex])
            eb1_t = tok(s_sp)
            for hidc in range(4):
                for p in range(4):
                    wv, wv_t, wv_par = ws.load(
                        di["ew1"][ex][:, hidc * 1024 + p * 256:hidc * 1024 + (p + 1) * 256], KT)
                    b.wait_many("PE", [wv_t, h2_tt])
                    for m2 in range(2):
                        m = p * 2 + m2
                        for n in range(4):
                            ps = PSMAIN[g5 % 3]
                            prev = ev5.get(g5 - 3)
                            if prev is not None:
                                b.wait("PE", *prev)
                            for k in range(KT):
                                b.rec("PE", lambda e, k=k, m2=m2, n=n, ps=ps, wv=wv: e.matmul(
                                    ps[:, 0:512], wv(k, m2 * 128, 128), h2v(k)[:, n * 512:(n + 1) * 512],
                                    start=(k == 0), stop=(k == KT - 1)), inc=(s_pe if k == KT - 1 else None))
                            b.wait("ACT", s_pe)
                            b.wait_many("ACT", [eb1_t])
                            hm = hidc * 8 + m
                            b.rec("ACT", lambda e, m=m, n=n, ps=ps, hm=hm: e.activation(
                                out=ghv(m)[:, n * 512:(n + 1) * 512], in_=ps[:, 0:512],
                                func=AF.Gelu, bias=eb1c[:, hm:hm + 1]), inc=s_act)
                            ev5[g5] = tok(s_act)
                            g5 += 1
                    ws.consumed(wv_par, tok(s_act))
                gelu_t = tok(s_act)
                b.wait_many("DVE", [gelu_t, bcw3_t])
                for k in range(KT):
                    b.rec("DVE", lambda e, k=k: e.tensor_tensor(ghv(k), ghv(k), B_cw3, OP.mult), inc=s_dve)
                ghs_t = tok(s_dve)
                for p in range(4):
                    wv, wv_t, wv_par = ws.load(
                        di["ew2"][ex][hidc * 1024:(hidc + 1) * 1024, p * 256:(p + 1) * 256], KT)
                    b.wait_many("PE", [wv_t, ghs_t])
                    for m2 in range(2):
                        m = p * 2 + m2
                        for n in range(4):
                            ps = PSMAIN[g5 % 3]
                            prev = ev5.get(g5 - 3)
                            if prev is not None:
                                b.wait("PE", *prev)
                            for k in range(KT):
                                b.rec("PE", lambda e, k=k, m2=m2, n=n, ps=ps, wv=wv: e.matmul(
                                    ps[:, 0:512], wv(k, m2 * 128, 128), ghv(k)[:, n * 512:(n + 1) * 512],
                                    start=(k == 0), stop=(k == KT - 1)), inc=(s_pe if k == KT - 1 else None))
                            b.wait("DVE", s_pe)
                            b.rec("DVE", lambda e, m=m, n=n, ps=ps: e.tensor_tensor(
                                xs(m)[:, n * 512:(n + 1) * 512], xs(m)[:, n * 512:(n + 1) * 512],
                                ps[:, 0:512], OP.add), inc=s_dve)
                            ev5[g5] = tok(s_dve)
                            g5 += 1
                    ws.consumed(wv_par, tok(s_dve))
        moe3_t = tok(s_dve)
        tap("res3", AF32[:, 0:16384], [moe3_t])
        st3_t = post_ln(moe3_t)
        tap("x3", AF32[:, 0:16384], [st3_t])
        last_t = st3_t

    if n_stages >= 4:
        def xb4(k):
            return ABF[:, k * T:(k + 1) * T]
        def guv(k):
            return ABF[:, 16384 + k * T:16384 + (k + 1) * T]
        simsbuf = [AF32[:, 16384:16384 + 512], AF32[:, 16896:16896 + 512]]
        bgc = AF32[:, 19968:19968 + 32]
        buc = AF32[:, 20000:20000 + 32]
        bdc = AF32[:, 20032:20032 + 8]

        b.wait_many("DVE", [st3_t])
        b.wait("DVE", s_pe)
        b.wait("DVE", s_pool)
        for k in range(KT):
            b.rec("DVE", lambda e, k=k: e.tensor_copy(xb4(k), xs(k)), inc=s_dve)
        sh4_t = tok(s_dve)
        b.wait("SP", s_dve)
        b.wait("SP", s_act)
        b.wait("SP", s_pe)
        dma("SP", s_sp, bgc, di["bg"])
        dma("SP", s_sp, buc, di["bu"])
        dma("SP", s_sp, bdc, di["bd"])
        b4_t = tok(s_sp)

        DO_MLP = sub4 in ("all", "mlp")
        DO_PEER = sub4.startswith("peer") or sub4 == "all"
        g6 = 0
        ev6 = {}
        silu_sc = ABF[:, 43520:43520 + 1024]
        for hg in range(4 if DO_MLP else 0):
            for p in range(4):
                wg, wg_t, wg_par = ws.load(di["Wg"][:, hg * 1024 + p * 256:hg * 1024 + (p + 1) * 256], KT)
                wu, wu_t, wu_par = ws.load(di["Wu"][:, hg * 1024 + p * 256:hg * 1024 + (p + 1) * 256], KT)
                b.wait_many("PE", [wg_t, wu_t, sh4_t])
                for m2 in range(2):
                    m = p * 2 + m2
                    hm = hg * 8 + m
                    for n in range(4):
                        ps = PSMAIN[g6 % 2]
                        ssl = silu_sc[:, (g6 % 2) * 512:(g6 % 2) * 512 + 512]
                        prev = ev6.get(g6 - 2)
                        if prev is not None:
                            b.wait("PE", *prev)
                        for k in range(KT):
                            b.rec("PE", lambda e, k=k, m2=m2, n=n, ps=ps, wg=wg: e.matmul(
                                ps[:, 0:512], wg(k, m2 * 128, 128), xb4(k)[:, n * 512:(n + 1) * 512],
                                start=(k == 0), stop=(k == KT - 1)), inc=(s_pe if k == KT - 1 else None))
                        b.wait("ACT", s_pe)
                        b.wait_many("ACT", [b4_t])
                        if g6 >= 2:
                            b.wait("ACT", s_dve, ev6[g6 - 2][1])
                        b.rec("ACT", lambda e, ps=ps, ssl=ssl, hm=hm: e.activation(
                            out=ssl, in_=ps[:, 0:512], func=AF.Silu,
                            bias=bgc[:, hm // 8 * 0 + hm:hm + 1][:, 0:1] if False else bgc[:, hm:hm + 1]), inc=s_act)
                        sil_t = tok(s_act)
                        b.wait("PE", *sil_t)
                        for k in range(KT):
                            b.rec("PE", lambda e, k=k, m2=m2, n=n, ps=ps, wu=wu: e.matmul(
                                ps[:, 0:512], wu(k, m2 * 128, 128), xb4(k)[:, n * 512:(n + 1) * 512],
                                start=(k == 0), stop=(k == KT - 1)), inc=(s_pe if k == KT - 1 else None))
                        b.wait("DVE", s_pe)
                        b.wait_many("DVE", [sil_t, b4_t])
                        b.rec("DVE", lambda e, m=m, n=n, ps=ps, ssl=ssl, hm=hm: e.scalar_tensor_tensor(
                            guv(m)[:, n * 512:(n + 1) * 512], ps[:, 0:512], buc[:, hm:hm + 1],
                            ssl, OP.add, OP.mult), inc=s_dve)
                        ev6[g6] = tok(s_dve)
                        g6 += 1
                ws.consumed(wg_par, tok(s_dve))
                ws.consumed(wu_par, tok(s_dve))
            gu_t = tok(s_dve)
            for p in range(4):
                wd, wd_t, wd_par = ws.load(
                    di["Wd"][hg * 1024:(hg + 1) * 1024, p * 256:(p + 1) * 256], KT)
                b.wait_many("PE", [wd_t, gu_t])
                for m2 in range(2):
                    m = p * 2 + m2
                    for n in range(4):
                        ps = PSMAIN[g6 % 2]
                        prev = ev6.get(g6 - 2)
                        if prev is not None:
                            b.wait("PE", *prev)
                        for k in range(KT):
                            b.rec("PE", lambda e, k=k, m2=m2, n=n, ps=ps, wd=wd: e.matmul(
                                ps[:, 0:512], wd(k, m2 * 128, 128), guv(k)[:, n * 512:(n + 1) * 512],
                                start=(k == 0), stop=(k == KT - 1)), inc=(s_pe if k == KT - 1 else None))
                        b.wait("DVE", s_pe)
                        if hg == 0:
                            b.rec("DVE", lambda e, m=m, n=n, ps=ps: e.scalar_tensor_tensor(
                                xs(m)[:, n * 512:(n + 1) * 512], ps[:, 0:512], bdc[:, m:m + 1],
                                xs(m)[:, n * 512:(n + 1) * 512], OP.add, OP.add), inc=s_dve)
                        else:
                            b.rec("DVE", lambda e, m=m, n=n, ps=ps: e.tensor_tensor(
                                xs(m)[:, n * 512:(n + 1) * 512], xs(m)[:, n * 512:(n + 1) * 512],
                                ps[:, 0:512], OP.add), inc=s_dve)
                        ev6[g6] = tok(s_dve)
                        g6 += 1
                ws.consumed(wd_par, tok(s_dve))
        mlp_t = tok(s_dve)

        Mo = 16384
        def mv(tc):
            return ABF[:, Mo + tc * 4096:Mo + (tc + 1) * 4096]
        QF_o = 32768
        SW_o = 32768
        PB_o = 36864
        WT_o = 36864
        SCR_o = 40960
        KB_o = 43008
        kbuf = ABF[0:64, KB_o:KB_o + 512]
        b.wait("POOL", s_pe)
        b.wait("POOL", s_dve)
        dma("POOL", s_ini, kbuf, di["keysT"])
        kb_t = tok(s_ini)
        g7 = 0
        ev7 = {}
        for nch in range(4 if DO_PEER else 0):
            n0 = nch * 512
            wq, wq_t, wq_par = ws.load(di["Wq"], KT)
            b.wait_many("PE", [wq_t, sh4_t, mlp_t])
            for m in range(8):
                ps = PSMAIN[g7 % 3]
                prev = ev7.get(g7 - 3)
                if prev is not None:
                    b.wait("PE", *prev)
                for k in range(KT):
                    b.rec("PE", lambda e, k=k, m=m, n0=n0, ps=ps, wq=wq: e.matmul(
                        ps[0:64, 0:512], wq(k, m * 64, 64), xb4(k)[:, n0:n0 + 512],
                        start=(k == 0), stop=(k == KT - 1)), inc=(s_pe if k == KT - 1 else None))
                b.wait("DVE", s_pe)
                b.wait("DVE", s_pool)
                b.wait("DVE", s_act)
                b.rec("DVE", lambda e, m=m, ps=ps: e.tensor_copy(
                    ABF[0:64, QF_o + m * 512:QF_o + (m + 1) * 512], ps[0:64, 0:512]), inc=s_dve)
                ev7[g7] = tok(s_dve)
                g7 += 1
            ws.consumed(wq_par, tok(s_dve))
            q_t = tok(s_dve)
            for tc in range(4 if sub4 not in ("peer_q",) else 0):
                sims = simsbuf[(nch * 4 + tc) % 2]
                b.wait_many("PE", [q_t, kb_t])
                if "sims_ev" in ev7:
                    b.wait("PE", *ev7["sims_ev"])
                for pp in range(2):
                    for h in range(4):
                        hp = pp * 4 + h
                        b.rec("PE", lambda e, hp=hp, tc=tc: e.matmul(
                            psC[0:128, hp * 64:(hp + 1) * 64],
                            ABF[0:64, QF_o + hp * 512 + tc * 128:QF_o + hp * 512 + (tc + 1) * 128],
                            kbuf[0:64, hp * 64:(hp + 1) * 64],
                            start=True, stop=True), inc=(s_pe if hp == 7 else None))
                b.wait("DVE", s_pe)
                if (nch * 4 + tc) >= 2:
                    b.wait("DVE", s_act, ev7[("mask_done", (nch * 4 + tc) - 2)][1])
                b.rec("DVE", lambda e, sims=sims: e.tensor_copy(sims, psC[0:128, 0:512]), inc=s_dve)
                ev7["sims_ev"] = tok(s_dve)
                sim_t = tok(s_dve)
                sx8a = AF32[:, 17408:17440]
                sy8a = AF32[:, 17440:17472]
                p8fa = AF32[:, 17472:17504]
                rdna = AF32[:, 17504:17508]
                negc = AF32[:, 17512:17528]
                exall = ABF[:, SCR_o:SCR_o + 256]
                eyall = ABF[:, SCR_o + 256:SCR_o + 512]
                ex8a = ABF[:, SCR_o + 512:SCR_o + 544]
                ey8a = ABF[:, SCR_o + 544:SCR_o + 576]
                pca = ABF[:, SCR_o + 576:SCR_o + 832]
                p8a = ABF[:, SCR_o + 832:SCR_o + 864]
                if sub4 == "peer_sim":
                    continue
                b.wait_many("DVE", [sim_t])
                for h in range(4):
                    b.rec("DVE", lambda e, h=h: e.max(out=sx8a[:, h * 8:(h + 1) * 8],
                                                      in_=sims[:, h * 64:(h + 1) * 64]), inc=s_dve)
                    b.rec("DVE", lambda e, h=h: e.max(out=sy8a[:, h * 8:(h + 1) * 8],
                                                      in_=sims[:, 256 + h * 64:256 + (h + 1) * 64]), inc=s_dve)
                b.wait("DVE", s_dve)
                b.rec("DVE", lambda e: e.tensor_scalar(
                    negc[:, 0:8].rearrange("p (a o) -> p a o", o=1),
                    AF32[:, 17408:17472].rearrange("p (a c) -> p a c", c=8)[:, :, 0:1],
                    -1.0, None, OP.mult), inc=s_dve)
                st1_m = tok(s_dve)
                b.wait_many("ACT", [st1_m])
                for h in range(4):
                    b.rec("ACT", lambda e, h=h: e.activation(
                        out=exall[:, h * 64:(h + 1) * 64], in_=sims[:, h * 64:(h + 1) * 64],
                        func=AF.Exp, bias=negc[:, h:h + 1]), inc=s_act)
                    b.rec("ACT", lambda e, h=h: e.activation(
                        out=eyall[:, h * 64:(h + 1) * 64], in_=sims[:, 256 + h * 64:256 + (h + 1) * 64],
                        func=AF.Exp, bias=negc[:, 4 + h:5 + h]), inc=s_act)
                    b.rec("ACT", lambda e, h=h: e.activation(
                        out=ex8a[:, h * 8:(h + 1) * 8], in_=sx8a[:, h * 8:(h + 1) * 8],
                        func=AF.Exp, bias=negc[:, h:h + 1]), inc=s_act)
                    b.rec("ACT", lambda e, h=h: e.activation(
                        out=ey8a[:, h * 8:(h + 1) * 8], in_=sy8a[:, h * 8:(h + 1) * 8],
                        func=AF.Exp, bias=negc[:, 4 + h:5 + h]), inc=s_act)
                exp_t = tok(s_act)
                ev7[("mask_done", nch * 4 + tc)] = exp_t
                b.wait_many("DVE", [exp_t])
                for h in range(4):
                    b.rec("DVE", lambda e, h=h: e.tensor_tensor(
                        pca[:, h * 64:(h + 1) * 64].rearrange("p (i j) -> p i j", i=8),
                        ex8a[:, h * 8:(h + 1) * 8][:, :, None].broadcast_to((128, 8, 8)),
                        ey8a[:, h * 8:(h + 1) * 8][:, None, :].broadcast_to((128, 8, 8)), OP.mult), inc=s_dve)
                b.wait("DVE", s_dve)
                for h in range(4):
                    b.rec("DVE", lambda e, h=h: e.max(out=p8a[:, h * 8:(h + 1) * 8],
                                                      in_=pca[:, h * 64:(h + 1) * 64]), inc=s_dve)
                b.wait("DVE", s_dve)
                b.rec("DVE", lambda e: e.tensor_copy(p8fa, p8a), inc=s_dve)
                b.wait("DVE", s_dve)
                b.rec("DVE", lambda e: e.tensor_reduce(
                    rdna, p8fa.rearrange("p (h c) -> p h c", h=4), AX.X, OP.add), inc=s_dve)
                b.wait("DVE", s_dve)
                b.rec("DVE", lambda e: e.reciprocal(rdna, rdna), inc=s_dve)
                b.wait("DVE", s_dve)
                for h in range(4):
                    b.rec("DVE", lambda e, h=h: e.tensor_tensor(
                        p8fa[:, h * 8 + 7:h * 8 + 8], p8fa[:, h * 8 + 7:h * 8 + 8],
                        rdna[:, h:h + 1], OP.mult), inc=s_dve)
                    b.rec("DVE", lambda e, h=h: e.tensor_scalar(
                        eyall[:, h * 64:(h + 1) * 64], eyall[:, h * 64:(h + 1) * 64],
                        rdna[:, h:h + 1], None, OP.mult), inc=s_dve)
                sc_t = tok(s_dve)
                pbuf = ABF[:, PB_o:PB_o + 4096]
                b.wait("DVE", s_pe)
                for h in range(4):
                    b.wait_many("DVE", [sc_t])
                    b.wait("DVE", s_pool)
                    b.rec("DVE", lambda e, h=h: e.tensor_tensor(
                        pbuf.rearrange("p (i j) -> p i j", i=64),
                        exall[:, h * 64:(h + 1) * 64][:, :, None].broadcast_to((128, 64, 64)),
                        eyall[:, h * 64:(h + 1) * 64][:, None, :].broadcast_to((128, 64, 64)),
                        OP.mult), inc=s_dve)
                    b.wait("DVE", s_dve)
                    b.rec("DVE", lambda e, h=h: e.scalar_tensor_tensor(
                        pbuf, pbuf, p8fa[:, h * 8 + 7:h * 8 + 8], pbuf, OP.is_ge, OP.mult), inc=s_dve)
                    msk_t = tok(s_dve)
                    if h == 0:
                        b.wait_many("POOL", [msk_t])
                        b.rec("POOL", lambda e, tc=tc: e.tensor_copy(mv(tc), pbuf), inc=s_pool)
                    else:
                        b.wait_many("POOL", [msk_t])
                        b.rec("POOL", lambda e, tc=tc: e.tensor_tensor(
                            mv(tc), mv(tc), pbuf, OP.add), inc=s_pool)
            m_t = tok(s_pool)
            for eg in range(4 if sub4 not in ("peer_mask",) else 0):
                swg = ABF[:, SW_o:SW_o + 4096]
                for p in range(2):
                    dv, dv_t, dv_par = ws.load(
                        di["deT"][:, eg * 1024 + p * 512:eg * 1024 + (p + 1) * 512], KT)
                    b.wait_many("PE", [dv_t, sh4_t])
                    for m2 in range(4):
                        met = p * 4 + m2
                        ps = PSMAIN[g7 % 3]
                        prev = ev7.get(g7 - 3)
                        if prev is not None:
                            b.wait("PE", *prev)
                        for k in range(KT):
                            b.rec("PE", lambda e, k=k, m2=m2, n0=n0, ps=ps, dv=dv: e.matmul(
                                ps[:, 0:512], dv(k, m2 * 128, 128), xb4(k)[:, n0:n0 + 512],
                                start=(k == 0), stop=(k == KT - 1)), inc=(s_pe if k == KT - 1 else None))
                        b.wait("ACT", s_pe)
                        b.wait("ACT", s_pool)
                        b.rec("ACT", lambda e, met=met, ps=ps: e.activation(
                            out=swg[:, met * 512:(met + 1) * 512], in_=ps[:, 0:512],
                            func=AF.Silu), inc=s_act)
                        ev7[g7] = tok(s_act)
                        g7 += 1
                    ws.consumed(dv_par, tok(s_act))
                sw_t = tok(s_act)
                tpi = 0
                tp_ev = {}
                for et in range(8):
                    for tc in range(4):
                        slot = tpi % 2
                        b.wait_many("PE", [m_t, const_tb])
                        if tpi >= 2:
                            b.wait("PE", s_dve, tp_ev[tpi - 2])
                        b.rec("PE", lambda e, et=et, tc=tc, slot=slot: e.transpose(
                            psT[:, slot * 1024:slot * 1024 + 128],
                            mv(tc)[:, eg * 1024 + et * 128:eg * 1024 + (et + 1) * 128], eye_bc),
                            inc=s_pe)
                        b.wait("DVE", s_pe)
                        b.wait_many("DVE", [sw_t])
                        b.rec("DVE", lambda e, et=et, tc=tc, slot=slot: e.tensor_tensor(
                            ABF[:, WT_o + et * 512 + tc * 128:WT_o + et * 512 + (tc + 1) * 128],
                            psT[:, slot * 1024:slot * 1024 + 128],
                            swg[:, et * 512 + tc * 128:et * 512 + (tc + 1) * 128], OP.mult), inc=s_dve)
                        tp_ev[tpi] = s_dve.n
                        tpi += 1
                wt_t = tok(s_dve)
                for p in range(4 if sub4 not in ("peer_sw",) else 0):
                    uv, uv_t, uv_par = ws.load(
                        di["ue"][eg * 1024:(eg + 1) * 1024, p * 256:(p + 1) * 256], KT)
                    b.wait_many("PE", [uv_t, wt_t])
                    for m2 in range(2):
                        m = p * 2 + m2
                        ps = PSMAIN[g7 % 3]
                        prev = ev7.get(g7 - 3)
                        if prev is not None:
                            b.wait("PE", *prev)
                        for k in range(KT):
                            b.rec("PE", lambda e, k=k, m2=m2, ps=ps, uv=uv: e.matmul(
                                ps[:, 0:512], uv(k, m2 * 128, 128),
                                ABF[:, WT_o + k * 512:WT_o + (k + 1) * 512],
                                start=(k == 0), stop=(k == KT - 1)), inc=(s_pe if k == KT - 1 else None))
                        b.wait("DVE", s_pe)
                        b.rec("DVE", lambda e, m=m, n0=n0, ps=ps: e.tensor_tensor(
                            xs(m)[:, n0:n0 + 512], xs(m)[:, n0:n0 + 512], ps[:, 0:512], OP.add), inc=s_dve)
                        ev7[g7] = tok(s_dve)
                        g7 += 1
                    ws.consumed(uv_par, tok(s_dve))
        peer_t = tok(s_dve)
        tap("res4", AF32[:, 0:16384], [peer_t])
        st4_t = post_ln(peer_t)
        last_t = st4_t


    b.wait_many("ACT", [last_t])
    for k in range(KT):
        dma("ACT", s_sc, y_out[k * 128:(k + 1) * 128, :], xs(k))
    b.wait("ACT", s_sc)
    b.wait("SP", s_sc)

    b.play()
    return nc


def prep_inputs(inputs, core):
    ci = {}
    bidx, half = core // 2, core % 2
    x = np.asarray(inputs["L0_attn_x"])
    sl = x[bidx, half * T:(half + 1) * T, :]
    ci["x_tf"] = np.ascontiguousarray(sl)
    ci["x_ft"] = np.ascontiguousarray(sl.T)

    def coltile(v, n):
        return np.ascontiguousarray(np.asarray(v).reshape(n, 128).T)

    ci["WbT"] = np.ascontiguousarray(np.asarray(inputs["L0_attn_Wb"]).T)
    ci["WpT"] = np.ascontiguousarray(np.asarray(inputs["L0_attn_Wp"]).T)
    ci["initBT"] = np.ascontiguousarray(np.asarray(inputs["L0_attn_init_B"]).T)
    ci["initA"] = np.ascontiguousarray(np.asarray(inputs["L0_attn_init_A"]))
    ci["initAT"] = np.ascontiguousarray(np.asarray(inputs["L0_attn_init_A"]).T)
    ci["initB"] = np.ascontiguousarray(np.asarray(inputs["L0_attn_init_B"]))
    ci["g0"] = coltile(inputs["L0_attn_ln_gamma"], KT)
    ci["be0"] = coltile(inputs["L0_attn_ln_beta"], KT)
    ci["routerT"] = np.ascontiguousarray(np.asarray(inputs["L0_ffn_router_w"]).T)
    ci["w1"] = np.asarray(inputs["L0_ffn_w1"])
    ci["w2"] = np.asarray(inputs["L0_ffn_w2"])
    ci["n1w"] = coltile(inputs["L1_attn_norm1_w"], KT)
    ci["n2w"] = coltile(inputs["L1_attn_norm2_w"], KT)
    ci["ipw"] = np.asarray(inputs["L1_attn_in_proj_w"])
    ci["ipb"] = coltile(inputs["L1_attn_in_proj_b"], 32)
    ci["convw"] = np.ascontiguousarray(
        np.asarray(inputs["L1_attn_conv_w"]).reshape(2048, 3).reshape(16, 128, 3).transpose(1, 0, 2).reshape(128, 48))
    ci["convb"] = coltile(inputs["L1_attn_conv_b"], 16)
    ci["dtw"] = np.asarray(inputs["L1_attn_dt_w"])
    ci["dtb"] = np.asarray(inputs["L1_attn_dt_b"]).reshape(64, 1)
    ci["bpw"] = np.asarray(inputs["L1_attn_bp_w"])
    ci["bpb"] = np.asarray(inputs["L1_attn_bp_b"]).reshape(64, 1)
    ci["cpw"] = np.asarray(inputs["L1_attn_cp_w"])
    ci["cpb"] = np.asarray(inputs["L1_attn_cp_b"]).reshape(64, 1)
    ci["s2iw"] = np.asarray(inputs["L1_attn_s2i_w"])
    ci["s2ib"] = coltile(inputs["L1_attn_s2i_b"], 16)
    ci["Dpar"] = coltile(inputs["L1_attn_D_param"], 16)
    ci["outw"] = np.asarray(inputs["L1_attn_out_w"])
    ci["outb"] = coltile(inputs["L1_attn_out_b"], KT)
    ci["gatew"] = np.asarray(inputs["L1_attn_gate_w"])
    ci["gateb"] = np.asarray(inputs["L1_attn_gate_b"]).reshape(4, 1)
    ci["gateb_bc"] = np.tile(np.asarray(inputs["L1_attn_gate_b"]).reshape(1, 4), (128, 1))
    ci["eb2all"] = np.asarray(inputs["L1_attn_e_b2"]).astype(BF16NP)
    ci["ew1"] = np.asarray(inputs["L1_attn_e_w1"])
    ci["eb1"] = np.stack([coltile(np.asarray(inputs["L1_attn_e_b1"])[e], 32) for e in range(4)])
    ci["ew2"] = np.asarray(inputs["L1_attn_e_w2"])
    ci["eb2"] = np.stack([coltile(np.asarray(inputs["L1_attn_e_b2"])[e], KT) for e in range(4)])
    ci["Wg"] = np.asarray(inputs["L1_ffn_Wg"])
    ci["bg"] = coltile(inputs["L1_ffn_bg"], 32)
    ci["Wu"] = np.asarray(inputs["L1_ffn_Wu"])
    ci["bu"] = coltile(inputs["L1_ffn_bu"], 32)
    ci["Wd"] = np.asarray(inputs["L1_ffn_Wd"])
    ci["bd"] = coltile(inputs["L1_ffn_bd"], KT)
    ci["Wq"] = np.asarray(inputs["L1_ffn_Wq"])
    kT = np.ascontiguousarray(
        np.asarray(inputs["L1_ffn_keys"]).transpose(3, 2, 0, 1).reshape(64, 2 * 4 * 64))
    ci["keysT"] = kT.astype(BF16NP)
    ci["deT"] = np.ascontiguousarray(np.asarray(inputs["L1_ffn_down_embed"]).T)
    ci["ue"] = np.asarray(inputs["L1_ffn_up_embed"])
    ci["eye_b"] = np.eye(128, dtype=np.float32).astype(BF16NP)
    ci["eye_f"] = np.eye(128, dtype=np.float32)
    sel = np.zeros((8, 1024), np.float32)
    for e_ in range(8):
        sel[e_, e_ * 128:(e_ + 1) * 128] = 1.0
    ci["sel8"] = sel
    ci["ones_col"] = np.ones((128, 1), np.float32)
    ci["ones_row"] = np.ones((1, 128), np.float32)
    ci["constv"] = np.tile(np.array([0.0, 1.0, 2.0, 1e-5, 1e-6, -1.0, 0.5, -0.5], np.float32), (128, 1))
    mp = np.zeros((8, 1), np.float32)
    mn = np.zeros((8, 1), np.float32)
    if half == 1:
        mp[2 * bidx] = 1.0
    if half == 0:
        mn[2 * bidx + 1] = 1.0
    ci["mask_prev"] = mp
    ci["mask_next_b"] = mn.astype(BF16NP)
    for k in list(ci):
        v = ci[k]
        if v.dtype not in (np.float32, BF16NP):
            ci[k] = np.ascontiguousarray(v, dtype=np.float32)
    return ci


def kernel(**inputs):
    nc = build_program(n_stages=4)
    in_maps = [prep_inputs(inputs, c) for c in range(8)]
    res = run_bass_kernel_spmd(nc, in_maps, core_ids=list(range(8)))
    out = np.empty((4, 4096, D), np.float32)
    for c in range(8):
        bidx, half = c // 2, c % 2
        out[bidx, half * T:(half + 1) * T, :] = res.results[c]["y"].T
    return out


# version 07 (final)
# speedup vs baseline: 1.8336x; 1.0559x over previous
import numpy as np
import ml_dtypes
import concourse.bass as bass
import concourse.mybir as mybir
from concourse.bass_utils import run_bass_kernel_spmd

F32 = mybir.dt.float32
BF16 = mybir.dt.bfloat16
AF = mybir.ActivationFunctionType
OP = mybir.AluOpType
AX = mybir.AxisListType
BF16NP = ml_dtypes.bfloat16

T = 2048
D = 1024
KT = 8
CH = 8
LR = 0.02

ENGS = ("PE", "DVE", "ACT", "POOL", "SP")


class Sem:
    def __init__(self, h, name):
        self.h = h; self.name = name; self.n = 0


class Builder:
    def __init__(self, nc):
        self.nc = nc
        self.q = {e: [] for e in ENGS}
        self.sems = {}
        self.last_wait = {e: {} for e in ENGS}
        self._ctx = []

    def sem(self, name):
        ctx = self.nc.semaphore(name)
        h = ctx.__enter__()
        self._ctx.append(ctx)
        s = Sem(h, name)
        self.sems[name] = s
        return s

    def rec(self, eng, make, inc=None, inc_n=1):
        if inc is not None:
            inc.n += inc_n
            def f(e, make=make, s=inc, k=inc_n):
                make(e).then_inc(s.h, k)
            self.q[eng].append(f)
        else:
            self.q[eng].append(lambda e, make=make: make(e))

    def wait(self, eng, sem, val=None):
        if val is None:
            val = sem.n
        lw = self.last_wait[eng]
        if lw.get(sem.name, -1) >= val:
            return
        lw[sem.name] = val
        self.q[eng].append(lambda e, s=sem, v=val: e.wait_ge(s.h, v))

    def wait_many(self, eng, toks):
        for (sem, val) in toks:
            self.wait(eng, sem, val)

    def wait_force(self, eng, sem, val):
        self.last_wait[eng][sem.name] = max(self.last_wait[eng].get(sem.name, -1), val)
        self.q[eng].append(lambda e, s=sem, v=val: e.wait_ge(s.h, v))

    def play(self):
        with self.nc.Block() as block:
            @block.tensor
            def _(e):
                for f in self.q["PE"]: f(e)
            @block.vector
            def _(e):
                for f in self.q["DVE"]: f(e)
            @block.scalar
            def _(e):
                for f in self.q["ACT"]: f(e)
            @block.gpsimd
            def _(e):
                for f in self.q["POOL"]: f(e)
            @block.sync
            def _(e):
                for f in self.q["SP"]: f(e)


def build_program(n_stages=4, taps=(), sub4="all"):
    nc = bass.Bass("TRN2", target_bir_lowering=False)
    b = Builder(nc)

    di = {}
    def din(name, shape, dtype=F32):
        di[name] = nc.dram_tensor(name, list(shape), dtype, kind="ExternalInput").ap()
        return di[name]

    din("x_tf", [T, D]); din("x_ft", [D, T])
    din("WbT", [D, D]); din("WpT", [D, D])
    din("initBT", [D, 16]); din("initA", [D, 16]); din("initAT", [16, D]); din("initB", [16, D])
    din("g0", [128, KT]); din("be0", [128, KT])
    din("routerT", [D, 8]); din("w1", [D, 8 * D]); din("w2", [8 * D, D])
    din("n1w", [128, KT]); din("n2w", [128, KT])
    din("ipw", [D, 4 * D]); din("ipb", [128, 32])
    din("convw", [128, 48]); din("convb", [128, 16])
    din("dtw", [2 * D, 64]); din("dtb", [64, 1])
    din("bpw", [2 * D, 64]); din("bpb", [64, 1])
    din("cpw", [2 * D, 64]); din("cpb", [64, 1])
    din("s2iw", [64, 2 * D]); din("s2ib", [128, 16]); din("Dpar", [128, 16])
    din("outw", [2 * D, D]); din("outb", [128, KT])
    din("gatew", [D, 4]); din("gateb", [4, 1]); din("gateb_bc", [128, 4]); din("eb2all", [4, 1024], BF16)
    din("ew1", [4, D, 4 * D]); din("eb1", [4, 128, 32])
    din("ew2", [4, 4 * D, D]); din("eb2", [4, 128, KT])
    din("Wg", [D, 4 * D]); din("bg", [128, 32])
    din("Wu", [D, 4 * D]); din("bu", [128, 32])
    din("Wd", [4 * D, D]); din("bd", [128, KT])
    din("Wq", [D, 512]); din("keysT", [64, 512], BF16)
    din("deT", [D, 4 * D]); din("ue", [4 * D, D])
    din("eye_b", [128, 128], BF16); din("eye_f", [128, 128])
    din("sel8", [8, 1024])
    din("ones_col", [128, 1]); din("ones_row", [1, 128])
    din("constv", [128, 8])
    din("mask_prev", [8, 1]); din("mask_next_b", [8, 1], BF16)

    y_out = nc.dram_tensor("y", [D, T], F32, kind="ExternalOutput").ap()
    dbg = {}
    for name, shape, dt in taps:
        dbg[name] = nc.dram_tensor("dbg_" + name, list(shape), dt, kind="ExternalOutput").ap()

    cc_src_d = nc.dram_tensor("cc_src_d", [128, 8], BF16).ap()
    cc_dst_d = nc.dram_tensor("cc_dst_d", [8 * 128, 8], BF16).ap()
    cc_srcA = nc.dram_tensor("cc_srcA", [16, D], F32).ap()
    cc_dstA = nc.dram_tensor("cc_dstA", [8 * 16, D], F32).ap()
    cc_srcB = nc.dram_tensor("cc_srcB", [128, 128], F32).ap()
    cc_dstB = nc.dram_tensor("cc_dstB", [8 * 128, 128], F32).ap()
    cc_srcH = nc.dram_tensor("cc_srcH", [128, KT * 2], F32).ap()
    cc_dstH = nc.dram_tensor("cc_dstH", [8 * 128, KT * 2], F32).ap()
    cc_srcS = nc.dram_tensor("cc_srcS", [64, 1], F32).ap()
    cc_dstS = nc.dram_tensor("cc_dstS", [8 * 64, 1], F32).ap()
    scr_xm = nc.dram_tensor("scr_xm", [2 * D, T], BF16).ap()
    scr_sg = nc.dram_tensor("scr_sg", [2 * D, T], BF16).ap()

    sbctx = []
    def sbuf(name, shape, dtype=F32):
        ctx = nc.sbuf_tensor(name, list(shape), dtype)
        t = ctx.__enter__(); sbctx.append(ctx)
        return t

    AF32 = sbuf("AF32", [128, 24576])
    ABF = sbuf("ABF", [128, 45056], BF16)
    wbuf = sbuf("wbuf", [128, 2 * 4096], BF16)
    kc = sbuf("kc", [128, 448])
    selc = sbuf("selc", [8, 1024])
    kcb = sbuf("kcb", [128, 264], BF16)

    def f32v(off, cols, part=128, p0=0):
        return AF32[p0:p0 + part, off:off + cols]
    def bfv(off, cols, part=128, p0=0):
        return ABF[p0:p0 + part, off:off + cols]

    xs_o = 0
    def xs(k):
        return AF32[:, xs_o + k * T: xs_o + (k + 1) * T]
    B1_o = 16384
    SQ_o = 17408
    SM_o = 18432
    ACCA_o = 18944
    ACCB_o = 19968
    FREE_F32 = 20096

    eye_bc = kcb[:, 0:128]
    ones_col = kc[:, 0:1]
    ones_row = kc[0:1, 8:136]
    g0c = kc[:, 136:144]
    be0c = kc[:, 144:152]
    maskbc = kc[:, 152:160]
    n1wc = kc[:, 160:168]
    n2wc = kc[:, 168:176]
    eye_fc = kc[:, 184:312]
    CONSTS = [0.0, 1.0, 2.0, 1e-5, 1e-6, -1.0, 0.5, -0.5]
    for ci_, cv_ in enumerate(CONSTS):
        nc.const_aps.register(F32, cv_, kc[:, 176 + ci_:177 + ci_])

    psctx = []
    def psum(name, shape, dtype=F32):
        ctx = nc.psum_tensor(name, list(shape), dtype)
        t = ctx.__enter__(); psctx.append(ctx)
        return t

    psA = psum("psA", [128, 512])
    psB = psum("psB", [128, 512])
    psE = psum("psE", [128, 512])
    psC = psum("psC", [128, 1024])
    psD = psum("psD", [128, 512])
    psT = psum("psT", [128, 2048], BF16)

    PSMAIN = [psA, psB, psE]

    s_sp = b.sem("s_sp"); s_sc = b.sem("s_sc"); s_gd = b.sem("s_gd")
    s_cst = b.sem("s_cst"); s_ini = b.sem("s_ini"); s_gsl = b.sem("s_gsl")
    s_sg = [b.sem("s_sg0"), b.sem("s_sg1")]
    s_wl = [b.sem("s_wl0"), b.sem("s_wl1")]
    s_pe = b.sem("s_pe"); s_dve = b.sem("s_dve"); s_act = b.sem("s_act")
    s_pool = b.sem("s_pool"); s_cc = b.sem("s_cc")

    def tok(sem):
        return (sem, sem.n)

    GATED_DMA_SEMS = ("s_sp", "s_sc", "s_cst", "s_ini", "s_gsl")

    def dma(stream, sem, out, in_, **kw):
        b.rec(stream, lambda e, o=out, i=in_, kw=kw: e.dma_start(out=o, in_=i, **kw),
              inc=sem, inc_n=16)
        if sem.name in GATED_DMA_SEMS:
            b.wait(stream, sem)
        return tok(sem)

    def tap(name, src_ap, after=()):
        if name in dbg:
            b.wait_many("SP", after)
            dma("SP", s_sp, dbg[name], src_ap)
            for e_ in ENGS:
                b.wait(e_, s_sp)

    def fence_all(extra=()):
        toks = [tok(s_pe), tok(s_dve), tok(s_act), tok(s_pool), tok(s_sp), tok(s_sc),
                tok(s_gd), tok(s_wl[0]), tok(s_wl[1]), tok(s_cc)] + list(extra)
        for e in ENGS:
            b.wait_many(e, toks)

    class WStream:
        def __init__(self):
            self.idx = 0
            self.freed = [[], []]
        def load(self, pieces, n_k):
            par = self.idx % 2; self.idx += 1
            base = par * 4096
            for (sem, val) in self.freed[par]:
                b.wait("POOL", sem, val)
            self.freed[par] = []
            aps = pieces if isinstance(pieces, list) else [pieces]
            tot = sum(a.shape[-1] for a in aps)
            assert n_k * tot <= 4096, (n_k, tot)
            c0 = 0
            for a in aps:
                cw = a.shape[-1]
                dst = wbuf[:, base:base + n_k * tot].rearrange(
                    "p (k c) -> p k c", k=n_k)[:, :, c0:c0 + cw]
                b.rec("POOL", lambda e, o=dst, i=a.rearrange("(k p) c -> p k c", p=128):
                      e.dma_start(out=o, in_=i), inc=s_wl[par], inc_n=16)
                c0 += cw
            t = tok(s_wl[par])
            def view(k, c0=0, cw=None, base=base, tot=tot):
                cw = tot - c0 if cw is None else cw
                return wbuf[:, base + k * tot + c0: base + k * tot + c0 + cw]
            return view, t, par
        def consumed(self, par, t):
            self.freed[par].append(t)

    ws = WStream()

    def ln_full(src_fn, kt, n_feat, eps, with_mean, dst_rows, wait_toks=(), n_tok=T):
        rstd_row, negmur_row = dst_rows
        nn = (n_tok + 511) // 512
        pe_hist = []
        for n in range(nn):
            n0 = n * 512
            nw = min(512, n_tok - n0)
            b.wait("PE", s_dve)
            b.wait("PE", s_act)
            for k in range(kt):
                slot = k % 2
                sq = AF32[:, SQ_o + slot * 512: SQ_o + slot * 512 + nw]
                if len(pe_hist) >= 2:
                    b.wait("ACT", s_pe, pe_hist[-2])
                elif len(pe_hist) == 0:
                    b.wait("ACT", s_pe)
                b.wait_many("ACT", wait_toks)
                b.rec("ACT", lambda e, i=src_fn(k)[:, n0:n0 + nw], o=sq:
                      e.activation(out=o, in_=i, func=AF.Square), inc=s_act)
                b.wait("PE", s_act)
                b.wait_many("PE", wait_toks)
                if with_mean:
                    b.rec("PE", lambda e, r=src_fn(k)[:, n0:n0 + nw], p=psD[0:1, 0:nw],
                                 st=(k == 0), lt=(k == kt - 1):
                          e.matmul(p, ones_col, r, start=st, stop=lt))
                b.rec("PE", lambda e, r=sq, p=psC[0:1, 0:nw], st=(k == 0), lt=(k == kt - 1):
                      e.matmul(p, ones_col, r, start=st, stop=lt), inc=s_pe)
                pe_hist.append(s_pe.n)
            b.wait("DVE", s_pe)
            if with_mean:
                b.rec("DVE", lambda e, p=psD[0:1, 0:nw], o=rowA(nw) if False else negmur_row[:, n0:n0 + nw]:
                      e.tensor_scalar(o, p, 1.0 / n_feat, None, OP.mult), inc=s_dve)
                b.wait("DVE", s_dve)
                b.rec("DVE", lambda e, mu=negmur_row[:, n0:n0 + nw], o=rstd_row[:, n0:n0 + nw]:
                      e.tensor_tensor(o, mu, mu, OP.mult), inc=s_dve)
                b.wait("DVE", s_dve)
                b.rec("DVE", lambda e, p=psC[0:1, 0:nw], m2=rstd_row[:, n0:n0 + nw],
                             o=rstd_row[:, n0:n0 + nw]:
                      e.scalar_tensor_tensor(o, p, 1.0 / n_feat, m2, OP.mult, OP.subtract), inc=s_dve)
            else:
                b.rec("DVE", lambda e, p=psC[0:1, 0:nw], o=rstd_row[:, n0:n0 + nw]:
                      e.tensor_scalar(o, p, 1.0 / n_feat, None, OP.mult), inc=s_dve)
        dve_t = tok(s_dve)
        b.wait_many("ACT", [dve_t])
        b.rec("ACT", lambda e, v=rstd_row, o=rstd_row:
              e.activation(out=o, in_=v, func=AF.Ln, bias=float(eps)), inc=s_act)
        b.wait("ACT", s_act)
        b.rec("ACT", lambda e, v=rstd_row, o=rstd_row:
              e.activation(out=o, in_=v, func=AF.Exp, scale=-0.5), inc=s_act)
        t_act = tok(s_act)
        if with_mean:
            b.wait_many("DVE", [t_act])
            b.rec("DVE", lambda e, mu=negmur_row, r=rstd_row, o=negmur_row:
                  e.scalar_tensor_tensor(o, mu, -1.0, r, OP.mult, OP.mult), inc=s_dve)
            return tok(s_dve)
        return t_act

    def bcast_chunk(row_ap, n0, nw, dst):
        b.wait("PE", s_dve)
        b.wait("PE", s_act)
        b.rec("PE", lambda e, r=row_ap[:, n0:n0 + nw], p=psD[:, 0:nw]:
              e.matmul(p, ones_row, r, start=True, stop=True), inc=s_pe)
        b.wait("DVE", s_pe)
        b.rec("DVE", lambda e, p=psD[:, 0:nw], o=dst[:, 0:nw]:
              e.tensor_copy(o, p), inc=s_dve)
        return tok(s_dve)

    dma("SP", s_sp, kc[:, 0:1], di["ones_col"])
    dma("SP", s_sp, kc[0:1, 8:136], di["ones_row"])
    dma("SP", s_sp, kc[:, 136:144], di["g0"])
    dma("SP", s_sp, kc[:, 144:152], di["be0"])
    dma("SP", s_sp, kc[:, 152:160],
        di["mask_prev"].rearrange("r c -> (r c)").partition_broadcast(128))
    dma("SP", s_sp, kc[:, 160:168], di["n1w"])
    dma("SP", s_sp, kc[:, 168:176], di["n2w"])
    dma("SP", s_sp, kc[:, 176:184], di["constv"])
    dma("SP", s_sp, kc[:, 184:312], di["eye_f"])
    dma("SP", s_sp, selc[:], di["sel8"])
    dma("POOL", s_cst, kcb[:, 0:128], di["eye_b"])
    dma("POOL", s_cst, kcb[0:8, 128:129], di["mask_next_b"])
    const_t = tok(s_sp); const_tb = tok(s_cst)

    for k in range(KT):
        dma("SP", s_sp, xs(k), di["x_ft"][k * 128:(k + 1) * 128, :])
    x_t = tok(s_sp)

    XN_o = 0
    Tp = T + 1
    def xn(k):
        return ABF[:, XN_o + k * Tp: XN_o + (k + 1) * Tp]
    XBC_o = 16392
    def xbc(k):
        return ABF[:, XBC_o + k * 512: XBC_o + (k + 1) * 512]
    VC_o = 20488
    def vch(par, k):
        return ABF[:, VC_o + par * 2048 + k * 256: VC_o + par * 2048 + (k + 1) * 256]
    S3_o = 24584
    DAB_o = 28680
    DBT_o = 36872
    B2T_o = 37896
    MID_o = 38024
    TMP_o = 40072
    INIT_o = 41096

    def dab(c):
        return ABF[0:16, DAB_o + c * 1024: DAB_o + (c + 1) * 1024]

    r_rstd = AF32[0:1, FREE_F32:FREE_F32 + T]
    r_negmur = AF32[0:1, FREE_F32 + T:FREE_F32 + 2 * T]
    ln1_t = ln_full(xs, KT, D, 1e-5, True, (r_rstd, r_negmur), wait_toks=[x_t, const_t])

    B1c = AF32[:, B1_o:B1_o + 512]
    B2c = AF32[:, B1_o + 512:B1_o + 1024]
    FT1 = AF32[:, SQ_o:SQ_o + 512]
    for n in range(4):
        n0 = n * 512
        b.wait_many("PE", [ln1_t])
        bcast_chunk(r_rstd, n0, 512, B1c)
        bcast_chunk(r_negmur, n0, 512, B2c)
        bc_t = tok(s_dve)
        b.wait_many("DVE", [bc_t, x_t, const_t])
        for k in range(KT):
            b.rec("DVE", lambda e, k=k, n0=n0: e.tensor_tensor(
                FT1, xs(k)[:, n0:n0 + 512], B1c, OP.mult), inc=s_dve)
            b.wait("DVE", s_dve)
            b.rec("DVE", lambda e, k=k, n0=n0: e.tensor_tensor(
                FT1, FT1, B2c, OP.add), inc=s_dve)
            b.wait("DVE", s_dve)
            b.rec("DVE", lambda e, k=k, n0=n0: e.tensor_scalar(
                xn(k)[:, n0:n0 + 512], FT1, g0c[:, k:k + 1], be0c[:, k:k + 1],
                OP.mult, OP.add), inc=s_dve)
            b.wait("DVE", s_dve)
    xn_t = tok(s_dve)

    b.wait("SP", s_pe)
    dma("SP", s_sp, AF32[0:16, FREE_F32 + 2048:FREE_F32 + 3072], di["initAT"])
    dma("SP", s_sp, AF32[:, FREE_F32 + 1152:FREE_F32 + 1280].rearrange("p (k c) -> p k c", k=KT),
        di["initBT"].rearrange("(k p) c -> p k c", p=128))
    init_tf = tok(s_sp)

    b.wait_many("DVE", [xn_t])
    for k in range(KT):
        b.rec("DVE", lambda e, k=k: e.tensor_copy(
            ABF[:, TMP_o + 96 + k:TMP_o + 97 + k], ABF[:, XN_o + k * Tp:XN_o + k * Tp + 1]),
            inc=s_dve)
    b.wait_many("SP", [tok(s_dve)])
    dma("SP", s_sp, cc_src_d, ABF[:, TMP_o + 96:TMP_o + 104])
    b.wait("POOL", s_sp)
    b.rec("POOL", lambda e: e.collective_compute(
        "AllGather", OP.bypass, replica_groups=[list(range(8))],
        ins=[cc_src_d[:].opt()], outs=[cc_dst_d[:].opt()]), inc=s_cc)
    b.wait("POOL", s_cc)
    gsl = ABF[:, TMP_o:TMP_o + 64]
    dma("POOL", s_gsl, gsl.rearrange("p (r k) -> p r k", r=8),
        cc_dst_d.rearrange("(r p) k -> p r k", p=128))
    mnb = ABF[:, TMP_o + 64:TMP_o + 72]
    dma("POOL", s_gsl, mnb, di["mask_next_b"].rearrange("r c -> (r c)").partition_broadcast(128))
    g_t = tok(s_gsl)
    b.wait_many("DVE", [g_t])
    b.rec("DVE", lambda e: e.tensor_tensor(
        gsl.rearrange("p (r k) -> p r k", r=8),
        gsl.rearrange("p (r k) -> p r k", r=8),
        mnb.rearrange("p (r o) -> p r o", o=1).broadcast_to((128, 8, 8)), OP.mult), inc=s_dve)
    b.wait("DVE", s_dve)
    def _red_shift(e):
        with nc.allow_low_precision(reason="0/1-mask select: single nonzero term"):
            return e.tensor_reduce(ABF[:, TMP_o + 72:TMP_o + 80],
                                   gsl.rearrange("p (r k) -> p k r", r=8), AX.X, OP.add)
    b.rec("DVE", _red_shift, inc=s_dve)
    b.wait("DVE", s_dve)
    for k in range(KT):
        b.rec("DVE", lambda e, k=k: e.tensor_copy(
            ABF[:, XN_o + k * Tp + T:XN_o + k * Tp + T + 1], ABF[:, TMP_o + 72 + k:TMP_o + 73 + k]),
            inc=s_dve)
    shift_t = tok(s_dve)

    initBTb = ABF[:, INIT_o:INIT_o + 128]
    initAb = ABF[:, INIT_o + 128:INIT_o + 256]
    dma("POOL", s_ini, initBTb.rearrange("p (k c) -> p k c", k=KT),
        di["initBT"].rearrange("(k p) c -> p k c", p=128))
    dma("POOL", s_ini, initAb.rearrange("p (k c) -> p k c", k=KT),
        di["initA"].rearrange("(k p) c -> p k c", p=128))
    init_tb = tok(s_ini)
    GA = AF32[0:16, FREE_F32:FREE_F32 + 1024]
    GB = AF32[:, FREE_F32 + 1024:FREE_F32 + 1152]
    initBT_f = AF32[:, FREE_F32 + 1152:FREE_F32 + 1280]
    initAT_f = AF32[0:16, FREE_F32 + 2048:FREE_F32 + 3072]

    wv0, wv0_t, _ = ws.load(di["WpT"][:, 0:512], KT)
    wv1, wv1_t, _ = ws.load(di["WpT"][:, 512:1024], KT)
    def wpT(k, m):
        if m < 4:
            return wv0(k, m * 128, 128)
        return wv1(k, (m - 4) * 128, 128)

    projin = ABF[:, INIT_o + 256:INIT_o + 256 + 32]
    projerr = ABF[:, INIT_o + 288:INIT_o + 288 + 32]
    prev_evac = {}
    tp_i = 0
    tp_evac = {}
    for c in range(CH):
        par = c % 2
        t0 = c * 256
        b.wait_many("PE", [shift_t, wv0_t, wv1_t])
        for m in range(KT):
            ps = PSMAIN[m % 3]
            if c > 0 or m >= 3:
                pm = (c, m - 3) if m >= 3 else (c - 1, m + 5)
                b.wait("PE", s_dve, prev_evac[pm][1])
            for k in range(KT):
                b.rec("PE", lambda e, k=k, m=m, t0=t0, ps=ps: e.matmul(
                    ps[:, 0:256], wpT(k, m), xn(k)[:, t0 + 1:t0 + 257],
                    start=(k == 0), stop=(k == KT - 1)), inc=(s_pe if k == KT - 1 else None))
            b.wait("DVE", s_pe)
            b.rec("DVE", lambda e, m=m, par=par, ps=ps: e.tensor_copy(
                vch(par, m), ps[:, 0:256]), inc=s_dve)
            prev_evac[(c, m)] = tok(s_dve)
        v_t = tok(s_dve)
        xT = lambda h: ABF[:, S3_o + 2048 + h * 1024: S3_o + 2048 + (h + 1) * 1024]
        b.wait("POOL", s_pe)
        dma("POOL", s_gd, ABF[:, S3_o + 2048:S3_o + 4096].rearrange("p (h f) -> p h f", h=2),
            di["x_tf"][t0:t0 + 256, :].rearrange("(h p) f -> p h f", p=128))
        xT_t = tok(s_gd)
        VT = lambda h: ABF[:, S3_o + h * 1024: S3_o + (h + 1) * 1024]
        for h in range(2):
            for k in range(KT):
                slot = tp_i % 2
                b.wait_many("PE", [v_t, const_tb])
                if tp_i >= 2:
                    b.wait("PE", s_dve, tp_evac[tp_i - 2])
                b.rec("PE", lambda e, h=h, k=k, par=par, slot=slot: e.transpose(
                    psT[:, slot * 1024:slot * 1024 + 128],
                    vch(par, k)[:, h * 128:(h + 1) * 128], eye_bc),
                    inc=s_pe)
                b.wait("DVE", s_pe)
                b.rec("DVE", lambda e, h=h, k=k, slot=slot: e.tensor_copy(
                    VT(h)[:, k * 128:(k + 1) * 128], psT[:, slot * 1024:slot * 1024 + 128]),
                    inc=s_dve)
                tp_evac[tp_i] = s_dve.n
                tp_i += 1
        vt_t = tok(s_dve)
        b.wait("DVE", s_pe)
        b.wait_many("DVE", [x_t])
        for k in range(KT):
            b.rec("DVE", lambda e, k=k, t0=t0: e.tensor_copy(
                xbc(k)[:, 0:256], xs(k)[:, t0:t0 + 256]), inc=s_dve)
        xc_t = tok(s_dve)
        b.wait_many("PE", [xc_t, init_tb, v_t])
        b.wait("PE", s_dve)
        for h in range(2):
            if h == 1:
                b.wait("PE", s_dve)
            for k in range(KT):
                b.rec("PE", lambda e, h=h, k=k: e.matmul(
                    psC[0:128, 0:16], xbc(k)[:, h * 128:(h + 1) * 128],
                    initBTb.rearrange("p (k c) -> p k c", k=KT)[:, k, :],
                    start=(k == 0), stop=(k == KT - 1)))
                b.rec("PE", lambda e, h=h, k=k, par=par: e.matmul(
                    psC[0:128, 512:528], vch(par, k)[:, h * 128:(h + 1) * 128],
                    initAb.rearrange("p (k c) -> p k c", k=KT)[:, k, :],
                    start=(k == 0), stop=(k == KT - 1)), inc=(s_pe if k == KT - 1 else None))
            b.wait("DVE", s_pe)
            b.rec("DVE", lambda e, h=h: e.tensor_copy(projin[:, h * 16:(h + 1) * 16], psC[0:128, 0:16]), inc=s_dve)
            b.rec("DVE", lambda e, h=h: e.tensor_copy(projerr[:, h * 16:(h + 1) * 16], psC[0:128, 512:528]), inc=s_dve)
        pr_t = tok(s_dve)
        b.wait_many("PE", [pr_t, vt_t, xT_t])
        for h in range(2):
            for nn2 in range(2):
                b.rec("PE", lambda e, h=h, nn2=nn2: e.matmul(
                    psC[0:16, nn2 * 512:(nn2 + 1) * 512], projin[:, h * 16:(h + 1) * 16],
                    VT(h)[:, nn2 * 512:(nn2 + 1) * 512], start=(h == 0), stop=(h == 1)))
        for h in range(2):
            for m in range(KT):
                b.rec("PE", lambda e, m=m, h=h: e.matmul(
                    psD[0:128, m * 16:(m + 1) * 16],
                    xT(h)[:, m * 128:(m + 1) * 128], projerr[:, h * 16:(h + 1) * 16],
                    start=(h == 0 and m == 0), stop=(h == 1 and m == KT - 1)),
                    inc=(s_pe if (m == KT - 1 and h == 1) else None))
        b.wait("DVE", s_pe)
        b.rec("DVE", lambda e, c=c: e.tensor_copy(dab(c), psC[0:16, 0:1024]), inc=s_dve)
        b.rec("DVE", lambda e, c=c: e.tensor_copy(
            ABF[:, DBT_o + c * 128:DBT_o + (c + 1) * 128], psD[0:128, 0:128]), inc=s_dve)
    d_t = tok(s_dve)
    ws.consumed(0, tok(s_pe))
    ws.consumed(1, tok(s_pe))

    accA = AF32[0:16, ACCA_o:ACCA_o + 1024]
    accBT = AF32[:, ACCB_o:ACCB_o + 128]
    b.wait_many("DVE", [d_t])
    b.rec("DVE", lambda e: e.memset(accA, 0.0), inc=s_dve)
    b.rec("DVE", lambda e: e.memset(accBT, 0.0), inc=s_dve)
    for c in range(CH):
        b.wait("DVE", s_dve)
        b.rec("DVE", lambda e, c=c: e.tensor_tensor(accA, accA, dab(c), OP.add), inc=s_dve)
        b.rec("DVE", lambda e, c=c: e.tensor_tensor(
            accBT, accBT, ABF[:, DBT_o + c * 128:DBT_o + (c + 1) * 128], OP.add), inc=s_dve)
    sum_t = tok(s_dve)
    b.wait_many("SP", [sum_t])
    dma("SP", s_sp, cc_srcA, accA)
    dma("SP", s_sp, cc_srcB, accBT)
    b.wait("POOL", s_sp)
    b.rec("POOL", lambda e: e.collective_compute(
        "AllGather", OP.bypass, replica_groups=[list(range(8))],
        ins=[cc_srcA[:].opt()], outs=[cc_dstA[:].opt()]), inc=s_cc)
    b.rec("POOL", lambda e: e.collective_compute(
        "AllGather", OP.bypass, replica_groups=[list(range(8))],
        ins=[cc_srcB[:].opt()], outs=[cc_dstB[:].opt()]), inc=s_cc)
    cc_ab = tok(s_cc)

    b.wait_many("DVE", [cc_ab])
    b.rec("DVE", lambda e: e.memset(accA, 0.0), inc=s_dve)
    b.rec("DVE", lambda e: e.memset(accBT, 0.0), inc=s_dve)
    for r in range(8):
        b.wait_many("SP", [cc_ab, tok(s_dve)])
        dma("SP", s_sp, GA, cc_dstA[r * 16:(r + 1) * 16, :])
        dma("SP", s_sp, GB, cc_dstB[r * 128:(r + 1) * 128, :])
        b.wait("DVE", s_sp)
        b.rec("DVE", lambda e, r=r: e.scalar_tensor_tensor(
            accA, GA, maskbc[0:16, r:r + 1], accA, OP.mult, OP.add), inc=s_dve)
        b.rec("DVE", lambda e, r=r: e.scalar_tensor_tensor(
            accBT, GB, maskbc[0:128, r:r + 1], accBT, OP.mult, OP.add), inc=s_dve)
    acc_t = tok(s_dve)

    midv = lambda c: ABF[0:16, MID_o + c * 256:MID_o + (c + 1) * 256]
    B2Tb = ABF[:, B2T_o:B2T_o + 128]
    coefcol = AF32[:, SM_o:SM_o + 1]
    sq16 = AF32[0:16, SQ_o:SQ_o + 1024]
    sq128 = AF32[:, B1_o:B1_o + 128]
    b.wait_many("DVE", [init_tf])
    for c in range(CH):
        dep = acc_t if c == 0 else tok(s_dve)
        b.wait_many("ACT", [dep])
        b.rec("ACT", lambda e: e.activation(out=sq16, in_=accA, func=AF.Square), inc=s_act)
        b.rec("ACT", lambda e: e.activation(out=sq128, in_=accBT, func=AF.Square), inc=s_act)
        b.wait("DVE", s_act)
        b.rec("DVE", lambda e: e.tensor_reduce(AF32[0:16, SM_o + 8:SM_o + 9], sq16, AX.X, OP.add), inc=s_dve)
        b.rec("DVE", lambda e: e.tensor_reduce(AF32[0:128, SM_o + 9:SM_o + 10], sq128, AX.X, OP.add), inc=s_dve)
        b.wait("PE", s_dve)
        b.rec("PE", lambda e: e.matmul(psD[0:1, 0:1], ones_col[0:16, :], AF32[0:16, SM_o + 8:SM_o + 9],
                                       start=True, stop=False))
        b.rec("PE", lambda e: e.matmul(psD[0:1, 0:1], ones_col, AF32[0:128, SM_o + 9:SM_o + 10],
                                       start=False, stop=True), inc=s_pe)
        b.wait("ACT", s_pe)
        b.rec("ACT", lambda e: e.activation(out=AF32[0:1, SM_o + 10:SM_o + 11], in_=psD[0:1, 0:1],
                                            func=AF.Sqrt, scale=LR * LR), inc=s_act)
        b.wait("DVE", s_act)
        b.rec("DVE", lambda e: e.tensor_scalar(AF32[0:1, SM_o + 11:SM_o + 12],
                                               AF32[0:1, SM_o + 10:SM_o + 11], 1e-6, None, OP.add), inc=s_dve)
        b.wait("DVE", s_dve)
        b.rec("DVE", lambda e: e.reciprocal(AF32[0:1, SM_o + 12:SM_o + 13],
                                            AF32[0:1, SM_o + 11:SM_o + 12]), inc=s_dve)
        b.wait("DVE", s_dve)
        b.rec("DVE", lambda e: e.tensor_scalar(AF32[0:1, SM_o + 13:SM_o + 14],
                                               AF32[0:1, SM_o + 12:SM_o + 13], 1.0, None, OP.min), inc=s_dve)
        b.wait("PE", s_dve)
        b.rec("PE", lambda e: e.matmul(psD[0:128, 4:5], ones_row, AF32[0:1, SM_o + 13:SM_o + 14],
                                       start=True, stop=True), inc=s_pe)
        b.wait("DVE", s_pe)
        b.rec("DVE", lambda e: e.tensor_scalar(coefcol, psD[0:128, 4:5], -LR, None, OP.mult), inc=s_dve)
        b.wait("DVE", s_dve)
        b.rec("DVE", lambda e, c=c: e.tensor_copy(ABF[0:16, TMP_o:TMP_o + 1024], dab(c)), inc=s_dve)
        b.wait("DVE", s_dve)
        b.rec("DVE", lambda e, c=c: e.scalar_tensor_tensor(
            dab(c), accA, coefcol[0:16, :], initAT_f, OP.mult, OP.add), inc=s_dve)
        b.rec("DVE", lambda e, c=c: e.scalar_tensor_tensor(
            B2Tb, accBT, coefcol[0:128, :], initBT_f, OP.mult, OP.add), inc=s_dve)
        ab_t = tok(s_dve)
        b.wait("DVE", s_dve)
        b.rec("DVE", lambda e, c=c: e.tensor_tensor(accA, accA, ABF[0:16, TMP_o:TMP_o + 1024], OP.add), inc=s_dve)
        b.rec("DVE", lambda e, c=c: e.tensor_tensor(
            accBT, accBT, ABF[:, DBT_o + c * 128:DBT_o + (c + 1) * 128], OP.add), inc=s_dve)
        b.wait("DVE", s_pe)
        b.wait_many("DVE", [x_t])
        for k in range(KT):
            b.rec("DVE", lambda e, k=k, c=c: e.tensor_copy(
                xbc(k)[:, 0:256], xs(k)[:, c * 256:c * 256 + 256]), inc=s_dve)
        xc2_t = tok(s_dve)
        b.wait_many("PE", [ab_t, xc2_t])
        for k in range(KT):
            b.rec("PE", lambda e, k=k, lt=(k == KT - 1): e.matmul(
                psC[0:16, 0:256], B2Tb[:, k * 16:(k + 1) * 16], xbc(k)[:, 0:256],
                start=(k == 0), stop=lt), inc=(s_pe if k == KT - 1 else None))
        b.wait("ACT", s_pe)
        b.rec("ACT", lambda e, c=c: e.activation(out=midv(c), in_=psC[0:16, 0:256],
                                                 func=AF.Copy, scale=2.0), inc=s_act)
        b.wait("DVE", s_pe)
    mid_t = tok(s_act)
    a2_t = tok(s_dve)

    wb0, wb0_t, wb0p = ws.load(di["WbT"][:, 0:512], KT)
    wb1, wb1_t, wb1p = ws.load(di["WbT"][:, 512:1024], KT)
    def wbT(k, m):
        return wb0(k, m * 128, 128) if m < 4 else wb1(k, (m - 4) * 128, 128)
    ev_prev = {}
    g = 0
    for n in range(4):
        n0 = n * 512
        b.wait_many("DVE", [x_t, mid_t])
        if n >= 1:
            b.wait("DVE", s_pe)
        for k in range(KT):
            b.rec("DVE", lambda e, k=k, n0=n0: e.tensor_copy(
                xbc(k), xs(k)[:, n0:n0 + 512]), inc=s_dve)
        cast_t = tok(s_dve)
        b.wait_many("PE", [cast_t, wb0_t, wb1_t, mid_t, a2_t])
        for m in range(KT):
            ps = PSMAIN[g % 3]
            if g >= 3:
                b.wait("PE", *ev_prev[g - 3])
            for k in range(KT):
                b.rec("PE", lambda e, k=k, m=m, ps=ps: e.matmul(
                    ps[:, 0:512], wbT(k, m), xbc(k), start=(k == 0), stop=False))
            for ci in range(2):
                c = n * 2 + ci
                b.rec("PE", lambda e, c=c, m=m, ci=ci, ps=ps: e.matmul(
                    ps[:, ci * 256:(ci + 1) * 256],
                    dab(c)[:, m * 128:(m + 1) * 128], midv(c),
                    start=False, stop=(ci == 1)), inc=(s_pe if ci == 1 else None))
            b.wait("DVE", s_pe)
            b.rec("DVE", lambda e, m=m, n0=n0, ps=ps: e.tensor_tensor(
                xs(m)[:, n0:n0 + 512], xs(m)[:, n0:n0 + 512], ps[:, 0:512], OP.add), inc=s_dve)
            ev_prev[g] = tok(s_dve)
            g += 1
    ws.consumed(wb0p, tok(s_dve))
    ws.consumed(wb1p, tok(s_dve))
    attn_t = tok(s_dve)
    tap("res1", AF32[:, 0:16384], [attn_t])

    def post_ln(src_tok):
        lt = ln_full(xs, KT, D, 1e-5, True, (r_rstd, r_negmur), wait_toks=[src_tok])
        for n in range(4):
            n0 = n * 512
            b.wait_many("PE", [lt])
            bcast_chunk(r_rstd, n0, 512, B1c)
            bcast_chunk(r_negmur, n0, 512, B2c)
            bc_t = tok(s_dve)
            b.wait_many("DVE", [bc_t])
            for k in range(KT):
                b.rec("DVE", lambda e, k=k, n0=n0: e.tensor_tensor(
                    FT1, xs(k)[:, n0:n0 + 512], B1c, OP.mult), inc=s_dve)
                b.wait("DVE", s_dve)
                b.rec("DVE", lambda e, k=k, n0=n0: e.tensor_tensor(
                    xs(k)[:, n0:n0 + 512], FT1, B2c, OP.add), inc=s_dve)
                b.wait("DVE", s_dve)
        return tok(s_dve)

    st1_t = post_ln(attn_t)
    tap("x1", AF32[:, 0:16384], [st1_t])
    last_t = st1_t

    if n_stages >= 2:
        def xb2(k):
            return ABF[:, k * T:(k + 1) * T]
        def rbuf(k):
            return ABF[:, 16384 + k * T:16384 + (k + 1) * T]
        routb = ABF[:, 32768:32832]
        B_cw = AF32[:, FREE_F32:FREE_F32 + T]
        cwF = AF32[0:8, FREE_F32 + 2048:FREE_F32 + 2048 + T]
        probs_all = AF32[:, SM_o:SM_o + 128]
        cw8_all = AF32[:, SM_o + 128:SM_o + 256]

        b.wait_many("DVE", [st1_t])
        b.wait("DVE", s_pe)
        b.wait("DVE", s_gsl)
        for k in range(KT):
            b.rec("DVE", lambda e, k=k: e.tensor_copy(xb2(k), xs(k)), inc=s_dve)
        sh_t = tok(s_dve)

        b.wait("POOL", s_pe)
        dma("POOL", s_ini, routb.rearrange("p (k c) -> p k c", k=KT),
            di["routerT"].rearrange("(k p) c -> p k c", p=128))
        rt_t = tok(s_ini)

        b.wait_many("PE", [sh_t, rt_t])
        for tt in range(16):
            b.wait("PE", s_dve)
            b.wait("PE", s_act)
            for k in range(KT):
                b.rec("PE", lambda e, tt=tt, k=k: e.matmul(
                    psC[0:128, 0:8], xb2(k)[:, tt * 128:(tt + 1) * 128],
                    routb.rearrange("p (k c) -> p k c", k=KT)[:, k, :],
                    start=(k == 0), stop=(k == KT - 1)), inc=(s_pe if k == KT - 1 else None))
            b.wait("ACT", s_pe)
            b.rec("ACT", lambda e, tt=tt: e.activation(
                out=probs_all[:, tt * 8:(tt + 1) * 8], in_=psC[0:128, 0:8],
                func=AF.Sigmoid), inc=s_act)
            b.wait("DVE", s_act)
            m8 = AF32[:, SM_o + 256:SM_o + 264]
            sA = AF32[:, SM_o + 264:SM_o + 265]
            sB = AF32[:, SM_o + 265:SM_o + 266]
            b.rec("DVE", lambda e, tt=tt: e.max(out=m8, in_=probs_all[:, tt * 8:(tt + 1) * 8]), inc=s_dve)
            b.wait("DVE", s_dve)
            b.rec("DVE", lambda e: e.tensor_tensor(sA, m8[:, 0:1], m8[:, 1:2], OP.add), inc=s_dve)
            b.wait("DVE", s_dve)
            b.rec("DVE", lambda e: e.tensor_scalar(sA, sA, 1e-20, None, OP.add), inc=s_dve)
            b.wait("DVE", s_dve)
            b.rec("DVE", lambda e: e.reciprocal(sB, sA), inc=s_dve)
            b.wait("DVE", s_dve)
            c1 = AF32[:, SM_o + 266:SM_o + 267]
            c2 = AF32[:, SM_o + 267:SM_o + 268]
            b.rec("DVE", lambda e: e.tensor_tensor(c1, m8[:, 0:1], sB, OP.mult), inc=s_dve)
            b.rec("DVE", lambda e: e.tensor_tensor(c2, m8[:, 1:2], sB, OP.mult), inc=s_dve)
            b.wait("DVE", s_dve)
            t8 = AF32[:, SM_o + 268:SM_o + 276]
            b.rec("DVE", lambda e, tt=tt: e.tensor_scalar(
                t8, probs_all[:, tt * 8:(tt + 1) * 8], m8[:, 0:1], c1, OP.is_equal, OP.mult), inc=s_dve)
            b.rec("DVE", lambda e, tt=tt: e.tensor_scalar(
                cw8_all[:, tt * 8:(tt + 1) * 8], probs_all[:, tt * 8:(tt + 1) * 8],
                m8[:, 1:2], c2, OP.is_equal, OP.mult), inc=s_dve)
            b.wait("DVE", s_dve)
            b.rec("DVE", lambda e, tt=tt: e.tensor_tensor(
                cw8_all[:, tt * 8:(tt + 1) * 8], cw8_all[:, tt * 8:(tt + 1) * 8], t8, OP.add), inc=s_dve)
            cw_t = tok(s_dve)
            b.wait_many("PE", [cw_t, const_t])
            b.rec("PE", lambda e, tt=tt: e.transpose(
                psC[0:8, 512:640], cw8_all[:, tt * 8:(tt + 1) * 8], eye_fc), inc=s_pe)
            b.wait("DVE", s_pe)
            b.rec("DVE", lambda e, tt=tt: e.tensor_copy(
                cwF[:, tt * 128:(tt + 1) * 128], psC[0:8, 512:640]), inc=s_dve)
        cwf_t = tok(s_dve)

        ev2 = {}
        g2 = 0
        for ex in range(8):
            b.wait_many("PE", [cwf_t, const_t])
            b.wait("PE", s_dve)
            for n in range(4):
                b.wait("PE", s_dve)
                b.rec("PE", lambda e, ex=ex, n=n: e.matmul(
                    psD[:, 0:512], selc[0:8, ex * 128:(ex + 1) * 128],
                    cwF[:, n * 512:(n + 1) * 512], start=True, stop=True), inc=s_pe)
                b.wait("DVE", s_pe)
                b.wait("DVE", s_pool)
                b.rec("DVE", lambda e, n=n: e.tensor_copy(
                    B_cw[:, n * 512:(n + 1) * 512], psD[:, 0:512]), inc=s_dve)
            bcw_t = tok(s_dve)
            for p in range(4):
                wv, wv_t, wv_par = ws.load(di["w1"][:, ex * D + p * 256: ex * D + (p + 1) * 256], KT)
                b.wait_many("PE", [wv_t, sh_t])
                for m2 in range(2):
                    m = p * 2 + m2
                    for n in range(4):
                        ps = PSMAIN[g2 % 3]
                        prev = ev2.get(g2 - 3)
                        if prev is not None:
                            b.wait("PE", *prev)
                        for k in range(KT):
                            b.rec("PE", lambda e, k=k, m2=m2, n=n, ps=ps, wv=wv: e.matmul(
                                ps[:, 0:512], wv(k, m2 * 128, 128), xb2(k)[:, n * 512:(n + 1) * 512],
                                start=(k == 0), stop=(k == KT - 1)), inc=(s_pe if k == KT - 1 else None))
                        b.wait("ACT", s_pe)
                        b.rec("ACT", lambda e, m=m, n=n, ps=ps: e.activation(
                            out=rbuf(m)[:, n * 512:(n + 1) * 512], in_=ps[:, 0:512],
                            func=AF.Relu), inc=s_act)
                        ev2[g2] = tok(s_act)
                        g2 += 1
                ws.consumed(wv_par, tok(s_act))
            relu_t = tok(s_act)
            b.wait_many("DVE", [relu_t])
            for k in range(KT):
                b.rec("DVE", lambda e, k=k: e.tensor_tensor(rbuf(k), rbuf(k), rbuf(k), OP.mult), inc=s_dve)
            sq_t = tok(s_dve)
            b.wait_many("DVE", [sq_t, bcw_t])
            for k in range(KT):
                b.rec("DVE", lambda e, k=k: e.tensor_tensor(rbuf(k), rbuf(k), B_cw, OP.mult), inc=s_dve)
            a_t = tok(s_dve)
            for p in range(4):
                wv, wv_t, wv_par = ws.load(di["w2"][ex * D:(ex + 1) * D, p * 256:(p + 1) * 256], KT)
                b.wait_many("PE", [wv_t, a_t])
                for m2 in range(2):
                    m = p * 2 + m2
                    for n in range(4):
                        ps = PSMAIN[g2 % 3]
                        prev = ev2.get(g2 - 3)
                        if prev is not None:
                            b.wait("PE", *prev)
                        for k in range(KT):
                            b.rec("PE", lambda e, k=k, m2=m2, n=n, ps=ps, wv=wv: e.matmul(
                                ps[:, 0:512], wv(k, m2 * 128, 128), rbuf(k)[:, n * 512:(n + 1) * 512],
                                start=(k == 0), stop=(k == KT - 1)), inc=(s_pe if k == KT - 1 else None))
                        b.wait("DVE", s_pe)
                        b.rec("DVE", lambda e, m=m, n=n, ps=ps: e.tensor_tensor(
                            xs(m)[:, n * 512:(n + 1) * 512], xs(m)[:, n * 512:(n + 1) * 512],
                            ps[:, 0:512], OP.add), inc=s_dve)
                        ev2[g2] = tok(s_dve)
                        g2 += 1
                ws.consumed(wv_par, tok(s_dve))
        moe_t = tok(s_dve)
        tap("res2", AF32[:, 0:16384], [moe_t])
        st2_t = post_ln(moe_t)
        tap("x2", AF32[:, 0:16384], [st2_t])
        last_t = st2_t

    if n_stages >= 3:
        DTv = AF32[0:64, 16384:16384 + T]
        BMv = AF32[0:64, 18432:18432 + T]
        CMv = AF32[0:64, 20480:20480 + T]
        SSv = AF32[0:64, 22528:22528 + T]
        XM_o = 0
        def xmt(k):
            return ABF[:, XM_o + k * T:XM_o + (k + 1) * T]
        HCH_o = 32768
        def hch(k):
            return ABF[:, HCH_o + k * 512:HCH_o + (k + 1) * 512]
        SGD_o = 36864
        DTW_o = 32768
        S2IW_o = 43008
        YN_o = 40960
        SGI_o = 32768
        halo_pre = kcb[:, 130:162]
        halo_save = kcb[:, 162:194]
        masksum = kc[:, 320:321]
        halo_x = kc[:, 328:344]
        halo_sq = kc[:, 344:360]
        halo_bc = kc[:, 360:362]
        halo_h = kcb[:, 194:210]

        b.wait_many("SP", [st2_t])
        dma("SP", s_sp, cc_srcH,
            AF32[:, 0:16384].rearrange("p (k t) -> p k t", k=KT)[:, :, T - 2:T])
        b.wait("POOL", s_sp)
        b.rec("POOL", lambda e: e.collective_compute(
            "AllGather", OP.bypass, replica_groups=[list(range(8))],
            ins=[cc_srcH[:].opt()], outs=[cc_dstH[:].opt()]), inc=s_cc)
        cc_h = tok(s_cc)
        b.wait_many("DVE", [const_t])
        b.rec("DVE", lambda e: e.tensor_reduce(masksum, maskbc, AX.X, OP.add), inc=s_dve)
        b.wait("POOL", *cc_h)
        b.wait("POOL", s_dve)
        gslH = ABF[:, SGD_o:SGD_o + 128]
        gh_f = AF32[:, 24448:24448 + 128]
        dma("POOL", s_gsl, gh_f.rearrange("p (r c) -> p r c", r=8),
            cc_dstH.rearrange("(r p) c -> p r c", p=128))
        ghf_t = tok(s_gsl)
        b.wait_many("DVE", [ghf_t])
        b.rec("DVE", lambda e: e.tensor_tensor(
            gh_f.rearrange("p (r c) -> p r c", r=8),
            gh_f.rearrange("p (r c) -> p r c", r=8),
            maskbc.rearrange("p (r o) -> p r o", o=1).broadcast_to((128, 8, 16)), OP.mult), inc=s_dve)
        b.wait("DVE", s_dve)
        b.rec("DVE", lambda e: e.tensor_reduce(
            halo_x, gh_f.rearrange("p (r c) -> p c r", r=8), AX.X, OP.add), inc=s_dve)
        halo_t = tok(s_dve)
        b.wait_many("ACT", [halo_t])
        b.rec("ACT", lambda e: e.activation(out=halo_sq, in_=halo_x, func=AF.Square), inc=s_act)
        b.wait("PE", s_act)
        b.wait("PE", s_dve)
        b.rec("PE", lambda e: e.matmul(psD[0:1, 0:16], ones_col, halo_sq, start=True, stop=True), inc=s_pe)
        b.wait("DVE", s_pe)
        b.rec("DVE", lambda e: e.tensor_reduce(
            AF32[0:1, 24400:24402], psD[0:1, 0:16].rearrange("p (k c) -> p c k", k=KT),
            AX.X, OP.add), inc=s_dve)
        b.wait("ACT", s_dve)
        b.rec("ACT", lambda e: e.activation(out=AF32[0:1, 24402:24404], in_=AF32[0:1, 24400:24402],
                                            func=AF.Ln, bias=1e-6, scale=1.0 / D), inc=s_act)
        b.wait("ACT", s_act)
        b.rec("ACT", lambda e: e.activation(out=AF32[0:1, 24402:24404], in_=AF32[0:1, 24402:24404],
                                            func=AF.Exp, scale=-0.5), inc=s_act)
        b.wait("PE", s_act)
        b.rec("PE", lambda e: e.matmul(psD[0:128, 16:18], ones_row, AF32[0:1, 24402:24404],
                                       start=True, stop=True), inc=s_pe)
        b.wait("DVE", s_pe)
        b.rec("DVE", lambda e: e.tensor_copy(AF32[:, 24404:24406], psD[0:128, 16:18]), inc=s_dve)
        b.wait("DVE", s_dve)
        b.rec("DVE", lambda e: e.tensor_tensor(
            halo_x.rearrange("p (k c) -> p k c", k=KT),
            halo_x.rearrange("p (k c) -> p k c", k=KT),
            AF32[:, 24404:24406].rearrange("p (o c) -> p o c", o=1).broadcast_to((128, 8, 2)), OP.mult), inc=s_dve)
        b.wait("DVE", s_dve)
        b.rec("DVE", lambda e: e.tensor_tensor(
            halo_h.rearrange("p (k c) -> p k c", k=KT),
            halo_x.rearrange("p (k c) -> p k c", k=KT),
            n1wc.rearrange("p (k o) -> p k o", o=1).broadcast_to((128, 8, 2)), OP.mult), inc=s_dve)
        haloh_t = tok(s_dve)

        b.wait("SP", s_act)
        b.wait("SP", s_dve)
        dma("SP", s_sp, AF32[:, 24512:24544], di["ipb"])
        ipb_t = tok(s_sp)
        ipbc = AF32[:, 24512:24544]
        r_rstd3 = AF32[0:1, 19456:19456 + T]
        rms1_t = ln_full(xs, KT, D, 1e-6, False, (r_rstd3, None), wait_toks=[st2_t])

        B1c3 = AF32[:, 18432:18432 + 512]
        FT3 = AF32[:, 18432 + 512:18432 + 1024]
        ga = 0
        ev3 = {}
        sg_par = 0
        b.wait_many("ACT", [ipb_t])
        for n in range(4):
            n0 = n * 512
            b.wait_many("PE", [rms1_t])
            bcast_chunk(r_rstd3, n0, 512, B1c3)
            bc3_t = tok(s_dve)
            b.wait_many("DVE", [bc3_t])
            b.wait("DVE", s_pe)
            for k in range(KT):
                b.rec("DVE", lambda e, k=k, n0=n0: e.tensor_tensor(
                    FT3, xs(k)[:, n0:n0 + 512], B1c3, OP.mult), inc=s_dve)
                b.wait("DVE", s_dve)
                b.rec("DVE", lambda e, k=k: e.tensor_scalar(
                    hch(k), FT3, n1wc[:, k:k + 1], None, OP.mult), inc=s_dve)
                b.wait("DVE", s_dve)
            hc_t = tok(s_dve)
            for p in range(16):
                wv, wv_t, wv_par = ws.load(di["ipw"][:, p * 256:(p + 1) * 256], KT)
                b.wait_many("PE", [wv_t, hc_t])
                if n == 0 and p < 8:
                    b.wait_many("PE", [haloh_t, const_t])
                for m2 in range(2):
                    m = p * 2 + m2
                    ps = PSMAIN[ga % 3]
                    prev = ev3.get(ga - 3)
                    if prev is not None:
                        b.wait("PE", *prev)
                    for k in range(KT):
                        b.rec("PE", lambda e, k=k, m2=m2, ps=ps, wv=wv: e.matmul(
                            ps[:, 0:512], wv(k, m2 * 128, 128), hch(k),
                            start=(k == 0), stop=(k == KT - 1)), inc=(s_pe if k == KT - 1 else None))
                    if m < 16:
                        b.wait("ACT", s_pe)
                        b.rec("ACT", lambda e, m=m, n0=n0, ps=ps: e.activation(
                            out=xmt(m)[:, n0:n0 + 512], in_=ps[:, 0:512],
                            func=AF.Identity, bias=ipbc[:, m:m + 1]), inc=s_act)
                        ev3[ga] = tok(s_act)
                    else:
                        spar = sg_par % 2
                        sgd = ABF[:, SGD_o + spar * 512:SGD_o + spar * 512 + 512]
                        b.wait("ACT", s_pe)
                        b.wait("ACT", s_sg[spar], 16 * (sg_par // 2))
                        b.rec("ACT", lambda e, ps=ps, sgd=sgd, m=m: e.activation(
                            out=sgd, in_=ps[:, 0:512], func=AF.Sigmoid,
                            bias=ipbc[:, m:m + 1]), inc=s_act)
                        ev3[ga] = tok(s_act)
                        b.wait_many("ACT", [ev3[ga]])
                        dma("ACT", s_sg[spar], scr_sg[(m - 16) * 128:(m - 16) * 128 + 128, n0:n0 + 512], sgd)
                        sg_par += 1
                    ga += 1
                if n == 0 and p < 8:
                    for m2 in range(2):
                        m = p * 2 + m2
                        b.wait("PE", s_dve)
                        for k in range(KT):
                            b.rec("PE", lambda e, k=k, m2=m2, wv=wv: e.matmul(
                                psD[0:128, 20:22], wv(k, m2 * 128, 128),
                                halo_h.rearrange("p (k c) -> p k c", k=KT)[:, k, :],
                                start=(k == 0), stop=(k == KT - 1)), inc=(s_pe if k == KT - 1 else None))
                        b.wait("DVE", s_pe)
                        b.wait_many("DVE", [ipb_t])
                        b.rec("DVE", lambda e, m=m: e.tensor_scalar(
                            halo_pre[:, m * 2:(m + 1) * 2], psD[0:128, 20:22],
                            ipbc[:, m:m + 1], masksum, OP.add, OP.mult), inc=s_dve)
                if n == 0 and p < 8:
                    ws.consumed(wv_par, tok(s_dve))
                else:
                    ws.consumed(wv_par, tok(s_act))
        inp_t = tok(s_act)
        inp_td = tok(s_dve)

        b.wait("POOL", s_pe)
        dma("POOL", s_ini, ABF[:, DTW_o:DTW_o + 1024].rearrange("p (k c) -> p k c", k=16),
            di["dtw"].rearrange("(k p) c -> p k c", p=128))
        dma("POOL", s_ini, ABF[:, DTW_o + 1024:DTW_o + 2048].rearrange("p (k c) -> p k c", k=16),
            di["bpw"].rearrange("(k p) c -> p k c", p=128))
        dma("POOL", s_ini, ABF[:, DTW_o + 2048:DTW_o + 3072].rearrange("p (k c) -> p k c", k=16),
            di["cpw"].rearrange("(k p) c -> p k c", p=128))
        b.wait("SP", s_act)
        b.wait("SP", s_pe)
        dma("SP", s_sp, AF32[:, 24528:24528 + 48], di["convw"])
        dma("SP", s_sp, AF32[:, 24512:24512 + 16], di["convb"])
        dma("SP", s_sp, AF32[0:64, 24448:24449], di["dtb"])
        dma("SP", s_sp, AF32[0:64, 24449:24450], di["bpb"])
        dma("SP", s_sp, AF32[0:64, 24450:24451], di["cpb"])
        w3_t = tok(s_ini); w3f_t = tok(s_sp)
        convw_c = AF32[:, 24528:24528 + 48]
        convb_c = AF32[:, 24512:24512 + 16]
        CS1 = ABF[:, SGD_o + 1024:SGD_o + 1536]
        CS2 = ABF[:, SGD_o + 1536:SGD_o + 2048]
        silu_hist = []
        for c in range(8):
            t0 = c * 256
            for kk in range(16):
                xk = xmt(kk)
                w0 = convw_c[:, kk * 3:kk * 3 + 1]
                w1_ = convw_c[:, kk * 3 + 1:kk * 3 + 2]
                w2_ = convw_c[:, kk * 3 + 2:kk * 3 + 3]
                csA = (CS1 if kk % 2 == 0 else CS2)[:, 0:256]
                csB = (CS1 if kk % 2 == 0 else CS2)[:, 256:512]
                b.wait_many("DVE", [inp_t, inp_td, w3f_t])
                if len(silu_hist) >= 2:
                    b.wait("DVE", s_act, silu_hist[-2])
                hsrc = halo_pre if c == 0 else halo_save
                b.rec("DVE", lambda e, hsrc=hsrc, w0=w0, csA=csA, kk=kk: e.tensor_scalar(
                    csA[:, 0:2], hsrc[:, kk * 2:(kk + 1) * 2], w0, None, OP.mult), inc=s_dve)
                b.rec("DVE", lambda e, xk=xk, w0=w0, csA=csA, t0=t0: e.tensor_scalar(
                    csA[:, 2:256], xk[:, t0:t0 + 254], w0, None, OP.mult), inc=s_dve)
                b.rec("DVE", lambda e, hsrc=hsrc, w1_=w1_, csB=csB, kk=kk: e.tensor_scalar(
                    csB[:, 0:1], hsrc[:, kk * 2 + 1:kk * 2 + 2], w1_, None, OP.mult), inc=s_dve)
                b.rec("DVE", lambda e, xk=xk, w1_=w1_, csB=csB, t0=t0: e.tensor_scalar(
                    csB[:, 1:256], xk[:, t0:t0 + 255], w1_, None, OP.mult), inc=s_dve)
                b.wait("DVE", s_dve)
                b.rec("DVE", lambda e, csA=csA, csB=csB: e.tensor_tensor(
                    csA[:, 0:256], csA[:, 0:256], csB[:, 0:256], OP.add), inc=s_dve)
                b.wait("DVE", s_dve)
                b.rec("DVE", lambda e, xk=xk, w2_=w2_, csB=csB, t0=t0: e.tensor_scalar(
                    csB[:, 0:256], xk[:, t0:t0 + 256], w2_, None, OP.mult), inc=s_dve)
                b.wait("DVE", s_dve)
                b.rec("DVE", lambda e, csA=csA, csB=csB: e.tensor_tensor(
                    csA[:, 0:256], csA[:, 0:256], csB[:, 0:256], OP.add), inc=s_dve)
                pool_t = tok(s_dve)
                b.rec("DVE", lambda e, xk=xk, t0=t0, kk=kk: e.tensor_copy(
                    halo_save[:, kk * 2:(kk + 1) * 2], xk[:, t0 + 254:t0 + 256]), inc=s_dve)
                hs_t = tok(s_dve)
                b.wait_many("ACT", [pool_t, hs_t])
                b.rec("ACT", lambda e, xk=xk, csA=csA, t0=t0, kk=kk: e.activation(
                    out=xk[:, t0:t0 + 256], in_=csA[:, 0:256], func=AF.Silu,
                    bias=convb_c[:, kk:kk + 1]), inc=s_act)
                silu_hist.append(s_act.n)
            conv_t = tok(s_act)
            b.wait_many("PE", [conv_t, w3_t])
            b.wait("PE", s_dve)
            for kk in range(16):
                b.rec("PE", lambda e, kk=kk, t0=t0: e.matmul(
                    psC[0:64, 0:256], ABF[:, DTW_o + kk * 64:DTW_o + (kk + 1) * 64],
                    xmt(kk)[:, t0:t0 + 256], start=(kk == 0), stop=(kk == 15)))
                b.rec("PE", lambda e, kk=kk, t0=t0: e.matmul(
                    psC[0:64, 512:768], ABF[:, DTW_o + 1024 + kk * 64:DTW_o + 1024 + (kk + 1) * 64],
                    xmt(kk)[:, t0:t0 + 256], start=(kk == 0), stop=(kk == 15)))
                b.rec("PE", lambda e, kk=kk, t0=t0: e.matmul(
                    psD[0:64, 0:256], ABF[:, DTW_o + 2048 + kk * 64:DTW_o + 2048 + (kk + 1) * 64],
                    xmt(kk)[:, t0:t0 + 256], start=(kk == 0), stop=(kk == 15)),
                    inc=(s_pe if kk == 15 else None))
            b.wait("ACT", s_pe)
            b.rec("ACT", lambda e, t0=t0: e.activation(
                out=DTv[:, t0:t0 + 256], in_=psC[0:64, 0:256], func=AF.Sigmoid,
                bias=AF32[0:64, 24448:24449]), inc=s_act)
            b.rec("ACT", lambda e, t0=t0: e.activation(
                out=BMv[:, t0:t0 + 256], in_=psC[0:64, 512:768], func=AF.Identity,
                bias=AF32[0:64, 24449:24450]), inc=s_act)
            b.rec("ACT", lambda e, t0=t0: e.activation(
                out=CMv[:, t0:t0 + 256], in_=psD[0:64, 0:256], func=AF.Identity,
                bias=AF32[0:64, 24450:24451]), inc=s_act)
            b.wait("DVE", s_act)
            b.wait("PE", s_act)
        proj_t = tok(s_act)

        b.wait_many("DVE", [proj_t])
        b.rec("DVE", lambda e: e.tensor_tensor(BMv, DTv, BMv, OP.mult), inc=s_dve)
        b.wait("DVE", s_dve)
        b.rec("DVE", lambda e: e.tensor_scalar(DTv, DTv, -1.0, 1.0, OP.mult, OP.add), inc=s_dve)
        b.wait("DVE", s_dve)
        b.rec("DVE", lambda e: e.tensor_tensor_scan(SSv, DTv, BMv, 0.0, OP.mult, OP.add), inc=s_dve)
        b.wait("DVE", s_dve)
        b.rec("DVE", lambda e: e.tensor_tensor_scan(
            BMv, DTv, kc[0:64, 176:177].broadcast_to((64, T)), 1.0, OP.mult, OP.mult), inc=s_dve)
        scan_t = tok(s_dve)
        b.wait_many("SP", [scan_t])
        dma("SP", s_sp, cc_srcS, SSv[:, T - 1:T])
        b.wait("POOL", s_sp)
        b.rec("POOL", lambda e: e.collective_compute(
            "AllGather", OP.bypass, replica_groups=[list(range(8))],
            ins=[cc_srcS[:].opt()], outs=[cc_dstS[:].opt()]), inc=s_cc)
        b.wait("POOL", s_cc)
        b.wait("POOL", s_dve)
        dma("POOL", s_gsl, AF32[0:64, 16384:16392].rearrange("p (r o) -> p r o", r=8),
            cc_dstS.rearrange("(r p) o -> p r o", p=64), allow_slow_non_contiguous=True)
        gs_t = tok(s_gsl)
        b.wait_many("DVE", [gs_t])
        b.rec("DVE", lambda e: e.tensor_tensor(
            AF32[0:64, 16384:16392], AF32[0:64, 16384:16392],
            maskbc[0:64, :], OP.mult), inc=s_dve)
        b.wait("DVE", s_dve)
        b.rec("DVE", lambda e: e.tensor_reduce(
            AF32[0:64, 16392:16393], AF32[0:64, 16384:16392], AX.X, OP.add), inc=s_dve)
        b.wait("DVE", s_dve)
        b.rec("DVE", lambda e: e.scalar_tensor_tensor(
            SSv, BMv, AF32[0:64, 16392:16393], SSv, OP.mult, OP.add), inc=s_dve)
        b.wait("DVE", s_dve)
        b.rec("DVE", lambda e: e.tensor_tensor(CMv, CMv, SSv, OP.mult), inc=s_dve)
        y_t = tok(s_dve)

        r64a = AF32[0:1, 16384:16384 + T]
        r64b = AF32[0:1, 18432:18432 + T]
        sq64 = AF32[0:64, 22528:22528 + 512]
        nn64 = 4
        for n in range(nn64):
            n0 = n * 512
            b.wait_many("ACT", [y_t])
            b.wait("ACT", s_pe)
            b.rec("ACT", lambda e, n0=n0: e.activation(out=sq64, in_=CMv[:, n0:n0 + 512],
                                                       func=AF.Square), inc=s_act)
            b.wait("PE", s_act)
            b.wait_many("PE", [y_t])
            b.wait("PE", s_dve)
            b.rec("PE", lambda e, n0=n0: e.matmul(psD[0:1, 0:512], ones_col[0:64, :],
                                                  CMv[:, n0:n0 + 512], start=True, stop=True))
            b.rec("PE", lambda e, n0=n0: e.matmul(psC[0:1, 0:512], ones_col[0:64, :],
                                                  sq64, start=True, stop=True), inc=s_pe)
            b.wait("DVE", s_pe)
            b.rec("DVE", lambda e, n0=n0: e.tensor_scalar(
                r64b[:, n0:n0 + 512], psD[0:1, 0:512], 1.0 / 64, None, OP.mult), inc=s_dve)
            b.wait("DVE", s_dve)
            b.rec("DVE", lambda e, n0=n0: e.tensor_tensor(
                r64a[:, n0:n0 + 512], r64b[:, n0:n0 + 512], r64b[:, n0:n0 + 512], OP.mult), inc=s_dve)
            b.wait("DVE", s_dve)
            b.rec("DVE", lambda e, n0=n0: e.scalar_tensor_tensor(
                r64a[:, n0:n0 + 512], psC[0:1, 0:512], 1.0 / 64, r64a[:, n0:n0 + 512],
                OP.mult, OP.subtract), inc=s_dve)
        b.wait("ACT", s_dve)
        b.rec("ACT", lambda e: e.activation(out=r64a, in_=r64a, func=AF.Ln, bias=1e-5), inc=s_act)
        b.wait("ACT", s_act)
        b.rec("ACT", lambda e: e.activation(out=r64a, in_=r64a, func=AF.Exp, scale=-0.5), inc=s_act)
        b.wait("DVE", s_act)
        b.rec("DVE", lambda e: e.scalar_tensor_tensor(r64b, r64b, -1.0, r64a, OP.mult, OP.mult), inc=s_dve)
        ln64_t = tok(s_dve)
        ynv = ABF[0:64, YN_o:YN_o + T]
        for n in range(4):
            n0 = n * 512
            b.wait_many("PE", [ln64_t])
            b.wait("PE", s_dve)
            b.wait("PE", s_act)
            b.rec("PE", lambda e, n0=n0: e.matmul(psD[0:64, 0:512], ones_row[:, 0:64], r64a[:, n0:n0 + 512],
                                                  start=True, stop=True))
            b.rec("PE", lambda e, n0=n0: e.matmul(psC[0:64, 0:512], ones_row[:, 0:64], r64b[:, n0:n0 + 512],
                                                  start=True, stop=True), inc=s_pe)
            b.wait("DVE", s_pe)
            b.rec("DVE", lambda e, n0=n0: e.tensor_tensor(
                SSv[:, 0:512], CMv[:, n0:n0 + 512], psD[0:64, 0:512], OP.mult), inc=s_dve)
            b.wait("DVE", s_dve)
            b.rec("DVE", lambda e, n0=n0: e.tensor_tensor(
                ynv[:, n0:n0 + 512], SSv[:, 0:512], psC[0:64, 0:512], OP.add), inc=s_dve)
            b.wait("DVE", s_dve)
        yn_t = tok(s_dve)

        b.wait("POOL", s_pe)
        dma("POOL", s_ini, ABF[0:64, S2IW_o:S2IW_o + 2048], di["s2iw"])
        s2i_t = tok(s_ini)
        b.wait("SP", s_pool)
        b.wait("SP", s_act)
        b.wait("SP", s_pe)
        b.wait("SP", s_dve)
        dma("SP", s_sp, AF32[:, 19968:19968 + 16], di["s2ib"])
        dma("SP", s_sp, AF32[:, 19984:19984 + 16], di["Dpar"])
        dma("SP", s_sp, AF32[:, 20000:20000 + 8], di["outb"])
        s2ib_c = AF32[:, 19968:19968 + 16]
        dpar_c = AF32[:, 19984:19984 + 16]
        outb_c = AF32[:, 20000:20000 + 8]
        w4f_t = tok(s_sp)
        scr_h2 = nc.dram_tensor("scr_h2", [D, T], BF16).ap()
        SGI_o3 = 32768
        def y3v(m):
            return ABF[:, SGI_o3 + m * 512:SGI_o3 + (m + 1) * 512]
        Achunk = AF32[:, 20480:20480 + KT * 512]
        def ach(k):
            return AF32[:, 20480 + k * 512:20480 + (k + 1) * 512]
        sq3 = AF32[:, 18432:18432 + 1024]
        r_rstd2 = AF32[0:1, 16384:16384 + T]
        B1rm2 = AF32[:, 19456:19456 + 512]
        ga4 = 0
        ev4 = {}
        for n in range(4):
            n0 = n * 512
            b.wait("POOL", s_pe)
            b.wait("POOL", s_sc)
            b.wait("POOL", s_sg[0])
            b.wait("POOL", s_sg[1])
            dma("POOL", s_gd, ABF[:, SGI_o3:SGI_o3 + 8192].rearrange("p (k t) -> p k t", k=16),
                scr_sg.rearrange("(k p) t -> p k t", p=128)[:, :, n0:n0 + 512])
            sg_t = tok(s_gd)
            for m in range(16):
                ps = PSMAIN[ga4 % 3]
                pstb = AF32[:, 20480 + (m % 2) * 512:20480 + (m % 2) * 512 + 512]
                prev = ev4.get(ga4 - 3)
                b.wait_many("PE", [yn_t, s2i_t])
                if prev is not None:
                    b.wait("PE", *prev)
                b.rec("PE", lambda e, m=m, n0=n0, ps=ps: e.matmul(
                    ps[:, 0:512], ABF[0:64, S2IW_o + m * 128:S2IW_o + (m + 1) * 128],
                    ynv[:, n0:n0 + 512], start=True, stop=True), inc=s_pe)
                b.wait("ACT", s_pe)
                b.wait_many("ACT", [w4f_t])
                if ga4 >= 2:
                    b.wait("ACT", s_dve, ev4[ga4 - 2][1])
                b.rec("ACT", lambda e, m=m, ps=ps, pstb=pstb: e.activation(
                    out=pstb, in_=ps[:, 0:512], func=AF.Identity,
                    bias=s2ib_c[:, m:m + 1]), inc=s_act)
                act_t = tok(s_act)
                b.wait_many("DVE", [act_t, inp_t, sg_t])
                b.rec("DVE", lambda e, m=m, n0=n0, pstb=pstb: e.scalar_tensor_tensor(
                    pstb, xmt(m)[:, n0:n0 + 512], dpar_c[:, m:m + 1], pstb,
                    OP.mult, OP.add), inc=s_dve)
                b.wait("DVE", s_dve)
                b.rec("DVE", lambda e, m=m, pstb=pstb: e.tensor_tensor(
                    y3v(m), y3v(m), pstb, OP.mult), inc=s_dve)
                ev4[ga4] = tok(s_dve)
                ga4 += 1
            y3_t = tok(s_dve)
            for p in range(4):
                wv, wv_t, wv_par = ws.load(di["outw"][:, p * 256:(p + 1) * 256], 16)
                b.wait_many("PE", [wv_t, y3_t])
                for m2 in range(2):
                    m = p * 2 + m2
                    ps = PSMAIN[ga4 % 3]
                    prev = ev4.get(ga4 - 3)
                    if prev is not None:
                        b.wait("PE", *prev)
                    for k in range(16):
                        b.rec("PE", lambda e, k=k, m2=m2, ps=ps, wv=wv: e.matmul(
                            ps[:, 0:512], wv(k, m2 * 128, 128), y3v(k),
                            start=(k == 0), stop=(k == 15)), inc=(s_pe if k == 15 else None))
                    b.wait("DVE", s_pe)
                    b.rec("DVE", lambda e, m=m, n0=n0, ps=ps: e.scalar_tensor_tensor(
                        ach(m), ps[:, 0:512], outb_c[:, m:m + 1], xs(m)[:, n0:n0 + 512],
                        OP.add, OP.add), inc=s_dve)
                    ev4[ga4] = tok(s_dve)
                    ga4 += 1
                ws.consumed(wv_par, tok(s_dve))
            a_t = tok(s_dve)
            b.wait_many("DVE", [a_t])
            for m in range(KT):
                b.rec("DVE", lambda e, m=m, n0=n0: e.tensor_tensor(
                    xs(m)[:, n0:n0 + 512], xs(m)[:, n0:n0 + 512], ach(m), OP.add), inc=s_dve)
            b.wait("PE", s_dve)
            b.wait("PE", s_act)
            pe_h = []
            for k in range(KT):
                sl = sq3[:, (k % 2) * 512:(k % 2) * 512 + 512]
                if len(pe_h) >= 2:
                    b.wait("ACT", s_pe, pe_h[-2])
                b.wait_many("ACT", [a_t])
                b.rec("ACT", lambda e, k=k, sl=sl: e.activation(
                    out=sl, in_=ach(k), func=AF.Square), inc=s_act)
                b.wait("PE", s_act)
                b.rec("PE", lambda e, k=k, sl=sl: e.matmul(
                    psC[0:1, 0:512], ones_col, sl, start=(k == 0), stop=(k == KT - 1)), inc=s_pe)
                pe_h.append(s_pe.n)
            b.wait("DVE", s_pe)
            b.rec("DVE", lambda e, n0=n0: e.tensor_scalar(
                r_rstd2[:, n0:n0 + 512], psC[0:1, 0:512], 1.0 / D, None, OP.mult), inc=s_dve)
            b.wait("ACT", s_dve)
            b.rec("ACT", lambda e, n0=n0: e.activation(
                out=r_rstd2[:, n0:n0 + 512], in_=r_rstd2[:, n0:n0 + 512], func=AF.Ln, bias=1e-6), inc=s_act)
            b.wait("ACT", s_act)
            b.rec("ACT", lambda e, n0=n0: e.activation(
                out=r_rstd2[:, n0:n0 + 512], in_=r_rstd2[:, n0:n0 + 512], func=AF.Exp, scale=-0.5), inc=s_act)
            rr2_t = tok(s_act)
            b.wait_many("PE", [rr2_t])
            b.wait("PE", s_dve)
            b.rec("PE", lambda e, n0=n0: e.matmul(psD[:, 0:512], ones_row,
                                                  r_rstd2[:, n0:n0 + 512], start=True, stop=True), inc=s_pe)
            b.wait("DVE", s_pe)
            b.rec("DVE", lambda e: e.tensor_copy(B1rm2, psD[:, 0:512]), inc=s_dve)
            b.wait("DVE", s_dve)
            for k in range(KT):
                b.rec("DVE", lambda e, k=k: e.tensor_tensor(sq3[:, 0:512], ach(k), B1rm2, OP.mult), inc=s_dve)
                b.wait("DVE", s_dve)
                b.rec("DVE", lambda e, k=k: e.tensor_scalar(
                    y3v(k), sq3[:, 0:512], n2wc[:, k:k + 1], None, OP.mult), inc=s_dve)
                b.wait("DVE", s_dve)
            h2_t = tok(s_dve)
            b.wait_many("ACT", [h2_t])
            dma("ACT", s_sc, scr_h2.rearrange("(k p) t -> p k t", p=128)[:, :, n0:n0 + 512],
                ABF[:, SGI_o3:SGI_o3 + 4096].rearrange("p (k t) -> p k t", k=KT))
        mix_t = tok(s_dve)
        tap("xmix", AF32[:, 0:16384], [mix_t])

        def h2v(k):
            return ABF[:, k * T:(k + 1) * T]
        def ghv(k):
            return ABF[:, 16384 + k * T:16384 + (k + 1) * T]
        cw4Fb = ABF[0:4, 32768:32768 + T]
        eb2ab = ABF[0:4, 34816:34816 + 1024]
        gatewb = ABF[:, 35840:35840 + 32]
        B_cw3 = AF32[:, 18432:18432 + T]
        glog = AF32[:, 20480:20480 + 128]
        cw4_all = AF32[:, 20608:20608 + 64]
        gm8 = AF32[:, 20672:20672 + 8]
        gsm = AF32[:, 20680:20680 + 8]
        cwF4 = AF32[0:4, 22528:22528 + T]
        eb1c = AF32[:, 20704:20704 + 32]

        b.wait_many("SP", [mix_t])
        b.wait("POOL", s_dve)
        b.wait("POOL", s_sc)
        dma("POOL", s_gd, ABF[:, 0:16384].rearrange("p (k t) -> p k t", k=KT),
            scr_h2.rearrange("(k p) t -> p k t", p=128))
        dma("POOL", s_ini, gatewb.rearrange("p (k c) -> p k c", k=KT),
            di["gatew"].rearrange("(k p) c -> p k c", p=128))
        dma("POOL", s_ini, eb2ab, di["eb2all"])
        dma("SP", s_sp, AF32[:, 20696:20700], di["gateb_bc"])
        h2_tt = tok(s_gd); gw_t = tok(s_ini); gb_t = tok(s_sp)

        b.wait_many("DVE", [mix_t])
        b.rec("DVE", lambda e: e.memset(glog, -1e30), inc=s_dve)
        gpad_t = tok(s_dve)
        grow_t = gb_t
        b.wait_many("PE", [h2_tt, gw_t, gpad_t])
        for tt in range(16):
            b.wait("PE", s_dve)
            for k in range(KT):
                b.rec("PE", lambda e, tt=tt, k=k: e.matmul(
                    psC[0:128, 0:4], h2v(k)[:, tt * 128:(tt + 1) * 128],
                    gatewb.rearrange("p (k c) -> p k c", k=KT)[:, k, :],
                    start=(k == 0), stop=(k == KT - 1)), inc=(s_pe if k == KT - 1 else None))
            b.wait("DVE", s_pe)
            b.wait_many("DVE", [grow_t])
            b.rec("DVE", lambda e, tt=tt: e.tensor_tensor(
                glog[:, tt * 8:tt * 8 + 4], psC[0:128, 0:4],
                AF32[:, 20696:20700], OP.add), inc=s_dve)
            b.wait("DVE", s_dve)
            b.rec("DVE", lambda e, tt=tt: e.max(out=gm8, in_=glog[:, tt * 8:(tt + 1) * 8]), inc=s_dve)
            b.wait("DVE", s_dve)
            b.rec("DVE", lambda e: e.tensor_scalar(gsm[:, 0:1], gm8[:, 0:1], -1.0, None, OP.mult), inc=s_dve)
            b.wait("ACT", s_dve)
            b.rec("ACT", lambda e: e.activation(out=gsm[:, 1:2], in_=gm8[:, 1:2], func=AF.Exp,
                                                bias=gsm[:, 0:1]), inc=s_act)
            b.wait("DVE", s_act)
            b.rec("DVE", lambda e: e.tensor_scalar(gsm[:, 2:3], gsm[:, 1:2], 1.0, None, OP.add), inc=s_dve)
            b.wait("DVE", s_dve)
            b.rec("DVE", lambda e: e.reciprocal(gsm[:, 3:4], gsm[:, 2:3]), inc=s_dve)
            b.wait("DVE", s_dve)
            b.rec("DVE", lambda e: e.tensor_tensor(gsm[:, 4:5], gsm[:, 1:2], gsm[:, 3:4], OP.mult), inc=s_dve)
            b.wait("DVE", s_dve)
            t4 = AF32[:, 20692:20696]
            b.rec("DVE", lambda e, tt=tt: e.tensor_scalar(
                t4, glog[:, tt * 8:tt * 8 + 4], gm8[:, 0:1], gsm[:, 3:4], OP.is_equal, OP.mult), inc=s_dve)
            b.rec("DVE", lambda e, tt=tt: e.tensor_scalar(
                cw4_all[:, tt * 4:(tt + 1) * 4], glog[:, tt * 8:tt * 8 + 4],
                gm8[:, 1:2], gsm[:, 4:5], OP.is_equal, OP.mult), inc=s_dve)
            b.wait("DVE", s_dve)
            b.rec("DVE", lambda e, tt=tt: e.tensor_tensor(
                cw4_all[:, tt * 4:(tt + 1) * 4], cw4_all[:, tt * 4:(tt + 1) * 4], t4, OP.add), inc=s_dve)
            cwt_t = tok(s_dve)
            b.wait_many("PE", [cwt_t])
            b.rec("PE", lambda e, tt=tt: e.transpose(
                psC[0:4, 512:640], cw4_all[:, tt * 4:(tt + 1) * 4], eye_fc), inc=s_pe)
            b.wait("DVE", s_pe)
            b.rec("DVE", lambda e, tt=tt: e.tensor_copy(
                cwF4[:, tt * 128:(tt + 1) * 128], psC[0:4, 512:640]), inc=s_dve)
        cwf4_t = tok(s_dve)
        b.wait("DVE", s_dve)
        b.rec("DVE", lambda e: e.tensor_copy(cw4Fb, cwF4), inc=s_dve)
        cwf4b_t = tok(s_dve)

        b.wait_many("PE", [cwf4b_t, gw_t])
        g5 = 0
        ev5 = {}
        for m in range(KT):
            for n in range(4):
                ps = PSMAIN[g5 % 3]
                prev = ev5.get(g5 - 3)
                if prev is not None:
                    b.wait("PE", *prev)
                b.rec("PE", lambda e, m=m, n=n, ps=ps: e.matmul(
                    ps[:, 0:512], eb2ab[:, m * 128:(m + 1) * 128],
                    cw4Fb[:, n * 512:(n + 1) * 512], start=True, stop=True), inc=s_pe)
                b.wait("DVE", s_pe)
                b.rec("DVE", lambda e, m=m, n=n, ps=ps: e.tensor_tensor(
                    xs(m)[:, n * 512:(n + 1) * 512], xs(m)[:, n * 512:(n + 1) * 512],
                    ps[:, 0:512], OP.add), inc=s_dve)
                ev5[g5] = tok(s_dve)
                g5 += 1

        for ex in range(4):
            b.wait_many("PE", [cwf4_t, const_t])
            for n in range(4):
                b.wait("PE", s_dve)
                b.wait("PE", s_pool)
                b.rec("PE", lambda e, ex=ex, n=n: e.matmul(
                    psD[:, 0:512], selc[0:4, ex * 128:(ex + 1) * 128],
                    cwF4[:, n * 512:(n + 1) * 512], start=True, stop=True), inc=s_pe)
                b.wait("DVE", s_pe)
                b.rec("DVE", lambda e, n=n: e.tensor_copy(
                    B_cw3[:, n * 512:(n + 1) * 512], psD[:, 0:512]), inc=s_dve)
            bcw3_t = tok(s_dve)
            b.wait("SP", s_act)
            dma("SP", s_sp, eb1c, di["eb1"][ex])
            eb1_t = tok(s_sp)
            for hidc in range(4):
                for p in range(4):
                    wv, wv_t, wv_par = ws.load(
                        di["ew1"][ex][:, hidc * 1024 + p * 256:hidc * 1024 + (p + 1) * 256], KT)
                    b.wait_many("PE", [wv_t, h2_tt])
                    for m2 in range(2):
                        m = p * 2 + m2
                        for n in range(4):
                            ps = PSMAIN[g5 % 3]
                            prev = ev5.get(g5 - 3)
                            if prev is not None:
                                b.wait("PE", *prev)
                            for k in range(KT):
                                b.rec("PE", lambda e, k=k, m2=m2, n=n, ps=ps, wv=wv: e.matmul(
                                    ps[:, 0:512], wv(k, m2 * 128, 128), h2v(k)[:, n * 512:(n + 1) * 512],
                                    start=(k == 0), stop=(k == KT - 1)), inc=(s_pe if k == KT - 1 else None))
                            b.wait("ACT", s_pe)
                            b.wait_many("ACT", [eb1_t])
                            hm = hidc * 8 + m
                            b.rec("ACT", lambda e, m=m, n=n, ps=ps, hm=hm: e.activation(
                                out=ghv(m)[:, n * 512:(n + 1) * 512], in_=ps[:, 0:512],
                                func=AF.Gelu, bias=eb1c[:, hm:hm + 1]), inc=s_act)
                            ev5[g5] = tok(s_act)
                            g5 += 1
                    ws.consumed(wv_par, tok(s_act))
                gelu_t = tok(s_act)
                b.wait_many("DVE", [gelu_t, bcw3_t])
                for k in range(KT):
                    b.rec("DVE", lambda e, k=k: e.tensor_tensor(ghv(k), ghv(k), B_cw3, OP.mult), inc=s_dve)
                ghs_t = tok(s_dve)
                for p in range(4):
                    wv, wv_t, wv_par = ws.load(
                        di["ew2"][ex][hidc * 1024:(hidc + 1) * 1024, p * 256:(p + 1) * 256], KT)
                    b.wait_many("PE", [wv_t, ghs_t])
                    for m2 in range(2):
                        m = p * 2 + m2
                        for n in range(4):
                            ps = PSMAIN[g5 % 3]
                            prev = ev5.get(g5 - 3)
                            if prev is not None:
                                b.wait("PE", *prev)
                            for k in range(KT):
                                b.rec("PE", lambda e, k=k, m2=m2, n=n, ps=ps, wv=wv: e.matmul(
                                    ps[:, 0:512], wv(k, m2 * 128, 128), ghv(k)[:, n * 512:(n + 1) * 512],
                                    start=(k == 0), stop=(k == KT - 1)), inc=(s_pe if k == KT - 1 else None))
                            b.wait("DVE", s_pe)
                            b.rec("DVE", lambda e, m=m, n=n, ps=ps: e.tensor_tensor(
                                xs(m)[:, n * 512:(n + 1) * 512], xs(m)[:, n * 512:(n + 1) * 512],
                                ps[:, 0:512], OP.add), inc=s_dve)
                            ev5[g5] = tok(s_dve)
                            g5 += 1
                    ws.consumed(wv_par, tok(s_dve))
        moe3_t = tok(s_dve)
        tap("res3", AF32[:, 0:16384], [moe3_t])
        st3_t = post_ln(moe3_t)
        tap("x3", AF32[:, 0:16384], [st3_t])
        last_t = st3_t

    if n_stages >= 4:
        def xb4(k):
            return ABF[:, k * T:(k + 1) * T]
        def guv(k):
            return ABF[:, 16384 + k * T:16384 + (k + 1) * T]
        simsbuf = [AF32[:, 16384:16384 + 512], AF32[:, 16896:16896 + 512]]
        bgc = AF32[:, 19968:19968 + 32]
        buc = AF32[:, 20000:20000 + 32]
        bdc = AF32[:, 20032:20032 + 8]

        b.wait_many("DVE", [st3_t])
        b.wait("DVE", s_pe)
        b.wait("DVE", s_pool)
        for k in range(KT):
            b.rec("DVE", lambda e, k=k: e.tensor_copy(xb4(k), xs(k)), inc=s_dve)
        sh4_t = tok(s_dve)
        b.wait("SP", s_dve)
        b.wait("SP", s_act)
        b.wait("SP", s_pe)
        dma("SP", s_sp, bgc, di["bg"])
        dma("SP", s_sp, buc, di["bu"])
        dma("SP", s_sp, bdc, di["bd"])
        b4_t = tok(s_sp)

        DO_MLP = sub4 in ("all", "mlp")
        DO_PEER = sub4.startswith("peer") or sub4 == "all"
        g6 = 0
        ev6 = {}
        silu_sc = ABF[:, 43520:43520 + 1024]
        for hg in range(4 if DO_MLP else 0):
            for p in range(4):
                wg, wg_t, wg_par = ws.load(di["Wg"][:, hg * 1024 + p * 256:hg * 1024 + (p + 1) * 256], KT)
                wu, wu_t, wu_par = ws.load(di["Wu"][:, hg * 1024 + p * 256:hg * 1024 + (p + 1) * 256], KT)
                b.wait_many("PE", [wg_t, wu_t, sh4_t])
                for m2 in range(2):
                    m = p * 2 + m2
                    hm = hg * 8 + m
                    for n in range(4):
                        ps = PSMAIN[g6 % 2]
                        ssl = silu_sc[:, (g6 % 2) * 512:(g6 % 2) * 512 + 512]
                        prev = ev6.get(g6 - 2)
                        if prev is not None:
                            b.wait("PE", *prev)
                        for k in range(KT):
                            b.rec("PE", lambda e, k=k, m2=m2, n=n, ps=ps, wg=wg: e.matmul(
                                ps[:, 0:512], wg(k, m2 * 128, 128), xb4(k)[:, n * 512:(n + 1) * 512],
                                start=(k == 0), stop=(k == KT - 1)), inc=(s_pe if k == KT - 1 else None))
                        b.wait("ACT", s_pe)
                        b.wait_many("ACT", [b4_t])
                        if g6 >= 2:
                            b.wait("ACT", s_dve, ev6[g6 - 2][1])
                        b.rec("ACT", lambda e, ps=ps, ssl=ssl, hm=hm: e.activation(
                            out=ssl, in_=ps[:, 0:512], func=AF.Silu,
                            bias=bgc[:, hm // 8 * 0 + hm:hm + 1][:, 0:1] if False else bgc[:, hm:hm + 1]), inc=s_act)
                        sil_t = tok(s_act)
                        b.wait("PE", *sil_t)
                        for k in range(KT):
                            b.rec("PE", lambda e, k=k, m2=m2, n=n, ps=ps, wu=wu: e.matmul(
                                ps[:, 0:512], wu(k, m2 * 128, 128), xb4(k)[:, n * 512:(n + 1) * 512],
                                start=(k == 0), stop=(k == KT - 1)), inc=(s_pe if k == KT - 1 else None))
                        b.wait("DVE", s_pe)
                        b.wait_many("DVE", [sil_t, b4_t])
                        b.rec("DVE", lambda e, m=m, n=n, ps=ps, ssl=ssl, hm=hm: e.scalar_tensor_tensor(
                            guv(m)[:, n * 512:(n + 1) * 512], ps[:, 0:512], buc[:, hm:hm + 1],
                            ssl, OP.add, OP.mult), inc=s_dve)
                        ev6[g6] = tok(s_dve)
                        g6 += 1
                ws.consumed(wg_par, tok(s_dve))
                ws.consumed(wu_par, tok(s_dve))
            gu_t = tok(s_dve)
            for p in range(4):
                wd, wd_t, wd_par = ws.load(
                    di["Wd"][hg * 1024:(hg + 1) * 1024, p * 256:(p + 1) * 256], KT)
                b.wait_many("PE", [wd_t, gu_t])
                for m2 in range(2):
                    m = p * 2 + m2
                    for n in range(4):
                        ps = PSMAIN[g6 % 2]
                        prev = ev6.get(g6 - 2)
                        if prev is not None:
                            b.wait("PE", *prev)
                        for k in range(KT):
                            b.rec("PE", lambda e, k=k, m2=m2, n=n, ps=ps, wd=wd: e.matmul(
                                ps[:, 0:512], wd(k, m2 * 128, 128), guv(k)[:, n * 512:(n + 1) * 512],
                                start=(k == 0), stop=(k == KT - 1)), inc=(s_pe if k == KT - 1 else None))
                        b.wait("DVE", s_pe)
                        if hg == 0:
                            b.rec("DVE", lambda e, m=m, n=n, ps=ps: e.scalar_tensor_tensor(
                                xs(m)[:, n * 512:(n + 1) * 512], ps[:, 0:512], bdc[:, m:m + 1],
                                xs(m)[:, n * 512:(n + 1) * 512], OP.add, OP.add), inc=s_dve)
                        else:
                            b.rec("DVE", lambda e, m=m, n=n, ps=ps: e.tensor_tensor(
                                xs(m)[:, n * 512:(n + 1) * 512], xs(m)[:, n * 512:(n + 1) * 512],
                                ps[:, 0:512], OP.add), inc=s_dve)
                        ev6[g6] = tok(s_dve)
                        g6 += 1
                ws.consumed(wd_par, tok(s_dve))
        mlp_t = tok(s_dve)

        Mo = 16384
        def mv(tc):
            return ABF[:, Mo + tc * 4096:Mo + (tc + 1) * 4096]
        QF_o = 32768
        SW_o = 32768
        PB_o = 36864
        WT_o = 36864
        SCR_o = 40960
        KB_o = 43008
        kbuf = ABF[0:64, KB_o:KB_o + 512]
        b.wait("POOL", s_pe)
        b.wait("POOL", s_dve)
        dma("POOL", s_ini, kbuf, di["keysT"])
        kb_t = tok(s_ini)
        g7 = 0
        ev7 = {}
        for nch in range(4 if DO_PEER else 0):
            n0 = nch * 512
            wq, wq_t, wq_par = ws.load(di["Wq"], KT)
            b.wait_many("PE", [wq_t, sh4_t, mlp_t])
            for m in range(8):
                ps = PSMAIN[g7 % 3]
                prev = ev7.get(g7 - 3)
                if prev is not None:
                    b.wait("PE", *prev)
                for k in range(KT):
                    b.rec("PE", lambda e, k=k, m=m, n0=n0, ps=ps, wq=wq: e.matmul(
                        ps[0:64, 0:512], wq(k, m * 64, 64), xb4(k)[:, n0:n0 + 512],
                        start=(k == 0), stop=(k == KT - 1)), inc=(s_pe if k == KT - 1 else None))
                b.wait("ACT", s_pe)
                b.rec("ACT", lambda e, m=m, ps=ps: e.activation(
                    out=ABF[0:64, QF_o + m * 512:QF_o + (m + 1) * 512], in_=ps[0:64, 0:512],
                    func=AF.Copy), inc=s_act)
                ev7[g7] = tok(s_act)
                g7 += 1
            ws.consumed(wq_par, tok(s_act))
            q_t = tok(s_act)
            for tc in range(4 if sub4 not in ("peer_q",) else 0):
                sims = simsbuf[(nch * 4 + tc) % 2]
                b.wait_many("PE", [q_t, kb_t])
                if "sims_ev" in ev7:
                    b.wait("PE", *ev7["sims_ev"])
                for pp in range(2):
                    for h in range(4):
                        hp = pp * 4 + h
                        b.rec("PE", lambda e, hp=hp, tc=tc: e.matmul(
                            psC[0:128, hp * 64:(hp + 1) * 64],
                            ABF[0:64, QF_o + hp * 512 + tc * 128:QF_o + hp * 512 + (tc + 1) * 128],
                            kbuf[0:64, hp * 64:(hp + 1) * 64],
                            start=True, stop=True), inc=(s_pe if hp == 7 else None))
                b.wait("ACT", s_pe)
                if (nch * 4 + tc) >= 2:
                    b.wait_force("ACT", s_act, ev7[("mask_done", (nch * 4 + tc) - 2)][1])
                b.rec("ACT", lambda e, sims=sims: e.activation(
                    out=sims, in_=psC[0:128, 0:512], func=AF.Copy), inc=s_act)
                ev7["sims_ev"] = tok(s_act)
                sim_t = tok(s_act)
                sx8a = AF32[:, 17408:17440]
                sy8a = AF32[:, 17440:17472]
                p8fa = AF32[:, 17472:17504]
                rdna = AF32[:, 17504:17508]
                negc = AF32[:, 17512:17528]
                exall = ABF[:, SCR_o:SCR_o + 256]
                eyall = ABF[:, SCR_o + 256:SCR_o + 512]
                ex8a = ABF[:, SCR_o + 512:SCR_o + 544]
                ey8a = ABF[:, SCR_o + 544:SCR_o + 576]
                pca = ABF[:, SCR_o + 576:SCR_o + 832]
                p8a = ABF[:, SCR_o + 832:SCR_o + 864]
                if sub4 == "peer_sim":
                    continue
                b.wait_many("DVE", [sim_t])
                for h in range(4):
                    b.rec("DVE", lambda e, h=h, sims=sims: e.max(out=sx8a[:, h * 8:(h + 1) * 8],
                                                      in_=sims[:, h * 64:(h + 1) * 64]), inc=s_dve)
                    b.rec("DVE", lambda e, h=h, sims=sims: e.max(out=sy8a[:, h * 8:(h + 1) * 8],
                                                      in_=sims[:, 256 + h * 64:256 + (h + 1) * 64]), inc=s_dve)
                b.wait("DVE", s_dve)
                b.rec("DVE", lambda e: e.tensor_scalar(
                    negc[:, 0:8].rearrange("p (a o) -> p a o", o=1),
                    AF32[:, 17408:17472].rearrange("p (a c) -> p a c", c=8)[:, :, 0:1],
                    -1.0, None, OP.mult), inc=s_dve)
                st1_m = tok(s_dve)
                b.wait_many("ACT", [st1_m])
                for h in range(4):
                    b.rec("ACT", lambda e, h=h, sims=sims: e.activation(
                        out=exall[:, h * 64:(h + 1) * 64], in_=sims[:, h * 64:(h + 1) * 64],
                        func=AF.Exp, bias=negc[:, h:h + 1]), inc=s_act)
                    b.rec("ACT", lambda e, h=h, sims=sims: e.activation(
                        out=eyall[:, h * 64:(h + 1) * 64], in_=sims[:, 256 + h * 64:256 + (h + 1) * 64],
                        func=AF.Exp, bias=negc[:, 4 + h:5 + h]), inc=s_act)
                    b.rec("ACT", lambda e, h=h: e.activation(
                        out=ex8a[:, h * 8:(h + 1) * 8], in_=sx8a[:, h * 8:(h + 1) * 8],
                        func=AF.Exp, bias=negc[:, h:h + 1]), inc=s_act)
                    b.rec("ACT", lambda e, h=h: e.activation(
                        out=ey8a[:, h * 8:(h + 1) * 8], in_=sy8a[:, h * 8:(h + 1) * 8],
                        func=AF.Exp, bias=negc[:, 4 + h:5 + h]), inc=s_act)
                exp_t = tok(s_act)
                ev7[("mask_done", nch * 4 + tc)] = exp_t
                b.wait_many("DVE", [exp_t])
                for h in range(4):
                    b.rec("DVE", lambda e, h=h: e.tensor_tensor(
                        pca[:, h * 64:(h + 1) * 64].rearrange("p (i j) -> p i j", i=8),
                        ex8a[:, h * 8:(h + 1) * 8][:, :, None].broadcast_to((128, 8, 8)),
                        ey8a[:, h * 8:(h + 1) * 8][:, None, :].broadcast_to((128, 8, 8)), OP.mult), inc=s_dve)
                b.wait("DVE", s_dve)
                for h in range(4):
                    b.rec("DVE", lambda e, h=h: e.max(out=p8a[:, h * 8:(h + 1) * 8],
                                                      in_=pca[:, h * 64:(h + 1) * 64]), inc=s_dve)
                b.wait("DVE", s_dve)
                b.rec("DVE", lambda e: e.tensor_copy(p8fa, p8a), inc=s_dve)
                b.wait("DVE", s_dve)
                b.rec("DVE", lambda e: e.tensor_reduce(
                    rdna, p8fa.rearrange("p (h c) -> p h c", h=4), AX.X, OP.add), inc=s_dve)
                b.wait("DVE", s_dve)
                b.rec("DVE", lambda e: e.reciprocal(rdna, rdna), inc=s_dve)
                b.wait("DVE", s_dve)
                for h in range(4):
                    b.rec("DVE", lambda e, h=h: e.tensor_tensor(
                        p8fa[:, h * 8 + 7:h * 8 + 8], p8fa[:, h * 8 + 7:h * 8 + 8],
                        rdna[:, h:h + 1], OP.mult), inc=s_dve)
                    b.rec("DVE", lambda e, h=h: e.tensor_scalar(
                        eyall[:, h * 64:(h + 1) * 64], eyall[:, h * 64:(h + 1) * 64],
                        rdna[:, h:h + 1], None, OP.mult), inc=s_dve)
                sc_t = tok(s_dve)
                pbuf = ABF[:, PB_o:PB_o + 4096]
                b.wait("DVE", s_pe)
                for h in range(4):
                    b.wait_many("DVE", [sc_t])
                    b.wait_force("DVE", s_dve, s_dve.n)
                    b.rec("DVE", lambda e, h=h: e.tensor_tensor(
                        pbuf.rearrange("p (i j) -> p i j", i=64),
                        exall[:, h * 64:(h + 1) * 64][:, :, None].broadcast_to((128, 64, 64)),
                        eyall[:, h * 64:(h + 1) * 64][:, None, :].broadcast_to((128, 64, 64)),
                        OP.mult), inc=s_dve)
                    b.wait("DVE", s_dve)
                    b.rec("DVE", lambda e, h=h: e.scalar_tensor_tensor(
                        pbuf, pbuf, p8fa[:, h * 8 + 7:h * 8 + 8], pbuf, OP.is_ge, OP.mult), inc=s_dve)
                    b.wait("DVE", s_dve)
                    if h == 0:
                        b.rec("DVE", lambda e, tc=tc: e.tensor_copy(mv(tc), pbuf), inc=s_dve)
                    else:
                        b.rec("DVE", lambda e, tc=tc: e.tensor_tensor(
                            mv(tc), mv(tc), pbuf, OP.add), inc=s_dve)
            m_t = tok(s_dve)
            for eg in range(4 if sub4 not in ("peer_mask",) else 0):
                swg = ABF[:, SW_o:SW_o + 4096]
                for p in range(2):
                    dv, dv_t, dv_par = ws.load(
                        di["deT"][:, eg * 1024 + p * 512:eg * 1024 + (p + 1) * 512], KT)
                    b.wait_many("PE", [dv_t, sh4_t])
                    for m2 in range(4):
                        met = p * 4 + m2
                        ps = PSMAIN[g7 % 3]
                        prev = ev7.get(g7 - 3)
                        if prev is not None:
                            b.wait("PE", *prev)
                        for k in range(KT):
                            b.rec("PE", lambda e, k=k, m2=m2, n0=n0, ps=ps, dv=dv: e.matmul(
                                ps[:, 0:512], dv(k, m2 * 128, 128), xb4(k)[:, n0:n0 + 512],
                                start=(k == 0), stop=(k == KT - 1)), inc=(s_pe if k == KT - 1 else None))
                        b.wait("ACT", s_pe)
                        b.wait("ACT", s_pool)
                        b.rec("ACT", lambda e, met=met, ps=ps: e.activation(
                            out=swg[:, met * 512:(met + 1) * 512], in_=ps[:, 0:512],
                            func=AF.Silu), inc=s_act)
                        ev7[g7] = tok(s_act)
                        g7 += 1
                    ws.consumed(dv_par, tok(s_act))
                sw_t = tok(s_act)
                tpi = 0
                tp_ev = {}
                for et in range(8):
                    for tc in range(4):
                        slot = tpi % 2
                        b.wait_many("PE", [m_t, const_tb])
                        if tpi >= 2:
                            b.wait("PE", s_dve, tp_ev[tpi - 2])
                        b.rec("PE", lambda e, et=et, tc=tc, slot=slot: e.transpose(
                            psT[:, slot * 1024:slot * 1024 + 128],
                            mv(tc)[:, eg * 1024 + et * 128:eg * 1024 + (et + 1) * 128], eye_bc),
                            inc=s_pe)
                        b.wait("DVE", s_pe)
                        b.wait_many("DVE", [sw_t])
                        b.rec("DVE", lambda e, et=et, tc=tc, slot=slot: e.tensor_tensor(
                            ABF[:, WT_o + et * 512 + tc * 128:WT_o + et * 512 + (tc + 1) * 128],
                            psT[:, slot * 1024:slot * 1024 + 128],
                            swg[:, et * 512 + tc * 128:et * 512 + (tc + 1) * 128], OP.mult), inc=s_dve)
                        tp_ev[tpi] = s_dve.n
                        tpi += 1
                wt_t = tok(s_dve)
                for p in range(4 if sub4 not in ("peer_sw",) else 0):
                    uv, uv_t, uv_par = ws.load(
                        di["ue"][eg * 1024:(eg + 1) * 1024, p * 256:(p + 1) * 256], KT)
                    b.wait_many("PE", [uv_t, wt_t])
                    for m2 in range(2):
                        m = p * 2 + m2
                        ps = PSMAIN[g7 % 3]
                        prev = ev7.get(g7 - 3)
                        if prev is not None:
                            b.wait("PE", *prev)
                        for k in range(KT):
                            b.rec("PE", lambda e, k=k, m2=m2, ps=ps, uv=uv: e.matmul(
                                ps[:, 0:512], uv(k, m2 * 128, 128),
                                ABF[:, WT_o + k * 512:WT_o + (k + 1) * 512],
                                start=(k == 0), stop=(k == KT - 1)), inc=(s_pe if k == KT - 1 else None))
                        b.wait("DVE", s_pe)
                        b.rec("DVE", lambda e, m=m, n0=n0, ps=ps: e.tensor_tensor(
                            xs(m)[:, n0:n0 + 512], xs(m)[:, n0:n0 + 512], ps[:, 0:512], OP.add), inc=s_dve)
                        ev7[g7] = tok(s_dve)
                        g7 += 1
                    ws.consumed(uv_par, tok(s_dve))
        peer_t = tok(s_dve)
        tap("res4", AF32[:, 0:16384], [peer_t])
        st4_t = post_ln(peer_t)
        last_t = st4_t


    b.wait_many("ACT", [last_t])
    for k in range(KT):
        dma("ACT", s_sc, y_out[k * 128:(k + 1) * 128, :], xs(k))
    b.wait("ACT", s_sc)
    b.wait("SP", s_sc)

    b.play()
    return nc


def prep_inputs(inputs, core):
    ci = {}
    bidx, half = core // 2, core % 2
    x = np.asarray(inputs["L0_attn_x"])
    sl = x[bidx, half * T:(half + 1) * T, :]
    ci["x_tf"] = np.ascontiguousarray(sl)
    ci["x_ft"] = np.ascontiguousarray(sl.T)

    def coltile(v, n):
        return np.ascontiguousarray(np.asarray(v).reshape(n, 128).T)

    ci["WbT"] = np.ascontiguousarray(np.asarray(inputs["L0_attn_Wb"]).T)
    ci["WpT"] = np.ascontiguousarray(np.asarray(inputs["L0_attn_Wp"]).T)
    ci["initBT"] = np.ascontiguousarray(np.asarray(inputs["L0_attn_init_B"]).T)
    ci["initA"] = np.ascontiguousarray(np.asarray(inputs["L0_attn_init_A"]))
    ci["initAT"] = np.ascontiguousarray(np.asarray(inputs["L0_attn_init_A"]).T)
    ci["initB"] = np.ascontiguousarray(np.asarray(inputs["L0_attn_init_B"]))
    ci["g0"] = coltile(inputs["L0_attn_ln_gamma"], KT)
    ci["be0"] = coltile(inputs["L0_attn_ln_beta"], KT)
    ci["routerT"] = np.ascontiguousarray(np.asarray(inputs["L0_ffn_router_w"]).T)
    ci["w1"] = np.asarray(inputs["L0_ffn_w1"])
    ci["w2"] = np.asarray(inputs["L0_ffn_w2"])
    ci["n1w"] = coltile(inputs["L1_attn_norm1_w"], KT)
    ci["n2w"] = coltile(inputs["L1_attn_norm2_w"], KT)
    ci["ipw"] = np.asarray(inputs["L1_attn_in_proj_w"])
    ci["ipb"] = coltile(inputs["L1_attn_in_proj_b"], 32)
    ci["convw"] = np.ascontiguousarray(
        np.asarray(inputs["L1_attn_conv_w"]).reshape(2048, 3).reshape(16, 128, 3).transpose(1, 0, 2).reshape(128, 48))
    ci["convb"] = coltile(inputs["L1_attn_conv_b"], 16)
    ci["dtw"] = np.asarray(inputs["L1_attn_dt_w"])
    ci["dtb"] = np.asarray(inputs["L1_attn_dt_b"]).reshape(64, 1)
    ci["bpw"] = np.asarray(inputs["L1_attn_bp_w"])
    ci["bpb"] = np.asarray(inputs["L1_attn_bp_b"]).reshape(64, 1)
    ci["cpw"] = np.asarray(inputs["L1_attn_cp_w"])
    ci["cpb"] = np.asarray(inputs["L1_attn_cp_b"]).reshape(64, 1)
    ci["s2iw"] = np.asarray(inputs["L1_attn_s2i_w"])
    ci["s2ib"] = coltile(inputs["L1_attn_s2i_b"], 16)
    ci["Dpar"] = coltile(inputs["L1_attn_D_param"], 16)
    ci["outw"] = np.asarray(inputs["L1_attn_out_w"])
    ci["outb"] = coltile(inputs["L1_attn_out_b"], KT)
    ci["gatew"] = np.asarray(inputs["L1_attn_gate_w"])
    ci["gateb"] = np.asarray(inputs["L1_attn_gate_b"]).reshape(4, 1)
    ci["gateb_bc"] = np.tile(np.asarray(inputs["L1_attn_gate_b"]).reshape(1, 4), (128, 1))
    ci["eb2all"] = np.asarray(inputs["L1_attn_e_b2"]).astype(BF16NP)
    ci["ew1"] = np.asarray(inputs["L1_attn_e_w1"])
    ci["eb1"] = np.stack([coltile(np.asarray(inputs["L1_attn_e_b1"])[e], 32) for e in range(4)])
    ci["ew2"] = np.asarray(inputs["L1_attn_e_w2"])
    ci["eb2"] = np.stack([coltile(np.asarray(inputs["L1_attn_e_b2"])[e], KT) for e in range(4)])
    ci["Wg"] = np.asarray(inputs["L1_ffn_Wg"])
    ci["bg"] = coltile(inputs["L1_ffn_bg"], 32)
    ci["Wu"] = np.asarray(inputs["L1_ffn_Wu"])
    ci["bu"] = coltile(inputs["L1_ffn_bu"], 32)
    ci["Wd"] = np.asarray(inputs["L1_ffn_Wd"])
    ci["bd"] = coltile(inputs["L1_ffn_bd"], KT)
    ci["Wq"] = np.asarray(inputs["L1_ffn_Wq"])
    kT = np.ascontiguousarray(
        np.asarray(inputs["L1_ffn_keys"]).transpose(3, 2, 0, 1).reshape(64, 2 * 4 * 64))
    ci["keysT"] = kT.astype(BF16NP)
    ci["deT"] = np.ascontiguousarray(np.asarray(inputs["L1_ffn_down_embed"]).T)
    ci["ue"] = np.asarray(inputs["L1_ffn_up_embed"])
    ci["eye_b"] = np.eye(128, dtype=np.float32).astype(BF16NP)
    ci["eye_f"] = np.eye(128, dtype=np.float32)
    sel = np.zeros((8, 1024), np.float32)
    for e_ in range(8):
        sel[e_, e_ * 128:(e_ + 1) * 128] = 1.0
    ci["sel8"] = sel
    ci["ones_col"] = np.ones((128, 1), np.float32)
    ci["ones_row"] = np.ones((1, 128), np.float32)
    ci["constv"] = np.tile(np.array([0.0, 1.0, 2.0, 1e-5, 1e-6, -1.0, 0.5, -0.5], np.float32), (128, 1))
    mp = np.zeros((8, 1), np.float32)
    mn = np.zeros((8, 1), np.float32)
    if half == 1:
        mp[2 * bidx] = 1.0
    if half == 0:
        mn[2 * bidx + 1] = 1.0
    ci["mask_prev"] = mp
    ci["mask_next_b"] = mn.astype(BF16NP)
    for k in list(ci):
        v = ci[k]
        if v.dtype not in (np.float32, BF16NP):
            ci[k] = np.ascontiguousarray(v, dtype=np.float32)
    return ci


def kernel(**inputs):
    nc = build_program(n_stages=4)
    in_maps = [prep_inputs(inputs, c) for c in range(8)]
    res = run_bass_kernel_spmd(nc, in_maps, core_ids=list(range(8)))
    out = np.empty((4, 4096, D), np.float32)
    for c in range(8):
        bidx, half = c // 2, c % 2
        out[bidx, half * T:(half + 1) * T, :] = res.results[c]["y"].T
    return out
